# version 02
# baseline (speedup 1.0000x reference)
import numpy as np
from contextlib import ExitStack

import concourse.bass as bass
import concourse.mybir as mybir
from concourse.alu_op_type import AluOpType
from concourse.bass_utils import run_bass_kernel_spmd

F32 = mybir.dt.float32
BF16 = mybir.dt.bfloat16
I32 = mybir.dt.int32
AF = mybir.ActivationFunctionType
OP = AluOpType
AX = mybir.AxisListType.X

NCORES = 8
GROUPS = [[0, 1, 2, 3], [4, 5, 6, 7]]
B, S, D = 2, 8192, 1024
TLOC = S // 4
NTT = TLOC // 128
NDC = D // 128
EPS = 1e-5

DEBUG = []
SKIP_CC = False



class Prog:
    ENGINES = ("tensor", "vector", "scalar", "gpsimd", "sync")

    def __init__(self, nc, es):
        self.nc = nc
        self.es = es
        self.items = {e: [] for e in self.ENGINES}
        self.sems = {}
        self.counts = {}
        self.watermark = {}

    def sem(self, name):
        if name not in self.sems:
            self.sems[name] = self.es.enter_context(self.nc.semaphore(name))
            self.counts[name] = 0
        return self.sems[name]

    def op(self, eng, fn, inc=None, inc_n=1):
        total = None
        if inc is not None:
            self.sem(inc)
            self.counts[inc] += inc_n
            total = self.counts[inc]
            h = self.sems[inc]

            def run(e, fn=fn, h=h, n=inc_n):
                fn(e).then_inc(h, n)
        else:
            def run(e, fn=fn):
                fn(e)
        self.items[eng].append(run)
        return total

    def wait(self, eng, sem_name, value=None):
        v = self.counts.get(sem_name, 0) if value is None else value
        if v <= 0:
            return
        key = (eng, sem_name)
        if self.watermark.get(key, 0) >= v:
            return
        self.watermark[key] = v
        h = self.sem(sem_name)
        self.items[eng].append(lambda e, h=h, v=v: e.wait_ge(h, v))

    def drain(self, eng):
        self.items[eng].append(lambda e: e.drain())

    def dma(self, queue, out, in_, sem_name, **kw):
        return self.op(queue, lambda e: e.dma_start(out=out, in_=in_, **kw),
                       inc=sem_name, inc_n=16)

    def build(self):
        with self.nc.Block() as block:
            for eng in self.ENGINES:
                items = self.items[eng]
                if not items:
                    continue

                def make(items):
                    def body(e):
                        for it in items:
                            it(e)
                    return body
                getattr(block, eng)(make(items))


def mk_ap(base, extra_off, dims):
    return bass.AP(tensor=base.tensor, offset=base.offset + extra_off, ap=dims)



def _pack_cols(v, p=128):
    v = np.ascontiguousarray(v, dtype=np.float32)
    return np.ascontiguousarray(v.reshape(-1, p).T)


def _sign_tile():
    s = np.ones((128, 512), np.float32)
    s[:, 1::2] = -1.0
    return s


def _rope_tables(t0, n, dmodel=1024):
    base = 10000.0
    half = dmodel // 2
    pos = np.arange(t0, t0 + n, dtype=np.float64)[:, None]
    ids = np.arange(half, dtype=np.float32)
    theta = ((np.float32(base) ** (-2.0 * ids)).astype(np.float32) / np.float32(half))
    r = (pos * theta[None, :].astype(np.float64)).astype(np.float32)
    cos, sin = np.cos(r), np.sin(r)
    c2 = np.empty((n, dmodel), np.float32)
    s2 = np.empty((n, dmodel), np.float32)
    c2[:, 0::2] = cos
    c2[:, 1::2] = cos
    s2[:, 0::2] = -sin
    s2[:, 1::2] = sin
    return c2, s2


def prep_core_inputs(core, inputs, layers=(0, 1, 2, 3)):
    b = core // 4
    r = core % 4
    t0 = r * TLOC
    g = lambda k: np.asarray(inputs[k], dtype=np.float32)
    d = {}
    d["x_in"] = np.ascontiguousarray(g("L0_attn_x")[b, t0:t0 + TLOC, :])
    d["c_ident"] = np.eye(128, dtype=np.float32)
    d["c_sign"] = _sign_tile()
    if 0 in layers:
        d["l0_wq"] = np.ascontiguousarray(g("L0_attn_W_Q").transpose(1, 0, 2).reshape(D, D))
        d["l0_wk"] = np.ascontiguousarray(g("L0_attn_W_K").transpose(1, 0, 2).reshape(D, D))
        d["l0_wv"] = np.ascontiguousarray(g("L0_attn_W_V").transpose(1, 0, 2).reshape(D, D))
        d["l0_gw"] = np.ascontiguousarray(g("L0_ffn_gate_w").T)
        d["l0_w1"] = np.ascontiguousarray(g("L0_ffn_w1"))
        d["l0_b1"] = np.ascontiguousarray(
            g("L0_ffn_b1").reshape(14, 16, 128).transpose(2, 0, 1).reshape(128, 224))
        d["l0_w2"] = np.ascontiguousarray(g("L0_ffn_w2"))
        d["l0_b2"] = np.ascontiguousarray(g("L0_ffn_b2"))
    if 1 in layers:
        d["l1_wqT"] = np.ascontiguousarray(g("L1_attn_wq").T)
        d["l1_wkT"] = np.ascontiguousarray(g("L1_attn_wk").T)
        d["l1_wvT"] = np.ascontiguousarray(g("L1_attn_wv").T)
        d["l1_bq"] = _pack_cols(g("L1_attn_bq"))
        d["l1_bk"] = _pack_cols(g("L1_attn_bk"))
        d["l1_bv"] = np.ascontiguousarray(g("L1_attn_bv").reshape(1, D))
        d["l1_g1"] = _pack_cols(g("L1_ffn_g1"))
        d["l1_be1"] = _pack_cols(g("L1_ffn_be1"))
        d["l1_g2"] = _pack_cols(g("L1_ffn_g2"))
        d["l1_be2"] = _pack_cols(g("L1_ffn_be2"))
        d["l1_mixw"] = np.ascontiguousarray(g("L1_ffn_mix_w"))
        d["l1_mixb"] = np.ascontiguousarray(g("L1_ffn_mix_b").reshape(1, D))
        d["l1_rw"] = np.ascontiguousarray(g("L1_ffn_router_w"))
        d["l1_w1"] = np.ascontiguousarray(g("L1_ffn_w1"))
        d["l1_b1"] = np.ascontiguousarray(
            g("L1_ffn_b1").reshape(8, 32, 128).transpose(2, 0, 1).reshape(128, 256))
        d["l1_w2"] = np.ascontiguousarray(g("L1_ffn_w2"))
        d["l1_b2"] = np.ascontiguousarray(g("L1_ffn_b2"))
    if 2 in layers:
        chs = slice(512 * r, 512 * (r + 1))
        d["l2_win"] = np.ascontiguousarray(g("L2_attn_W_in"))
        cw = g("L2_attn_conv_w")[:, 0, :]
        d["l2_cw"] = np.ascontiguousarray(
            cw[chs].reshape(4, 128, 4).transpose(1, 0, 2).reshape(128, 16))
        d["l2_cb"] = _pack_cols(g("L2_attn_conv_b")[chs])
        al = g("L2_attn_A_log")[:, chs]
        d["l2_alog"] = np.ascontiguousarray(
            al.T.reshape(4, 128, 16).transpose(1, 0, 2).reshape(128, 64))
        d["l2_dp"] = _pack_cols(g("L2_attn_D_param")[chs])
        d["l2_wout"] = np.ascontiguousarray(g("L2_attn_W_out")[chs])
        d["l2_wg"] = np.ascontiguousarray(np.asarray(inputs["L2_ffn_w_gate"]).T.astype(np.int32))
        d["l2_wu"] = np.ascontiguousarray(np.asarray(inputs["L2_ffn_w_up"]).T.astype(np.int32))
        d["l2_wd"] = np.ascontiguousarray(np.asarray(inputs["L2_ffn_w_down"]).T).astype(np.int32)
    if 3 in layers:
        d["l3_wdq"] = np.ascontiguousarray(g("L3_attn_Wd_q_w"))
        d["l3_bdq"] = _pack_cols(g("L3_attn_Wd_q_b"))
        d["l3_wuq"] = np.ascontiguousarray(g("L3_attn_Wu_q_w"))
        d["l3_buq"] = _pack_cols(g("L3_attn_Wu_q_b"))
        d["l3_wqr"] = np.ascontiguousarray(g("L3_attn_Wq_r_w"))
        d["l3_bqr"] = np.ascontiguousarray(g("L3_attn_Wq_r_b").reshape(1, D))
        d["l3_wkr"] = np.ascontiguousarray(g("L3_attn_Wk_r_w"))
        d["l3_bkr"] = np.ascontiguousarray(g("L3_attn_Wk_r_b").reshape(1, D))
        d["l3_wdkv"] = np.ascontiguousarray(g("L3_attn_Wd_kv_w"))
        d["l3_bdkv"] = _pack_cols(g("L3_attn_Wd_kv_b"))
        d["l3_wuk"] = np.ascontiguousarray(g("L3_attn_Wu_k_w"))
        d["l3_buk"] = _pack_cols(g("L3_attn_Wu_k_b"))
        d["l3_wuv"] = np.ascontiguousarray(g("L3_attn_Wu_v_w"))
        d["l3_buv"] = np.ascontiguousarray(g("L3_attn_Wu_v_b").reshape(1, D))
        d["l3_wo"] = np.ascontiguousarray(g("L3_attn_Wo_w"))
        d["l3_bo"] = np.ascontiguousarray(g("L3_attn_Wo_b").reshape(1, D))
        d["l3_gw"] = np.ascontiguousarray(g("L3_ffn_gate_w"))
        d["l3_gb"] = np.ascontiguousarray(g("L3_ffn_gate_b").reshape(1, 8))
        d["l3_ew"] = np.ascontiguousarray(g("L3_ffn_expert_w"))
        d["l3_eb"] = np.ascontiguousarray(g("L3_ffn_expert_b"))
        c2, s2 = _rope_tables(t0, TLOC)
        d["l3_cos"] = c2
        d["l3_sin"] = s2
    return d




class Tens:
    pass


def alloc_common(nc, es, layers):
    T = Tens()
    T.x_in = nc.dram_tensor("x_in", [TLOC, D], F32, kind="ExternalInput").ap()
    T.x_out = nc.dram_tensor("x_out", [TLOC, D], F32, kind="ExternalOutput").ap()
    T.c_ident = nc.dram_tensor("c_ident", [128, 128], F32, kind="ExternalInput").ap()
    T.c_sign = nc.dram_tensor("c_sign", [128, 512], F32, kind="ExternalInput").ap()

    sb = lambda nm, shp, dt: es.enter_context(nc.sbuf_tensor(nm, shp, dt))
    T.x_res = sb("x_res", [128, NTT * D], F32)
    T.xn_fm = sb("xn_fm", [128, NDC * TLOC], BF16)
    T.ar1 = sb("ar1", [128, NTT * D], F32)
    T.ar1b = T.ar1[:].bitcast(BF16)
    T.ar2 = sb("ar2", [128, 16384], BF16)
    T.scratch = sb("scratch", [128, 1024], F32)
    T.kvstage = sb("kvstage", [128, 1536], BF16)
    T.b1_sb = sb("b1_sb", [128, 256], F32)
    T.b14 = sb("b14", [14, 512], BF16)
    T.ident_b = sb("ident_b", [128, 128], BF16)
    T.ident_f = sb("ident_f", [128, 128], F32)
    T.sign_sb = sb("sign_sb", [128, 512], BF16) if 2 in layers else None
    T.stats = sb("stats", [128, 64], F32)
    T.small = sb("small", [128, 16], F32)
    T.w_all = sb("w_all", [128, 224], F32)
    T.wT16 = sb("wT16", [16, 128], BF16)
    T.ones1 = sb("ones1", [1, 128], BF16)
    T.ones1f = sb("ones1f", [1, 128], F32)
    T.ones128 = sb("ones128", [128, 1], BF16)
    T.gw_sb = sb("gw_sb", [128, 112], F32)

    T.ps = [es.enter_context(nc.psum_tensor(f"ps{i}", [128, 512], F32)) for i in range(6)]
    T.pst = es.enter_context(nc.psum_tensor("pst", [128, 1024], BF16))
    T.psm = es.enter_context(nc.psum_tensor("psm", [128, 512], F32))

    T.g_of = lambda jt, c0, n: T.ar1b[:, 1024 * jt + c0: 1024 * jt + c0 + n]
    T.kbuf = T.xn_fm[:, 0:8192]
    T.v65 = T.xn_fm[:, 8192:10272]
    T.vplain = T.xn_fm[:, 10272:11296]
    T.probs = T.xn_fm[:, 11296:13344]
    T.qw = T.ar1b[:, 16384:24576]
    T.tmp64 = T.ar1[:, 12288:12352]
    T.avr = T.ar1[:, 12352:12864]
    T.atmp = T.ar1[:, 12864:13888]
    T.nstage = sb("nstage", [1, 1024], BF16)
    T.boundt = T.small[:, 15:16]
    T.epst = T.small[:, 14:15]
    T.bkp = sb("bkp", [128, 8], F32)
    T.bqp = sb("bqp", [128, 8], F32)
    T.g1p = sb("g1p", [128, 8], F32)
    T.be1p = sb("be1p", [128, 8], F32)
    T.g2p = sb("g2p", [128, 8], F32)
    T.be2p = sb("be2p", [128, 8], F32)
    return T


def alloc_l0(nc, T):
    T.l0_wq = nc.dram_tensor("l0_wq", [D, D], F32, kind="ExternalInput").ap()
    T.l0_wk = nc.dram_tensor("l0_wk", [D, D], F32, kind="ExternalInput").ap()
    T.l0_wv = nc.dram_tensor("l0_wv", [D, D], F32, kind="ExternalInput").ap()
    T.l0_gw = nc.dram_tensor("l0_gw", [D, 14], F32, kind="ExternalInput").ap()
    T.l0_w1 = nc.dram_tensor("l0_w1", [14, D, 2048], F32, kind="ExternalInput").ap()
    T.l0_b1 = nc.dram_tensor("l0_b1", [128, 224], F32, kind="ExternalInput").ap()
    T.l0_w2 = nc.dram_tensor("l0_w2", [14, 2048, D], F32, kind="ExternalInput").ap()
    T.l0_b2 = nc.dram_tensor("l0_b2", [14, D], F32, kind="ExternalInput").ap()
    T.l0_kin = nc.dram_tensor("l0_kin", [1025, TLOC], BF16).ap()
    T.l0_kout = nc.dram_tensor("l0_kout", [4 * 1025, TLOC], BF16).ap()
    T.l0_vin = nc.dram_tensor("l0_vin", [TLOC, D], BF16).ap()
    T.l0_vout = nc.dram_tensor("l0_vout", [4 * TLOC, D], BF16).ap()



def emit_setup(P, T):
    for j in range(NTT):
        P.dma("sync", T.x_res[:, D * j:D * (j + 1)], T.x_in[128 * j:128 * (j + 1), :], "d_x")
    P.dma("sync", T.ident_f[:], T.c_ident, "d_x")
    P.dma("gpsimd", T.ident_b[:], T.c_ident, "d_g")
    if T.sign_sb is not None:
        P.dma("gpsimd", T.sign_sb[:], T.c_sign, "d_g")
    P.wait("vector", "d_x")
    P.wait("vector", "d_g")
    P.op("vector", lambda v: v.memset(T.ones1[:], 1.0))
    P.op("vector", lambda v: v.memset(T.ones1f[:], 1.0))
    P.op("vector", lambda v: v.memset(T.ones128[:], 1.0))
    P.op("vector", lambda v: v.memset(T.epst, EPS))
    P.op("vector", lambda v: v.memset(T.avr[:], 0.0), inc="v_setup")


def emit_ln(P, T, tag=""):
    P.wait("vector", "d_x")
    P.wait("scalar", "d_x")
    P.wait("tensor", "v_setup")
    P.wait("tensor", "d_g")
    for j in range(NTT):
        xt = T.x_res[:, D * j:D * (j + 1)]
        m = T.stats[:, j:j + 1]
        rs = T.stats[:, 16 + j:16 + j + 1]
        sums = T.small[:, 0:1]
        ssq = T.small[:, 1:2]
        P.op("vector", lambda v, xt=xt, s=sums: v.reduce_sum(s, xt, axis=AX))
        P.drain("vector")
        P.op("vector", lambda v, m=m, s=sums: v.tensor_scalar(m, s, 1.0 / D, None, OP.mult))
        P.drain("vector")
        P.op("vector", lambda v, xt=xt, m=m: v.tensor_scalar(
            T.scratch[:], xt, m, None, OP.subtract), inc="v_ln")
        P.wait("scalar", "v_ln")
        P.op("scalar", lambda sc, ssq=ssq: sc.activation(
            T.ar1[:, 0:1024], T.scratch[:], AF.Square, accum_out=ssq))
        P.drain("scalar")
        P.op("scalar", lambda sc, ssq=ssq: sc.activation(
            T.small[:, 2:3], ssq, AF.Ln, bias=T.epst, scale=1.0 / D))
        P.drain("scalar")
        P.op("scalar", lambda sc, rs=rs: sc.activation(
            rs, T.small[:, 2:3], AF.Exp, scale=-0.5), inc="s_ln")
        P.wait("vector", "s_ln")
        P.op("vector", lambda v, rs=rs: v.tensor_scalar(
            T.kvstage[:, 0:1024], T.scratch[:], rs, None, OP.mult), inc="v_ln")
        P.wait("tensor", "v_ln")
        for c in range(NDC):
            P.op("tensor", lambda t, c=c: t.transpose(
                T.pst[:, 128 * c:128 * (c + 1)],
                T.kvstage[:, 128 * c:128 * (c + 1)], T.ident_b[:]),
                inc="t_ln" if c == NDC - 1 else None)
        P.wait("vector", "t_ln")
        for c in range(NDC):
            P.op("vector", lambda v, c=c, j=j: v.tensor_copy(
                T.xn_fm[:, TLOC * c + 128 * j:TLOC * c + 128 * (j + 1)],
                T.pst[:, 128 * c:128 * (c + 1)]),
                inc="v_xn" if c == NDC - 1 else None)
        P.wait("tensor", "v_xn")


def load_w_chunks(P, T, dst, src, n_k, cols, sem):
    for c in range(n_k):
        P.dma("gpsimd", dst[:, cols * c:cols * (c + 1)], src[128 * c:128 * (c + 1), :], sem)


def emit_l0_kv(P, T):
    load_w_chunks(P, T, T.qw, T.l0_wk, 8, 1024, "d_wkv")
    P.wait("tensor", "d_wkv")
    P.wait("tensor", "v_xn")
    slot_free = [("v_setup", 0), ("v_setup", 0)]
    first_nrm = True
    grp = 0
    for g in range(8):
        for tg in range(4):
            bank = T.ps[grp % 2]
            if grp >= 2:
                P.wait("tensor", "v_kv", grp - 1)
            for c in range(NDC):
                P.op("tensor", lambda t, b=bank, c=c, g=g, tg=tg: t.matmul(
                    b[:, 0:512],
                    T.qw[:, 1024 * c + 128 * g:1024 * c + 128 * g + 128],
                    T.xn_fm[:, TLOC * c + 512 * tg:TLOC * c + 512 * (tg + 1)],
                    start=(c == 0), stop=(c == NDC - 1)),
                    inc="t_kv" if c == NDC - 1 else None)
            slot = grp % 2
            stg = T.kvstage[:, 512 * slot:512 * (slot + 1)]
            P.wait("vector", "t_kv")
            P.wait("vector", *slot_free[slot])
            P.op("vector", lambda v, b=bank, stg=stg: v.tensor_copy(stg, b[:, 0:512]),
                 inc="v_kv")
            P.wait("scalar", "v_kv")
            P.wait("scalar", "t_nrm", P.counts.get("t_nrm", 0))
            P.op("scalar", lambda sc, stg=stg: sc.activation(
                T.kvstage[:, 1024:1536], stg, AF.Square), inc="s_kv")
            P.wait("tensor", "s_kv")
            P.wait("tensor", "v_nrm", P.counts.get("v_nrm", 0))
            P.op("tensor", lambda t: t.matmul(
                T.psm[0:1, 0:512], T.ones128[:], T.kvstage[:, 1024:1536],
                start=True, stop=True), inc="t_nrm")
            P.wait("vector", "t_nrm")
            P.op("vector", lambda v: v.reduce_max(T.small[0:1, 3:4], T.psm[0:1, 0:512], axis=AX))
            P.drain("vector")
            if first_nrm:
                P.op("vector", lambda v: v.tensor_copy(T.small[0:1, 4:5], T.small[0:1, 3:4]),
                     inc="v_nrm")
                first_nrm = False
            else:
                P.op("vector", lambda v: v.tensor_tensor(
                    T.small[0:1, 4:5], T.small[0:1, 4:5], T.small[0:1, 3:4], OP.max),
                     inc="v_nrm")
            P.drain("vector")
            P.wait("sync", "v_kv")
            n = P.dma("sync", T.l0_kin[128 * g:128 * (g + 1), 512 * tg:512 * (tg + 1)], stg,
                      f"d_kst{slot}")
            slot_free[slot] = (f"d_kst{slot}", n)
            grp += 1
    P.wait("vector", "v_nrm")
    P.op("vector", lambda v: v.tensor_copy(T.nstage[0:1, 0:1], T.small[0:1, 4:5]), inc="v_kmax")
    P.wait("sync", "v_kmax")
    P.dma("sync", mk_ap(T.l0_kin, 1024 * TLOC, [[1, 1], [1, 1]]), T.nstage[0:1, 0:1], "d_nst")
    P.wait("gpsimd", "t_kv", P.counts["t_kv"])
    load_w_chunks(P, T, T.qw, T.l0_wv, 8, 1024, "d_wkv")
    P.wait("tensor", "d_wkv")
    for j in range(NTT):
        for nc2 in range(2):
            bank = T.ps[grp % 2]
            P.wait("tensor", "v_kv", grp - 1)
            for c in range(NDC):
                P.op("tensor", lambda t, b=bank, c=c, j=j, n=nc2: t.matmul(
                    b[:, 0:512],
                    T.xn_fm[:, TLOC * c + 128 * j:TLOC * c + 128 * (j + 1)],
                    T.qw[:, 1024 * c + 512 * n:1024 * c + 512 * (n + 1)],
                    start=(c == 0), stop=(c == NDC - 1)),
                    inc="t_kv" if c == NDC - 1 else None)
            slot = grp % 2
            stg = T.kvstage[:, 512 * slot:512 * (slot + 1)]
            P.wait("vector", "t_kv")
            P.wait("vector", *slot_free[slot])
            P.op("vector", lambda v, b=bank, stg=stg: v.tensor_copy(stg, b[:, 0:512]),
                 inc="v_kv")
            P.wait("sync", "v_kv")
            n = P.dma("sync", T.l0_vin[128 * j:128 * (j + 1), 512 * nc2:512 * (nc2 + 1)], stg,
                      f"d_kst{slot}")
            slot_free[slot] = (f"d_kst{slot}", n)
            grp += 1
    P.wait("gpsimd", "d_kst0")
    P.wait("gpsimd", "d_kst1")
    P.wait("gpsimd", "d_nst")
    if SKIP_CC:
        P.counts["cc"] = P.counts.get("cc", 0) + 1
        P.sem("cc")
        P.op("gpsimd", lambda g: g.nop(), inc=None)
        P.counts["cc"] -= 1
        P.op("gpsimd", lambda g: g.nop(), inc="cc")
    else:
        P.op("gpsimd", lambda g: g.collective_compute(
            "AllGather", OP.bypass, replica_groups=GROUPS,
            ins=[T.l0_kin.opt()], outs=[T.l0_kout.opt()]), inc="cc")
    if SKIP_CC:
        P.counts["cc"] = P.counts.get("cc", 0) + 1
        P.sem("cc")
        P.op("gpsimd", lambda g: g.nop(), inc=None)
        P.counts["cc"] -= 1
        P.op("gpsimd", lambda g: g.nop(), inc="cc")
    else:
        P.op("gpsimd", lambda g: g.collective_compute(
            "AllGather", OP.bypass, replica_groups=GROUPS,
            ins=[T.l0_vin.opt()], outs=[T.l0_vout.opt()]), inc="cc")


def emit_l0_gate(P, T):
    for c in range(NDC):
        P.dma("sync", T.gw_sb[:, 14 * c:14 * (c + 1)], T.l0_gw[128 * c:128 * (c + 1), :],
              "d_gw")
    P.wait("tensor", "d_gw")
    P.wait("tensor", "v_xn")
    P.wait("tensor", "v_nrm")
    P.wait("vector", "d_x")
    for j in range(NTT):
        m = T.stats[:, j:j + 1]
        rs = T.stats[:, 16 + j:16 + j + 1]
        P.wait("vector", "s_ln", P.counts.get("s_ln", 0))
        P.wait("vector", "t_gxp", P.counts.get("t_gxp", 0))
        P.drain("vector")
        P.op("vector", lambda v, j=j, m=m, rs=rs: v.tensor_scalar(
            T.scratch[:], T.x_res[:, 1024 * j:1024 * (j + 1)], m, rs,
            OP.subtract, OP.mult), inc="v_gxn")
        P.wait("tensor", "v_gxn")
        P.wait("tensor", "v_kv", P.counts.get("v_kv", 0))
        for c in range(NDC):
            P.op("tensor", lambda t, c=c: t.transpose(
                T.ps[c // 4][:, 128 * (c % 4):128 * (c % 4) + 128],
                T.scratch[:, 128 * c:128 * (c + 1)], T.ident_f[:]),
                inc="t_gxp" if c == NDC - 1 else None)
        P.wait("vector", "t_gxp")
        for c in range(NDC):
            P.op("vector", lambda v, c=c: v.tensor_copy(
                T.atmp[:, 128 * c:128 * (c + 1)],
                T.ps[c // 4][:, 128 * (c % 4):128 * (c % 4) + 128]),
                inc="v_gxt" if c == NDC - 1 else None)
        P.wait("tensor", "s_gate", P.counts.get("s_gate", 0))
        P.wait("tensor", "v_gxt")
        for c in range(NDC):
            P.op("tensor", lambda t, c=c, j=j: t.matmul(
                T.psm[:, 0:14],
                T.atmp[:, 128 * c:128 * (c + 1)],
                T.gw_sb[:, 14 * c:14 * (c + 1)],
                start=(c == 0), stop=(c == NDC - 1)),
                inc="t_gate" if c == NDC - 1 else None)
        w = T.w_all[:, 14 * j:14 * (j + 1)]
        e_t = T.scratch[:, 0:14]
        P.wait("vector", "t_gate")
        P.op("vector", lambda v: v.reduce_max(T.small[:, 0:1], T.psm[:, 0:14], axis=AX))
        P.drain("vector")
        P.op("vector", lambda v: v.tensor_scalar(
            T.small[:, 1:2], T.small[:, 0:1], -1.0, None, OP.mult), inc="v_gate")
        P.wait("scalar", "v_gate")
        P.op("scalar", lambda sc, e_t=e_t: sc.activation(
            e_t, T.psm[:, 0:14], AF.Exp, bias=T.small[:, 1:2]), inc="s_gate")
        P.wait("vector", "s_gate")
        P.op("vector", lambda v, e_t=e_t: v.tensor_scalar(
            T.scratch[:, 32:46], e_t, 1.0 - 1e-6, None, OP.is_ge))
        P.drain("vector")
        P.op("vector", lambda v, e_t=e_t: v.scalar_tensor_tensor(
            T.scratch[:, 64:78], T.scratch[:, 32:46], -2.0, e_t, OP.mult, OP.add))
        P.drain("vector")
        P.op("vector", lambda v: v.reduce_max(T.small[:, 2:3], T.scratch[:, 64:78], axis=AX))
        P.drain("vector")
        P.op("vector", lambda v, e_t=e_t: v.tensor_scalar(
            T.scratch[:, 96:110], e_t, T.small[:, 2:3], None, OP.is_ge))
        P.op("vector", lambda v: v.tensor_scalar(
            T.small[:, 3:4], T.small[:, 2:3], 1.0, None, OP.add))
        P.drain("vector")
        P.op("vector", lambda v: v.reciprocal(T.small[:, 4:5], T.small[:, 3:4]))
        P.drain("vector")
        P.op("vector", lambda v, e_t=e_t: v.tensor_tensor(
            T.scratch[:, 128:142], e_t, T.scratch[:, 96:110], OP.mult))
        P.drain("vector")
        P.op("vector", lambda v, w=w: v.tensor_scalar(
            w, T.scratch[:, 128:142], T.small[:, 4:5], None, OP.mult), inc="v_gate")


def emit_l0_ffn(P, T):
    P.dma("sync", T.b1_sb[:, 0:224], T.l0_b1, "d_b1")
    P.wait("vector", "v_gate")
    for j in range(NTT):
        P.wait("tensor", "v_gate", P.counts["v_gate"])
        P.wait("tensor", "v_b2t", P.counts.get("v_b2t", 0))
        P.wait("tensor", "v_b2p", P.counts.get("v_b2p", 0))
        P.wait("tensor", "s_gate")
        P.op("tensor", lambda t, j=j: t.transpose(
            T.psm[0:14, 0:128], T.w_all[:, 14 * j:14 * (j + 1)], T.ident_f[:]), inc="t_b2t")
        P.wait("vector", "t_b2t")
        P.op("vector", lambda v: v.tensor_copy(T.wT16[0:14, 0:128], T.psm[0:14, 0:128]),
             inc="v_b2t")
        for nc2 in range(2):
            P.wait("gpsimd", "t_b2p", P.counts.get("t_b2p", 0))
            P.dma("gpsimd", T.b14[0:14, 0:512], T.l0_b2[0:14, 512 * nc2:512 * (nc2 + 1)],
                  "d_b2")
            P.wait("tensor", "v_b2t")
            P.wait("tensor", "d_b2")
            P.wait("tensor", "v_b2p", P.counts.get("v_b2p", 0))
            P.op("tensor", lambda t: t.matmul(
                T.psm[:, 0:512], T.wT16[0:14, 0:128], T.b14[0:14, 0:512],
                start=True, stop=True), inc="t_b2p")
            P.wait("vector", "t_b2p")
            P.op("vector", lambda v, j=j, n=nc2: v.tensor_copy(
                T.ar1[:, 1024 * j + 512 * n:1024 * j + 512 * (n + 1)], T.psm[:, 0:512]),
                 inc="v_b2p")
    P.wait("tensor", "v_kv")
    P.wait("scalar", "d_kst0")
    P.wait("scalar", "d_kst1")
    P.wait("scalar", "d_nst")
    qslot_free = [("v_setup", 0), ("v_setup", 0)]
    hs_free = [("v_setup", 0), ("v_setup", 0)]
    nq_total = 0
    for e in range(14):
        for q in range(4):
            qs = nq_total % 2
            base = 8192 * qs
            P.wait("gpsimd", *qslot_free[qs])
            for c in range(NDC):
                P.dma("gpsimd",
                      mk_ap(T.ar2[:, 0:1], base + 128 * c,
                            [[16384, 128], [1024, 4], [1, 128]]),
                      mk_ap(T.l0_w1, e * D * 2048 + 2048 * 128 * c + 512 * q,
                            [[2048, 128], [128, 4], [1, 128]]), f"d_w1{qs}")
            P.dma("gpsimd",
                  mk_ap(T.ar2[:, 0:1], base + 4096, [[16384, 128], [1024, 4], [1, 1024]]),
                  mk_ap(T.l0_w2, e * 2048 * D + 512 * q * D,
                        [[1024, 128], [128 * 1024, 4], [1, 1024]]), f"d_w1{qs}")
            P.wait("tensor", f"d_w1{qs}")
            last_t_y = None
            for tg in range(8):
                for f in range(4):
                    ft = 4 * q + f
                    hb = T.ps[f % 2]
                    P.wait("tensor", "s_h", P.counts.get("s_h", 0) - 1)
                    for c in range(NDC):
                        P.op("tensor", lambda t, hb=hb, c=c, tg=tg, base=base, f=f: t.matmul(
                            hb[:, 0:256],
                            T.ar2[:, base + 1024 * f + 128 * c:base + 1024 * f + 128 * (c + 1)],
                            T.xn_fm[:, TLOC * c + 256 * tg:TLOC * c + 256 * (tg + 1)],
                            start=(c == 0), stop=(c == NDC - 1)),
                            inc="t_h" if c == NDC - 1 else None)
                    hslot = f % 2
                    hs = T.kvstage[:, 512 * hslot:512 * hslot + 256]
                    P.wait("scalar", "t_h")
                    P.wait("scalar", "d_b1")
                    P.wait("scalar", *hs_free[hslot])
                    P.op("scalar", lambda sc, hb=hb, hs=hs, e=e, ft=ft: sc.activation(
                        hs, hb[:, 0:256], AF.Gelu,
                        bias=T.b1_sb[:, 16 * e + ft:16 * e + ft + 1]), inc="s_h")
                    P.wait("tensor", "s_h")
                    for tt in range(2):
                        for nc2 in range(2):
                            yb = T.ps[2 + tt * 2 + nc2]
                            n = P.op("tensor", lambda t, yb=yb, hs=hs, tt=tt, n=nc2,
                                     base=base, f=f: t.matmul(
                                yb[:, 0:512],
                                hs[:, 128 * tt:128 * (tt + 1)],
                                T.ar2[:, base + 4096 + 1024 * f + 512 * n:
                                      base + 4096 + 1024 * f + 512 * (n + 1)],
                                start=(f == 0), stop=(f == 3)),
                                inc="t_y" if (tt, nc2) == (1, 1) else None)
                    hs_free[hslot] = ("t_y", P.counts["t_y"])
                    last_t_y = P.counts["t_y"]
                P.wait("vector", "t_y")
                P.wait("vector", "v_b2p", P.counts["v_b2p"])
                for tt in range(2):
                    jt = tg * 2 + tt
                    for nc2 in range(2):
                        yb = T.ps[2 + tt * 2 + nc2]
                        dst = T.ar1[:, 1024 * jt + 512 * nc2:1024 * jt + 512 * (nc2 + 1)]
                        P.op("vector", lambda v, yb=yb, dst=dst, jt=jt, e=e:
                             v.scalar_tensor_tensor(
                                 dst, yb[:, 0:512],
                                 T.w_all[:, 14 * jt + e:14 * jt + e + 1], dst,
                                 OP.mult, OP.add),
                             inc="v_cmb" if (tt, nc2) == (1, 1) else None)
                P.wait("tensor", "v_cmb")
            qslot_free[qs] = ("t_y", last_t_y)
            nq_total += 1
    for j in range(NTT):
        m = T.stats[:, j:j + 1]
        rs = T.stats[:, 16 + j:16 + j + 1]
        yj = T.ar1[:, 1024 * j:1024 * (j + 1)]
        P.wait("vector", "v_cmb", P.counts["v_cmb"])
        P.wait("vector", "s_g", P.counts.get("s_g", 0))
        P.op("vector", lambda v, j=j, m=m, rs=rs: v.tensor_scalar(
            T.scratch[:], T.x_res[:, 1024 * j:1024 * (j + 1)], m, rs, OP.subtract, OP.mult))
        P.drain("vector")
        P.op("vector", lambda v, yj=yj: v.tensor_tensor(yj, T.scratch[:], yj, OP.add))
        P.drain("vector")
        P.op("vector", lambda v, yj=yj: v.tensor_scalar(
            T.scratch[:], yj, 100.0, -100.0, OP.min, OP.max), inc="v_g")
        P.drain("vector")
        P.wait("scalar", "v_g")
        P.op("scalar", lambda sc, j=j: sc.activation(
            T.ar1b[:, 1024 * j:1024 * (j + 1)], T.scratch[:], AF.Silu), inc="s_g")


def emit_l0_attn(P, T):
    SCALE = 0.125
    P.wait("gpsimd", "s_g")
    load_w_chunks(P, T, T.qw, T.l0_wq, 8, 1024, "d_wq")
    P.wait("tensor", "d_wq")
    P.wait("tensor", "s_h")
    P.wait("tensor", "v_b2p")
    first_qn = True
    for hp in range(8):
        for tg in range(4):
            bank = T.ps[(hp * 4 + tg) % 2]
            if hp * 4 + tg >= 2:
                P.wait("tensor", "v_qe", hp * 4 + tg - 1)
            for c in range(NDC):
                P.op("tensor", lambda t, b=bank, c=c, hp=hp, tg=tg: t.matmul(
                    b[:, 0:512],
                    T.qw[:, 1024 * c + 128 * hp:1024 * c + 128 * hp + 128],
                    T.xn_fm[:, TLOC * c + 512 * tg:TLOC * c + 512 * (tg + 1)],
                    start=(c == 0), stop=(c == NDC - 1)),
                    inc="t_q" if c == NDC - 1 else None)
            dst = T.ar2[:, 2048 * hp + 512 * tg:2048 * hp + 512 * (tg + 1)]
            P.wait("vector", "t_q")
            P.op("vector", lambda v, b=bank, dst=dst: v.tensor_copy(dst, b[:, 0:512]),
                 inc="v_qe")
            P.wait("scalar", "v_qe")
            P.wait("scalar", "t_qn", P.counts.get("t_qn", 0))
            P.op("scalar", lambda sc, dst=dst: sc.activation(
                T.kvstage[:, 1024:1536], dst, AF.Square), inc="s_qn")
            P.wait("tensor", "s_qn")
            P.wait("tensor", "v_qn", P.counts.get("v_qn", 0))
            P.op("tensor", lambda t: t.matmul(
                T.psm[0:1, 0:512], T.ones128[:], T.kvstage[:, 1024:1536],
                start=True, stop=True), inc="t_qn")
            P.wait("vector", "t_qn")
            P.op("vector", lambda v: v.reduce_max(T.small[0:1, 5:6], T.psm[0:1, 0:512], axis=AX))
            P.drain("vector")
            if first_qn:
                P.op("vector", lambda v: v.tensor_copy(T.small[0:1, 6:7], T.small[0:1, 5:6]),
                     inc="v_qn")
                first_qn = False
            else:
                P.op("vector", lambda v: v.tensor_tensor(
                    T.small[0:1, 6:7], T.small[0:1, 6:7], T.small[0:1, 5:6], OP.max),
                     inc="v_qn")
            P.drain("vector")
    P.wait("gpsimd", "cc")
    for r in range(4):
        P.dma("gpsimd", T.atmp[0:1, r:r + 1],
              mk_ap(T.l0_kout, (1025 * r + 1024) * TLOC, [[1, 1], [1, 1]]), "d_km")
    P.wait("vector", "d_km")
    P.op("vector", lambda v: v.reduce_max(T.small[0:1, 7:8], T.atmp[0:1, 0:4], axis=AX))
    P.drain("vector")
    P.wait("vector", "v_qn")
    P.op("vector", lambda v: v.tensor_tensor(
        T.small[0:1, 8:9], T.small[0:1, 6:7], T.small[0:1, 7:8], OP.mult), inc="v_bnd")
    P.wait("scalar", "v_bnd")
    P.op("scalar", lambda sc: sc.activation(
        T.small[0:1, 9:10], T.small[0:1, 8:9], AF.Sqrt, scale=SCALE * SCALE * 1.05),
        inc="s_bnd")
    P.wait("vector", "s_bnd")
    P.op("vector", lambda v: v.tensor_scalar(
        T.small[0:1, 10:11], T.small[0:1, 9:10], -1.0, -1.0, OP.mult, OP.add), inc="v_bnd")
    P.wait("tensor", "v_bnd")
    P.wait("tensor", "v_qn")
    P.op("tensor", lambda t: t.matmul(
        T.psm[:, 0:1], T.ones1f[:], T.small[0:1, 10:11], start=True, stop=True), inc="t_bnd")
    P.wait("vector", "t_bnd")
    P.op("vector", lambda v: v.tensor_copy(T.boundt, T.psm[:, 0:1]), inc="v_bnd")
    P.wait("vector", "t_q")
    P.op("vector", lambda v: v.memset(
        mk_ap(T.v65, 64, [[T.xn_fm.shape[1], 128], [130, 16], [65, 2]]), 1.0), inc="v_v65i")
    P.wait("gpsimd", "t_q")
    for hp in range(8):
        P.wait("sync", "t_q")
        if hp > 0:
            P.wait("sync", "t_sc", P.counts.get("t_sc", 0))
        for r in range(4):
            P.dma("sync", T.kbuf[:, 2048 * r:2048 * (r + 1)],
                  T.l0_kout[1025 * r + 128 * hp:1025 * r + 128 * (hp + 1), :], "d_ka")
        vblk_copy = {}
        av_done = {}
        for qg in range(4):
            for b in range(8):
                P.wait("gpsimd", "v_v65", P.counts.get("v_v65", 0))
                P.dma("gpsimd", T.vplain[:],
                      mk_ap(T.l0_vout, (2048 * (b // 2) + 1024 * (b % 2)) * D + 128 * hp,
                            [[D, 128], [128 * D, 8], [1, 128]]), "d_va")
                P.wait("vector", "d_va")
                P.wait("vector", "t_av", P.counts.get("t_av", 0) - 16)
                P.op("vector", lambda v, b=b: v.tensor_copy(
                    mk_ap(T.v65, 1040 * (b % 2), [[T.xn_fm.shape[1], 128], [130, 8], [65, 2], [1, 64]]),
                    mk_ap(T.vplain, 0, [[T.xn_fm.shape[1], 128], [128, 8], [64, 2], [1, 64]])),
                    inc="v_v65")
                vblk_copy[b] = P.counts["v_v65"]
                for kc in range(8 * b, 8 * b + 8):
                    pb0 = T.ps[kc % 2]
                    pb1 = T.ps[2 + kc % 2]
                    P.wait("tensor", "d_ka")
                    P.wait("tensor", "s_ex", P.counts.get("s_ex", 0) - 2)
                    for h, pb in ((0, pb0), (1, pb1)):
                        P.op("tensor", lambda t, pb=pb, h=h, kc=kc, hp=hp, qg=qg: t.matmul(
                            pb[:, 0:512],
                            T.kbuf[64 * h:64 * h + 64, 128 * kc:128 * (kc + 1)],
                            T.ar2[64 * h:64 * h + 64, 2048 * hp + 512 * qg:
                                  2048 * hp + 512 * (qg + 1)],
                            start=True, stop=True),
                            inc="t_sc" if h == 1 else None)
                    P.wait("tensor", "s_ex", P.counts.get("s_ex", 0) + 2)
                    P.wait("tensor", "v_v65i")
                    P.wait("tensor", "v_v65", vblk_copy[b])
                    if kc == 0:
                        P.wait("tensor", "v_av", P.counts.get("v_av", 0))
                    for h in (0, 1):
                        P.op("tensor", lambda t, h=h, kc=kc, b=b: t.matmul(
                            T.ps[4 + h][0:65, 0:512],
                            T.v65[:, 1040 * (b % 2) + 130 * (kc % 8) + 65 * h:
                                  1040 * (b % 2) + 130 * (kc % 8) + 65 * h + 65],
                            T.probs[:, 512 * (2 * h + kc % 2):512 * (2 * h + kc % 2 + 1)],
                            start=(kc == 0), stop=(kc == 63)),
                            inc="t_av")
                    av_done[(qg, kc)] = P.counts["t_av"]
                    P.wait("scalar", "t_sc")
                    P.wait("scalar", "v_bnd")
                    for h, pb in ((0, pb0), (1, pb1)):
                        P.wait("scalar", "t_av", P.counts["t_av"] - 5 + h)
                        P.op("scalar", lambda sc, pb=pb, h=h, kc=kc: sc.activation(
                            T.probs[:, 512 * (2 * h + kc % 2):512 * (2 * h + kc % 2 + 1)],
                            pb[:, 0:512], AF.Exp, bias=T.boundt, scale=SCALE),
                            inc="s_ex")
            for h in (0, 1):
                hh = 2 * hp + h
                P.wait("vector", "t_av")
                P.op("vector", lambda v, h=h: v.tensor_copy(
                    T.avr[0:65, 0:512], T.ps[4 + h][0:65, 0:512]), inc="v_av")
                P.wait("tensor", "v_av")
                P.wait("tensor", "v_nm", P.counts.get("v_nm", 0))
                for c in range(4):
                    P.op("tensor", lambda t, c=c: t.transpose(
                        T.psm[:, 128 * c:128 * (c + 1)],
                        T.avr[:, 128 * c:128 * (c + 1)], T.ident_f[:]),
                        inc="t_nm" if c == 3 else None)
                P.wait("vector", "t_nm")
                for c in range(4):
                    jt = qg * 4 + c
                    P.op("vector", lambda v, c=c: v.reciprocal(
                        T.small[:, 11:12], T.psm[:, 128 * c + 64:128 * c + 65]))
                    P.drain("vector")
                    P.op("vector", lambda v, c=c, jt=jt, hh=hh: v.scalar_tensor_tensor(
                        T.tmp64[:], T.psm[:, 128 * c:128 * c + 64], T.small[:, 11:12],
                        T.ar1b[:, 1024 * jt + 64 * hh:1024 * jt + 64 * (hh + 1)],
                        OP.mult, OP.mult))
                    P.drain("vector")
                    P.op("vector", lambda v, jt=jt, hh=hh: v.tensor_tensor(
                        T.x_res[:, 1024 * jt + 64 * hh:1024 * jt + 64 * (hh + 1)],
                        T.x_res[:, 1024 * jt + 64 * hh:1024 * jt + 64 * (hh + 1)],
                        T.tmp64[:], OP.add),
                        inc="v_nm" if c == 3 else None)
                    P.drain("vector")


def emit_store_x(P, T):
    P.wait("sync", "v_nm", P.counts.get("v_nm", 0))
    P.wait("sync", "v_nm2", P.counts.get("v_nm2", 0))
    P.wait("sync", "v_fin", P.counts.get("v_fin", 0))
    P.wait("sync", "v_xn", P.counts.get("v_xn", 0))
    for j in range(NTT):
        P.dma("sync", T.x_out[128 * j:128 * (j + 1), :], T.x_res[:, D * j:D * (j + 1)], "d_out")
    P.wait("sync", "d_out")



PHASES = ("ln", "kv", "gate", "ffn", "q", "attn", "cmb")


def build_program(layers=(0,), phases=PHASES):
    nc = bass.Bass("TRN2", target_bir_lowering=False)
    es = ExitStack()
    with es:
        T = alloc_common(nc, es, layers)
        if 0 in layers:
            alloc_l0(nc, T)
        if 1 in layers:
            alloc_l1(nc, T)
        if 2 in layers:
            alloc_l2(nc, T)
        if 3 in layers:
            alloc_l3(nc, T)
        P = Prog(nc, es)
        emit_setup(P, T)
        if 0 in layers:
            if "ln" in phases:
                emit_ln(P, T)
            if "kv" in phases:
                emit_l0_kv(P, T)
            if "gate" in phases:
                emit_l0_gate(P, T)
            if "ffn" in phases:
                emit_l0_ffn(P, T)
            if "attn" in phases:
                emit_l0_attn(P, T)
        if 1 in layers:
            if "ln" in phases:
                emit_ln(P, T)
            if "kv" in phases:
                emit_l1_kv(P, T)
            if "ffn" in phases:
                emit_l1_ffn(P, T)
            if "q" in phases:
                emit_l1_reln_q(P, T)
            if "attn" in phases:
                emit_l1_attn(P, T)
        if 2 in layers:
            if "ln" in phases:
                emit_ln(P, T)
            if "kv" in phases:
                emit_l2_xz(P, T)
            if "ffn" in phases:
                emit_l2_ffn(P, T)
            if "attn" in phases:
                emit_l2_scan(P, T)
            if "cmb" in phases:
                emit_l2_rs(P, T)
        if 3 in layers:
            if "ln" in phases:
                emit_ln(P, T)
            if "kv" in phases:
                emit_l3_kv(P, T)
            if "ffn" in phases:
                emit_l3_ffn(P, T)
            if "q" in phases:
                emit_l3_q(P, T)
            if "attn" in phases:
                emit_l3_attn(P, T)
            if "cmb" in phases:
                emit_l3_wo(P, T)
        emit_store_x(P, T)
        for fn in DEBUG:
            fn(P, T)
        P.build()
    return nc


def kernel(**inputs):
    x_cur = None
    for li in (0, 1, 2, 3):
        nc = build_program((li,))
        in_maps = [prep_core_inputs(c, inputs, (li,)) for c in range(NCORES)]
        if x_cur is not None:
            for c in range(NCORES):
                in_maps[c]["x_in"] = x_cur[c]
        res = run_bass_kernel_spmd(nc, in_maps, core_ids=list(range(NCORES)))
        x_cur = [np.ascontiguousarray(res.results[c]["x_out"]) for c in range(NCORES)]
    out = np.empty((B, S, D), np.float32)
    for c in range(NCORES):
        b, r = c // 4, c % 4
        out[b, r * TLOC:(r + 1) * TLOC, :] = x_cur[c]
    return out



def alloc_l1(nc, T):
    T.l1_wqT = nc.dram_tensor("l1_wqT", [D, D], F32, kind="ExternalInput").ap()
    T.l1_wkT = nc.dram_tensor("l1_wkT", [D, D], F32, kind="ExternalInput").ap()
    T.l1_wvT = nc.dram_tensor("l1_wvT", [D, D], F32, kind="ExternalInput").ap()
    T.l1_bq = nc.dram_tensor("l1_bq", [128, 8], F32, kind="ExternalInput").ap()
    T.l1_bk = nc.dram_tensor("l1_bk", [128, 8], F32, kind="ExternalInput").ap()
    T.l1_bv = nc.dram_tensor("l1_bv", [1, D], F32, kind="ExternalInput").ap()
    T.l1_g1 = nc.dram_tensor("l1_g1", [128, 8], F32, kind="ExternalInput").ap()
    T.l1_be1 = nc.dram_tensor("l1_be1", [128, 8], F32, kind="ExternalInput").ap()
    T.l1_g2 = nc.dram_tensor("l1_g2", [128, 8], F32, kind="ExternalInput").ap()
    T.l1_be2 = nc.dram_tensor("l1_be2", [128, 8], F32, kind="ExternalInput").ap()
    T.l1_mixw = nc.dram_tensor("l1_mixw", [D, D], F32, kind="ExternalInput").ap()
    T.l1_mixb = nc.dram_tensor("l1_mixb", [1, D], F32, kind="ExternalInput").ap()
    T.l1_rw = nc.dram_tensor("l1_rw", [D, 8], F32, kind="ExternalInput").ap()
    T.l1_w1 = nc.dram_tensor("l1_w1", [8, D, 4096], F32, kind="ExternalInput").ap()
    T.l1_b1 = nc.dram_tensor("l1_b1", [128, 256], F32, kind="ExternalInput").ap()
    T.l1_w2 = nc.dram_tensor("l1_w2", [8, 4096, D], F32, kind="ExternalInput").ap()
    T.l1_b2 = nc.dram_tensor("l1_b2", [8, D], F32, kind="ExternalInput").ap()
    T.l1_kin = nc.dram_tensor("l1_kin", [1025, TLOC], BF16).ap()
    T.l1_kout = nc.dram_tensor("l1_kout", [4 * 1025, TLOC], BF16).ap()
    T.l1_vin = nc.dram_tensor("l1_vin", [TLOC, D], BF16).ap()
    T.l1_vout = nc.dram_tensor("l1_vout", [4 * TLOC, D], BF16).ap()


def emit_row_to_packed(P, T, src_row_ap, dst_packed, tag):
    P.dma("sync", dst_packed[:], mk_ap(src_row_ap, 0, [[1, 128], [128, NDC]]), tag)


def emit_l1_kv(P, T):
    for c in range(NDC):
        P.dma("gpsimd", T.ar2[:, 1024 * c:1024 * (c + 1)],
              T.l1_wkT[128 * c:128 * (c + 1), :], "d_wkv1")
    P.dma("sync", T.bkp[:], T.l1_bk, "d_bp")
    P.wait("tensor", "d_wkv1")
    P.wait("tensor", "v_xn")
    P.wait("vector", "d_bp")
    grp = P.counts.get("t_kv", 0)
    slot_free = [("v_setup", 0), ("v_setup", 0)]
    first_nrm = True
    for tg in range(4):
        for g in range(8):
            bank = T.ps[grp % 2]
            P.wait("tensor", "v_kv", P.counts.get("v_kv", 0) - 1)
            for c in range(NDC):
                P.op("tensor", lambda t, b=bank, c=c, g=g, tg=tg: t.matmul(
                    b[:, 0:512],
                    T.ar2[:, 1024 * c + 128 * g:1024 * c + 128 * g + 128],
                    T.xn_fm[:, TLOC * c + 512 * tg:TLOC * c + 512 * (tg + 1)],
                    start=(c == 0), stop=(c == NDC - 1)),
                    inc="t_kv" if c == NDC - 1 else None)
            slot = grp % 2
            stg = T.kvstage[:, 512 * slot:512 * (slot + 1)]
            P.wait("vector", "t_kv")
            P.wait("vector", *slot_free[slot])
            P.op("vector", lambda v, b=bank, stg=stg, g=g: v.tensor_scalar(
                stg, b[:, 0:512], T.bkp[:, g:g + 1], None, OP.add), inc="v_kv")
            P.wait("scalar", "v_kv")
            P.wait("scalar", "t_nrm", P.counts.get("t_nrm", 0))
            P.op("scalar", lambda sc, stg=stg: sc.activation(
                T.kvstage[:, 1024:1536], stg, AF.Square), inc="s_kv")
            P.wait("tensor", "s_kv")
            if g == 0:
                P.wait("tensor", "v_nrm", P.counts.get("v_nrm", 0))
            P.op("tensor", lambda t, g=g: t.matmul(
                T.psm[0:1, 0:512], T.ones128[:], T.kvstage[:, 1024:1536],
                start=(g == 0), stop=(g == 7)), inc="t_nrm")
            P.wait("sync", "v_kv")
            n = P.dma("sync", T.l1_kin[128 * g:128 * (g + 1), 512 * tg:512 * (tg + 1)], stg,
                      f"d_kst{slot}")
            slot_free[slot] = (f"d_kst{slot}", n)
            grp += 1
        P.wait("vector", "t_nrm")
        P.op("vector", lambda v: v.reduce_max(T.small[0:1, 3:4], T.psm[0:1, 0:512], axis=AX))
        P.drain("vector")
        if first_nrm:
            P.op("vector", lambda v: v.tensor_copy(T.small[0:1, 4:5], T.small[0:1, 3:4]),
                 inc="v_nrm")
            first_nrm = False
        else:
            P.op("vector", lambda v: v.tensor_tensor(
                T.small[0:1, 4:5], T.small[0:1, 4:5], T.small[0:1, 3:4], OP.max),
                 inc="v_nrm")
        P.drain("vector")
    P.wait("vector", "v_nrm")
    P.op("vector", lambda v: v.tensor_copy(T.nstage[0:1, 0:1], T.small[0:1, 4:5]),
         inc="v_kmax")
    P.wait("sync", "v_kmax")
    P.dma("sync", mk_ap(T.l1_kin, 1024 * TLOC, [[1, 1], [1, 1]]), T.nstage[0:1, 0:1], "d_nst")
    P.wait("gpsimd", "t_kv", P.counts["t_kv"])
    for c in range(NDC):
        P.dma("gpsimd", T.ar2[:, 1024 * c:1024 * (c + 1)],
              T.l1_wvT[128 * c:128 * (c + 1), :], "d_wkv1")
    P.dma("gpsimd", T.nstage[0:1, 0:1024], T.l1_bv, "d_wkv1")
    P.wait("tensor", "d_wkv1")
    for j in range(NTT):
        for nc2 in range(2):
            bank = T.ps[grp % 2]
            P.wait("tensor", "v_kv", P.counts.get("v_kv", 0) - 1)
            for c in range(NDC):
                P.op("tensor", lambda t, b=bank, c=c, j=j, n=nc2: t.matmul(
                    b[:, 0:512],
                    T.xn_fm[:, TLOC * c + 128 * j:TLOC * c + 128 * (j + 1)],
                    T.ar2[:, 1024 * c + 512 * n:1024 * c + 512 * (n + 1)],
                    start=(c == 0), stop=False),
                    inc=None)
            P.op("tensor", lambda t, b=bank, n=nc2: t.matmul(
                b[:, 0:512], T.ones1[:], T.nstage[0:1, 512 * n:512 * (n + 1)],
                start=False, stop=True), inc="t_kv")
            slot = grp % 2
            stg = T.kvstage[:, 512 * slot:512 * (slot + 1)]
            P.wait("vector", "t_kv")
            P.wait("vector", *slot_free[slot])
            P.op("vector", lambda v, b=bank, stg=stg: v.tensor_copy(stg, b[:, 0:512]),
                 inc="v_kv")
            P.wait("sync", "v_kv")
            n = P.dma("sync", T.l1_vin[128 * j:128 * (j + 1), 512 * nc2:512 * (nc2 + 1)], stg,
                      f"d_kst{slot}")
            slot_free[slot] = (f"d_kst{slot}", n)
            grp += 1
    P.wait("gpsimd", "d_kst0")
    P.wait("gpsimd", "d_kst1")
    P.wait("gpsimd", "d_nst")
    if SKIP_CC:
        P.counts["cc"] = P.counts.get("cc", 0) + 1
        P.sem("cc")
        P.op("gpsimd", lambda g: g.nop(), inc=None)
        P.counts["cc"] -= 1
        P.op("gpsimd", lambda g: g.nop(), inc="cc")
    else:
        P.op("gpsimd", lambda g: g.collective_compute(
            "AllGather", OP.bypass, replica_groups=GROUPS,
            ins=[T.l1_kin.opt()], outs=[T.l1_kout.opt()]), inc="cc")
    if SKIP_CC:
        P.counts["cc"] = P.counts.get("cc", 0) + 1
        P.sem("cc")
        P.op("gpsimd", lambda g: g.nop(), inc=None)
        P.counts["cc"] -= 1
        P.op("gpsimd", lambda g: g.nop(), inc="cc")
    else:
        P.op("gpsimd", lambda g: g.collective_compute(
            "AllGather", OP.bypass, replica_groups=GROUPS,
            ins=[T.l1_vin.opt()], outs=[T.l1_vout.opt()]), inc="cc")


def emit_l1_ffn(P, T):
    atmp2 = T.ar2[:, 8192:10240].bitcast(F32)
    """x2 = xn + n1 + n1@mixw + mixb; n2 = LN(x2)*g2+be2; top-2 router (fp32);
    y_acc(ar1) = x2 + B2tot + sum_e w_e*(silu(n2@w1+b1)@w2); g = silu(y_acc)."""
    P.dma("sync", T.g1p[:], T.l1_g1, "d_bp")
    P.dma("sync", T.be1p[:], T.l1_be1, "d_bp")
    P.dma("sync", T.g2p[:], T.l1_g2, "d_bp")
    P.dma("sync", T.be2p[:], T.l1_be2, "d_bp")
    P.dma("sync", T.b1_sb[:, 0:256], T.l1_b1, "d_b1")
    for c in range(NDC):
        P.dma("sync", T.gw_sb[:, 8 * c:8 * (c + 1)], T.l1_rw[128 * c:128 * (c + 1), :],
              "d_gw")
    P.wait("gpsimd", "t_kv", P.counts.get("t_kv", 0))
    for c in range(NDC):
        P.dma("gpsimd", T.ar2[:, 1024 * c:1024 * (c + 1)],
              T.l1_mixw[128 * c:128 * (c + 1), :], "d_mixw")
    P.dma("gpsimd", T.nstage[0:1, 0:1024], T.l1_mixb, "d_mixw")
    P.wait("vector", "d_bp")
    P.wait("vector", "d_x")
    P.wait("vector", "d_kst0")
    P.wait("vector", "d_kst1")
    P.wait("vector", "d_nst")
    P.wait("vector", "t_kv", P.counts.get("t_kv", 0))
    for j in range(NTT):
        m = T.stats[:, j:j + 1]
        rs = T.stats[:, 16 + j:16 + j + 1]
        m2 = T.stats[:, 32 + j:32 + j + 1]
        rs2 = T.stats[:, 48 + j:48 + j + 1]
        P.wait("vector", "s_ln", P.counts.get("s_ln", 0))
        P.drain("vector")
        P.op("vector", lambda v, j=j, m=m, rs=rs: v.tensor_scalar(
            T.scratch[:], T.x_res[:, 1024 * j:1024 * (j + 1)], m, rs,
            OP.subtract, OP.mult), inc="v_f1")
        P.drain("vector")
        P.op("vector", lambda v: v.reduce_sum(T.small[:, 0:1], T.scratch[:], axis=AX))
        P.drain("vector")
        P.op("vector", lambda v, m2=m2: v.tensor_scalar(
            T.small[:, 1:2], T.small[:, 0:1], 1.0 / D, None, OP.mult))
        P.drain("vector")
        P.op("vector", lambda v: v.tensor_scalar(
            atmp2, T.scratch[:], T.small[:, 1:2], None, OP.subtract), inc="v_f1")
        P.wait("scalar", "v_f1")
        P.op("scalar", lambda sc: sc.activation(
            atmp2 if False else T.ar1[:, 0:1024], atmp2, AF.Square,
            accum_out=T.small[:, 2:3]))
        P.drain("scalar")
        P.op("scalar", lambda sc: sc.activation(
            T.small[:, 3:4], T.small[:, 2:3], AF.Ln, bias=T.epst, scale=1.0 / D))
        P.drain("scalar")
        P.op("scalar", lambda sc, rs2=rs2: sc.activation(
            rs2, T.small[:, 3:4], AF.Exp, scale=-0.5), inc="s_f1")
        P.wait("vector", "s_f1")
        P.op("vector", lambda v, m2=m2: v.tensor_copy(m2, T.small[:, 1:2]))
        P.drain("vector")
        P.op("vector", lambda v, rs2=rs2: v.tensor_scalar(
            T.kvstage[:, 0:1024], atmp2, rs2, None, OP.mult), inc="v_f1")
        P.wait("tensor", "v_f1")
        P.wait("tensor", "v_n1e", P.counts.get("v_n1e", 0))
        for c in range(NDC):
            P.op("tensor", lambda t, c=c: t.transpose(
                T.pst[:, 128 * c:128 * (c + 1)], T.kvstage[:, 128 * c:128 * (c + 1)],
                T.ident_b[:]), inc="t_n1" if c == NDC - 1 else None)
        P.wait("vector", "t_n1")
        for c in range(NDC):
            P.op("vector", lambda v, c=c, j=j: v.tensor_scalar(
                T.xn_fm[:, TLOC * c + 128 * j:TLOC * c + 128 * (j + 1)],
                T.pst[:, 128 * c:128 * (c + 1)],
                T.g1p[:, c:c + 1], T.be1p[:, c:c + 1], OP.mult, OP.add),
                inc="v_n1e" if c == NDC - 1 else None)
    P.wait("tensor", "d_mixw")
    P.wait("tensor", "d_gw")
    for j in range(NTT):
        m = T.stats[:, j:j + 1]
        rs = T.stats[:, 16 + j:16 + j + 1]
        P.wait("tensor", "v_n1e", P.counts["v_n1e"])
        P.wait("tensor", "v_x2", P.counts.get("v_x2", 0))
        for nc2 in range(2):
            bank = T.ps[2 + nc2]
            for c in range(NDC):
                P.op("tensor", lambda t, b=bank, c=c, j=j, n=nc2: t.matmul(
                    b[:, 0:512],
                    T.xn_fm[:, TLOC * c + 128 * j:TLOC * c + 128 * (j + 1)],
                    T.ar2[:, 1024 * c + 512 * n:1024 * c + 512 * (n + 1)],
                    start=(c == 0), stop=False), inc=None)
            P.op("tensor", lambda t, b=bank, n=nc2: t.matmul(
                b[:, 0:512], T.ones1[:], T.nstage[0:1, 512 * n:512 * (n + 1)],
                start=False, stop=True), inc="t_mix")
        for c in range(NDC):
            P.op("tensor", lambda t, c=c, j=j: t.transpose(
                T.pst[:, 128 * c:128 * (c + 1)],
                T.xn_fm[:, TLOC * c + 128 * j:TLOC * c + 128 * (j + 1)],
                T.ident_b[:]), inc="t_n1b" if c == NDC - 1 else None)
        P.wait("vector", "t_mix")
        P.wait("vector", "t_n1b")
        P.drain("vector")
        P.op("vector", lambda v, j=j, m=m, rs=rs: v.tensor_scalar(
            T.scratch[:], T.x_res[:, 1024 * j:1024 * (j + 1)], m, rs,
            OP.subtract, OP.mult))
        P.drain("vector")
        for nc2 in range(2):
            P.op("vector", lambda v, j=j, n=nc2: v.tensor_tensor(
                T.ar1[:, 1024 * j + 512 * n:1024 * j + 512 * (n + 1)],
                T.scratch[:, 512 * n:512 * (n + 1)], T.ps[2 + n][:, 0:512], OP.add))
        P.drain("vector")
        for c in range(NDC):
            P.op("vector", lambda v, c=c, j=j: v.tensor_tensor(
                T.ar1[:, 1024 * j + 128 * c:1024 * j + 128 * (c + 1)],
                T.ar1[:, 1024 * j + 128 * c:1024 * j + 128 * (c + 1)],
                T.pst[:, 128 * c:128 * (c + 1)], OP.add),
                inc="v_x2" if c == NDC - 1 else None)
        P.drain("vector")
        P.op("vector", lambda v, j=j: v.reduce_sum(
            T.small[:, 0:1], T.ar1[:, 1024 * j:1024 * (j + 1)], axis=AX))
        P.drain("vector")
        P.op("vector", lambda v: v.tensor_scalar(
            T.small[:, 1:2], T.small[:, 0:1], 1.0 / D, None, OP.mult))
        P.drain("vector")
        P.op("vector", lambda v, j=j: v.tensor_scalar(
            T.scratch[:], T.ar1[:, 1024 * j:1024 * (j + 1)], T.small[:, 1:2], None,
            OP.subtract), inc="v_x2")
        P.wait("scalar", "v_x2")
        P.op("scalar", lambda sc: sc.activation(
            T.ar1[:, 0:1024] if False else atmp2, T.scratch[:], AF.Square,
            accum_out=T.small[:, 2:3]))
        P.drain("scalar")
        P.op("scalar", lambda sc: sc.activation(
            T.small[:, 3:4], T.small[:, 2:3], AF.Ln, bias=T.epst, scale=1.0 / D))
        P.drain("scalar")
        P.op("scalar", lambda sc: sc.activation(
            T.small[:, 4:5], T.small[:, 3:4], AF.Exp, scale=-0.5), inc="s_x2")
        P.wait("vector", "s_x2")
        P.op("vector", lambda v: v.tensor_scalar(
            T.kvstage[:, 0:1024], T.scratch[:], T.small[:, 4:5], None, OP.mult),
            inc="v_x2")
        P.op("vector", lambda v: v.tensor_scalar(
            atmp2, T.scratch[:], T.small[:, 4:5], None, OP.mult), inc="v_x2")
        P.wait("tensor", "v_x2")
        P.wait("tensor", "v_n2e", P.counts.get("v_n2e", 0))
        for c in range(NDC):
            P.op("tensor", lambda t, c=c: t.transpose(
                T.pst[:, 128 * c:128 * (c + 1)], T.kvstage[:, 128 * c:128 * (c + 1)],
                T.ident_b[:]), inc="t_n2" if c == NDC - 1 else None)
        P.wait("vector", "t_n2")
        for c in range(NDC):
            P.op("vector", lambda v, c=c, j=j: v.tensor_scalar(
                T.xn_fm[:, TLOC * c + 128 * j:TLOC * c + 128 * (j + 1)],
                T.pst[:, 128 * c:128 * (c + 1)],
                T.g2p[:, c:c + 1], T.be2p[:, c:c + 1], OP.mult, OP.add),
                inc="v_n2e" if c == NDC - 1 else None)
        P.wait("tensor", "v_x2")
        P.wait("tensor", "v_rxt", P.counts.get("v_rxt", 0))
        P.wait("tensor", "v_kv", P.counts.get("v_kv", 0))
        for c in range(NDC):
            P.op("tensor", lambda t, c=c: t.transpose(
                T.ps[c // 4][:, 128 * (c % 4):128 * (c % 4) + 128],
                atmp2[:, 128 * c:128 * (c + 1)], T.ident_f[:]),
                inc="t_rxt" if c == NDC - 1 else None)
        P.wait("vector", "t_rxt")
        for c in range(NDC):
            P.op("vector", lambda v, c=c: v.tensor_scalar(
                T.scratch[:, 128 * c:128 * (c + 1)],
                T.ps[c // 4][:, 128 * (c % 4):128 * (c % 4) + 128],
                T.g2p[:, c:c + 1], T.be2p[:, c:c + 1], OP.mult, OP.add),
                inc="v_rxt" if c == NDC - 1 else None)
        P.wait("tensor", "v_rxt")
        P.wait("tensor", "s_gate", P.counts.get("s_gate", 0))
        for c in range(NDC):
            P.op("tensor", lambda t, c=c, j=j: t.matmul(
                T.psm[:, 0:8],
                T.scratch[:, 128 * c:128 * (c + 1)],
                T.gw_sb[:, 8 * c:8 * (c + 1)],
                start=(c == 0), stop=(c == NDC - 1)),
                inc="t_gate" if c == NDC - 1 else None)
        w = T.w_all[:, 14 * j:14 * j + 8]
        e_t = T.scratch[:, 0:8]
        P.wait("vector", "t_gate")
        P.drain("vector")
        P.op("vector", lambda v: v.reduce_max(T.small[:, 0:1], T.psm[:, 0:8], axis=AX))
        P.drain("vector")
        P.op("vector", lambda v: v.tensor_scalar(
            T.small[:, 1:2], T.small[:, 0:1], -1.0, None, OP.mult), inc="v_gate")
        P.wait("scalar", "v_gate")
        P.op("scalar", lambda sc, e_t=e_t: sc.activation(
            e_t, T.psm[:, 0:8], AF.Exp, bias=T.small[:, 1:2],
            accum_out=T.small[:, 5:6]), inc="s_gate")
        P.wait("vector", "s_gate")
        P.op("vector", lambda v: v.reciprocal(T.small[:, 6:7], T.small[:, 5:6]))
        P.drain("vector")
        P.op("vector", lambda v, e_t=e_t: v.tensor_scalar(
            T.scratch[:, 32:40], e_t, 1.0 - 1e-6, None, OP.is_ge))
        P.drain("vector")
        P.op("vector", lambda v, e_t=e_t: v.scalar_tensor_tensor(
            T.scratch[:, 64:72], T.scratch[:, 32:40], -2.0, e_t, OP.mult, OP.add))
        P.drain("vector")
        P.op("vector", lambda v: v.reduce_max(T.small[:, 2:3], T.scratch[:, 64:72], axis=AX))
        P.drain("vector")
        P.op("vector", lambda v, e_t=e_t: v.tensor_scalar(
            T.scratch[:, 96:104], e_t, T.small[:, 2:3], None, OP.is_ge))
        P.drain("vector")
        P.op("vector", lambda v, e_t=e_t: v.tensor_tensor(
            T.scratch[:, 128:136], e_t, T.scratch[:, 96:104], OP.mult))
        P.drain("vector")
        P.op("vector", lambda v, w=w: v.tensor_scalar(
            w, T.scratch[:, 128:136], T.small[:, 6:7], None, OP.mult), inc="v_gate")
    for j in range(NTT):
        P.wait("tensor", "v_gate", P.counts["v_gate"])
        P.wait("tensor", "v_b2t", P.counts.get("v_b2t", 0))
        P.wait("tensor", "v_b2p", P.counts.get("v_b2p", 0))
        P.wait("tensor", "s_gate")
        P.op("tensor", lambda t, j=j: t.transpose(
            T.psm[0:8, 0:128], T.w_all[:, 14 * j:14 * j + 8].bitcast(F32) if False else
            T.w_all[:, 14 * j:14 * j + 8], T.ident_f[:]), inc="t_b2t")
        P.wait("vector", "t_b2t")
        P.op("vector", lambda v: v.tensor_copy(T.wT16[0:8, 0:128], T.psm[0:8, 0:128]),
             inc="v_b2t")
        for nc2 in range(2):
            P.wait("gpsimd", "t_b2p", P.counts.get("t_b2p", 0))
            P.dma("gpsimd", T.b14[0:8, 0:512], T.l1_b2[0:8, 512 * nc2:512 * (nc2 + 1)],
                  "d_b2")
            P.wait("tensor", "v_b2t")
            P.wait("tensor", "d_b2")
            P.wait("tensor", "v_b2p", P.counts.get("v_b2p", 0))
            P.op("tensor", lambda t: t.matmul(
                T.psm[:, 0:512], T.wT16[0:8, 0:128], T.b14[0:8, 0:512],
                start=True, stop=True), inc="t_b2p")
            P.wait("vector", "t_b2p")
            P.op("vector", lambda v, j=j, n=nc2: v.tensor_tensor(
                T.ar1[:, 1024 * j + 512 * n:1024 * j + 512 * (n + 1)],
                T.ar1[:, 1024 * j + 512 * n:1024 * j + 512 * (n + 1)],
                T.psm[:, 0:512], OP.add), inc="v_b2p")
            P.drain("vector")
    P.wait("scalar", "d_b1")
    P.wait("gpsimd", "t_rxt", P.counts.get("t_rxt", 0))
    P.wait("gpsimd", "v_gate", P.counts.get("v_gate", 0))
    P.wait("gpsimd", "t_gate", P.counts.get("t_gate", 0))
    qslot_free = [("v_setup", 0), ("v_setup", 0)]
    nq_total = 0
    for e in range(8):
        for q in range(8):
            qs = nq_total % 2
            base = 8192 * qs
            P.wait("gpsimd", *qslot_free[qs])
            P.wait("gpsimd", "t_mix", P.counts.get("t_mix", 0))
            for c in range(NDC):
                P.dma("gpsimd",
                      mk_ap(T.ar2[:, 0:1], base + 128 * c,
                            [[16384, 128], [1024, 4], [1, 128]]),
                      mk_ap(T.l1_w1, e * D * 4096 + 4096 * 128 * c + 512 * q,
                            [[4096, 128], [128, 4], [1, 128]]), f"d_w1{qs}")
            P.dma("gpsimd",
                  mk_ap(T.ar2[:, 0:1], base + 4096, [[16384, 128], [1024, 4], [1, 1024]]),
                  mk_ap(T.l1_w2, e * 4096 * D + 512 * q * D,
                        [[1024, 128], [128 * 1024, 4], [1, 1024]]), f"d_w1{qs}")
            P.wait("tensor", f"d_w1{qs}")
            last_t_y = None
            for tg in range(8):
                for f in range(4):
                    ft = 4 * q + f
                    hb = T.ps[f % 2]
                    P.wait("tensor", "s_h", P.counts.get("s_h", 0) - 1)
                    for c in range(NDC):
                        P.op("tensor", lambda t, hb=hb, c=c, tg=tg, base=base, f=f: t.matmul(
                            hb[:, 0:256],
                            T.ar2[:, base + 1024 * f + 128 * c:base + 1024 * f + 128 * (c + 1)],
                            T.xn_fm[:, TLOC * c + 256 * tg:TLOC * c + 256 * (tg + 1)],
                            start=(c == 0), stop=(c == NDC - 1)),
                            inc="t_h" if c == NDC - 1 else None)
                    hslot = f % 2
                    hs = T.kvstage[:, 512 * hslot:512 * hslot + 256]
                    P.wait("scalar", "t_h")
                    P.op("scalar", lambda sc, hb=hb, hs=hs, e=e, ft=ft: sc.activation(
                        hs, hb[:, 0:256], AF.Silu,
                        bias=T.b1_sb[:, 32 * e + ft:32 * e + ft + 1]), inc="s_h")
                    P.wait("tensor", "s_h")
                    for tt in range(2):
                        for nc2 in range(2):
                            yb = T.ps[2 + tt * 2 + nc2]
                            P.op("tensor", lambda t, yb=yb, hs=hs, tt=tt, n=nc2,
                                 base=base, f=f: t.matmul(
                                yb[:, 0:512],
                                hs[:, 128 * tt:128 * (tt + 1)],
                                T.ar2[:, base + 4096 + 1024 * f + 512 * n:
                                      base + 4096 + 1024 * f + 512 * (n + 1)],
                                start=(f == 0), stop=(f == 3)),
                                inc="t_y" if (tt, nc2) == (1, 1) else None)
                P.wait("vector", "t_y")
                P.wait("vector", "v_b2p", P.counts["v_b2p"])
                for tt in range(2):
                    jt = tg * 2 + tt
                    for nc2 in range(2):
                        yb = T.ps[2 + tt * 2 + nc2]
                        dst = T.ar1[:, 1024 * jt + 512 * nc2:1024 * jt + 512 * (nc2 + 1)]
                        P.op("vector", lambda v, yb=yb, dst=dst, jt=jt, e=e:
                             v.scalar_tensor_tensor(
                                 dst, yb[:, 0:512],
                                 T.w_all[:, 14 * jt + e:14 * jt + e + 1], dst,
                                 OP.mult, OP.add),
                             inc="v_cmb" if (tt, nc2) == (1, 1) else None)
                P.wait("tensor", "v_cmb")
                last_t_y = P.counts["t_y"]
            qslot_free[qs] = ("t_y", last_t_y)
            nq_total += 1
    for j in range(NTT):
        P.wait("scalar", "v_cmb", P.counts["v_cmb"])
        P.drain("scalar")
        P.op("scalar", lambda sc, j=j: sc.activation(
            T.ar1b[:, 1024 * j:1024 * (j + 1)], T.ar1[:, 1024 * j:1024 * (j + 1)],
            AF.Silu), inc="s_g")


def emit_l1_reln_q(P, T):
    for j in range(NTT):
        m = T.stats[:, j:j + 1]
        rs = T.stats[:, 16 + j:16 + j + 1]
        P.wait("vector", "t_y", P.counts.get("t_y", 0))
        P.wait("vector", "s_g", P.counts.get("s_g", 0))
        P.drain("vector")
        P.op("vector", lambda v, j=j, m=m, rs=rs: v.tensor_scalar(
            T.kvstage[:, 0:1024], T.x_res[:, 1024 * j:1024 * (j + 1)], m, rs,
            OP.subtract, OP.mult), inc="v_ln")
        P.wait("tensor", "v_ln")
        P.wait("tensor", "v_xn", P.counts.get("v_xn", 0))
        for c in range(NDC):
            P.op("tensor", lambda t, c=c: t.transpose(
                T.pst[:, 128 * c:128 * (c + 1)],
                T.kvstage[:, 128 * c:128 * (c + 1)], T.ident_b[:]),
                inc="t_ln" if c == NDC - 1 else None)
        P.wait("vector", "t_ln")
        for c in range(NDC):
            P.op("vector", lambda v, c=c, j=j: v.tensor_copy(
                T.xn_fm[:, TLOC * c + 128 * j:TLOC * c + 128 * (j + 1)],
                T.pst[:, 128 * c:128 * (c + 1)]),
                inc="v_xn" if c == NDC - 1 else None)
        P.wait("tensor", "v_xn")
    P.wait("gpsimd", "s_g")
    P.wait("gpsimd", "t_y", P.counts.get("t_y", 0))
    for c in range(NDC):
        P.dma("gpsimd", T.ar2[:, 1024 * c:1024 * (c + 1)],
              T.l1_wqT[128 * c:128 * (c + 1), :], "d_wq1")
    P.dma("sync", T.bqp[:], T.l1_bq, "d_bq")
    P.wait("vector", "d_bq")
    P.wait("tensor", "d_wq1")
    P.wait("tensor", "v_xn")
    first_qn = True
    grp = 0
    for tg in range(4):
        for g in range(8):
            bank = T.ps[grp % 2]
            P.wait("tensor", "v_qe", P.counts.get("v_qe", 0) - 1)
            for c in range(NDC):
                P.op("tensor", lambda t, b=bank, c=c, g=g, tg=tg: t.matmul(
                    b[:, 0:512],
                    T.ar2[:, 1024 * c + 128 * g:1024 * c + 128 * g + 128],
                    T.xn_fm[:, TLOC * c + 512 * tg:TLOC * c + 512 * (tg + 1)],
                    start=(c == 0), stop=(c == NDC - 1)),
                    inc="t_q" if c == NDC - 1 else None)
            dst = T.ar1b[:, 16384 + 2048 * g + 512 * tg:16384 + 2048 * g + 512 * (tg + 1)]
            P.wait("vector", "t_q")
            P.op("vector", lambda v, b=bank, dst=dst, g=g: v.tensor_scalar(
                dst, b[:, 0:512], T.bqp[:, g:g + 1], None, OP.add), inc="v_qe")
            P.wait("scalar", "v_qe")
            P.wait("scalar", "t_qn", P.counts.get("t_qn", 0))
            P.op("scalar", lambda sc, dst=dst: sc.activation(
                T.kvstage[:, 1024:1536], dst, AF.Square), inc="s_qn")
            P.wait("tensor", "s_qn")
            if g == 0:
                P.wait("tensor", "v_qn", P.counts.get("v_qn", 0))
            P.op("tensor", lambda t, g=g: t.matmul(
                T.psm[0:1, 0:512], T.ones128[:], T.kvstage[:, 1024:1536],
                start=(g == 0), stop=(g == 7)), inc="t_qn")
            grp += 1
        P.wait("vector", "t_qn")
        P.op("vector", lambda v: v.reduce_max(T.small[0:1, 5:6], T.psm[0:1, 0:512], axis=AX))
        P.drain("vector")
        if first_qn:
            P.op("vector", lambda v: v.tensor_copy(T.small[0:1, 6:7], T.small[0:1, 5:6]),
                 inc="v_qn")
            first_qn = False
        else:
            P.op("vector", lambda v: v.tensor_tensor(
                T.small[0:1, 6:7], T.small[0:1, 6:7], T.small[0:1, 5:6], OP.max),
                 inc="v_qn")
        P.drain("vector")
    SCALE = 0.03125
    P.wait("gpsimd", "cc")
    for r in range(4):
        P.dma("gpsimd", T.scratch[0:1, 896 + r:897 + r],
              mk_ap(T.l1_kout, (1025 * r + 1024) * TLOC, [[1, 1], [1, 1]]), "d_km")
    P.wait("vector", "d_km")
    P.drain("vector")
    P.op("vector", lambda v: v.reduce_max(T.small[0:1, 7:8], T.scratch[0:1, 896:900], axis=AX))
    P.drain("vector")
    P.wait("vector", "v_qn")
    P.op("vector", lambda v: v.tensor_tensor(
        T.small[0:1, 8:9], T.small[0:1, 6:7], T.small[0:1, 7:8], OP.mult), inc="v_bnd")
    P.wait("scalar", "v_bnd")
    P.op("scalar", lambda sc: sc.activation(
        T.small[0:1, 9:10], T.small[0:1, 8:9], AF.Sqrt, scale=SCALE * SCALE * 1.05),
        inc="s_bnd")
    P.wait("vector", "s_bnd")
    P.op("vector", lambda v: v.tensor_scalar(
        T.small[0:1, 10:11], T.small[0:1, 9:10], -1.0, -1.0, OP.mult, OP.add), inc="v_bnd")
    P.wait("tensor", "v_bnd")
    P.wait("tensor", "v_qn")
    P.op("tensor", lambda t: t.matmul(
        T.psm[:, 0:1], T.ones1f[:], T.small[0:1, 10:11], start=True, stop=True), inc="t_bnd")
    P.wait("vector", "t_bnd")
    P.op("vector", lambda v: v.tensor_copy(T.boundt, T.psm[:, 0:1]), inc="v_bnd")


def emit_l1_attn(P, T):
    SCALE = 0.03125
    kb = [T.xn_fm[:, 0:1024], T.xn_fm[:, 1024:2048]]
    vb = [T.xn_fm[:, 2048:3072], T.xn_fm[:, 3072:4096]]
    prb = [T.xn_fm[:, 4224:4480], T.xn_fm[:, 4480:4736]]
    P.wait("gpsimd", "t_q")
    dka = [0, 0]
    dva = [0, 0]
    for qg in range(8):
        for kc in range(64):
            r = kc // 16
            sl = kc % 2
            P.wait("gpsimd", "t_sc", P.counts.get("t_sc", 0) - 1)
            dka[sl] = P.dma("gpsimd", kb[sl],
                            mk_ap(T.l1_kout, 1025 * r * TLOC + 128 * (kc % 16),
                                  [[TLOC, 128], [128 * TLOC, 8], [1, 128]]), f"d_ka{sl}")
            P.wait("gpsimd", "t_av", P.counts.get("t_av", 0) - 4)
            dva[sl] = P.dma("gpsimd", vb[sl],
                            T.l1_vout[2048 * r + 128 * (kc % 16):
                                      2048 * r + 128 * (kc % 16) + 128, :], f"d_va{sl}")
            P.wait("tensor", f"d_ka{sl}", dka[sl])
            P.wait("tensor", "s_ex", P.counts.get("s_ex", 0) - 1)
            pb = T.ps[kc % 2]
            for c in range(NDC):
                P.op("tensor", lambda t, pb=pb, c=c, kc=kc, qg=qg: t.matmul(
                    pb[:, 0:256],
                    kb[kc % 2][:, 128 * c:128 * (c + 1)],
                    T.ar1b[:, 16384 + 2048 * c + 256 * qg:16384 + 2048 * c + 256 * (qg + 1)],
                    start=(c == 0), stop=(c == NDC - 1)),
                    inc="t_sc" if c == NDC - 1 else None)
            P.wait("tensor", "s_ex", P.counts.get("s_ex", 0) + 1)
            P.wait("tensor", f"d_va{sl}", dva[sl])
            if kc == 0:
                P.wait("tensor", "v_nm", P.counts.get("v_nm", 0))
                P.wait("tensor", "v_dn", P.counts.get("v_dn", 0))
            for tt in range(2):
                for nc2 in range(2):
                    pav = T.ps[2 + 2 * tt + nc2]
                    P.op("tensor", lambda t, pav=pav, tt=tt, n=nc2, sl=sl, kc=kc: t.matmul(
                        pav[:, 0:512],
                        prb[kc % 2][:, 128 * tt:128 * (tt + 1)],
                        vb[sl][:, 512 * n:512 * (n + 1)],
                        start=(kc == 0), stop=(kc == 63)),
                        inc="t_av")
            P.op("tensor", lambda t, kc=kc: t.matmul(
                T.psm[0:1, 0:256], T.ones128[:], prb[kc % 2][:],
                start=(kc == 0), stop=(kc == 63)), inc="t_dn")
            P.wait("scalar", "t_sc")
            P.wait("scalar", "v_bnd")
            P.wait("scalar", "t_av", P.counts["t_av"] - 4)
            P.wait("scalar", "t_dn", P.counts["t_dn"] - 1)
            P.op("scalar", lambda sc, pb=pb, kc=kc: sc.activation(
                prb[kc % 2][:], pb[:, 0:256], AF.Exp, bias=T.boundt, scale=SCALE),
                inc="s_ex")
        P.wait("vector", "t_dn")
        P.op("vector", lambda v: v.tensor_copy(T.scratch[0:1, 768:1024], T.psm[0:1, 0:256]),
             inc="v_dn")
        P.wait("tensor", "v_dn")
        for tt in range(2):
            P.op("tensor", lambda t, tt=tt: t.transpose(
                T.psm[:, 256 + 128 * tt:256 + 128 * tt + 1],
                T.scratch[0:1, 768 + 128 * tt:768 + 128 * (tt + 1)], T.ident_f[0:1, 0:1]),
                inc="t_dnt" if tt == 1 else None)
        P.wait("vector", "t_dnt")
        for tt in range(2):
            P.op("vector", lambda v, tt=tt: v.reciprocal(
                T.small[:, 8 + tt:9 + tt], T.psm[:, 256 + 128 * tt:256 + 128 * tt + 1]))
        P.drain("vector")
        P.wait("vector", "t_av")
        for tt in range(2):
            jt = 2 * qg + tt
            for nc2 in range(2):
                pav = T.ps[2 + 2 * tt + nc2]
                P.op("vector", lambda v, pav=pav, tt=tt, jt=jt, n=nc2: v.scalar_tensor_tensor(
                    T.scratch[:, 0:512], pav[:, 0:512], T.small[:, 8 + tt:9 + tt],
                    T.ar1b[:, 1024 * jt + 512 * n:1024 * jt + 512 * (n + 1)],
                    OP.mult, OP.mult))
                P.drain("vector")
                P.op("vector", lambda v, jt=jt, n=nc2: v.tensor_tensor(
                    T.x_res[:, 1024 * jt + 512 * n:1024 * jt + 512 * (n + 1)],
                    T.x_res[:, 1024 * jt + 512 * n:1024 * jt + 512 * (n + 1)],
                    T.scratch[:, 0:512], OP.add),
                    inc="v_nm" if (tt, nc2) == (1, 1) else None)
                P.drain("vector")



def alloc_l2(nc, T):
    T.l2_win = nc.dram_tensor("l2_win", [D, 4096], F32, kind="ExternalInput").ap()
    T.l2_cw = nc.dram_tensor("l2_cw", [128, 16], F32, kind="ExternalInput").ap()
    T.l2_cb = nc.dram_tensor("l2_cb", [128, 4], F32, kind="ExternalInput").ap()
    T.l2_alog = nc.dram_tensor("l2_alog", [128, 64], F32, kind="ExternalInput").ap()
    T.l2_dp = nc.dram_tensor("l2_dp", [128, 4], F32, kind="ExternalInput").ap()
    T.l2_wout = nc.dram_tensor("l2_wout", [512, D], F32, kind="ExternalInput").ap()
    T.l2_wg = nc.dram_tensor("l2_wg", [D, 3072], I32, kind="ExternalInput").ap()
    T.l2_wu = nc.dram_tensor("l2_wu", [D, 3072], I32, kind="ExternalInput").ap()
    T.l2_wd = nc.dram_tensor("l2_wd", [3072, D], I32, kind="ExternalInput").ap()
    T.l2_uzin = nc.dram_tensor("l2_uzin", [4096, TLOC], BF16).ap()
    T.l2_uzout = nc.dram_tensor("l2_uzout", [4 * 4096, TLOC], BF16).ap()
    T.l2_rsin = nc.dram_tensor("l2_rsin", [S, D], F32).ap()
    T.l2_rsout = nc.dram_tensor("l2_rsout", [TLOC, D], F32).ap()


def emit_l2_xz(P, T):
    slot_free = [("v_setup", 0), ("v_setup", 0)]
    wslot_free = [("v_setup", 0), ("v_setup", 0)]
    grp = 0
    P.wait("tensor", "v_xn")
    for chcol in range(32):
        ws = chcol % 2
        P.wait("gpsimd", *wslot_free[ws])
        P.dma("gpsimd", T.ar2[:, 2048 * ws:2048 * ws + 1024],
              mk_ap(T.l2_win, 128 * chcol,
                    [[4096, 128], [128 * 4096, 8], [1, 128]]), f"d_xz{ws}")
        P.wait("tensor", f"d_xz{ws}")
        for tg in range(4):
            bank = T.ps[grp % 2]
            P.wait("tensor", "v_xz", P.counts.get("v_xz", 0) - 1)
            for c in range(NDC):
                P.op("tensor", lambda t, b=bank, c=c, ws=ws, tg=tg: t.matmul(
                    b[:, 0:512],
                    T.ar2[:, 2048 * ws + 128 * c:2048 * ws + 128 * (c + 1)],
                    T.xn_fm[:, TLOC * c + 512 * tg:TLOC * c + 512 * (tg + 1)],
                    start=(c == 0), stop=(c == NDC - 1)),
                    inc="t_xz" if c == NDC - 1 else None)
            slot = grp % 2
            stg = T.kvstage[:, 512 * slot:512 * (slot + 1)]
            P.wait("vector", "t_xz")
            P.wait("vector", *slot_free[slot])
            P.op("vector", lambda v, b=bank, stg=stg: v.tensor_copy(stg, b[:, 0:512]),
                 inc="v_xz")
            P.wait("sync", "v_xz")
            nd = P.dma("sync", T.l2_uzin[128 * chcol:128 * (chcol + 1),
                                         512 * tg:512 * (tg + 1)], stg, f"d_kst{slot}")
            slot_free[slot] = (f"d_kst{slot}", nd)
            grp += 1
        wslot_free[ws] = ("t_xz", P.counts["t_xz"])
    P.wait("gpsimd", "d_kst0")
    P.wait("gpsimd", "d_kst1")
    if SKIP_CC:
        P.counts["cc"] = P.counts.get("cc", 0) + 1
        P.sem("cc")
        P.op("gpsimd", lambda g: g.nop(), inc=None)
        P.counts["cc"] -= 1
        P.op("gpsimd", lambda g: g.nop(), inc="cc")
    else:
        P.op("gpsimd", lambda g: g.collective_compute(
            "AllGather", OP.bypass, replica_groups=GROUPS,
            ins=[T.l2_uzin.opt()], outs=[T.l2_uzout.opt()]), inc="cc")


def emit_l2_ffn(P, T):
    wbase = 12288
    wslot_free = [("v_setup", 0), ("v_setup", 0)]
    dslot_free = [("v_setup", 0), ("v_setup", 0)]
    hs_free = [("v_setup", 0), ("v_setup", 0)]
    P.wait("gpsimd", "t_xz", P.counts.get("t_xz", 0))
    P.wait("scalar", "d_kst0")
    P.wait("scalar", "d_kst1")
    for tg in range(8):
        P.wait("gpsimd", "t_do", P.counts.get("t_do", 0))
        for hk in range(24):
            ws = hk % 2
            P.wait("gpsimd", *wslot_free[ws])
            P.dma("gpsimd", mk_ap(T.ar2[:, 0:1], wbase + 2048 * ws,
                                  [[16384, 128], [128, 8], [1, 128]]),
                  mk_ap(T.l2_wg, 128 * hk, [[3072, 128], [128 * 3072, 8], [1, 128]]),
                  f"d_wg{ws}")
            P.dma("gpsimd", mk_ap(T.ar2[:, 0:1], wbase + 2048 * ws + 1024,
                                  [[16384, 128], [128, 8], [1, 128]]),
                  mk_ap(T.l2_wu, 128 * hk, [[3072, 128], [128 * 3072, 8], [1, 128]]),
                  f"d_wg{ws}")
            gb = T.ps[hk % 2]
            ub = T.ps[2 + hk % 2]
            P.wait("tensor", f"d_wg{ws}")
            P.wait("tensor", "v_h", P.counts.get("v_h", 0) - 1)
            for c in range(NDC):
                P.op("tensor", lambda t, gb=gb, c=c, ws=ws, tg=tg: t.matmul(
                    gb[:, 0:256],
                    T.ar2[:, wbase + 2048 * ws + 128 * c:wbase + 2048 * ws + 128 * (c + 1)],
                    T.xn_fm[:, TLOC * c + 256 * tg:TLOC * c + 256 * (tg + 1)],
                    start=(c == 0), stop=(c == NDC - 1)), inc=None)
            for c in range(NDC):
                P.op("tensor", lambda t, ub=ub, c=c, ws=ws, tg=tg: t.matmul(
                    ub[:, 0:256],
                    T.ar2[:, wbase + 2048 * ws + 1024 + 128 * c:
                          wbase + 2048 * ws + 1024 + 128 * (c + 1)],
                    T.xn_fm[:, TLOC * c + 256 * tg:TLOC * c + 256 * (tg + 1)],
                    start=(c == 0), stop=(c == NDC - 1)),
                    inc="t_gu" if c == NDC - 1 else None)
            hsl = hk % 2
            gsb = T.kvstage[:, 256 * hsl:256 * (hsl + 1)]
            P.wait("scalar", "t_gu")
            P.wait("scalar", *hs_free[hsl])
            P.op("scalar", lambda sc, gb=gb, gsb=gsb: sc.activation(
                gsb, gb[:, 0:256], AF.Silu, scale=1.0 / 127.0), inc="s_gu")
            P.wait("vector", "s_gu")
            nh = P.op("vector", lambda v, ub=ub, gsb=gsb, hk=hk: v.scalar_tensor_tensor(
                T.ar2[:, 256 * hk:256 * (hk + 1)], ub[:, 0:256], 1.0 / 127.0, gsb,
                OP.mult, OP.mult), inc="v_h")
            hs_free[hsl] = ("v_h", nh)
            wslot_free[ws] = ("t_gu", P.counts["t_gu"])
        for nc2 in range(2):
            P.wait("gpsimd", "t_gu", P.counts["t_gu"])
            for hk in range(24):
                ds = hk % 2
                P.wait("gpsimd", *dslot_free[ds])
                P.dma("gpsimd", T.ar2[:, wbase + 2048 * ds:wbase + 2048 * ds + 512],
                      T.l2_wd[128 * hk:128 * (hk + 1), 512 * nc2:512 * (nc2 + 1)],
                      f"d_wd{ds}")
                P.wait("tensor", f"d_wd{ds}")
                P.wait("tensor", "v_h", P.counts["v_h"])
                if hk == 0:
                    P.wait("tensor", "s_go", P.counts.get("s_go", 0))
                for tt in range(2):
                    P.op("tensor", lambda t, tt=tt, hk=hk, ds=ds: t.matmul(
                        T.ps[4 + tt][:, 0:512],
                        T.ar2[:, 256 * hk + 128 * tt:256 * hk + 128 * (tt + 1)],
                        T.ar2[:, wbase + 2048 * ds:wbase + 2048 * ds + 512],
                        start=(hk == 0), stop=(hk == 23)),
                        inc="t_do" if tt == 1 else None)
                dslot_free[ds] = ("t_do", P.counts["t_do"])
            P.wait("scalar", "t_do")
            for tt in range(2):
                jt = 2 * tg + tt
                P.op("scalar", lambda sc, tt=tt, jt=jt, n=nc2: sc.activation(
                    T.ar1b[:, 1024 * jt + 512 * n:1024 * jt + 512 * (n + 1)],
                    T.ps[4 + tt][:, 0:512], AF.Silu, scale=1.0 / 127.0), inc="s_go")


def emit_l2_scan(P, T):
    P.wait("sync", "v_xn", P.counts.get("v_xn", 0))
    P.wait("sync", "s_ln", P.counts.get("s_ln", 0))
    P.wait("sync", "v_gate", P.counts.get("v_gate", 0))
    P.wait("sync", "v_cmb", P.counts.get("v_cmb", 0))
    P.wait("sync", "t_b2t", P.counts.get("t_b2t", 0))
    P.dma("sync", T.w_all[:, 16:32], T.l2_cw, "d_l2p")
    P.dma("sync", T.w_all[:, 32:36], T.l2_cb, "d_l2p")
    P.dma("sync", T.w_all[:, 36:40], T.l2_dp, "d_l2p")
    P.dma("sync", T.stats[:, 0:64], T.l2_alog, "d_l2p")
    P.wait("scalar", "d_l2p")
    P.wait("scalar", "s_go", P.counts.get("s_go", 0))
    P.drain("scalar")
    P.op("scalar", lambda sc: sc.activation(T.stats[:, 0:64], T.stats[:, 0:64], AF.Exp),
         inc="s_ae")
    P.wait("gpsimd", "t_do", P.counts.get("t_do", 0))
    P.wait("gpsimd", "v_h", P.counts.get("v_h", 0))
    for ct in range(4):
        P.dma("gpsimd", T.ar2[:, 1024 * ct:1024 * (ct + 1)],
              T.l2_wout[128 * ct:128 * (ct + 1), :], "d_wo")
    u_sb = T.xn_fm[:, 0:4104]
    ut_sb = T.xn_fm[:, 4104:8200]
    z_sb = T.xn_fm[:, 8200:12296]
    accs = T.ar1[:, 8192:12288]
    yac = T.ar1[:, 12288:16384]
    carry = T.w_all[:, 64:80]
    sign_b = mk_ap(T.sign_sb[:], 0, [[512, 128], [0, 8], [1, 512]])
    P.wait("gpsimd", "cc")
    P.wait("gpsimd", "t_gu", P.counts.get("t_gu", 0))
    P.wait("vector", "s_go", P.counts.get("s_go", 0))
    stage_free = [("v_setup", 0), ("v_setup", 0)]
    for ct in range(4):
        for th in range(2):
            P.wait("vector", "d_u", P.counts.get("d_u", 0))
            P.drain("vector")
            if th == 0:
                P.op("vector", lambda v: v.memset(u_sb[:, 0:4], 0.0), inc="v_up")
            else:
                P.op("vector", lambda v: v.tensor_copy(u_sb[:, 0:4], u_sb[:, 4096:4100]),
                     inc="v_up")
            P.drain("vector")
            P.wait("gpsimd", "v_up")
            P.wait("gpsimd", "v_cv", P.counts.get("v_cv", 0))
            P.wait("gpsimd", "t_wo", P.counts.get("t_wo", 0))
            h = P.sem("d_u")

            def uz_load(g, ct=ct, th=th, h=h, u_sb=u_sb, z_sb=z_sb, T=T):
                for arm in g.switch_core_id(n=128):
                    r = (arm.logical if hasattr(arm, "logical") else int(arm)) % 4
                    for k, rr in enumerate((2 * th, 2 * th + 1)):
                        g.dma_start(
                            out=u_sb[:, 4 + 2048 * k:4 + 2048 * (k + 1)],
                            in_=T.l2_uzout[4096 * rr + 512 * r + 128 * ct:
                                           4096 * rr + 512 * r + 128 * (ct + 1), :]
                        ).then_inc(h, 16)
                        g.dma_start(
                            out=z_sb[:, 2048 * k:2048 * (k + 1)],
                            in_=T.l2_uzout[4096 * rr + 2048 + 512 * r + 128 * ct:
                                           4096 * rr + 2048 + 512 * r + 128 * (ct + 1), :]
                        ).then_inc(h, 16)
            P.items["gpsimd"].append(uz_load)
            P.counts["d_u"] += 64
            P.wait("vector", "d_l2p")
            P.wait("vector", "d_u")
            P.wait("vector", "v_ya", P.counts.get("v_ya", 0))
            P.drain("vector")
            P.op("vector", lambda v, ct=ct: v.tensor_scalar(
                accs[:, 0:4096], u_sb[:, 1:4097], T.w_all[:, 16 + 4 * ct:17 + 4 * ct],
                None, OP.mult))
            P.drain("vector")
            for jj in range(1, 4):
                P.op("vector", lambda v, jj=jj, ct=ct: v.scalar_tensor_tensor(
                    accs[:, 0:4096], u_sb[:, 1 + jj:4097 + jj],
                    T.w_all[:, 16 + 4 * ct + jj:17 + 4 * ct + jj],
                    accs[:, 0:4096], OP.mult, OP.add))
                P.drain("vector")
            P.op("vector", lambda v, ct=ct: v.scalar_tensor_tensor(
                mk_ap(ut_sb, 0, [[16384, 128], [512, 8], [1, 512]]),
                mk_ap(accs, 0, [[16384, 128], [512, 8], [1, 512]]),
                T.w_all[:, 32 + ct:33 + ct], sign_b, OP.add, OP.mult), inc="v_cv")
            P.drain("vector")
            P.wait("vector", "s_ae")
            for s_ in range(16):
                a_bc = T.stats[:, 16 * ct + s_:16 * ct + s_ + 1].broadcast_to([128, 4096])
                if th == 0:
                    P.op("vector", lambda v, a_bc=a_bc: v.tensor_tensor_scan(
                        accs[:, 0:4096], a_bc, ut_sb[:, 0:4096], 0.0,
                        OP.mult, OP.add))
                else:
                    P.op("vector", lambda v, a_bc=a_bc, s_=s_: v.tensor_tensor_scan(
                        accs[:, 0:4096], a_bc, ut_sb[:, 0:4096], carry[:, s_:s_ + 1],
                        OP.mult, OP.add))
                P.drain("vector")
                P.op("vector", lambda v, s_=s_: v.tensor_copy(
                    carry[:, s_:s_ + 1], accs[:, 4095:4096]))
                P.drain("vector")
                if s_ == 0:
                    P.op("vector", lambda v: v.tensor_copy(yac[:, 0:4096], accs[:, 0:4096]))
                else:
                    P.op("vector", lambda v: v.tensor_tensor(
                        yac[:, 0:4096], yac[:, 0:4096], accs[:, 0:4096], OP.add))
                P.drain("vector")
            P.op("vector", lambda v: v.tensor_copy(T.small[:, 11:12], T.small[:, 11:12])
                 if False else v.tensor_copy(T.small[:, 11:12], carry[:, 15:16]),
                 inc="v_ya")
            for cc5 in range(8):
                zc = z_sb[:, 512 * cc5:512 * (cc5 + 1)]
                P.wait("scalar", "v_ya", P.counts["v_ya"])
                P.wait("scalar", "v_ys", P.counts.get("v_ys", 0))
                P.wait("scalar", "d_rs0", P.counts.get("d_rs0", 0))
                P.wait("scalar", "d_rs1", P.counts.get("d_rs1", 0))
                P.op("scalar", lambda sc, zc=zc: sc.activation(
                    T.scratch[:, 0:512], zc, AF.Silu), inc="s_sz")
                P.wait("vector", "s_sz")
                P.drain("vector")
                P.op("vector", lambda v: v.tensor_tensor(
                    T.scratch[:, 512:1024], T.scratch[:, 0:512], T.sign_sb[:], OP.mult))
                P.drain("vector")
                P.op("vector", lambda v, cc5=cc5, ct=ct: v.scalar_tensor_tensor(
                    T.scratch[:, 0:512], ut_sb[:, 512 * cc5:512 * (cc5 + 1)],
                    T.w_all[:, 36 + ct:37 + ct], yac[:, 512 * cc5:512 * (cc5 + 1)],
                    OP.mult, OP.add))
                P.drain("vector")
                P.op("vector", lambda v, zc=zc: v.tensor_tensor(
                    zc, T.scratch[:, 0:512], T.scratch[:, 512:1024], OP.mult),
                    inc="v_ys")
                P.drain("vector")
            for tt in range(32):
                P.wait("tensor", "v_ys", P.counts["v_ys"])
                P.wait("tensor", "d_wo")
                for nc2 in range(2):
                    bank = T.ps[(tt * 2 + nc2) % 2]
                    P.wait("tensor", "v_woe", P.counts.get("v_woe", 0) - 1)
                    P.op("tensor", lambda t, b=bank, tt=tt, n=nc2, ct=ct: t.matmul(
                        b[:, 0:512],
                        z_sb[:, 128 * tt:128 * (tt + 1)],
                        T.ar2[:, 1024 * ct + 512 * n:1024 * ct + 512 * (n + 1)],
                        start=True, stop=True), inc="t_wo")
                    sl = (tt * 2 + nc2) % 2
                    stg = T.scratch[:, 512 * sl:512 * (sl + 1)]
                    P.wait("vector", "t_wo")
                    P.wait("vector", *stage_free[sl])
                    P.op("vector", lambda v, b=bank, stg=stg: v.tensor_copy(stg, b[:, 0:512]),
                         inc="v_woe")
                    P.wait("gpsimd", "v_woe")
                    tG = th * 4096 + 128 * tt
                    nd = P.dma("gpsimd", T.l2_rsin[tG:tG + 128, 512 * nc2:512 * (nc2 + 1)],
                               stg, f"d_rs{sl}",
                               **({} if ct == 0 else {"accum_op": OP.add}))
                    stage_free[sl] = (f"d_rs{sl}", nd)


def emit_l2_rs(P, T):
    P.wait("gpsimd", "d_rs0")
    P.wait("gpsimd", "d_rs1")
    if SKIP_CC:
        P.counts["cc"] = P.counts.get("cc", 0) + 1
        P.sem("cc")
        P.op("gpsimd", lambda g: g.nop(), inc=None)
        P.counts["cc"] -= 1
        P.op("gpsimd", lambda g: g.nop(), inc="cc")
    else:
        P.op("gpsimd", lambda g: g.collective_compute(
            "ReduceScatter", OP.add, replica_groups=GROUPS,
            ins=[T.l2_rsin.opt()], outs=[T.l2_rsout.opt()]), inc="cc")
    P.wait("sync", "cc")
    for j in range(NTT):
        P.wait("sync", "v_fin", P.counts.get("v_fin", 0))
        P.dma("sync", T.scratch[:], T.l2_rsout[128 * j:128 * (j + 1), :], "d_s")
        P.wait("vector", "d_s")
        P.wait("vector", "s_go", P.counts["s_go"])
        P.drain("vector")
        P.op("vector", lambda v, j=j: v.tensor_tensor(
            T.scratch[:], T.scratch[:], T.ar1b[:, 1024 * j:1024 * (j + 1)], OP.mult))
        P.drain("vector")
        P.op("vector", lambda v, j=j: v.tensor_tensor(
            T.x_res[:, 1024 * j:1024 * (j + 1)], T.x_res[:, 1024 * j:1024 * (j + 1)],
            T.scratch[:], OP.add), inc="v_fin")
        P.drain("vector")



def alloc_l3(nc, T):
    for nm, shp in [("l3_wdq", [D, 512]), ("l3_wuq", [512, D]), ("l3_wqr", [512, D]),
                    ("l3_wkr", [D, D]), ("l3_wdkv", [D, 512]), ("l3_wuk", [512, D]),
                    ("l3_wuv", [512, D]), ("l3_wo", [D, D]), ("l3_gw", [D, 8]),
                    ("l3_ew", [8, D, D]), ("l3_eb", [8, D]), ("l3_cos", [TLOC, D]),
                    ("l3_sin", [TLOC, D]), ("l3_bdq", [128, 4]), ("l3_buq", [128, 8]),
                    ("l3_bdkv", [128, 4]), ("l3_buk", [128, 8]),
                    ("l3_bqr", [1, D]), ("l3_bkr", [1, D]), ("l3_buv", [1, D]),
                    ("l3_bo", [1, D]), ("l3_gb", [1, 8])]:
        setattr(T, nm, nc.dram_tensor(nm, shp, F32, kind="ExternalInput").ap())
    T.l3_kin = nc.dram_tensor("l3_kin", [1025, TLOC], BF16).ap()
    T.l3_kout = nc.dram_tensor("l3_kout", [4 * 1025, TLOC], BF16).ap()
    T.l3_vin = nc.dram_tensor("l3_vin", [TLOC, D], BF16).ap()
    T.l3_vout = nc.dram_tensor("l3_vout", [4 * TLOC, D], BF16).ap()


def emit_l3_ct(P, T, wdq_ap, bias_dram, wsem, vsem):
    P.wait("gpsimd", "t_kv", P.counts.get("t_kv", 0))
    P.wait("gpsimd", "t_kr", P.counts.get("t_kr", 0))
    P.wait("gpsimd", "t_up", P.counts.get("t_up", 0))
    for c in range(NDC):
        P.dma("gpsimd", T.ar2[:, 12288 + 512 * c:12288 + 512 * (c + 1)],
              wdq_ap[128 * c:128 * (c + 1), :], wsem)
    P.dma("sync", T.bkp[:, 0:4], bias_dram, "d_bp3")
    P.wait("tensor", wsem)
    P.wait("tensor", "v_xn")
    for s_ in ("v_ckv", "v_ctq", "v_kc", "v_qc", "v_kv", "v_rpe"):
        P.wait("tensor", s_, P.counts.get(s_, 0))
    grp = 0
    for ck in range(4):
        for tg in range(4):
            bank = T.ps[grp % 2]
            P.wait("tensor", vsem, P.counts.get(vsem, 0) - 1)
            for c in range(NDC):
                P.op("tensor", lambda t, b=bank, c=c, ck=ck, tg=tg: t.matmul(
                    b[:, 0:512],
                    T.ar2[:, 12288 + 512 * c + 128 * ck:12288 + 512 * c + 128 * (ck + 1)],
                    T.xn_fm[:, TLOC * c + 512 * tg:TLOC * c + 512 * (tg + 1)],
                    start=(c == 0), stop=(c == NDC - 1)),
                    inc="t_ct" if c == NDC - 1 else None)
            P.wait("vector", "t_ct")
            P.wait("vector", "d_bp3")
            P.op("vector", lambda v, b=bank, ck=ck, tg=tg: v.tensor_scalar(
                T.ar2[:, 2048 * ck + 512 * tg:2048 * ck + 512 * (tg + 1)],
                b[:, 0:512], T.bkp[:, ck:ck + 1], None, OP.add), inc=vsem)
            grp += 1


def emit_l3_up_pair(P, T, wu_ap, bup_dram, dst_base, vsem):
    P.wait("gpsimd", "t_ct", P.counts["t_ct"])
    for c in range(4):
        P.dma("gpsimd", T.ar2[:, 12288 + 1024 * c:12288 + 1024 * (c + 1)],
              wu_ap[128 * c:128 * (c + 1), :], "d_wu3")
    P.dma("sync", T.bqp[:], bup_dram, "d_bp3b")
    P.wait("tensor", "d_wu3")
    for s_ in ("v_ckv", "v_ctq", "v_kc", "v_qc", "v_kv", "v_rpe"):
        P.wait("tensor", s_, P.counts.get(s_, 0))
    grp = 0
    for g in range(8):
        for tg in range(4):
            bank = T.ps[grp % 2]
            P.wait("tensor", vsem, P.counts.get(vsem, 0) - 1)
            for c in range(4):
                P.op("tensor", lambda t, b=bank, c=c, g=g, tg=tg: t.matmul(
                    b[:, 0:512],
                    T.ar2[:, 12288 + 1024 * c + 128 * g:12288 + 1024 * c + 128 * (g + 1)],
                    T.ar2[:, 2048 * c + 512 * tg:2048 * c + 512 * (tg + 1)],
                    start=(c == 0), stop=(c == 3)),
                    inc="t_up" if c == 3 else None)
            P.wait("vector", "t_up")
            P.wait("vector", "d_bp3b")
            P.op("vector", lambda v, b=bank, g=g, tg=tg, dst_base=dst_base: v.tensor_scalar(
                T.ar1b[:, dst_base + 2048 * g + 512 * tg:dst_base + 2048 * g + 512 * (tg + 1)],
                b[:, 0:512], T.bqp[:, g:g + 1], None, OP.add), inc=vsem)
            grp += 1


def emit_l3_rope_add(P, T, wr_ap, br_ap, rhs_is_ct, dst_base):
    ropef = T.ar2[:, 8192:12288].bitcast(F32)
    nk = 8 if not rhs_is_ct else 4
    for nc2 in range(2):
        P.wait("gpsimd", "t_kr", P.counts.get("t_kr", 0))
        P.wait("gpsimd", "t_up", P.counts.get("t_up", 0))
        P.wait("gpsimd", "t_ct", P.counts.get("t_ct", 0))
        for c in range(nk):
            P.dma("gpsimd", T.ar2[:, 12288 + 512 * c:12288 + 512 * (c + 1)],
                  wr_ap[128 * c:128 * (c + 1), 512 * nc2:512 * (nc2 + 1)], "d_wr3")
        P.dma("gpsimd", T.nstage[0:1, 0:1024], br_ap, "d_wr3")
        P.wait("tensor", "d_wr3")
        for j in range(NTT):
            P.wait("gpsimd", "v_rp", P.counts.get("v_rp", 0))
            P.dma("gpsimd", ropef[:, 512 * nc2:512 * (nc2 + 1)],
                  T.l3_cos[128 * j:128 * (j + 1), 512 * nc2:512 * (nc2 + 1)], "d_rt")
            P.dma("gpsimd", ropef[:, 1024 + 512 * nc2:1024 + 512 * (nc2 + 1)],
                  T.l3_sin[128 * j:128 * (j + 1), 512 * nc2:512 * (nc2 + 1)], "d_rt")
            bank = T.ps[2 + j % 2]
            P.wait("tensor", "v_rp", P.counts.get("v_rp", 0) - 1)
            for c in range(nk):
                rhs_src = (T.xn_fm[:, TLOC * c + 128 * j:TLOC * c + 128 * (j + 1)]
                           if not rhs_is_ct else
                           T.ar2[:, 2048 * c + 128 * j:2048 * c + 128 * (j + 1)])
                P.op("tensor", lambda t, b=bank, s=rhs_src, c=c: t.matmul(
                    b[:, 0:512], s, T.ar2[:, 12288 + 512 * c:12288 + 512 * (c + 1)],
                    start=(c == 0), stop=False), inc=None)
            P.op("tensor", lambda t, b=bank, n=nc2: t.matmul(
                b[:, 0:512], T.ones1[:], T.nstage[0:1, 512 * n:512 * (n + 1)],
                start=False, stop=True), inc="t_kr")
            c2 = ropef[:, 512 * nc2:512 * (nc2 + 1)]
            s2 = ropef[:, 1024 + 512 * nc2:1024 + 512 * (nc2 + 1)]
            P.wait("vector", "t_kr")
            P.wait("vector", "d_rt")
            P.drain("vector")
            P.op("vector", lambda v, b=bank, c2=c2: v.tensor_tensor(
                T.scratch[:, 0:512], b[:, 0:512], c2, OP.mult))
            P.drain("vector")
            P.op("vector", lambda v, b=bank: v.tensor_copy(
                mk_ap(T.scratch[:], 512, [[1024, 128], [2, 256]]),
                mk_ap(b[:, 0:1], 1, [[512, 128], [2, 256]])))
            P.op("vector", lambda v, b=bank: v.tensor_copy(
                mk_ap(T.scratch[:], 513, [[1024, 128], [2, 256]]),
                mk_ap(b[:, 0:1], 0, [[512, 128], [2, 256]])))
            P.drain("vector")
            P.op("vector", lambda v, s2=s2: v.tensor_tensor(
                T.scratch[:, 512:1024], T.scratch[:, 512:1024], s2, OP.mult))
            P.drain("vector")
            P.op("vector", lambda v: v.tensor_tensor(
                T.kvstage[:, 0:512], T.scratch[:, 0:512], T.scratch[:, 512:1024], OP.add),
                inc="v_rp")
            P.wait("tensor", "v_rp")
            P.wait("tensor", "v_rpe", P.counts.get("v_rpe", 0))
            for c4 in range(4):
                P.op("tensor", lambda t, c4=c4: t.transpose(
                    T.pst[:, 128 * c4:128 * (c4 + 1)],
                    T.kvstage[:, 128 * c4:128 * (c4 + 1)], T.ident_b[:]),
                    inc="t_rpt" if c4 == 3 else None)
            P.wait("vector", "t_rpt")
            for c4 in range(4):
                gp = nc2 * 4 + c4
                dst = T.ar1b[:, dst_base + 2048 * gp + 128 * j:
                             dst_base + 2048 * gp + 128 * (j + 1)]
                P.op("vector", lambda v, dst=dst, c4=c4: v.tensor_tensor(
                    dst, dst, T.pst[:, 128 * c4:128 * (c4 + 1)], OP.add),
                    inc="v_rpe" if c4 == 3 else None)
            P.wait("tensor", "v_rpe")


def emit_l3_kv(P, T):
    emit_l3_ct(P, T, T.l3_wdkv, T.l3_bdkv, "d_wd3", "v_ckv")
    emit_l3_up_pair(P, T, T.l3_wuk, T.l3_buk, 16384, "v_kc")
    emit_l3_rope_add(P, T, T.l3_wkr, T.l3_bkr, False, 16384)
    slot_free = [("v_setup", 0), ("v_setup", 0)]
    first_nrm = True
    sgrp = 0
    P.wait("vector", "v_rpe", P.counts["v_rpe"])
    for g in range(8):
        for tg in range(4):
            src = T.ar1b[:, 16384 + 2048 * g + 512 * tg:16384 + 2048 * g + 512 * (tg + 1)]
            slot = sgrp % 2
            P.wait("scalar", "v_rpe", P.counts["v_rpe"])
            P.wait("scalar", "t_nrm", P.counts.get("t_nrm", 0))
            P.op("scalar", lambda sc, src=src: sc.activation(
                T.kvstage[:, 1024:1536], src, AF.Square), inc="s_kv")
            P.wait("tensor", "s_kv")
            P.wait("tensor", "v_nrm", P.counts.get("v_nrm", 0))
            P.op("tensor", lambda t: t.matmul(
                T.psm[0:1, 0:512], T.ones128[:], T.kvstage[:, 1024:1536],
                start=True, stop=True), inc="t_nrm")
            P.wait("vector", "t_nrm")
            P.op("vector", lambda v: v.reduce_max(T.small[0:1, 3:4], T.psm[0:1, 0:512],
                                                  axis=AX))
            P.drain("vector")
            if first_nrm:
                P.op("vector", lambda v: v.tensor_copy(T.small[0:1, 4:5], T.small[0:1, 3:4]),
                     inc="v_nrm")
                first_nrm = False
            else:
                P.op("vector", lambda v: v.tensor_tensor(
                    T.small[0:1, 4:5], T.small[0:1, 4:5], T.small[0:1, 3:4], OP.max),
                     inc="v_nrm")
            P.drain("vector")
            P.wait("sync", *slot_free[slot])
            P.wait("sync", "v_rpe")
            nd = P.dma("sync", T.l3_kin[128 * g:128 * (g + 1), 512 * tg:512 * (tg + 1)],
                       src, f"d_kst{slot}")
            slot_free[slot] = (f"d_kst{slot}", nd)
            sgrp += 1
    P.wait("vector", "v_nrm")
    P.op("vector", lambda v: v.tensor_copy(T.nstage[0:1, 0:1], T.small[0:1, 4:5]),
         inc="v_kmax")
    P.wait("sync", "v_kmax")
    P.dma("sync", mk_ap(T.l3_kin, 1024 * TLOC, [[1, 1], [1, 1]]), T.nstage[0:1, 0:1],
          "d_nst")
    P.wait("gpsimd", "t_up", P.counts["t_up"])
    P.wait("gpsimd", "t_kr", P.counts["t_kr"])
    for c in range(4):
        P.dma("gpsimd", T.ar2[:, 12288 + 1024 * c:12288 + 1024 * (c + 1)],
              T.l3_wuv[128 * c:128 * (c + 1), :], "d_wv3")
    P.wait("gpsimd", "d_nst", P.counts.get("d_nst", 0))
    P.dma("gpsimd", T.nstage[0:1, 0:1024], T.l3_buv, "d_wv3")
    P.wait("tensor", "d_wv3")
    for j in range(NTT):
        for nc2 in range(2):
            bank = T.ps[(j * 2 + nc2) % 2]
            P.wait("tensor", "v_kv", P.counts.get("v_kv", 0) - 1)
            for c in range(4):
                P.op("tensor", lambda t, b=bank, c=c, j=j, n=nc2: t.matmul(
                    b[:, 0:512],
                    T.ar2[:, 2048 * c + 128 * j:2048 * c + 128 * (j + 1)],
                    T.ar2[:, 12288 + 1024 * c + 512 * n:12288 + 1024 * c + 512 * (n + 1)],
                    start=(c == 0), stop=False), inc=None)
            P.op("tensor", lambda t, b=bank, n=nc2: t.matmul(
                b[:, 0:512], T.ones1[:], T.nstage[0:1, 512 * n:512 * (n + 1)],
                start=False, stop=True), inc="t_kv")
            slot = (j * 2 + nc2) % 2
            stg = T.kvstage[:, 512 * slot:512 * (slot + 1)]
            P.wait("vector", "t_kv")
            P.wait("vector", *slot_free[slot])
            P.op("vector", lambda v, b=bank, stg=stg: v.tensor_copy(stg, b[:, 0:512]),
                 inc="v_kv")
            P.wait("sync", "v_kv")
            nd = P.dma("sync", T.l3_vin[128 * j:128 * (j + 1), 512 * nc2:512 * (nc2 + 1)],
                       stg, f"d_kst{slot}")
            slot_free[slot] = (f"d_kst{slot}", nd)
    P.wait("gpsimd", "d_kst0")
    P.wait("gpsimd", "d_kst1")
    P.wait("gpsimd", "d_nst")
    if SKIP_CC:
        P.counts["cc"] = P.counts.get("cc", 0) + 1
        P.sem("cc")
        P.op("gpsimd", lambda g: g.nop(), inc=None)
        P.counts["cc"] -= 1
        P.op("gpsimd", lambda g: g.nop(), inc="cc")
    else:
        P.op("gpsimd", lambda g: g.collective_compute(
            "AllGather", OP.bypass, replica_groups=GROUPS,
            ins=[T.l3_kin.opt()], outs=[T.l3_kout.opt()]), inc="cc")
    if SKIP_CC:
        P.counts["cc"] = P.counts.get("cc", 0) + 1
        P.sem("cc")
        P.op("gpsimd", lambda g: g.nop(), inc=None)
        P.counts["cc"] -= 1
        P.op("gpsimd", lambda g: g.nop(), inc="cc")
    else:
        P.op("gpsimd", lambda g: g.collective_compute(
            "AllGather", OP.bypass, replica_groups=GROUPS,
            ins=[T.l3_vin.opt()], outs=[T.l3_vout.opt()]), inc="cc")


def emit_l3_ffn(P, T):
    for c in range(NDC):
        P.dma("sync", T.gw_sb[:, 8 * c:8 * (c + 1)], T.l3_gw[128 * c:128 * (c + 1), :],
              "d_gw3")
    P.dma("sync", T.stats[0:1, 56:64], T.l3_gb, "d_gw3")
    P.wait("tensor", "d_gw3")
    P.wait("tensor", "v_xn")
    P.wait("vector", "d_x")
    for j in range(NTT):
        m = T.stats[:, j:j + 1]
        rs = T.stats[:, 16 + j:16 + j + 1]
        P.wait("vector", "s_ln", P.counts.get("s_ln", 0))
        P.wait("vector", "t_gxp", P.counts.get("t_gxp", 0))
        P.drain("vector")
        P.op("vector", lambda v, j=j, m=m, rs=rs: v.tensor_scalar(
            T.scratch[:], T.x_res[:, 1024 * j:1024 * (j + 1)], m, rs,
            OP.subtract, OP.mult), inc="v_gxn")
        P.wait("tensor", "v_gxn")
        P.wait("tensor", "v_kv", P.counts.get("v_kv", 0))
        P.wait("tensor", "v_ckv", P.counts.get("v_ckv", 0))
        for c in range(NDC):
            P.op("tensor", lambda t, c=c: t.transpose(
                T.ps[c // 4][:, 128 * (c % 4):128 * (c % 4) + 128],
                T.scratch[:, 128 * c:128 * (c + 1)], T.ident_f[:]),
                inc="t_gxp" if c == NDC - 1 else None)
        P.wait("vector", "t_gxp")
        for c in range(NDC):
            P.op("vector", lambda v, c=c: v.tensor_copy(
                T.atmp[:, 128 * c:128 * (c + 1)],
                T.ps[c // 4][:, 128 * (c % 4):128 * (c % 4) + 128]),
                inc="v_gxt" if c == NDC - 1 else None)
        P.wait("tensor", "v_gate", P.counts.get("v_gate", 0))
        P.wait("tensor", "v_gxt")
        for c in range(NDC):
            P.op("tensor", lambda t, c=c, j=j: t.matmul(
                T.psm[:, 0:8],
                T.atmp[:, 128 * c:128 * (c + 1)],
                T.gw_sb[:, 8 * c:8 * (c + 1)],
                start=(c == 0), stop=False), inc=None)
        P.op("tensor", lambda t: t.matmul(
            T.psm[:, 0:8], T.ones1f[:], T.stats[0:1, 56:64],
            start=False, stop=True), inc="t_gate")
        w = T.w_all[:, 14 * j:14 * j + 8]
        P.wait("vector", "t_gate")
        P.drain("vector")
        P.op("vector", lambda v: v.reduce_max(T.small[:, 0:1], T.psm[:, 0:8], axis=AX))
        P.drain("vector")
        P.op("vector", lambda v, w=w: v.tensor_scalar(
            w, T.psm[:, 0:8], T.small[:, 0:1], None, OP.is_ge), inc="v_gate")
        P.drain("vector")
    for j in range(NTT):
        P.wait("tensor", "v_gate", P.counts["v_gate"])
        P.wait("tensor", "v_b2t", P.counts.get("v_b2t", 0))
        P.wait("tensor", "v_b2p", P.counts.get("v_b2p", 0))
        P.op("tensor", lambda t, j=j: t.transpose(
            T.psm[0:8, 0:128], T.w_all[:, 14 * j:14 * j + 8], T.ident_f[:]), inc="t_b2t")
        P.wait("vector", "t_b2t")
        P.op("vector", lambda v: v.tensor_copy(T.wT16[0:8, 0:128], T.psm[0:8, 0:128]),
             inc="v_b2t")
        for nc2 in range(2):
            P.wait("gpsimd", "t_b2p", P.counts.get("t_b2p", 0))
            P.dma("gpsimd", T.b14[0:8, 0:512], T.l3_eb[0:8, 512 * nc2:512 * (nc2 + 1)],
                  "d_b2")
            P.wait("tensor", "v_b2t")
            P.wait("tensor", "d_b2")
            P.wait("tensor", "v_b2p", P.counts.get("v_b2p", 0))
            P.op("tensor", lambda t: t.matmul(
                T.psm[:, 0:512], T.wT16[0:8, 0:128], T.b14[0:8, 0:512],
                start=True, stop=True), inc="t_b2p")
            P.wait("vector", "t_b2p")
            P.op("vector", lambda v, j=j, n=nc2: v.tensor_copy(
                T.ar1[:, 1024 * j + 512 * n:1024 * j + 512 * (n + 1)], T.psm[:, 0:512]),
                 inc="v_b2p")
    eslot_free = [("v_setup", 0), ("v_setup", 0)]
    for e in range(8):
        es_ = e % 2
        P.wait("gpsimd", *eslot_free[es_])
        P.wait("gpsimd", "t_rpt", P.counts.get("t_rpt", 0))
        for c in range(NDC):
            P.dma("gpsimd", T.ar2[:, 8192 * es_ + 1024 * c:8192 * es_ + 1024 * (c + 1)],
                  mk_ap(T.l3_ew, e * D * D + 128 * c * D, [[1024, 128], [1, 1024]]),
                  f"d_we{es_}")
        P.wait("tensor", f"d_we{es_}")
        last = None
        for tt in range(16):
            for nc2 in range(2):
                yb = T.ps[2 + (tt * 2 + nc2) % 4]
                P.wait("tensor", "v_cmb", P.counts.get("v_cmb", 0) - 3)
                for c in range(NDC):
                    P.op("tensor", lambda t, yb=yb, c=c, tt=tt, n=nc2, es_=es_: t.matmul(
                        yb[:, 0:512],
                        T.xn_fm[:, TLOC * c + 128 * tt:TLOC * c + 128 * (tt + 1)],
                        T.ar2[:, 8192 * es_ + 1024 * c + 512 * n:
                              8192 * es_ + 1024 * c + 512 * (n + 1)],
                        start=(c == 0), stop=(c == NDC - 1)),
                        inc="t_y" if c == NDC - 1 else None)
                dst = T.ar1[:, 1024 * tt + 512 * nc2:1024 * tt + 512 * (nc2 + 1)]
                P.wait("vector", "t_y")
                P.wait("vector", "v_b2p", P.counts["v_b2p"])
                P.op("vector", lambda v, yb=yb, dst=dst, tt=tt, e=e: v.scalar_tensor_tensor(
                    dst, yb[:, 0:512], T.w_all[:, 14 * tt + e:14 * tt + e + 1], dst,
                    OP.mult, OP.add), inc="v_cmb")
                last = P.counts["t_y"]
        eslot_free[es_] = ("t_y", last)
    for j in range(NTT):
        P.wait("scalar", "v_cmb", P.counts["v_cmb"])
        P.drain("scalar")
        P.op("scalar", lambda sc, j=j: sc.activation(
            T.ar1b[:, 1024 * j:1024 * (j + 1)], T.ar1[:, 1024 * j:1024 * (j + 1)],
            AF.Silu), inc="s_g")


def emit_l3_q(P, T):
    P.wait("gpsimd", "t_y", P.counts.get("t_y", 0))
    P.wait("gpsimd", "s_g", P.counts.get("s_g", 0))
    emit_l3_ct(P, T, T.l3_wdq, T.l3_bdq, "d_wq3", "v_ctq")
    emit_l3_up_pair(P, T, T.l3_wuq, T.l3_buq, 16384, "v_qc")
    emit_l3_rope_add(P, T, T.l3_wqr, T.l3_bqr, True, 16384)
    SCALE = float(1.0 / np.sqrt(1040.0))
    first_qn = True
    P.wait("scalar", "v_rpe", P.counts["v_rpe"])
    for g in range(8):
        for tg in range(4):
            src = T.ar1b[:, 16384 + 2048 * g + 512 * tg:16384 + 2048 * g + 512 * (tg + 1)]
            P.wait("scalar", "t_qn", P.counts.get("t_qn", 0))
            P.op("scalar", lambda sc, src=src: sc.activation(
                T.kvstage[:, 1024:1536], src, AF.Square), inc="s_qn")
            P.wait("tensor", "s_qn")
            P.wait("tensor", "v_qn", P.counts.get("v_qn", 0))
            P.op("tensor", lambda t: t.matmul(
                T.psm[0:1, 0:512], T.ones128[:], T.kvstage[:, 1024:1536],
                start=True, stop=True), inc="t_qn")
            P.wait("vector", "t_qn")
            P.op("vector", lambda v: v.reduce_max(T.small[0:1, 5:6], T.psm[0:1, 0:512],
                                                  axis=AX))
            P.drain("vector")
            if first_qn:
                P.op("vector", lambda v: v.tensor_copy(T.small[0:1, 6:7], T.small[0:1, 5:6]),
                     inc="v_qn")
                first_qn = False
            else:
                P.op("vector", lambda v: v.tensor_tensor(
                    T.small[0:1, 6:7], T.small[0:1, 6:7], T.small[0:1, 5:6], OP.max),
                     inc="v_qn")
            P.drain("vector")
    P.wait("gpsimd", "cc")
    P.wait("gpsimd", "v_gate", P.counts.get("v_gate", 0))
    for r in range(4):
        P.dma("gpsimd", T.w_all[0:1, 218 + r:219 + r],
              mk_ap(T.l3_kout, (1025 * r + 1024) * TLOC, [[1, 1], [1, 1]]), "d_km")
    P.wait("vector", "d_km")
    P.drain("vector")
    P.op("vector", lambda v: v.reduce_max(T.small[0:1, 7:8], T.w_all[0:1, 218:222],
                                          axis=AX))
    P.drain("vector")
    P.wait("vector", "v_qn")
    P.op("vector", lambda v: v.tensor_tensor(
        T.small[0:1, 8:9], T.small[0:1, 6:7], T.small[0:1, 7:8], OP.mult), inc="v_bnd")
    P.wait("scalar", "v_bnd")
    P.op("scalar", lambda sc: sc.activation(
        T.small[0:1, 9:10], T.small[0:1, 8:9], AF.Sqrt, scale=SCALE * SCALE * 1.05),
        inc="s_bnd")
    P.wait("vector", "s_bnd")
    P.op("vector", lambda v: v.tensor_scalar(
        T.small[0:1, 10:11], T.small[0:1, 9:10], -1.0, -1.0, OP.mult, OP.add), inc="v_bnd")
    P.wait("tensor", "v_bnd")
    P.wait("tensor", "v_qn")
    P.op("tensor", lambda t: t.matmul(
        T.psm[:, 0:1], T.ones1f[:], T.small[0:1, 10:11], start=True, stop=True),
        inc="t_bnd")
    P.wait("vector", "t_bnd")
    P.op("vector", lambda v: v.tensor_copy(T.boundt, T.psm[:, 0:1]), inc="v_bnd")


def emit_l3_attn(P, T):
    SCALE = float(1.0 / np.sqrt(1040.0))
    P.wait("vector", "t_kr", P.counts.get("t_kr", 0))
    P.wait("vector", "t_y", P.counts.get("t_y", 0))
    P.op("vector", lambda v: v.memset(
        mk_ap(T.v65, 64, [[T.xn_fm.shape[1], 128], [130, 16], [65, 2]]), 1.0), inc="v_v65i")
    P.wait("gpsimd", "t_qn", P.counts.get("t_qn", 0))
    P.wait("sync", "t_qn", P.counts.get("t_qn", 0))
    for hp in range(8):
        P.wait("sync", "t_sc", P.counts.get("t_sc", 0))
        for r in range(4):
            P.dma("sync", T.kbuf[:, 2048 * r:2048 * (r + 1)],
                  T.l3_kout[1025 * r + 128 * hp:1025 * r + 128 * (hp + 1), :], "d_ka")
        vblk_copy = {}
        for qg in range(4):
            for b in range(8):
                P.wait("gpsimd", "v_v65", P.counts.get("v_v65", 0))
                P.dma("gpsimd", T.vplain[:],
                      mk_ap(T.l3_vout, (2048 * (b // 2) + 1024 * (b % 2)) * D + 128 * hp,
                            [[D, 128], [128 * D, 8], [1, 128]]), "d_va")
                P.wait("vector", "d_va")
                P.wait("vector", "t_av", P.counts.get("t_av", 0) - 16)
                P.op("vector", lambda v, b=b: v.tensor_copy(
                    mk_ap(T.v65, 1040 * (b % 2),
                          [[T.xn_fm.shape[1], 128], [130, 8], [65, 2], [1, 64]]),
                    mk_ap(T.vplain, 0, [[T.xn_fm.shape[1], 128], [128, 8], [64, 2], [1, 64]])),
                    inc="v_v65")
                vblk_copy[b] = P.counts["v_v65"]
                for kc in range(8 * b, 8 * b + 8):
                    pb0 = T.ps[kc % 2]
                    pb1 = T.ps[2 + kc % 2]
                    P.wait("tensor", "d_ka")
                    P.wait("tensor", "s_ex", P.counts.get("s_ex", 0) - 2)
                    for h, pb in ((0, pb0), (1, pb1)):
                        P.op("tensor", lambda t, pb=pb, h=h, kc=kc, hp=hp, qg=qg: t.matmul(
                            pb[:, 0:512],
                            T.kbuf[64 * h:64 * h + 64, 128 * kc:128 * (kc + 1)],
                            T.ar1b[64 * h:64 * h + 64,
                                   16384 + 2048 * hp + 512 * qg:
                                   16384 + 2048 * hp + 512 * (qg + 1)],
                            start=True, stop=True),
                            inc="t_sc" if h == 1 else None)
                    P.wait("tensor", "s_ex", P.counts.get("s_ex", 0) + 2)
                    P.wait("tensor", "v_v65i")
                    P.wait("tensor", "v_v65", vblk_copy[b])
                    if kc == 0:
                        P.wait("tensor", "v_av", P.counts.get("v_av", 0))
                    for h in (0, 1):
                        P.op("tensor", lambda t, h=h, kc=kc, b=b: t.matmul(
                            T.ps[4 + h][0:65, 0:512],
                            T.v65[:, 1040 * (b % 2) + 130 * (kc % 8) + 65 * h:
                                  1040 * (b % 2) + 130 * (kc % 8) + 65 * h + 65],
                            T.probs[:, 512 * (2 * h + kc % 2):512 * (2 * h + kc % 2 + 1)],
                            start=(kc == 0), stop=(kc == 63)),
                            inc="t_av")
                    P.wait("scalar", "t_sc")
                    P.wait("scalar", "v_bnd")
                    for h, pb in ((0, pb0), (1, pb1)):
                        P.wait("scalar", "t_av", P.counts["t_av"] - 5 + h)
                        P.op("scalar", lambda sc, pb=pb, h=h, kc=kc: sc.activation(
                            T.probs[:, 512 * (2 * h + kc % 2):512 * (2 * h + kc % 2 + 1)],
                            pb[:, 0:512], AF.Exp, bias=T.boundt, scale=SCALE),
                            inc="s_ex")
            for h in (0, 1):
                pav = T.ps[4 + h]
                dst = T.ar1b[64 * h:64 * (h + 1),
                             16384 + 2048 * hp + 512 * qg:16384 + 2048 * hp + 512 * (qg + 1)]
                P.wait("vector", "t_av")
                P.drain("vector")
                P.op("vector", lambda v, pav=pav: v.reciprocal(
                    T.scratch[0:1, 0:512], pav[64:65, 0:512]), inc="v_rc")
                P.wait("tensor", "v_rc")
                P.wait("tensor", "v_av", P.counts.get("v_av", 0))
                P.op("tensor", lambda t: t.matmul(
                    T.psm[:, 0:512], T.ones1f[:], T.scratch[0:1, 0:512],
                    start=True, stop=True), inc="t_rcb")
                P.wait("vector", "t_rcb")
                P.op("vector", lambda v, pav=pav, dst=dst: v.tensor_copy(
                    dst, pav[0:64, 0:512]))
                P.drain("vector")
                P.op("vector", lambda v, dst=dst: v.tensor_tensor(
                    dst, dst, T.psm[0:64, 0:512], OP.mult), inc="v_av")
                P.drain("vector")


def emit_l3_wo(P, T):
    P.wait("gpsimd", "t_av", P.counts.get("t_av", 0))
    for c in range(NDC):
        P.dma("gpsimd", T.ar2[:, 1024 * c:1024 * (c + 1)],
              T.l3_wo[128 * c:128 * (c + 1), :], "d_wo3")
    P.dma("gpsimd", T.nstage[0:1, 0:1024], T.l3_bo, "d_wo3")
    P.wait("tensor", "d_wo3")
    P.wait("tensor", "v_av", P.counts.get("v_av", 0))
    for tt in range(16):
        for nc2 in range(2):
            bank = T.ps[(tt * 2 + nc2) % 2]
            P.wait("tensor", "v_fin", P.counts.get("v_fin", 0) - 1)
            for c in range(NDC):
                P.op("tensor", lambda t, b=bank, c=c, tt=tt, n=nc2: t.matmul(
                    b[:, 0:512],
                    T.ar1b[:, 16384 + 2048 * c + 128 * tt:16384 + 2048 * c + 128 * (tt + 1)],
                    T.ar2[:, 1024 * c + 512 * n:1024 * c + 512 * (n + 1)],
                    start=(c == 0), stop=False), inc=None)
            P.op("tensor", lambda t, b=bank, n=nc2: t.matmul(
                b[:, 0:512], T.ones1[:], T.nstage[0:1, 512 * n:512 * (n + 1)],
                start=False, stop=True), inc="t_wo3")
            P.wait("vector", "t_wo3")
            P.wait("vector", "s_g", P.counts["s_g"])
            P.drain("vector")
            P.op("vector", lambda v, b=bank, tt=tt, n=nc2: v.tensor_tensor(
                T.scratch[:, 0:512], b[:, 0:512],
                T.ar1b[:, 1024 * tt + 512 * n:1024 * tt + 512 * (n + 1)], OP.mult))
            P.drain("vector")
            P.op("vector", lambda v, tt=tt, n=nc2: v.tensor_tensor(
                T.x_res[:, 1024 * tt + 512 * n:1024 * tt + 512 * (n + 1)],
                T.x_res[:, 1024 * tt + 512 * n:1024 * tt + 512 * (n + 1)],
                T.scratch[:, 0:512], OP.add), inc="v_fin")
            P.drain("vector")


# version 04
# speedup vs baseline: 1.0083x; 1.0083x over previous
import numpy as np
from contextlib import ExitStack

import concourse.bass as bass
import concourse.mybir as mybir
from concourse.alu_op_type import AluOpType
from concourse.bass_utils import run_bass_kernel_spmd

F32 = mybir.dt.float32
BF16 = mybir.dt.bfloat16
I32 = mybir.dt.int32
AF = mybir.ActivationFunctionType
OP = AluOpType
AX = mybir.AxisListType.X

NCORES = 8
GROUPS = [[0, 1, 2, 3], [4, 5, 6, 7]]
B, S, D = 2, 8192, 1024
TLOC = S // 4
NTT = TLOC // 128
NDC = D // 128
EPS = 1e-5

DEBUG = []
SKIP_CC = False


SEM_ALIAS = {
    "t_xz": "t_mix", "t_gu": "t_n1", "t_do": "t_n1b", "t_wo": "t_n2",
    "v_xz": "v_n1e", "v_h": "v_x2", "v_ya": "v_n2e", "v_ys": "v_f1",
    "v_up": "v_gxn", "v_cv": "v_gxt", "v_woe": "v_rxt",
    "s_gu": "s_f1", "s_go": "s_x2", "s_ae": "s_f1", "s_sz": "s_x2",
    "t_ct": "t_mix", "t_up": "t_n1", "t_kr": "t_n1b", "t_rpt": "t_n2",
    "t_rcb": "t_rxt", "t_wo3": "t_rxt",
    "v_ckv": "v_n1e", "v_ctq": "v_x2", "v_kc": "v_n2e", "v_qc": "v_f1",
    "v_rp": "v_gxn", "v_rpe": "v_gxt", "v_rc": "v_f1",
    "t_dn": "t_gate", "t_dnt": "t_b2t", "v_dn": "v_b2t", "v_nm2": "v_b2p",
    "d_gw3": "d_gw", "d_l2p": "d_gw",
    "d_bp3": "d_bp", "d_bp3b": "d_bq",
    "d_s": "d_b1",
    "d_wkv1": "d_wkv", "d_wd3": "d_wkv",
    "d_wq1": "d_wq", "d_wu3": "d_wq",
    "d_wr3": "d_mixw",
    "d_wv3": "d_xz0", "d_wo3": "d_xz1",
    "d_wq3": "d_wg0", "d_rt": "d_wg1",
}




class Prog:
    ENGINES = ("tensor", "vector", "scalar", "gpsimd", "sync")

    def __init__(self, nc, es):
        self.nc = nc
        self.es = es
        self.items = {e: [] for e in self.ENGINES}
        self.sems = {}
        self.counts = {}
        self.watermark = {}

    def sem(self, name):
        name = SEM_ALIAS.get(name, name)
        if name not in self.sems:
            self.sems[name] = self.es.enter_context(self.nc.semaphore(name))
            self.counts[name] = 0
        return self.sems[name]

    def op(self, eng, fn, inc=None, inc_n=1):
        total = None
        if inc is not None:
            inc = SEM_ALIAS.get(inc, inc)
            self.sem(inc)
            self.counts[inc] += inc_n
            total = self.counts[inc]
            h = self.sems[inc]

            def run(e, fn=fn, h=h, n=inc_n):
                fn(e).then_inc(h, n)
        else:
            def run(e, fn=fn):
                fn(e)
        self.items[eng].append(run)
        return total

    def wait(self, eng, sem_name, value=None):
        sem_name = SEM_ALIAS.get(sem_name, sem_name)
        v = self.counts.get(sem_name, 0) if value is None else value
        if v <= 0:
            return
        key = (eng, sem_name)
        if self.watermark.get(key, 0) >= v:
            return
        self.watermark[key] = v
        h = self.sem(sem_name)
        self.items[eng].append(lambda e, h=h, v=v: e.wait_ge(h, v))

    def cnt(self, name):
        return self.counts.get(SEM_ALIAS.get(name, name), 0)

    def drain(self, eng):
        self.items[eng].append(lambda e: e.drain())

    def dma(self, queue, out, in_, sem_name, **kw):
        return self.op(queue, lambda e: e.dma_start(out=out, in_=in_, **kw),
                       inc=sem_name, inc_n=16)

    def build(self):
        with self.nc.Block() as block:
            for eng in self.ENGINES:
                items = self.items[eng]
                if not items:
                    continue

                def make(items):
                    def body(e):
                        for it in items:
                            it(e)
                    return body
                getattr(block, eng)(make(items))


def mk_ap(base, extra_off, dims):
    return bass.AP(tensor=base.tensor, offset=base.offset + extra_off, ap=dims)



def _pack_cols(v, p=128):
    v = np.ascontiguousarray(v, dtype=np.float32)
    return np.ascontiguousarray(v.reshape(-1, p).T)


def _sign_tile():
    s = np.ones((128, 512), np.float32)
    s[:, 1::2] = -1.0
    return s


def _rope_tables(t0, n, dmodel=1024):
    base = 10000.0
    half = dmodel // 2
    pos = np.arange(t0, t0 + n, dtype=np.float64)[:, None]
    ids = np.arange(half, dtype=np.float32)
    theta = ((np.float32(base) ** (-2.0 * ids)).astype(np.float32) / np.float32(half))
    r = (pos * theta[None, :].astype(np.float64)).astype(np.float32)
    cos, sin = np.cos(r), np.sin(r)
    c2 = np.empty((n, dmodel), np.float32)
    s2 = np.empty((n, dmodel), np.float32)
    c2[:, 0::2] = cos
    c2[:, 1::2] = cos
    s2[:, 0::2] = -sin
    s2[:, 1::2] = sin
    return c2, s2


def prep_core_inputs(core, inputs, layers=(0, 1, 2, 3)):
    b = core // 4
    r = core % 4
    t0 = r * TLOC
    g = lambda k: np.asarray(inputs[k], dtype=np.float32)
    d = {}
    d["x_in"] = np.ascontiguousarray(g("L0_attn_x")[b, t0:t0 + TLOC, :])
    d["c_ident"] = np.eye(128, dtype=np.float32)
    d["c_sign"] = _sign_tile()
    if 0 in layers:
        d["l0_wq"] = np.ascontiguousarray(g("L0_attn_W_Q").transpose(1, 0, 2).reshape(D, D))
        d["l0_wk"] = np.ascontiguousarray(g("L0_attn_W_K").transpose(1, 0, 2).reshape(D, D))
        d["l0_wv"] = np.ascontiguousarray(g("L0_attn_W_V").transpose(1, 0, 2).reshape(D, D))
        d["l0_gw"] = np.ascontiguousarray(g("L0_ffn_gate_w").T)
        d["l0_w1"] = np.ascontiguousarray(g("L0_ffn_w1"))
        d["l0_b1"] = np.ascontiguousarray(
            g("L0_ffn_b1").reshape(14, 16, 128).transpose(2, 0, 1).reshape(128, 224))
        d["l0_w2"] = np.ascontiguousarray(g("L0_ffn_w2"))
        d["l0_b2"] = np.ascontiguousarray(g("L0_ffn_b2"))
    if 1 in layers:
        d["l1_wqT"] = np.ascontiguousarray(g("L1_attn_wq").T)
        d["l1_wkT"] = np.ascontiguousarray(g("L1_attn_wk").T)
        d["l1_wvT"] = np.ascontiguousarray(g("L1_attn_wv").T)
        d["l1_bq"] = _pack_cols(g("L1_attn_bq"))
        d["l1_bk"] = _pack_cols(g("L1_attn_bk"))
        d["l1_bv"] = np.ascontiguousarray(g("L1_attn_bv").reshape(1, D))
        d["l1_g1"] = _pack_cols(g("L1_ffn_g1"))
        d["l1_be1"] = _pack_cols(g("L1_ffn_be1"))
        d["l1_g2"] = _pack_cols(g("L1_ffn_g2"))
        d["l1_be2"] = _pack_cols(g("L1_ffn_be2"))
        d["l1_mixw"] = np.ascontiguousarray(g("L1_ffn_mix_w"))
        d["l1_mixb"] = np.ascontiguousarray(g("L1_ffn_mix_b").reshape(1, D))
        d["l1_rw"] = np.ascontiguousarray(g("L1_ffn_router_w"))
        d["l1_w1"] = np.ascontiguousarray(g("L1_ffn_w1"))
        d["l1_b1"] = np.ascontiguousarray(
            g("L1_ffn_b1").reshape(8, 32, 128).transpose(2, 0, 1).reshape(128, 256))
        d["l1_w2"] = np.ascontiguousarray(g("L1_ffn_w2"))
        d["l1_b2"] = np.ascontiguousarray(g("L1_ffn_b2"))
    if 2 in layers:
        chs = slice(512 * r, 512 * (r + 1))
        d["l2_win"] = np.ascontiguousarray(g("L2_attn_W_in"))
        cw = g("L2_attn_conv_w")[:, 0, :]
        d["l2_cw"] = np.ascontiguousarray(
            cw[chs].reshape(4, 128, 4).transpose(1, 0, 2).reshape(128, 16))
        d["l2_cb"] = _pack_cols(g("L2_attn_conv_b")[chs])
        al = g("L2_attn_A_log")[:, chs]
        d["l2_alog"] = np.ascontiguousarray(
            al.T.reshape(4, 128, 16).transpose(1, 0, 2).reshape(128, 64))
        d["l2_dp"] = _pack_cols(g("L2_attn_D_param")[chs])
        d["l2_wout"] = np.ascontiguousarray(g("L2_attn_W_out")[chs])
        d["l2_wg"] = np.ascontiguousarray(np.asarray(inputs["L2_ffn_w_gate"]).T.astype(np.int32))
        d["l2_wu"] = np.ascontiguousarray(np.asarray(inputs["L2_ffn_w_up"]).T.astype(np.int32))
        d["l2_wd"] = np.ascontiguousarray(np.asarray(inputs["L2_ffn_w_down"]).T).astype(np.int32)
    if 3 in layers:
        d["l3_wdq"] = np.ascontiguousarray(g("L3_attn_Wd_q_w"))
        d["l3_bdq"] = _pack_cols(g("L3_attn_Wd_q_b"))
        d["l3_wuq"] = np.ascontiguousarray(g("L3_attn_Wu_q_w"))
        d["l3_buq"] = _pack_cols(g("L3_attn_Wu_q_b"))
        d["l3_wqr"] = np.ascontiguousarray(g("L3_attn_Wq_r_w"))
        d["l3_bqr"] = np.ascontiguousarray(g("L3_attn_Wq_r_b").reshape(1, D))
        d["l3_wkr"] = np.ascontiguousarray(g("L3_attn_Wk_r_w"))
        d["l3_bkr"] = np.ascontiguousarray(g("L3_attn_Wk_r_b").reshape(1, D))
        d["l3_wdkv"] = np.ascontiguousarray(g("L3_attn_Wd_kv_w"))
        d["l3_bdkv"] = _pack_cols(g("L3_attn_Wd_kv_b"))
        d["l3_wuk"] = np.ascontiguousarray(g("L3_attn_Wu_k_w"))
        d["l3_buk"] = _pack_cols(g("L3_attn_Wu_k_b"))
        d["l3_wuv"] = np.ascontiguousarray(g("L3_attn_Wu_v_w"))
        d["l3_buv"] = np.ascontiguousarray(g("L3_attn_Wu_v_b").reshape(1, D))
        d["l3_wo"] = np.ascontiguousarray(g("L3_attn_Wo_w"))
        d["l3_bo"] = np.ascontiguousarray(g("L3_attn_Wo_b").reshape(1, D))
        d["l3_gw"] = np.ascontiguousarray(g("L3_ffn_gate_w"))
        d["l3_gb"] = np.ascontiguousarray(g("L3_ffn_gate_b").reshape(1, 8))
        d["l3_ew"] = np.ascontiguousarray(g("L3_ffn_expert_w"))
        d["l3_eb"] = np.ascontiguousarray(g("L3_ffn_expert_b"))
        c2, s2 = _rope_tables(t0, TLOC)
        d["l3_cos"] = c2
        d["l3_sin"] = s2
    return d




class Tens:
    pass


def alloc_common(nc, es, layers):
    T = Tens()
    T.x_in = nc.dram_tensor("x_in", [TLOC, D], F32, kind="ExternalInput").ap()
    T.x_out = nc.dram_tensor("x_out", [TLOC, D], F32, kind="ExternalOutput").ap()
    T.c_ident = nc.dram_tensor("c_ident", [128, 128], F32, kind="ExternalInput").ap()
    T.c_sign = nc.dram_tensor("c_sign", [128, 512], F32, kind="ExternalInput").ap()

    sb = lambda nm, shp, dt: es.enter_context(nc.sbuf_tensor(nm, shp, dt))
    T.x_res = sb("x_res", [128, NTT * D], F32)
    T.xn_fm = sb("xn_fm", [128, NDC * TLOC], BF16)
    T.ar1 = sb("ar1", [128, NTT * D], F32)
    T.ar1b = T.ar1[:].bitcast(BF16)
    T.ar2 = sb("ar2", [128, 16384], BF16)
    T.scratch = sb("scratch", [128, 1024], F32)
    T.kvstage = sb("kvstage", [128, 1536], BF16)
    T.b1_sb = sb("b1_sb", [128, 256], F32)
    T.b14 = sb("b14", [14, 512], BF16)
    T.ident_b = sb("ident_b", [128, 128], BF16)
    T.ident_f = sb("ident_f", [128, 128], F32)
    T.sign_sb = sb("sign_sb", [128, 512], BF16) if 2 in layers else None
    T.stats = sb("stats", [128, 64], F32)
    T.small = sb("small", [128, 16], F32)
    T.w_all = sb("w_all", [128, 224], F32)
    T.wT16 = sb("wT16", [16, 128], BF16)
    T.ones1 = sb("ones1", [1, 128], BF16)
    T.ones1f = sb("ones1f", [1, 128], F32)
    T.ones128 = sb("ones128", [128, 1], BF16)
    T.gw_sb = sb("gw_sb", [128, 112], F32)

    T.ps = [es.enter_context(nc.psum_tensor(f"ps{i}", [128, 512], F32)) for i in range(6)]
    T.pst = es.enter_context(nc.psum_tensor("pst", [128, 1024], BF16))
    T.psm = es.enter_context(nc.psum_tensor("psm", [128, 512], F32))

    T.g_of = lambda jt, c0, n: T.ar1b[:, 1024 * jt + c0: 1024 * jt + c0 + n]
    T.kbuf = T.xn_fm[:, 0:8192]
    T.v65 = T.xn_fm[:, 8192:10272]
    T.vplain = T.xn_fm[:, 10272:11296]
    T.probs = T.xn_fm[:, 11296:13344]
    T.qw = T.ar1b[:, 16384:24576]
    T.tmp64 = T.ar1[:, 12288:12352]
    T.avr = T.ar1[:, 12352:12864]
    T.atmp = T.ar1[:, 12864:13888]
    T.nstage = sb("nstage", [1, 1024], BF16)
    T.boundt = T.small[:, 15:16]
    T.epst = T.small[:, 14:15]
    T.bkp = sb("bkp", [128, 8], F32)
    T.bqp = sb("bqp", [128, 8], F32)
    T.g1p = sb("g1p", [128, 8], F32)
    T.be1p = sb("be1p", [128, 8], F32)
    T.g2p = sb("g2p", [128, 8], F32)
    T.be2p = sb("be2p", [128, 8], F32)
    return T


def alloc_l0(nc, T):
    T.l0_wq = nc.dram_tensor("l0_wq", [D, D], F32, kind="ExternalInput").ap()
    T.l0_wk = nc.dram_tensor("l0_wk", [D, D], F32, kind="ExternalInput").ap()
    T.l0_wv = nc.dram_tensor("l0_wv", [D, D], F32, kind="ExternalInput").ap()
    T.l0_gw = nc.dram_tensor("l0_gw", [D, 14], F32, kind="ExternalInput").ap()
    T.l0_w1 = nc.dram_tensor("l0_w1", [14, D, 2048], F32, kind="ExternalInput").ap()
    T.l0_b1 = nc.dram_tensor("l0_b1", [128, 224], F32, kind="ExternalInput").ap()
    T.l0_w2 = nc.dram_tensor("l0_w2", [14, 2048, D], F32, kind="ExternalInput").ap()
    T.l0_b2 = nc.dram_tensor("l0_b2", [14, D], F32, kind="ExternalInput").ap()
    T.l0_kin = nc.dram_tensor("l0_kin", [1025, TLOC], BF16).ap()
    T.l0_kout = nc.dram_tensor("l0_kout", [4 * 1025, TLOC], BF16).ap()
    T.l0_vin = nc.dram_tensor("l0_vin", [TLOC, D], BF16).ap()
    T.l0_vout = nc.dram_tensor("l0_vout", [4 * TLOC, D], BF16).ap()



def emit_setup(P, T):
    for j in range(NTT):
        P.dma("sync", T.x_res[:, D * j:D * (j + 1)], T.x_in[128 * j:128 * (j + 1), :], "d_x")
    P.dma("sync", T.ident_f[:], T.c_ident, "d_x")
    P.dma("gpsimd", T.ident_b[:], T.c_ident, "d_g")
    if T.sign_sb is not None:
        P.dma("gpsimd", T.sign_sb[:], T.c_sign, "d_g")
    P.wait("vector", "d_x")
    P.wait("vector", "d_g")
    P.op("vector", lambda v: v.memset(T.ones1[:], 1.0))
    P.op("vector", lambda v: v.memset(T.ones1f[:], 1.0))
    P.op("vector", lambda v: v.memset(T.ones128[:], 1.0))
    P.op("vector", lambda v: v.memset(T.epst, EPS))
    P.op("vector", lambda v: v.memset(T.avr[:], 0.0), inc="v_setup")


def emit_ln(P, T, tag=""):
    P.wait("vector", "d_x")
    P.wait("scalar", "d_x")
    P.wait("tensor", "v_setup")
    P.wait("tensor", "d_g")
    for j in range(NTT):
        xt = T.x_res[:, D * j:D * (j + 1)]
        m = T.stats[:, j:j + 1]
        rs = T.stats[:, 16 + j:16 + j + 1]
        sums = T.small[:, 0:1]
        ssq = T.small[:, 1:2]
        P.op("vector", lambda v, xt=xt, s=sums: v.reduce_sum(s, xt, axis=AX))
        P.drain("vector")
        P.op("vector", lambda v, m=m, s=sums: v.tensor_scalar(m, s, 1.0 / D, None, OP.mult))
        P.drain("vector")
        P.op("vector", lambda v, xt=xt, m=m: v.tensor_scalar(
            T.scratch[:], xt, m, None, OP.subtract), inc="v_ln")
        P.wait("scalar", "v_ln")
        P.op("scalar", lambda sc, ssq=ssq: sc.activation(
            T.ar1[:, 0:1024], T.scratch[:], AF.Square, accum_out=ssq))
        P.drain("scalar")
        P.op("scalar", lambda sc, ssq=ssq: sc.activation(
            T.small[:, 2:3], ssq, AF.Ln, bias=T.epst, scale=1.0 / D))
        P.drain("scalar")
        P.op("scalar", lambda sc, rs=rs: sc.activation(
            rs, T.small[:, 2:3], AF.Exp, scale=-0.5), inc="s_ln")
        P.wait("vector", "s_ln")
        P.op("vector", lambda v, rs=rs: v.tensor_scalar(
            T.kvstage[:, 0:1024], T.scratch[:], rs, None, OP.mult), inc="v_ln")
        P.wait("tensor", "v_ln")
        for c in range(NDC):
            P.op("tensor", lambda t, c=c: t.transpose(
                T.pst[:, 128 * c:128 * (c + 1)],
                T.kvstage[:, 128 * c:128 * (c + 1)], T.ident_b[:]),
                inc="t_ln" if c == NDC - 1 else None)
        P.wait("vector", "t_ln")
        for c in range(NDC):
            P.op("vector", lambda v, c=c, j=j: v.tensor_copy(
                T.xn_fm[:, TLOC * c + 128 * j:TLOC * c + 128 * (j + 1)],
                T.pst[:, 128 * c:128 * (c + 1)]),
                inc="v_xn" if c == NDC - 1 else None)
        P.wait("tensor", "v_xn")


def load_w_chunks(P, T, dst, src, n_k, cols, sem):
    for c in range(n_k):
        P.dma("gpsimd", dst[:, cols * c:cols * (c + 1)], src[128 * c:128 * (c + 1), :], sem)


def emit_l0_kv(P, T):
    load_w_chunks(P, T, T.qw, T.l0_wk, 8, 1024, "d_wkv")
    P.wait("tensor", "d_wkv")
    P.wait("tensor", "v_xn")
    slot_free = [("v_setup", 0), ("v_setup", 0)]
    first_nrm = True
    grp = 0
    for g in range(8):
        for tg in range(4):
            bank = T.ps[grp % 2]
            if grp >= 2:
                P.wait("tensor", "v_kv", grp - 1)
            for c in range(NDC):
                P.op("tensor", lambda t, b=bank, c=c, g=g, tg=tg: t.matmul(
                    b[:, 0:512],
                    T.qw[:, 1024 * c + 128 * g:1024 * c + 128 * g + 128],
                    T.xn_fm[:, TLOC * c + 512 * tg:TLOC * c + 512 * (tg + 1)],
                    start=(c == 0), stop=(c == NDC - 1)),
                    inc="t_kv" if c == NDC - 1 else None)
            slot = grp % 2
            stg = T.kvstage[:, 512 * slot:512 * (slot + 1)]
            P.wait("vector", "t_kv")
            P.wait("vector", *slot_free[slot])
            P.op("vector", lambda v, b=bank, stg=stg: v.tensor_copy(stg, b[:, 0:512]),
                 inc="v_kv")
            P.wait("scalar", "v_kv")
            P.wait("scalar", "t_nrm", P.cnt("t_nrm"))
            P.op("scalar", lambda sc, stg=stg: sc.activation(
                T.kvstage[:, 1024:1536], stg, AF.Square), inc="s_kv")
            P.wait("tensor", "s_kv")
            P.wait("tensor", "v_nrm", P.cnt("v_nrm"))
            P.op("tensor", lambda t: t.matmul(
                T.psm[0:1, 0:512], T.ones128[:], T.kvstage[:, 1024:1536],
                start=True, stop=True), inc="t_nrm")
            P.wait("vector", "t_nrm")
            P.op("vector", lambda v: v.reduce_max(T.small[0:1, 3:4], T.psm[0:1, 0:512], axis=AX))
            P.drain("vector")
            if first_nrm:
                P.op("vector", lambda v: v.tensor_copy(T.small[0:1, 4:5], T.small[0:1, 3:4]),
                     inc="v_nrm")
                first_nrm = False
            else:
                P.op("vector", lambda v: v.tensor_tensor(
                    T.small[0:1, 4:5], T.small[0:1, 4:5], T.small[0:1, 3:4], OP.max),
                     inc="v_nrm")
            P.drain("vector")
            P.wait("sync", "v_kv")
            n = P.dma("sync", T.l0_kin[128 * g:128 * (g + 1), 512 * tg:512 * (tg + 1)], stg,
                      f"d_kst{slot}")
            slot_free[slot] = (f"d_kst{slot}", n)
            grp += 1
    P.wait("vector", "v_nrm")
    P.op("vector", lambda v: v.tensor_copy(T.nstage[0:1, 0:1], T.small[0:1, 4:5]), inc="v_kmax")
    P.wait("sync", "v_kmax")
    P.dma("sync", mk_ap(T.l0_kin, 1024 * TLOC, [[1, 1], [1, 1]]), T.nstage[0:1, 0:1], "d_nst")
    P.wait("gpsimd", "t_kv", P.cnt("t_kv"))
    load_w_chunks(P, T, T.qw, T.l0_wv, 8, 1024, "d_wkv")
    P.wait("tensor", "d_wkv")
    for j in range(NTT):
        for nc2 in range(2):
            bank = T.ps[grp % 2]
            P.wait("tensor", "v_kv", grp - 1)
            for c in range(NDC):
                P.op("tensor", lambda t, b=bank, c=c, j=j, n=nc2: t.matmul(
                    b[:, 0:512],
                    T.xn_fm[:, TLOC * c + 128 * j:TLOC * c + 128 * (j + 1)],
                    T.qw[:, 1024 * c + 512 * n:1024 * c + 512 * (n + 1)],
                    start=(c == 0), stop=(c == NDC - 1)),
                    inc="t_kv" if c == NDC - 1 else None)
            slot = grp % 2
            stg = T.kvstage[:, 512 * slot:512 * (slot + 1)]
            P.wait("vector", "t_kv")
            P.wait("vector", *slot_free[slot])
            P.op("vector", lambda v, b=bank, stg=stg: v.tensor_copy(stg, b[:, 0:512]),
                 inc="v_kv")
            P.wait("sync", "v_kv")
            n = P.dma("sync", T.l0_vin[128 * j:128 * (j + 1), 512 * nc2:512 * (nc2 + 1)], stg,
                      f"d_kst{slot}")
            slot_free[slot] = (f"d_kst{slot}", n)
            grp += 1
    P.wait("gpsimd", "d_kst0")
    P.wait("gpsimd", "d_kst1")
    P.wait("gpsimd", "d_nst")
    if SKIP_CC:
        P.op("gpsimd", lambda g: g.nop(), inc="cc")
    else:
        P.op("gpsimd", lambda g: g.collective_compute(
            "AllGather", OP.bypass, replica_groups=GROUPS,
            ins=[T.l0_kin.opt()], outs=[T.l0_kout.opt()]), inc="cc")
    if SKIP_CC:
        P.op("gpsimd", lambda g: g.nop(), inc="cc")
    else:
        P.op("gpsimd", lambda g: g.collective_compute(
            "AllGather", OP.bypass, replica_groups=GROUPS,
            ins=[T.l0_vin.opt()], outs=[T.l0_vout.opt()]), inc="cc")


def emit_l0_gate(P, T):
    for c in range(NDC):
        P.dma("sync", T.gw_sb[:, 14 * c:14 * (c + 1)], T.l0_gw[128 * c:128 * (c + 1), :],
              "d_gw")
    P.wait("tensor", "d_gw")
    P.wait("tensor", "v_xn")
    P.wait("tensor", "v_nrm")
    P.wait("vector", "d_x")
    for j in range(NTT):
        m = T.stats[:, j:j + 1]
        rs = T.stats[:, 16 + j:16 + j + 1]
        P.wait("vector", "s_ln", P.cnt("s_ln"))
        P.wait("vector", "t_gxp", P.cnt("t_gxp"))
        P.drain("vector")
        P.op("vector", lambda v, j=j, m=m, rs=rs: v.tensor_scalar(
            T.scratch[:], T.x_res[:, 1024 * j:1024 * (j + 1)], m, rs,
            OP.subtract, OP.mult), inc="v_gxn")
        P.wait("tensor", "v_gxn")
        P.wait("tensor", "v_kv", P.cnt("v_kv"))
        for c in range(NDC):
            P.op("tensor", lambda t, c=c: t.transpose(
                T.ps[c // 4][:, 128 * (c % 4):128 * (c % 4) + 128],
                T.scratch[:, 128 * c:128 * (c + 1)], T.ident_f[:]),
                inc="t_gxp" if c == NDC - 1 else None)
        P.wait("vector", "t_gxp")
        for c in range(NDC):
            P.op("vector", lambda v, c=c: v.tensor_copy(
                T.atmp[:, 128 * c:128 * (c + 1)],
                T.ps[c // 4][:, 128 * (c % 4):128 * (c % 4) + 128]),
                inc="v_gxt" if c == NDC - 1 else None)
        P.wait("tensor", "s_gate", P.cnt("s_gate"))
        P.wait("tensor", "v_gxt")
        for c in range(NDC):
            P.op("tensor", lambda t, c=c, j=j: t.matmul(
                T.psm[:, 0:14],
                T.atmp[:, 128 * c:128 * (c + 1)],
                T.gw_sb[:, 14 * c:14 * (c + 1)],
                start=(c == 0), stop=(c == NDC - 1)),
                inc="t_gate" if c == NDC - 1 else None)
        w = T.w_all[:, 14 * j:14 * (j + 1)]
        e_t = T.scratch[:, 0:14]
        P.wait("vector", "t_gate")
        P.op("vector", lambda v: v.reduce_max(T.small[:, 0:1], T.psm[:, 0:14], axis=AX))
        P.drain("vector")
        P.op("vector", lambda v: v.tensor_scalar(
            T.small[:, 1:2], T.small[:, 0:1], -1.0, None, OP.mult), inc="v_gate")
        P.wait("scalar", "v_gate")
        P.op("scalar", lambda sc, e_t=e_t: sc.activation(
            e_t, T.psm[:, 0:14], AF.Exp, bias=T.small[:, 1:2]), inc="s_gate")
        P.wait("vector", "s_gate")
        P.op("vector", lambda v, e_t=e_t: v.tensor_scalar(
            T.scratch[:, 32:46], e_t, 1.0 - 1e-6, None, OP.is_ge))
        P.drain("vector")
        P.op("vector", lambda v, e_t=e_t: v.scalar_tensor_tensor(
            T.scratch[:, 64:78], T.scratch[:, 32:46], -2.0, e_t, OP.mult, OP.add))
        P.drain("vector")
        P.op("vector", lambda v: v.reduce_max(T.small[:, 2:3], T.scratch[:, 64:78], axis=AX))
        P.drain("vector")
        P.op("vector", lambda v, e_t=e_t: v.tensor_scalar(
            T.scratch[:, 96:110], e_t, T.small[:, 2:3], None, OP.is_ge))
        P.op("vector", lambda v: v.tensor_scalar(
            T.small[:, 3:4], T.small[:, 2:3], 1.0, None, OP.add))
        P.drain("vector")
        P.op("vector", lambda v: v.reciprocal(T.small[:, 4:5], T.small[:, 3:4]))
        P.drain("vector")
        P.op("vector", lambda v, e_t=e_t: v.tensor_tensor(
            T.scratch[:, 128:142], e_t, T.scratch[:, 96:110], OP.mult))
        P.drain("vector")
        P.op("vector", lambda v, w=w: v.tensor_scalar(
            w, T.scratch[:, 128:142], T.small[:, 4:5], None, OP.mult), inc="v_gate")


def emit_l0_ffn(P, T):
    P.dma("sync", T.b1_sb[:, 0:224], T.l0_b1, "d_b1")
    P.wait("vector", "v_gate")
    for j in range(NTT):
        P.wait("tensor", "v_gate", P.cnt("v_gate"))
        P.wait("tensor", "v_b2t", P.cnt("v_b2t"))
        P.wait("tensor", "v_b2p", P.cnt("v_b2p"))
        P.wait("tensor", "s_gate")
        P.op("tensor", lambda t, j=j: t.transpose(
            T.psm[0:14, 0:128], T.w_all[:, 14 * j:14 * (j + 1)], T.ident_f[:]), inc="t_b2t")
        P.wait("vector", "t_b2t")
        P.op("vector", lambda v: v.tensor_copy(T.wT16[0:14, 0:128], T.psm[0:14, 0:128]),
             inc="v_b2t")
        for nc2 in range(2):
            P.wait("gpsimd", "t_b2p", P.cnt("t_b2p"))
            P.dma("gpsimd", T.b14[0:14, 0:512], T.l0_b2[0:14, 512 * nc2:512 * (nc2 + 1)],
                  "d_b2")
            P.wait("tensor", "v_b2t")
            P.wait("tensor", "d_b2")
            P.wait("tensor", "v_b2p", P.cnt("v_b2p"))
            P.op("tensor", lambda t: t.matmul(
                T.psm[:, 0:512], T.wT16[0:14, 0:128], T.b14[0:14, 0:512],
                start=True, stop=True), inc="t_b2p")
            P.wait("vector", "t_b2p")
            P.op("vector", lambda v, j=j, n=nc2: v.tensor_copy(
                T.ar1[:, 1024 * j + 512 * n:1024 * j + 512 * (n + 1)], T.psm[:, 0:512]),
                 inc="v_b2p")
    P.wait("tensor", "v_kv")
    P.wait("scalar", "d_kst0")
    P.wait("scalar", "d_kst1")
    P.wait("scalar", "d_nst")
    qslot_free = [("v_setup", 0), ("v_setup", 0)]
    hs_free = [("v_setup", 0), ("v_setup", 0)]
    nq_total = 0
    for e in range(14):
        for q in range(4):
            qs = nq_total % 2
            base = 8192 * qs
            P.wait("gpsimd", *qslot_free[qs])
            for c in range(NDC):
                P.dma("gpsimd",
                      mk_ap(T.ar2[:, 0:1], base + 128 * c,
                            [[16384, 128], [1024, 4], [1, 128]]),
                      mk_ap(T.l0_w1, e * D * 2048 + 2048 * 128 * c + 512 * q,
                            [[2048, 128], [128, 4], [1, 128]]), f"d_w1{qs}")
            P.dma("gpsimd",
                  mk_ap(T.ar2[:, 0:1], base + 4096, [[16384, 128], [1024, 4], [1, 1024]]),
                  mk_ap(T.l0_w2, e * 2048 * D + 512 * q * D,
                        [[1024, 128], [128 * 1024, 4], [1, 1024]]), f"d_w1{qs}")
            P.wait("tensor", f"d_w1{qs}")
            last_t_y = None
            for tg in range(8):
                for f in range(4):
                    ft = 4 * q + f
                    hb = T.ps[f % 2]
                    P.wait("tensor", "s_h", P.cnt("s_h") - 1)
                    for c in range(NDC):
                        P.op("tensor", lambda t, hb=hb, c=c, tg=tg, base=base, f=f: t.matmul(
                            hb[:, 0:256],
                            T.ar2[:, base + 1024 * f + 128 * c:base + 1024 * f + 128 * (c + 1)],
                            T.xn_fm[:, TLOC * c + 256 * tg:TLOC * c + 256 * (tg + 1)],
                            start=(c == 0), stop=(c == NDC - 1)),
                            inc="t_h" if c == NDC - 1 else None)
                    hslot = f % 2
                    hs = T.kvstage[:, 512 * hslot:512 * hslot + 256]
                    P.wait("scalar", "t_h")
                    P.wait("scalar", "d_b1")
                    P.wait("scalar", *hs_free[hslot])
                    P.op("scalar", lambda sc, hb=hb, hs=hs, e=e, ft=ft: sc.activation(
                        hs, hb[:, 0:256], AF.Gelu,
                        bias=T.b1_sb[:, 16 * e + ft:16 * e + ft + 1]), inc="s_h")
                    P.wait("tensor", "s_h")
                    for tt in range(2):
                        for nc2 in range(2):
                            yb = T.ps[2 + tt * 2 + nc2]
                            n = P.op("tensor", lambda t, yb=yb, hs=hs, tt=tt, n=nc2,
                                     base=base, f=f: t.matmul(
                                yb[:, 0:512],
                                hs[:, 128 * tt:128 * (tt + 1)],
                                T.ar2[:, base + 4096 + 1024 * f + 512 * n:
                                      base + 4096 + 1024 * f + 512 * (n + 1)],
                                start=(f == 0), stop=(f == 3)),
                                inc="t_y" if (tt, nc2) == (1, 1) else None)
                    hs_free[hslot] = ("t_y", P.cnt("t_y"))
                    last_t_y = P.cnt("t_y")
                P.wait("vector", "t_y")
                P.wait("vector", "v_b2p", P.cnt("v_b2p"))
                for tt in range(2):
                    jt = tg * 2 + tt
                    for nc2 in range(2):
                        yb = T.ps[2 + tt * 2 + nc2]
                        dst = T.ar1[:, 1024 * jt + 512 * nc2:1024 * jt + 512 * (nc2 + 1)]
                        P.op("vector", lambda v, yb=yb, dst=dst, jt=jt, e=e:
                             v.scalar_tensor_tensor(
                                 dst, yb[:, 0:512],
                                 T.w_all[:, 14 * jt + e:14 * jt + e + 1], dst,
                                 OP.mult, OP.add),
                             inc="v_cmb" if (tt, nc2) == (1, 1) else None)
                P.wait("tensor", "v_cmb")
            qslot_free[qs] = ("t_y", last_t_y)
            nq_total += 1
    for j in range(NTT):
        m = T.stats[:, j:j + 1]
        rs = T.stats[:, 16 + j:16 + j + 1]
        yj = T.ar1[:, 1024 * j:1024 * (j + 1)]
        P.wait("vector", "v_cmb", P.cnt("v_cmb"))
        P.wait("vector", "s_g", P.cnt("s_g"))
        P.op("vector", lambda v, j=j, m=m, rs=rs: v.tensor_scalar(
            T.scratch[:], T.x_res[:, 1024 * j:1024 * (j + 1)], m, rs, OP.subtract, OP.mult))
        P.drain("vector")
        P.op("vector", lambda v, yj=yj: v.tensor_tensor(yj, T.scratch[:], yj, OP.add))
        P.drain("vector")
        P.op("vector", lambda v, yj=yj: v.tensor_scalar(
            T.scratch[:], yj, 100.0, -100.0, OP.min, OP.max), inc="v_g")
        P.drain("vector")
        P.wait("scalar", "v_g")
        P.op("scalar", lambda sc, j=j: sc.activation(
            T.ar1b[:, 1024 * j:1024 * (j + 1)], T.scratch[:], AF.Silu), inc="s_g")


def emit_l0_attn(P, T):
    SCALE = 0.125
    P.wait("gpsimd", "s_g")
    load_w_chunks(P, T, T.qw, T.l0_wq, 8, 1024, "d_wq")
    P.wait("tensor", "d_wq")
    P.wait("tensor", "s_h")
    P.wait("tensor", "v_b2p")
    first_qn = True
    for hp in range(8):
        for tg in range(4):
            bank = T.ps[(hp * 4 + tg) % 2]
            if hp * 4 + tg >= 2:
                P.wait("tensor", "v_qe", hp * 4 + tg - 1)
            for c in range(NDC):
                P.op("tensor", lambda t, b=bank, c=c, hp=hp, tg=tg: t.matmul(
                    b[:, 0:512],
                    T.qw[:, 1024 * c + 128 * hp:1024 * c + 128 * hp + 128],
                    T.xn_fm[:, TLOC * c + 512 * tg:TLOC * c + 512 * (tg + 1)],
                    start=(c == 0), stop=(c == NDC - 1)),
                    inc="t_q" if c == NDC - 1 else None)
            dst = T.ar2[:, 2048 * hp + 512 * tg:2048 * hp + 512 * (tg + 1)]
            P.wait("vector", "t_q")
            P.op("vector", lambda v, b=bank, dst=dst: v.tensor_copy(dst, b[:, 0:512]),
                 inc="v_qe")
            P.wait("scalar", "v_qe")
            P.wait("scalar", "t_qn", P.cnt("t_qn"))
            P.op("scalar", lambda sc, dst=dst: sc.activation(
                T.kvstage[:, 1024:1536], dst, AF.Square), inc="s_qn")
            P.wait("tensor", "s_qn")
            P.wait("tensor", "v_qn", P.cnt("v_qn"))
            P.op("tensor", lambda t: t.matmul(
                T.psm[0:1, 0:512], T.ones128[:], T.kvstage[:, 1024:1536],
                start=True, stop=True), inc="t_qn")
            P.wait("vector", "t_qn")
            P.op("vector", lambda v: v.reduce_max(T.small[0:1, 5:6], T.psm[0:1, 0:512], axis=AX))
            P.drain("vector")
            if first_qn:
                P.op("vector", lambda v: v.tensor_copy(T.small[0:1, 6:7], T.small[0:1, 5:6]),
                     inc="v_qn")
                first_qn = False
            else:
                P.op("vector", lambda v: v.tensor_tensor(
                    T.small[0:1, 6:7], T.small[0:1, 6:7], T.small[0:1, 5:6], OP.max),
                     inc="v_qn")
            P.drain("vector")
    P.wait("gpsimd", "cc")
    for r in range(4):
        P.dma("gpsimd", T.atmp[0:1, r:r + 1],
              mk_ap(T.l0_kout, (1025 * r + 1024) * TLOC, [[1, 1], [1, 1]]), "d_km")
    P.wait("vector", "d_km")
    P.op("vector", lambda v: v.reduce_max(T.small[0:1, 7:8], T.atmp[0:1, 0:4], axis=AX))
    P.drain("vector")
    P.wait("vector", "v_qn")
    P.op("vector", lambda v: v.tensor_tensor(
        T.small[0:1, 8:9], T.small[0:1, 6:7], T.small[0:1, 7:8], OP.mult), inc="v_bnd")
    P.wait("scalar", "v_bnd")
    P.op("scalar", lambda sc: sc.activation(
        T.small[0:1, 9:10], T.small[0:1, 8:9], AF.Sqrt, scale=SCALE * SCALE * 1.05),
        inc="s_bnd")
    P.wait("vector", "s_bnd")
    P.op("vector", lambda v: v.tensor_scalar(
        T.small[0:1, 10:11], T.small[0:1, 9:10], -1.0, -1.0, OP.mult, OP.add), inc="v_bnd")
    P.wait("tensor", "v_bnd")
    P.wait("tensor", "v_qn")
    P.op("tensor", lambda t: t.matmul(
        T.psm[:, 0:1], T.ones1f[:], T.small[0:1, 10:11], start=True, stop=True), inc="t_bnd")
    P.wait("vector", "t_bnd")
    P.op("vector", lambda v: v.tensor_copy(T.boundt, T.psm[:, 0:1]), inc="v_bnd")
    P.wait("vector", "t_q")
    P.op("vector", lambda v: v.memset(
        mk_ap(T.v65, 64, [[T.xn_fm.shape[1], 128], [130, 16], [65, 2]]), 1.0), inc="v_v65i")
    P.wait("gpsimd", "t_q")
    for hp in range(8):
        P.wait("sync", "t_q")
        if hp > 0:
            P.wait("sync", "t_sc", P.cnt("t_sc"))
        for r in range(4):
            P.dma("sync", T.kbuf[:, 2048 * r:2048 * (r + 1)],
                  T.l0_kout[1025 * r + 128 * hp:1025 * r + 128 * (hp + 1), :], "d_ka")
        vblk_copy = {}
        av_done = {}
        for qg in range(4):
            for b in range(8):
                P.wait("gpsimd", "v_v65", P.cnt("v_v65"))
                P.dma("gpsimd", T.vplain[:],
                      mk_ap(T.l0_vout, (2048 * (b // 2) + 1024 * (b % 2)) * D + 128 * hp,
                            [[D, 128], [128 * D, 8], [1, 128]]), "d_va")
                P.wait("vector", "d_va")
                P.wait("vector", "t_av", P.cnt("t_av") - 16)
                P.op("vector", lambda v, b=b: v.tensor_copy(
                    mk_ap(T.v65, 1040 * (b % 2), [[T.xn_fm.shape[1], 128], [130, 8], [65, 2], [1, 64]]),
                    mk_ap(T.vplain, 0, [[T.xn_fm.shape[1], 128], [128, 8], [64, 2], [1, 64]])),
                    inc="v_v65")
                vblk_copy[b] = P.cnt("v_v65")
                for kc in range(8 * b, 8 * b + 8):
                    pb0 = T.ps[kc % 2]
                    pb1 = T.ps[2 + kc % 2]
                    P.wait("tensor", "d_ka")
                    P.wait("tensor", "s_ex", P.cnt("s_ex") - 2)
                    for h, pb in ((0, pb0), (1, pb1)):
                        P.op("tensor", lambda t, pb=pb, h=h, kc=kc, hp=hp, qg=qg: t.matmul(
                            pb[:, 0:512],
                            T.kbuf[64 * h:64 * h + 64, 128 * kc:128 * (kc + 1)],
                            T.ar2[64 * h:64 * h + 64, 2048 * hp + 512 * qg:
                                  2048 * hp + 512 * (qg + 1)],
                            start=True, stop=True),
                            inc="t_sc" if h == 1 else None)
                    P.wait("tensor", "s_ex", P.cnt("s_ex") + 2)
                    P.wait("tensor", "v_v65i")
                    P.wait("tensor", "v_v65", vblk_copy[b])
                    if kc == 0:
                        P.wait("tensor", "v_av", P.cnt("v_av"))
                    for h in (0, 1):
                        P.op("tensor", lambda t, h=h, kc=kc, b=b: t.matmul(
                            T.ps[4 + h][0:65, 0:512],
                            T.v65[:, 1040 * (b % 2) + 130 * (kc % 8) + 65 * h:
                                  1040 * (b % 2) + 130 * (kc % 8) + 65 * h + 65],
                            T.probs[:, 512 * (2 * h + kc % 2):512 * (2 * h + kc % 2 + 1)],
                            start=(kc == 0), stop=(kc == 63)),
                            inc="t_av")
                    av_done[(qg, kc)] = P.cnt("t_av")
                    P.wait("scalar", "t_sc")
                    P.wait("scalar", "v_bnd")
                    for h, pb in ((0, pb0), (1, pb1)):
                        P.wait("scalar", "t_av", P.cnt("t_av") - 5 + h)
                        P.op("scalar", lambda sc, pb=pb, h=h, kc=kc: sc.activation(
                            T.probs[:, 512 * (2 * h + kc % 2):512 * (2 * h + kc % 2 + 1)],
                            pb[:, 0:512], AF.Exp, bias=T.boundt, scale=SCALE),
                            inc="s_ex")
            for h in (0, 1):
                hh = 2 * hp + h
                P.wait("vector", "t_av")
                P.op("vector", lambda v, h=h: v.tensor_copy(
                    T.avr[0:65, 0:512], T.ps[4 + h][0:65, 0:512]), inc="v_av")
                P.wait("tensor", "v_av")
                P.wait("tensor", "v_nm", P.cnt("v_nm"))
                for c in range(4):
                    P.op("tensor", lambda t, c=c: t.transpose(
                        T.psm[:, 128 * c:128 * (c + 1)],
                        T.avr[:, 128 * c:128 * (c + 1)], T.ident_f[:]),
                        inc="t_nm" if c == 3 else None)
                P.wait("vector", "t_nm")
                for c in range(4):
                    jt = qg * 4 + c
                    P.op("vector", lambda v, c=c: v.reciprocal(
                        T.small[:, 11:12], T.psm[:, 128 * c + 64:128 * c + 65]))
                    P.drain("vector")
                    P.op("vector", lambda v, c=c, jt=jt, hh=hh: v.scalar_tensor_tensor(
                        T.tmp64[:], T.psm[:, 128 * c:128 * c + 64], T.small[:, 11:12],
                        T.ar1b[:, 1024 * jt + 64 * hh:1024 * jt + 64 * (hh + 1)],
                        OP.mult, OP.mult))
                    P.drain("vector")
                    P.op("vector", lambda v, jt=jt, hh=hh: v.tensor_tensor(
                        T.x_res[:, 1024 * jt + 64 * hh:1024 * jt + 64 * (hh + 1)],
                        T.x_res[:, 1024 * jt + 64 * hh:1024 * jt + 64 * (hh + 1)],
                        T.tmp64[:], OP.add),
                        inc="v_nm" if c == 3 else None)
                    P.drain("vector")


def emit_store_x(P, T):
    P.wait("sync", "v_nm", P.cnt("v_nm"))
    P.wait("sync", "v_nm2", P.cnt("v_nm2"))
    P.wait("sync", "v_fin", P.cnt("v_fin"))
    P.wait("sync", "v_xn", P.cnt("v_xn"))
    for j in range(NTT):
        P.dma("sync", T.x_out[128 * j:128 * (j + 1), :], T.x_res[:, D * j:D * (j + 1)], "d_out")
    P.wait("sync", "d_out")



PHASES = ("ln", "kv", "gate", "ffn", "q", "attn", "cmb")

BOUNDARY_SEMS = ("v_xn", "v_ln", "s_ln", "t_ln", "v_kv", "t_kv", "s_kv", "t_nrm", "v_nrm",
                 "d_kst0", "d_kst1", "d_nst", "cc", "v_gate", "s_gate", "t_gate", "v_gxt",
                 "t_gxp", "v_b2t", "t_b2t", "v_b2p", "t_b2p", "d_b2", "t_h", "s_h", "t_y",
                 "v_cmb", "s_g", "v_g", "t_q", "v_qe", "s_qn", "t_qn", "v_qn", "v_bnd",
                 "s_bnd", "t_bnd", "d_ka", "d_va", "v_v65", "v_v65i", "t_sc", "s_ex",
                 "t_av", "v_av", "t_nm", "v_nm", "v_nm2", "v_rc", "t_rcb", "d_km",
                 "t_mix", "t_n1", "v_n1e", "t_n1b", "v_x2", "s_x2", "t_n2", "v_n2e",
                 "t_rxt", "v_rxt", "v_f1", "s_f1", "d_w10", "d_w11", "t_dnt", "t_dn",
                 "v_dn", "v_ye", "t_xz", "v_xz", "t_gu", "s_gu", "v_h", "t_do", "s_go",
                 "d_u", "v_up", "v_cv", "s_ae", "v_ya", "s_sz", "v_ys", "t_wo", "v_woe",
                 "d_rs0", "d_rs1", "d_s", "v_fin", "t_ct", "v_ckv", "v_ctq", "t_up",
                 "v_kc", "v_qc", "t_kr", "v_rp", "v_rpe", "t_rpt", "s_qn", "d_rt",
                 "t_wo3", "d_wo", "d_l2p", "d_wkv", "d_wq", "d_gw", "d_b1", "d_wkv1",
                 "d_wq1", "d_bq", "d_bp", "d_mixw", "d_xz0", "d_xz1", "d_wg0", "d_wg1",
                 "d_wd0", "d_wd1", "d_wd3", "d_wu3", "d_wr3", "d_wv3", "d_we0", "d_we1",
                 "d_gw3", "d_wq3", "d_bp3", "d_bp3b", "d_wo3", "d_w3", "d_w3k", "d_km")


def emit_boundary(P, T):
    for eng in Prog.ENGINES:
        for s_, v in sorted(P.counts.items()):
            if v > 0:
                P.wait(eng, s_, v)


def build_program(layers=(0,), phases=PHASES):
    nc = bass.Bass("TRN2", target_bir_lowering=False)
    es = ExitStack()
    with es:
        T = alloc_common(nc, es, layers)
        if 0 in layers:
            alloc_l0(nc, T)
        if 1 in layers:
            alloc_l1(nc, T)
        if 2 in layers:
            alloc_l2(nc, T)
        if 3 in layers:
            alloc_l3(nc, T)
        P = Prog(nc, es)
        emit_setup(P, T)
        first = True
        if 0 in layers:
            first = False
            if "ln" in phases:
                emit_ln(P, T)
            if "kv" in phases:
                emit_l0_kv(P, T)
            if "gate" in phases:
                emit_l0_gate(P, T)
            if "ffn" in phases:
                emit_l0_ffn(P, T)
            if "attn" in phases:
                emit_l0_attn(P, T)
        if 1 in layers:
            if not first:
                emit_boundary(P, T)
            first = False
            if "ln" in phases:
                emit_ln(P, T)
            if "kv" in phases:
                emit_l1_kv(P, T)
            if "ffn" in phases:
                emit_l1_ffn(P, T)
            if "q" in phases:
                emit_l1_reln_q(P, T)
            if "attn" in phases:
                emit_l1_attn(P, T)
        if 2 in layers:
            if not first:
                emit_boundary(P, T)
            first = False
            if "ln" in phases:
                emit_ln(P, T)
            if "kv" in phases:
                emit_l2_xz(P, T)
            if "ffn" in phases:
                emit_l2_ffn(P, T)
            if "attn" in phases:
                emit_l2_scan(P, T)
            if "cmb" in phases:
                emit_l2_rs(P, T)
        if 3 in layers:
            if not first:
                emit_boundary(P, T)
            first = False
            if "ln" in phases:
                emit_ln(P, T)
            if "kv" in phases:
                emit_l3_kv(P, T)
            if "ffn" in phases:
                emit_l3_ffn(P, T)
            if "q" in phases:
                emit_l3_q(P, T)
            if "attn" in phases:
                emit_l3_attn(P, T)
            if "cmb" in phases:
                emit_l3_wo(P, T)
        emit_store_x(P, T)
        for fn in DEBUG:
            fn(P, T)
        P.build()
    global LAST_PROG
    LAST_PROG = P
    return nc


FUSED = True


def kernel(**inputs):
    if FUSED:
        nc = build_program((0, 1, 2, 3))
        in_maps = [prep_core_inputs(c, inputs, (0, 1, 2, 3)) for c in range(NCORES)]
        res = run_bass_kernel_spmd(nc, in_maps, core_ids=list(range(NCORES)))
        x_cur = [res.results[c]["x_out"] for c in range(NCORES)]
    else:
        x_cur = None
        for li in (0, 1, 2, 3):
            nc = build_program((li,))
            in_maps = [prep_core_inputs(c, inputs, (li,)) for c in range(NCORES)]
            if x_cur is not None:
                for c in range(NCORES):
                    in_maps[c]["x_in"] = x_cur[c]
            res = run_bass_kernel_spmd(nc, in_maps, core_ids=list(range(NCORES)))
            x_cur = [np.ascontiguousarray(res.results[c]["x_out"]) for c in range(NCORES)]
    out = np.empty((B, S, D), np.float32)
    for c in range(NCORES):
        b, r = c // 4, c % 4
        out[b, r * TLOC:(r + 1) * TLOC, :] = x_cur[c]
    return out



def alloc_l1(nc, T):
    T.l1_wqT = nc.dram_tensor("l1_wqT", [D, D], F32, kind="ExternalInput").ap()
    T.l1_wkT = nc.dram_tensor("l1_wkT", [D, D], F32, kind="ExternalInput").ap()
    T.l1_wvT = nc.dram_tensor("l1_wvT", [D, D], F32, kind="ExternalInput").ap()
    T.l1_bq = nc.dram_tensor("l1_bq", [128, 8], F32, kind="ExternalInput").ap()
    T.l1_bk = nc.dram_tensor("l1_bk", [128, 8], F32, kind="ExternalInput").ap()
    T.l1_bv = nc.dram_tensor("l1_bv", [1, D], F32, kind="ExternalInput").ap()
    T.l1_g1 = nc.dram_tensor("l1_g1", [128, 8], F32, kind="ExternalInput").ap()
    T.l1_be1 = nc.dram_tensor("l1_be1", [128, 8], F32, kind="ExternalInput").ap()
    T.l1_g2 = nc.dram_tensor("l1_g2", [128, 8], F32, kind="ExternalInput").ap()
    T.l1_be2 = nc.dram_tensor("l1_be2", [128, 8], F32, kind="ExternalInput").ap()
    T.l1_mixw = nc.dram_tensor("l1_mixw", [D, D], F32, kind="ExternalInput").ap()
    T.l1_mixb = nc.dram_tensor("l1_mixb", [1, D], F32, kind="ExternalInput").ap()
    T.l1_rw = nc.dram_tensor("l1_rw", [D, 8], F32, kind="ExternalInput").ap()
    T.l1_w1 = nc.dram_tensor("l1_w1", [8, D, 4096], F32, kind="ExternalInput").ap()
    T.l1_b1 = nc.dram_tensor("l1_b1", [128, 256], F32, kind="ExternalInput").ap()
    T.l1_w2 = nc.dram_tensor("l1_w2", [8, 4096, D], F32, kind="ExternalInput").ap()
    T.l1_b2 = nc.dram_tensor("l1_b2", [8, D], F32, kind="ExternalInput").ap()
    T.l1_kin = nc.dram_tensor("l1_kin", [1025, TLOC], BF16).ap()
    T.l1_kout = nc.dram_tensor("l1_kout", [4 * 1025, TLOC], BF16).ap()
    T.l1_vin = nc.dram_tensor("l1_vin", [TLOC, D], BF16).ap()
    T.l1_vout = nc.dram_tensor("l1_vout", [4 * TLOC, D], BF16).ap()


def emit_row_to_packed(P, T, src_row_ap, dst_packed, tag):
    P.dma("sync", dst_packed[:], mk_ap(src_row_ap, 0, [[1, 128], [128, NDC]]), tag)


def emit_l1_kv(P, T):
    for c in range(NDC):
        P.dma("gpsimd", T.ar2[:, 1024 * c:1024 * (c + 1)],
              T.l1_wkT[128 * c:128 * (c + 1), :], "d_wkv1")
    P.dma("sync", T.bkp[:], T.l1_bk, "d_bp")
    P.wait("tensor", "d_wkv1")
    P.wait("tensor", "v_xn")
    P.wait("vector", "d_bp")
    grp = P.cnt("t_kv")
    slot_free = [("v_setup", 0), ("v_setup", 0)]
    first_nrm = True
    for tg in range(4):
        for g in range(8):
            bank = T.ps[grp % 2]
            P.wait("tensor", "v_kv", P.cnt("v_kv") - 1)
            for c in range(NDC):
                P.op("tensor", lambda t, b=bank, c=c, g=g, tg=tg: t.matmul(
                    b[:, 0:512],
                    T.ar2[:, 1024 * c + 128 * g:1024 * c + 128 * g + 128],
                    T.xn_fm[:, TLOC * c + 512 * tg:TLOC * c + 512 * (tg + 1)],
                    start=(c == 0), stop=(c == NDC - 1)),
                    inc="t_kv" if c == NDC - 1 else None)
            slot = grp % 2
            stg = T.kvstage[:, 512 * slot:512 * (slot + 1)]
            P.wait("vector", "t_kv")
            P.wait("vector", *slot_free[slot])
            P.op("vector", lambda v, b=bank, stg=stg, g=g: v.tensor_scalar(
                stg, b[:, 0:512], T.bkp[:, g:g + 1], None, OP.add), inc="v_kv")
            P.wait("scalar", "v_kv")
            P.wait("scalar", "t_nrm", P.cnt("t_nrm"))
            P.op("scalar", lambda sc, stg=stg: sc.activation(
                T.kvstage[:, 1024:1536], stg, AF.Square), inc="s_kv")
            P.wait("tensor", "s_kv")
            if g == 0:
                P.wait("tensor", "v_nrm", P.cnt("v_nrm"))
            P.op("tensor", lambda t, g=g: t.matmul(
                T.psm[0:1, 0:512], T.ones128[:], T.kvstage[:, 1024:1536],
                start=(g == 0), stop=(g == 7)), inc="t_nrm")
            P.wait("sync", "v_kv")
            n = P.dma("sync", T.l1_kin[128 * g:128 * (g + 1), 512 * tg:512 * (tg + 1)], stg,
                      f"d_kst{slot}")
            slot_free[slot] = (f"d_kst{slot}", n)
            grp += 1
        P.wait("vector", "t_nrm")
        P.op("vector", lambda v: v.reduce_max(T.small[0:1, 3:4], T.psm[0:1, 0:512], axis=AX))
        P.drain("vector")
        if first_nrm:
            P.op("vector", lambda v: v.tensor_copy(T.small[0:1, 4:5], T.small[0:1, 3:4]),
                 inc="v_nrm")
            first_nrm = False
        else:
            P.op("vector", lambda v: v.tensor_tensor(
                T.small[0:1, 4:5], T.small[0:1, 4:5], T.small[0:1, 3:4], OP.max),
                 inc="v_nrm")
        P.drain("vector")
    P.wait("vector", "v_nrm")
    P.op("vector", lambda v: v.tensor_copy(T.nstage[0:1, 0:1], T.small[0:1, 4:5]),
         inc="v_kmax")
    P.wait("sync", "v_kmax")
    P.dma("sync", mk_ap(T.l1_kin, 1024 * TLOC, [[1, 1], [1, 1]]), T.nstage[0:1, 0:1], "d_nst")
    P.wait("gpsimd", "t_kv", P.cnt("t_kv"))
    for c in range(NDC):
        P.dma("gpsimd", T.ar2[:, 1024 * c:1024 * (c + 1)],
              T.l1_wvT[128 * c:128 * (c + 1), :], "d_wkv1")
    P.wait("gpsimd", "d_nst", P.cnt("d_nst"))
    P.dma("gpsimd", T.nstage[0:1, 0:1024], T.l1_bv, "d_wkv1")
    P.wait("tensor", "d_wkv1")
    for j in range(NTT):
        for nc2 in range(2):
            bank = T.ps[grp % 2]
            P.wait("tensor", "v_kv", P.cnt("v_kv") - 1)
            for c in range(NDC):
                P.op("tensor", lambda t, b=bank, c=c, j=j, n=nc2: t.matmul(
                    b[:, 0:512],
                    T.xn_fm[:, TLOC * c + 128 * j:TLOC * c + 128 * (j + 1)],
                    T.ar2[:, 1024 * c + 512 * n:1024 * c + 512 * (n + 1)],
                    start=(c == 0), stop=False),
                    inc=None)
            P.op("tensor", lambda t, b=bank, n=nc2: t.matmul(
                b[:, 0:512], T.ones1[:], T.nstage[0:1, 512 * n:512 * (n + 1)],
                start=False, stop=True), inc="t_kv")
            slot = grp % 2
            stg = T.kvstage[:, 512 * slot:512 * (slot + 1)]
            P.wait("vector", "t_kv")
            P.wait("vector", *slot_free[slot])
            P.op("vector", lambda v, b=bank, stg=stg: v.tensor_copy(stg, b[:, 0:512]),
                 inc="v_kv")
            P.wait("sync", "v_kv")
            n = P.dma("sync", T.l1_vin[128 * j:128 * (j + 1), 512 * nc2:512 * (nc2 + 1)], stg,
                      f"d_kst{slot}")
            slot_free[slot] = (f"d_kst{slot}", n)
            grp += 1
    P.wait("gpsimd", "d_kst0")
    P.wait("gpsimd", "d_kst1")
    P.wait("gpsimd", "d_nst")
    if SKIP_CC:
        P.op("gpsimd", lambda g: g.nop(), inc="cc")
    else:
        P.op("gpsimd", lambda g: g.collective_compute(
            "AllGather", OP.bypass, replica_groups=GROUPS,
            ins=[T.l1_kin.opt()], outs=[T.l1_kout.opt()]), inc="cc")
    if SKIP_CC:
        P.op("gpsimd", lambda g: g.nop(), inc="cc")
    else:
        P.op("gpsimd", lambda g: g.collective_compute(
            "AllGather", OP.bypass, replica_groups=GROUPS,
            ins=[T.l1_vin.opt()], outs=[T.l1_vout.opt()]), inc="cc")


def emit_l1_ffn(P, T):
    atmp2 = T.ar2[:, 8192:10240].bitcast(F32)
    """x2 = xn + n1 + n1@mixw + mixb; n2 = LN(x2)*g2+be2; top-2 router (fp32);
    y_acc(ar1) = x2 + B2tot + sum_e w_e*(silu(n2@w1+b1)@w2); g = silu(y_acc)."""
    P.dma("sync", T.g1p[:], T.l1_g1, "d_bp")
    P.dma("sync", T.be1p[:], T.l1_be1, "d_bp")
    P.dma("sync", T.g2p[:], T.l1_g2, "d_bp")
    P.dma("sync", T.be2p[:], T.l1_be2, "d_bp")
    P.dma("sync", T.b1_sb[:, 0:256], T.l1_b1, "d_b1")
    for c in range(NDC):
        P.dma("sync", T.gw_sb[:, 8 * c:8 * (c + 1)], T.l1_rw[128 * c:128 * (c + 1), :],
              "d_gw")
    P.wait("gpsimd", "t_kv", P.cnt("t_kv"))
    for c in range(NDC):
        P.dma("gpsimd", T.ar2[:, 1024 * c:1024 * (c + 1)],
              T.l1_mixw[128 * c:128 * (c + 1), :], "d_mixw")
    P.dma("gpsimd", T.nstage[0:1, 0:1024], T.l1_mixb, "d_mixw")
    P.wait("vector", "d_bp")
    P.wait("vector", "d_x")
    P.wait("vector", "d_kst0")
    P.wait("vector", "d_kst1")
    P.wait("vector", "d_nst")
    P.wait("vector", "t_kv", P.cnt("t_kv"))
    for j in range(NTT):
        m = T.stats[:, j:j + 1]
        rs = T.stats[:, 16 + j:16 + j + 1]
        m2 = T.stats[:, 32 + j:32 + j + 1]
        rs2 = T.stats[:, 48 + j:48 + j + 1]
        P.wait("vector", "s_ln", P.cnt("s_ln"))
        P.drain("vector")
        P.op("vector", lambda v, j=j, m=m, rs=rs: v.tensor_scalar(
            T.scratch[:], T.x_res[:, 1024 * j:1024 * (j + 1)], m, rs,
            OP.subtract, OP.mult), inc="v_f1")
        P.drain("vector")
        P.op("vector", lambda v: v.reduce_sum(T.small[:, 0:1], T.scratch[:], axis=AX))
        P.drain("vector")
        P.op("vector", lambda v, m2=m2: v.tensor_scalar(
            T.small[:, 1:2], T.small[:, 0:1], 1.0 / D, None, OP.mult))
        P.drain("vector")
        P.op("vector", lambda v: v.tensor_scalar(
            atmp2, T.scratch[:], T.small[:, 1:2], None, OP.subtract), inc="v_f1")
        P.wait("scalar", "v_f1")
        P.op("scalar", lambda sc: sc.activation(
            atmp2 if False else T.ar1[:, 0:1024], atmp2, AF.Square,
            accum_out=T.small[:, 2:3]))
        P.drain("scalar")
        P.op("scalar", lambda sc: sc.activation(
            T.small[:, 3:4], T.small[:, 2:3], AF.Ln, bias=T.epst, scale=1.0 / D))
        P.drain("scalar")
        P.op("scalar", lambda sc, rs2=rs2: sc.activation(
            rs2, T.small[:, 3:4], AF.Exp, scale=-0.5), inc="s_f1")
        P.wait("vector", "s_f1")
        P.op("vector", lambda v, m2=m2: v.tensor_copy(m2, T.small[:, 1:2]))
        P.drain("vector")
        P.op("vector", lambda v, rs2=rs2: v.tensor_scalar(
            T.kvstage[:, 0:1024], atmp2, rs2, None, OP.mult), inc="v_f1")
        P.wait("tensor", "v_f1")
        P.wait("tensor", "v_n1e", P.cnt("v_n1e"))
        for c in range(NDC):
            P.op("tensor", lambda t, c=c: t.transpose(
                T.pst[:, 128 * c:128 * (c + 1)], T.kvstage[:, 128 * c:128 * (c + 1)],
                T.ident_b[:]), inc="t_n1" if c == NDC - 1 else None)
        P.wait("vector", "t_n1")
        for c in range(NDC):
            P.op("vector", lambda v, c=c, j=j: v.tensor_scalar(
                T.xn_fm[:, TLOC * c + 128 * j:TLOC * c + 128 * (j + 1)],
                T.pst[:, 128 * c:128 * (c + 1)],
                T.g1p[:, c:c + 1], T.be1p[:, c:c + 1], OP.mult, OP.add),
                inc="v_n1e" if c == NDC - 1 else None)
    P.wait("tensor", "d_mixw")
    P.wait("tensor", "d_gw")
    for j in range(NTT):
        m = T.stats[:, j:j + 1]
        rs = T.stats[:, 16 + j:16 + j + 1]
        P.wait("tensor", "v_n1e", P.cnt("v_n1e"))
        P.wait("tensor", "v_x2", P.cnt("v_x2"))
        for nc2 in range(2):
            bank = T.ps[2 + nc2]
            for c in range(NDC):
                P.op("tensor", lambda t, b=bank, c=c, j=j, n=nc2: t.matmul(
                    b[:, 0:512],
                    T.xn_fm[:, TLOC * c + 128 * j:TLOC * c + 128 * (j + 1)],
                    T.ar2[:, 1024 * c + 512 * n:1024 * c + 512 * (n + 1)],
                    start=(c == 0), stop=False), inc=None)
            P.op("tensor", lambda t, b=bank, n=nc2: t.matmul(
                b[:, 0:512], T.ones1[:], T.nstage[0:1, 512 * n:512 * (n + 1)],
                start=False, stop=True), inc="t_mix")
        for c in range(NDC):
            P.op("tensor", lambda t, c=c, j=j: t.transpose(
                T.pst[:, 128 * c:128 * (c + 1)],
                T.xn_fm[:, TLOC * c + 128 * j:TLOC * c + 128 * (j + 1)],
                T.ident_b[:]), inc="t_n1b" if c == NDC - 1 else None)
        P.wait("vector", "t_mix")
        P.wait("vector", "t_n1b")
        P.drain("vector")
        P.op("vector", lambda v, j=j, m=m, rs=rs: v.tensor_scalar(
            T.scratch[:], T.x_res[:, 1024 * j:1024 * (j + 1)], m, rs,
            OP.subtract, OP.mult))
        P.drain("vector")
        for nc2 in range(2):
            P.op("vector", lambda v, j=j, n=nc2: v.tensor_tensor(
                T.ar1[:, 1024 * j + 512 * n:1024 * j + 512 * (n + 1)],
                T.scratch[:, 512 * n:512 * (n + 1)], T.ps[2 + n][:, 0:512], OP.add))
        P.drain("vector")
        for c in range(NDC):
            P.op("vector", lambda v, c=c, j=j: v.tensor_tensor(
                T.ar1[:, 1024 * j + 128 * c:1024 * j + 128 * (c + 1)],
                T.ar1[:, 1024 * j + 128 * c:1024 * j + 128 * (c + 1)],
                T.pst[:, 128 * c:128 * (c + 1)], OP.add),
                inc="v_x2" if c == NDC - 1 else None)
        P.drain("vector")
        P.op("vector", lambda v, j=j: v.reduce_sum(
            T.small[:, 0:1], T.ar1[:, 1024 * j:1024 * (j + 1)], axis=AX))
        P.drain("vector")
        P.op("vector", lambda v: v.tensor_scalar(
            T.small[:, 1:2], T.small[:, 0:1], 1.0 / D, None, OP.mult))
        P.drain("vector")
        P.op("vector", lambda v, j=j: v.tensor_scalar(
            T.scratch[:], T.ar1[:, 1024 * j:1024 * (j + 1)], T.small[:, 1:2], None,
            OP.subtract), inc="v_x2")
        P.wait("scalar", "v_x2")
        P.op("scalar", lambda sc: sc.activation(
            T.ar1[:, 0:1024] if False else atmp2, T.scratch[:], AF.Square,
            accum_out=T.small[:, 2:3]))
        P.drain("scalar")
        P.op("scalar", lambda sc: sc.activation(
            T.small[:, 3:4], T.small[:, 2:3], AF.Ln, bias=T.epst, scale=1.0 / D))
        P.drain("scalar")
        P.op("scalar", lambda sc: sc.activation(
            T.small[:, 4:5], T.small[:, 3:4], AF.Exp, scale=-0.5), inc="s_x2")
        P.wait("vector", "s_x2")
        P.op("vector", lambda v: v.tensor_scalar(
            T.kvstage[:, 0:1024], T.scratch[:], T.small[:, 4:5], None, OP.mult),
            inc="v_x2")
        P.op("vector", lambda v: v.tensor_scalar(
            atmp2, T.scratch[:], T.small[:, 4:5], None, OP.mult), inc="v_x2")
        P.wait("tensor", "v_x2")
        P.wait("tensor", "v_n2e", P.cnt("v_n2e"))
        for c in range(NDC):
            P.op("tensor", lambda t, c=c: t.transpose(
                T.pst[:, 128 * c:128 * (c + 1)], T.kvstage[:, 128 * c:128 * (c + 1)],
                T.ident_b[:]), inc="t_n2" if c == NDC - 1 else None)
        P.wait("vector", "t_n2")
        for c in range(NDC):
            P.op("vector", lambda v, c=c, j=j: v.tensor_scalar(
                T.xn_fm[:, TLOC * c + 128 * j:TLOC * c + 128 * (j + 1)],
                T.pst[:, 128 * c:128 * (c + 1)],
                T.g2p[:, c:c + 1], T.be2p[:, c:c + 1], OP.mult, OP.add),
                inc="v_n2e" if c == NDC - 1 else None)
        P.wait("tensor", "v_x2")
        P.wait("tensor", "v_rxt", P.cnt("v_rxt"))
        P.wait("tensor", "v_kv", P.cnt("v_kv"))
        for c in range(NDC):
            P.op("tensor", lambda t, c=c: t.transpose(
                T.ps[c // 4][:, 128 * (c % 4):128 * (c % 4) + 128],
                atmp2[:, 128 * c:128 * (c + 1)], T.ident_f[:]),
                inc="t_rxt" if c == NDC - 1 else None)
        P.wait("vector", "t_rxt")
        for c in range(NDC):
            P.op("vector", lambda v, c=c: v.tensor_scalar(
                T.scratch[:, 128 * c:128 * (c + 1)],
                T.ps[c // 4][:, 128 * (c % 4):128 * (c % 4) + 128],
                T.g2p[:, c:c + 1], T.be2p[:, c:c + 1], OP.mult, OP.add),
                inc="v_rxt" if c == NDC - 1 else None)
        P.wait("tensor", "v_rxt")
        P.wait("tensor", "s_gate", P.cnt("s_gate"))
        for c in range(NDC):
            P.op("tensor", lambda t, c=c, j=j: t.matmul(
                T.psm[:, 0:8],
                T.scratch[:, 128 * c:128 * (c + 1)],
                T.gw_sb[:, 8 * c:8 * (c + 1)],
                start=(c == 0), stop=(c == NDC - 1)),
                inc="t_gate" if c == NDC - 1 else None)
        w = T.w_all[:, 14 * j:14 * j + 8]
        e_t = T.scratch[:, 0:8]
        P.wait("vector", "t_gate")
        P.drain("vector")
        P.op("vector", lambda v: v.reduce_max(T.small[:, 0:1], T.psm[:, 0:8], axis=AX))
        P.drain("vector")
        P.op("vector", lambda v: v.tensor_scalar(
            T.small[:, 1:2], T.small[:, 0:1], -1.0, None, OP.mult), inc="v_gate")
        P.wait("scalar", "v_gate")
        P.op("scalar", lambda sc, e_t=e_t: sc.activation(
            e_t, T.psm[:, 0:8], AF.Exp, bias=T.small[:, 1:2],
            accum_out=T.small[:, 5:6]), inc="s_gate")
        P.wait("vector", "s_gate")
        P.op("vector", lambda v: v.reciprocal(T.small[:, 6:7], T.small[:, 5:6]))
        P.drain("vector")
        P.op("vector", lambda v, e_t=e_t: v.tensor_scalar(
            T.scratch[:, 32:40], e_t, 1.0 - 1e-6, None, OP.is_ge))
        P.drain("vector")
        P.op("vector", lambda v, e_t=e_t: v.scalar_tensor_tensor(
            T.scratch[:, 64:72], T.scratch[:, 32:40], -2.0, e_t, OP.mult, OP.add))
        P.drain("vector")
        P.op("vector", lambda v: v.reduce_max(T.small[:, 2:3], T.scratch[:, 64:72], axis=AX))
        P.drain("vector")
        P.op("vector", lambda v, e_t=e_t: v.tensor_scalar(
            T.scratch[:, 96:104], e_t, T.small[:, 2:3], None, OP.is_ge))
        P.drain("vector")
        P.op("vector", lambda v, e_t=e_t: v.tensor_tensor(
            T.scratch[:, 128:136], e_t, T.scratch[:, 96:104], OP.mult))
        P.drain("vector")
        P.op("vector", lambda v, w=w: v.tensor_scalar(
            w, T.scratch[:, 128:136], T.small[:, 6:7], None, OP.mult), inc="v_gate")
    for j in range(NTT):
        P.wait("tensor", "v_gate", P.cnt("v_gate"))
        P.wait("tensor", "v_b2t", P.cnt("v_b2t"))
        P.wait("tensor", "v_b2p", P.cnt("v_b2p"))
        P.wait("tensor", "s_gate")
        P.op("tensor", lambda t, j=j: t.transpose(
            T.psm[0:8, 0:128], T.w_all[:, 14 * j:14 * j + 8].bitcast(F32) if False else
            T.w_all[:, 14 * j:14 * j + 8], T.ident_f[:]), inc="t_b2t")
        P.wait("vector", "t_b2t")
        P.op("vector", lambda v: v.tensor_copy(T.wT16[0:8, 0:128], T.psm[0:8, 0:128]),
             inc="v_b2t")
        for nc2 in range(2):
            P.wait("gpsimd", "t_b2p", P.cnt("t_b2p"))
            P.dma("gpsimd", T.b14[0:8, 0:512], T.l1_b2[0:8, 512 * nc2:512 * (nc2 + 1)],
                  "d_b2")
            P.wait("tensor", "v_b2t")
            P.wait("tensor", "d_b2")
            P.wait("tensor", "v_b2p", P.cnt("v_b2p"))
            P.op("tensor", lambda t: t.matmul(
                T.psm[:, 0:512], T.wT16[0:8, 0:128], T.b14[0:8, 0:512],
                start=True, stop=True), inc="t_b2p")
            P.wait("vector", "t_b2p")
            P.op("vector", lambda v, j=j, n=nc2: v.tensor_tensor(
                T.ar1[:, 1024 * j + 512 * n:1024 * j + 512 * (n + 1)],
                T.ar1[:, 1024 * j + 512 * n:1024 * j + 512 * (n + 1)],
                T.psm[:, 0:512], OP.add), inc="v_b2p")
            P.drain("vector")
    P.wait("scalar", "d_b1")
    P.wait("gpsimd", "t_rxt", P.cnt("t_rxt"))
    P.wait("gpsimd", "v_gate", P.cnt("v_gate"))
    P.wait("gpsimd", "t_gate", P.cnt("t_gate"))
    qslot_free = [("v_setup", 0), ("v_setup", 0)]
    nq_total = 0
    for e in range(8):
        for q in range(8):
            qs = nq_total % 2
            base = 8192 * qs
            P.wait("gpsimd", *qslot_free[qs])
            P.wait("gpsimd", "t_mix", P.cnt("t_mix"))
            for c in range(NDC):
                P.dma("gpsimd",
                      mk_ap(T.ar2[:, 0:1], base + 128 * c,
                            [[16384, 128], [1024, 4], [1, 128]]),
                      mk_ap(T.l1_w1, e * D * 4096 + 4096 * 128 * c + 512 * q,
                            [[4096, 128], [128, 4], [1, 128]]), f"d_w1{qs}")
            P.dma("gpsimd",
                  mk_ap(T.ar2[:, 0:1], base + 4096, [[16384, 128], [1024, 4], [1, 1024]]),
                  mk_ap(T.l1_w2, e * 4096 * D + 512 * q * D,
                        [[1024, 128], [128 * 1024, 4], [1, 1024]]), f"d_w1{qs}")
            P.wait("tensor", f"d_w1{qs}")
            last_t_y = None
            for tg in range(8):
                for f in range(4):
                    ft = 4 * q + f
                    hb = T.ps[f % 2]
                    P.wait("tensor", "s_h", P.cnt("s_h") - 1)
                    for c in range(NDC):
                        P.op("tensor", lambda t, hb=hb, c=c, tg=tg, base=base, f=f: t.matmul(
                            hb[:, 0:256],
                            T.ar2[:, base + 1024 * f + 128 * c:base + 1024 * f + 128 * (c + 1)],
                            T.xn_fm[:, TLOC * c + 256 * tg:TLOC * c + 256 * (tg + 1)],
                            start=(c == 0), stop=(c == NDC - 1)),
                            inc="t_h" if c == NDC - 1 else None)
                    hslot = f % 2
                    hs = T.kvstage[:, 512 * hslot:512 * hslot + 256]
                    P.wait("scalar", "t_h")
                    P.op("scalar", lambda sc, hb=hb, hs=hs, e=e, ft=ft: sc.activation(
                        hs, hb[:, 0:256], AF.Silu,
                        bias=T.b1_sb[:, 32 * e + ft:32 * e + ft + 1]), inc="s_h")
                    P.wait("tensor", "s_h")
                    for tt in range(2):
                        for nc2 in range(2):
                            yb = T.ps[2 + tt * 2 + nc2]
                            P.op("tensor", lambda t, yb=yb, hs=hs, tt=tt, n=nc2,
                                 base=base, f=f: t.matmul(
                                yb[:, 0:512],
                                hs[:, 128 * tt:128 * (tt + 1)],
                                T.ar2[:, base + 4096 + 1024 * f + 512 * n:
                                      base + 4096 + 1024 * f + 512 * (n + 1)],
                                start=(f == 0), stop=(f == 3)),
                                inc="t_y" if (tt, nc2) == (1, 1) else None)
                P.wait("vector", "t_y")
                P.wait("vector", "v_b2p", P.cnt("v_b2p"))
                for tt in range(2):
                    jt = tg * 2 + tt
                    for nc2 in range(2):
                        yb = T.ps[2 + tt * 2 + nc2]
                        dst = T.ar1[:, 1024 * jt + 512 * nc2:1024 * jt + 512 * (nc2 + 1)]
                        P.op("vector", lambda v, yb=yb, dst=dst, jt=jt, e=e:
                             v.scalar_tensor_tensor(
                                 dst, yb[:, 0:512],
                                 T.w_all[:, 14 * jt + e:14 * jt + e + 1], dst,
                                 OP.mult, OP.add),
                             inc="v_cmb" if (tt, nc2) == (1, 1) else None)
                P.wait("tensor", "v_cmb")
                last_t_y = P.cnt("t_y")
            qslot_free[qs] = ("t_y", last_t_y)
            nq_total += 1
    for j in range(NTT):
        P.wait("scalar", "v_cmb", P.cnt("v_cmb"))
        P.drain("scalar")
        P.op("scalar", lambda sc, j=j: sc.activation(
            T.ar1b[:, 1024 * j:1024 * (j + 1)], T.ar1[:, 1024 * j:1024 * (j + 1)],
            AF.Silu), inc="s_g")


def emit_l1_reln_q(P, T):
    for j in range(NTT):
        m = T.stats[:, j:j + 1]
        rs = T.stats[:, 16 + j:16 + j + 1]
        P.wait("vector", "t_y", P.cnt("t_y"))
        P.wait("vector", "s_g", P.cnt("s_g"))
        P.drain("vector")
        P.op("vector", lambda v, j=j, m=m, rs=rs: v.tensor_scalar(
            T.kvstage[:, 0:1024], T.x_res[:, 1024 * j:1024 * (j + 1)], m, rs,
            OP.subtract, OP.mult), inc="v_ln")
        P.wait("tensor", "v_ln")
        P.wait("tensor", "v_xn", P.cnt("v_xn"))
        for c in range(NDC):
            P.op("tensor", lambda t, c=c: t.transpose(
                T.pst[:, 128 * c:128 * (c + 1)],
                T.kvstage[:, 128 * c:128 * (c + 1)], T.ident_b[:]),
                inc="t_ln" if c == NDC - 1 else None)
        P.wait("vector", "t_ln")
        for c in range(NDC):
            P.op("vector", lambda v, c=c, j=j: v.tensor_copy(
                T.xn_fm[:, TLOC * c + 128 * j:TLOC * c + 128 * (j + 1)],
                T.pst[:, 128 * c:128 * (c + 1)]),
                inc="v_xn" if c == NDC - 1 else None)
        P.wait("tensor", "v_xn")
    P.wait("gpsimd", "s_g")
    P.wait("gpsimd", "t_y", P.cnt("t_y"))
    for c in range(NDC):
        P.dma("gpsimd", T.ar2[:, 1024 * c:1024 * (c + 1)],
              T.l1_wqT[128 * c:128 * (c + 1), :], "d_wq1")
    P.dma("sync", T.bqp[:], T.l1_bq, "d_bq")
    P.wait("vector", "d_bq")
    P.wait("tensor", "d_wq1")
    P.wait("tensor", "v_xn")
    first_qn = True
    grp = 0
    for tg in range(4):
        for g in range(8):
            bank = T.ps[grp % 2]
            P.wait("tensor", "v_qe", P.cnt("v_qe") - 1)
            for c in range(NDC):
                P.op("tensor", lambda t, b=bank, c=c, g=g, tg=tg: t.matmul(
                    b[:, 0:512],
                    T.ar2[:, 1024 * c + 128 * g:1024 * c + 128 * g + 128],
                    T.xn_fm[:, TLOC * c + 512 * tg:TLOC * c + 512 * (tg + 1)],
                    start=(c == 0), stop=(c == NDC - 1)),
                    inc="t_q" if c == NDC - 1 else None)
            dst = T.ar1b[:, 16384 + 2048 * g + 512 * tg:16384 + 2048 * g + 512 * (tg + 1)]
            P.wait("vector", "t_q")
            P.op("vector", lambda v, b=bank, dst=dst, g=g: v.tensor_scalar(
                dst, b[:, 0:512], T.bqp[:, g:g + 1], None, OP.add), inc="v_qe")
            P.wait("scalar", "v_qe")
            P.wait("scalar", "t_qn", P.cnt("t_qn"))
            P.op("scalar", lambda sc, dst=dst: sc.activation(
                T.kvstage[:, 1024:1536], dst, AF.Square), inc="s_qn")
            P.wait("tensor", "s_qn")
            if g == 0:
                P.wait("tensor", "v_qn", P.cnt("v_qn"))
            P.op("tensor", lambda t, g=g: t.matmul(
                T.psm[0:1, 0:512], T.ones128[:], T.kvstage[:, 1024:1536],
                start=(g == 0), stop=(g == 7)), inc="t_qn")
            grp += 1
        P.wait("vector", "t_qn")
        P.op("vector", lambda v: v.reduce_max(T.small[0:1, 5:6], T.psm[0:1, 0:512], axis=AX))
        P.drain("vector")
        if first_qn:
            P.op("vector", lambda v: v.tensor_copy(T.small[0:1, 6:7], T.small[0:1, 5:6]),
                 inc="v_qn")
            first_qn = False
        else:
            P.op("vector", lambda v: v.tensor_tensor(
                T.small[0:1, 6:7], T.small[0:1, 6:7], T.small[0:1, 5:6], OP.max),
                 inc="v_qn")
        P.drain("vector")
    SCALE = 0.03125
    P.wait("gpsimd", "cc")
    for r in range(4):
        P.dma("gpsimd", T.scratch[0:1, 896 + r:897 + r],
              mk_ap(T.l1_kout, (1025 * r + 1024) * TLOC, [[1, 1], [1, 1]]), "d_km")
    P.wait("vector", "d_km")
    P.drain("vector")
    P.op("vector", lambda v: v.reduce_max(T.small[0:1, 7:8], T.scratch[0:1, 896:900], axis=AX))
    P.drain("vector")
    P.wait("vector", "v_qn")
    P.op("vector", lambda v: v.tensor_tensor(
        T.small[0:1, 8:9], T.small[0:1, 6:7], T.small[0:1, 7:8], OP.mult), inc="v_bnd")
    P.wait("scalar", "v_bnd")
    P.op("scalar", lambda sc: sc.activation(
        T.small[0:1, 9:10], T.small[0:1, 8:9], AF.Sqrt, scale=SCALE * SCALE * 1.05),
        inc="s_bnd")
    P.wait("vector", "s_bnd")
    P.op("vector", lambda v: v.tensor_scalar(
        T.small[0:1, 10:11], T.small[0:1, 9:10], -1.0, -1.0, OP.mult, OP.add), inc="v_bnd")
    P.wait("tensor", "v_bnd")
    P.wait("tensor", "v_qn")
    P.op("tensor", lambda t: t.matmul(
        T.psm[:, 0:1], T.ones1f[:], T.small[0:1, 10:11], start=True, stop=True), inc="t_bnd")
    P.wait("vector", "t_bnd")
    P.op("vector", lambda v: v.tensor_copy(T.boundt, T.psm[:, 0:1]), inc="v_bnd")


def emit_l1_attn(P, T):
    SCALE = 0.03125
    kb = [T.xn_fm[:, 0:1024], T.xn_fm[:, 1024:2048]]
    vb = [T.xn_fm[:, 2048:3072], T.xn_fm[:, 3072:4096]]
    prb = [T.xn_fm[:, 4224:4480], T.xn_fm[:, 4480:4736]]
    P.wait("gpsimd", "t_q")
    dka = [0, 0]
    dva = [0, 0]
    for qg in range(8):
        for kc in range(64):
            r = kc // 16
            sl = kc % 2
            P.wait("gpsimd", "t_sc", P.cnt("t_sc") - 1)
            dka[sl] = P.dma("gpsimd", kb[sl],
                            mk_ap(T.l1_kout, 1025 * r * TLOC + 128 * (kc % 16),
                                  [[TLOC, 128], [128 * TLOC, 8], [1, 128]]), f"d_ka{sl}")
            P.wait("gpsimd", "t_av", P.cnt("t_av") - 4)
            dva[sl] = P.dma("gpsimd", vb[sl],
                            T.l1_vout[2048 * r + 128 * (kc % 16):
                                      2048 * r + 128 * (kc % 16) + 128, :], f"d_va{sl}")
            P.wait("tensor", f"d_ka{sl}", dka[sl])
            P.wait("tensor", "s_ex", P.cnt("s_ex") - 1)
            pb = T.ps[kc % 2]
            for c in range(NDC):
                P.op("tensor", lambda t, pb=pb, c=c, kc=kc, qg=qg: t.matmul(
                    pb[:, 0:256],
                    kb[kc % 2][:, 128 * c:128 * (c + 1)],
                    T.ar1b[:, 16384 + 2048 * c + 256 * qg:16384 + 2048 * c + 256 * (qg + 1)],
                    start=(c == 0), stop=(c == NDC - 1)),
                    inc="t_sc" if c == NDC - 1 else None)
            P.wait("tensor", "s_ex", P.cnt("s_ex") + 1)
            P.wait("tensor", f"d_va{sl}", dva[sl])
            if kc == 0:
                P.wait("tensor", "v_nm", P.cnt("v_nm"))
                P.wait("tensor", "v_dn", P.cnt("v_dn"))
            for tt in range(2):
                for nc2 in range(2):
                    pav = T.ps[2 + 2 * tt + nc2]
                    P.op("tensor", lambda t, pav=pav, tt=tt, n=nc2, sl=sl, kc=kc: t.matmul(
                        pav[:, 0:512],
                        prb[kc % 2][:, 128 * tt:128 * (tt + 1)],
                        vb[sl][:, 512 * n:512 * (n + 1)],
                        start=(kc == 0), stop=(kc == 63)),
                        inc="t_av")
            P.op("tensor", lambda t, kc=kc: t.matmul(
                T.psm[0:1, 0:256], T.ones128[:], prb[kc % 2][:],
                start=(kc == 0), stop=(kc == 63)), inc="t_dn")
            P.wait("scalar", "t_sc")
            P.wait("scalar", "v_bnd")
            P.wait("scalar", "t_av", P.cnt("t_av") - 4)
            P.wait("scalar", "t_dn", P.cnt("t_dn") - 1)
            P.op("scalar", lambda sc, pb=pb, kc=kc: sc.activation(
                prb[kc % 2][:], pb[:, 0:256], AF.Exp, bias=T.boundt, scale=SCALE),
                inc="s_ex")
        P.wait("vector", "t_dn")
        P.op("vector", lambda v: v.tensor_copy(T.scratch[0:1, 768:1024], T.psm[0:1, 0:256]),
             inc="v_dn")
        P.wait("tensor", "v_dn")
        for tt in range(2):
            P.op("tensor", lambda t, tt=tt: t.transpose(
                T.psm[:, 256 + 128 * tt:256 + 128 * tt + 1],
                T.scratch[0:1, 768 + 128 * tt:768 + 128 * (tt + 1)], T.ident_f[0:1, 0:1]),
                inc="t_dnt" if tt == 1 else None)
        P.wait("vector", "t_dnt")
        for tt in range(2):
            P.op("vector", lambda v, tt=tt: v.reciprocal(
                T.small[:, 8 + tt:9 + tt], T.psm[:, 256 + 128 * tt:256 + 128 * tt + 1]))
        P.drain("vector")
        P.wait("vector", "t_av")
        for tt in range(2):
            jt = 2 * qg + tt
            for nc2 in range(2):
                pav = T.ps[2 + 2 * tt + nc2]
                P.op("vector", lambda v, pav=pav, tt=tt, jt=jt, n=nc2: v.scalar_tensor_tensor(
                    T.scratch[:, 0:512], pav[:, 0:512], T.small[:, 8 + tt:9 + tt],
                    T.ar1b[:, 1024 * jt + 512 * n:1024 * jt + 512 * (n + 1)],
                    OP.mult, OP.mult))
                P.drain("vector")
                P.op("vector", lambda v, jt=jt, n=nc2: v.tensor_tensor(
                    T.x_res[:, 1024 * jt + 512 * n:1024 * jt + 512 * (n + 1)],
                    T.x_res[:, 1024 * jt + 512 * n:1024 * jt + 512 * (n + 1)],
                    T.scratch[:, 0:512], OP.add),
                    inc="v_nm" if (tt, nc2) == (1, 1) else None)
                P.drain("vector")



def alloc_l2(nc, T):
    T.l2_win = nc.dram_tensor("l2_win", [D, 4096], F32, kind="ExternalInput").ap()
    T.l2_cw = nc.dram_tensor("l2_cw", [128, 16], F32, kind="ExternalInput").ap()
    T.l2_cb = nc.dram_tensor("l2_cb", [128, 4], F32, kind="ExternalInput").ap()
    T.l2_alog = nc.dram_tensor("l2_alog", [128, 64], F32, kind="ExternalInput").ap()
    T.l2_dp = nc.dram_tensor("l2_dp", [128, 4], F32, kind="ExternalInput").ap()
    T.l2_wout = nc.dram_tensor("l2_wout", [512, D], F32, kind="ExternalInput").ap()
    T.l2_wg = nc.dram_tensor("l2_wg", [D, 3072], I32, kind="ExternalInput").ap()
    T.l2_wu = nc.dram_tensor("l2_wu", [D, 3072], I32, kind="ExternalInput").ap()
    T.l2_wd = nc.dram_tensor("l2_wd", [3072, D], I32, kind="ExternalInput").ap()
    T.l2_uzin = nc.dram_tensor("l2_uzin", [4096, TLOC], BF16).ap()
    T.l2_uzout = nc.dram_tensor("l2_uzout", [4 * 4096, TLOC], BF16).ap()
    T.l2_rsin = nc.dram_tensor("l2_rsin", [S, D], F32).ap()
    T.l2_rsout = nc.dram_tensor("l2_rsout", [TLOC, D], F32).ap()


def emit_l2_xz(P, T):
    slot_free = [("v_setup", 0), ("v_setup", 0)]
    wslot_free = [("v_setup", 0), ("v_setup", 0)]
    grp = 0
    P.wait("tensor", "v_xn")
    for chcol in range(32):
        ws = chcol % 2
        P.wait("gpsimd", *wslot_free[ws])
        P.dma("gpsimd", T.ar2[:, 2048 * ws:2048 * ws + 1024],
              mk_ap(T.l2_win, 128 * chcol,
                    [[4096, 128], [128 * 4096, 8], [1, 128]]), f"d_xz{ws}")
        P.wait("tensor", f"d_xz{ws}")
        for tg in range(4):
            bank = T.ps[grp % 2]
            P.wait("tensor", "v_xz", P.cnt("v_xz") - 1)
            for c in range(NDC):
                P.op("tensor", lambda t, b=bank, c=c, ws=ws, tg=tg: t.matmul(
                    b[:, 0:512],
                    T.ar2[:, 2048 * ws + 128 * c:2048 * ws + 128 * (c + 1)],
                    T.xn_fm[:, TLOC * c + 512 * tg:TLOC * c + 512 * (tg + 1)],
                    start=(c == 0), stop=(c == NDC - 1)),
                    inc="t_xz" if c == NDC - 1 else None)
            slot = grp % 2
            stg = T.kvstage[:, 512 * slot:512 * (slot + 1)]
            P.wait("vector", "t_xz")
            P.wait("vector", *slot_free[slot])
            P.op("vector", lambda v, b=bank, stg=stg: v.tensor_copy(stg, b[:, 0:512]),
                 inc="v_xz")
            P.wait("sync", "v_xz")
            nd = P.dma("sync", T.l2_uzin[128 * chcol:128 * (chcol + 1),
                                         512 * tg:512 * (tg + 1)], stg, f"d_kst{slot}")
            slot_free[slot] = (f"d_kst{slot}", nd)
            grp += 1
        wslot_free[ws] = ("t_xz", P.cnt("t_xz"))
    P.wait("gpsimd", "d_kst0")
    P.wait("gpsimd", "d_kst1")
    if SKIP_CC:
        P.op("gpsimd", lambda g: g.nop(), inc="cc")
    else:
        P.op("gpsimd", lambda g: g.collective_compute(
            "AllGather", OP.bypass, replica_groups=GROUPS,
            ins=[T.l2_uzin.opt()], outs=[T.l2_uzout.opt()]), inc="cc")


def emit_l2_ffn(P, T):
    wbase = 12288
    wslot_free = [("v_setup", 0), ("v_setup", 0)]
    dslot_free = [("v_setup", 0), ("v_setup", 0)]
    hs_free = [("v_setup", 0), ("v_setup", 0)]
    P.wait("gpsimd", "t_xz", P.cnt("t_xz"))
    P.wait("scalar", "d_kst0")
    P.wait("scalar", "d_kst1")
    for tg in range(8):
        P.wait("gpsimd", "t_do", P.cnt("t_do"))
        for hk in range(24):
            ws = hk % 2
            P.wait("gpsimd", *wslot_free[ws])
            P.dma("gpsimd", mk_ap(T.ar2[:, 0:1], wbase + 2048 * ws,
                                  [[16384, 128], [128, 8], [1, 128]]),
                  mk_ap(T.l2_wg, 128 * hk, [[3072, 128], [128 * 3072, 8], [1, 128]]),
                  f"d_wg{ws}")
            P.dma("gpsimd", mk_ap(T.ar2[:, 0:1], wbase + 2048 * ws + 1024,
                                  [[16384, 128], [128, 8], [1, 128]]),
                  mk_ap(T.l2_wu, 128 * hk, [[3072, 128], [128 * 3072, 8], [1, 128]]),
                  f"d_wg{ws}")
            gb = T.ps[hk % 2]
            ub = T.ps[2 + hk % 2]
            P.wait("tensor", f"d_wg{ws}")
            P.wait("tensor", "v_h", P.cnt("v_h") - 1)
            for c in range(NDC):
                P.op("tensor", lambda t, gb=gb, c=c, ws=ws, tg=tg: t.matmul(
                    gb[:, 0:256],
                    T.ar2[:, wbase + 2048 * ws + 128 * c:wbase + 2048 * ws + 128 * (c + 1)],
                    T.xn_fm[:, TLOC * c + 256 * tg:TLOC * c + 256 * (tg + 1)],
                    start=(c == 0), stop=(c == NDC - 1)), inc=None)
            for c in range(NDC):
                P.op("tensor", lambda t, ub=ub, c=c, ws=ws, tg=tg: t.matmul(
                    ub[:, 0:256],
                    T.ar2[:, wbase + 2048 * ws + 1024 + 128 * c:
                          wbase + 2048 * ws + 1024 + 128 * (c + 1)],
                    T.xn_fm[:, TLOC * c + 256 * tg:TLOC * c + 256 * (tg + 1)],
                    start=(c == 0), stop=(c == NDC - 1)),
                    inc="t_gu" if c == NDC - 1 else None)
            hsl = hk % 2
            gsb = T.kvstage[:, 256 * hsl:256 * (hsl + 1)]
            P.wait("scalar", "t_gu")
            P.wait("scalar", *hs_free[hsl])
            P.op("scalar", lambda sc, gb=gb, gsb=gsb: sc.activation(
                gsb, gb[:, 0:256], AF.Silu, scale=1.0 / 127.0), inc="s_gu")
            P.wait("vector", "s_gu")
            nh = P.op("vector", lambda v, ub=ub, gsb=gsb, hk=hk: v.scalar_tensor_tensor(
                T.ar2[:, 256 * hk:256 * (hk + 1)], ub[:, 0:256], 1.0 / 127.0, gsb,
                OP.mult, OP.mult), inc="v_h")
            hs_free[hsl] = ("v_h", nh)
            wslot_free[ws] = ("t_gu", P.cnt("t_gu"))
        for nc2 in range(2):
            P.wait("gpsimd", "t_gu", P.cnt("t_gu"))
            for hk in range(24):
                ds = hk % 2
                P.wait("gpsimd", *dslot_free[ds])
                P.dma("gpsimd", T.ar2[:, wbase + 2048 * ds:wbase + 2048 * ds + 512],
                      T.l2_wd[128 * hk:128 * (hk + 1), 512 * nc2:512 * (nc2 + 1)],
                      f"d_wd{ds}")
                P.wait("tensor", f"d_wd{ds}")
                P.wait("tensor", "v_h", P.cnt("v_h"))
                if hk == 0:
                    P.wait("tensor", "s_go", P.cnt("s_go"))
                for tt in range(2):
                    P.op("tensor", lambda t, tt=tt, hk=hk, ds=ds: t.matmul(
                        T.ps[4 + tt][:, 0:512],
                        T.ar2[:, 256 * hk + 128 * tt:256 * hk + 128 * (tt + 1)],
                        T.ar2[:, wbase + 2048 * ds:wbase + 2048 * ds + 512],
                        start=(hk == 0), stop=(hk == 23)),
                        inc="t_do" if tt == 1 else None)
                dslot_free[ds] = ("t_do", P.cnt("t_do"))
            P.wait("scalar", "t_do")
            for tt in range(2):
                jt = 2 * tg + tt
                P.op("scalar", lambda sc, tt=tt, jt=jt, n=nc2: sc.activation(
                    T.ar1b[:, 1024 * jt + 512 * n:1024 * jt + 512 * (n + 1)],
                    T.ps[4 + tt][:, 0:512], AF.Silu, scale=1.0 / 127.0), inc="s_go")


def emit_l2_scan(P, T):
    P.wait("sync", "v_xn", P.cnt("v_xn"))
    P.wait("sync", "s_ln", P.cnt("s_ln"))
    P.wait("sync", "v_gate", P.cnt("v_gate"))
    P.wait("sync", "v_cmb", P.cnt("v_cmb"))
    P.wait("sync", "t_b2t", P.cnt("t_b2t"))
    P.dma("sync", T.w_all[:, 16:32], T.l2_cw, "d_l2p")
    P.dma("sync", T.w_all[:, 32:36], T.l2_cb, "d_l2p")
    P.dma("sync", T.w_all[:, 36:40], T.l2_dp, "d_l2p")
    P.dma("sync", T.stats[:, 0:64], T.l2_alog, "d_l2p")
    P.wait("scalar", "d_l2p")
    P.wait("scalar", "s_go", P.cnt("s_go"))
    P.drain("scalar")
    P.op("scalar", lambda sc: sc.activation(T.stats[:, 0:64], T.stats[:, 0:64], AF.Exp),
         inc="s_ae")
    P.wait("gpsimd", "t_do", P.cnt("t_do"))
    P.wait("gpsimd", "v_h", P.cnt("v_h"))
    for ct in range(4):
        P.dma("gpsimd", T.ar2[:, 1024 * ct:1024 * (ct + 1)],
              T.l2_wout[128 * ct:128 * (ct + 1), :], "d_wo")
    u_sb = T.xn_fm[:, 0:4104]
    ut_sb = T.xn_fm[:, 4104:8200]
    z_sb = T.xn_fm[:, 8200:12296]
    accs = T.ar1[:, 8192:12288]
    yac = T.ar1[:, 12288:16384]
    carry = T.w_all[:, 64:80]
    sign_b = mk_ap(T.sign_sb[:], 0, [[512, 128], [0, 8], [1, 512]])
    P.wait("gpsimd", "cc")
    P.wait("gpsimd", "t_gu", P.cnt("t_gu"))
    P.wait("vector", "s_go", P.cnt("s_go"))
    stage_free = [("v_setup", 0), ("v_setup", 0)]
    for ct in range(4):
        for th in range(2):
            P.wait("vector", "d_u", P.cnt("d_u"))
            P.drain("vector")
            if th == 0:
                P.op("vector", lambda v: v.memset(u_sb[:, 0:4], 0.0), inc="v_up")
            else:
                P.op("vector", lambda v: v.tensor_copy(u_sb[:, 0:4], u_sb[:, 4096:4100]),
                     inc="v_up")
            P.drain("vector")
            P.wait("gpsimd", "v_up")
            P.wait("gpsimd", "v_cv", P.cnt("v_cv"))
            P.wait("gpsimd", "t_wo", P.cnt("t_wo"))
            h = P.sem("d_u")

            def uz_load(g, ct=ct, th=th, h=h, u_sb=u_sb, z_sb=z_sb, T=T):
                for arm in g.switch_core_id(n=128):
                    r = (arm.logical if hasattr(arm, "logical") else int(arm)) % 4
                    for k, rr in enumerate((2 * th, 2 * th + 1)):
                        g.dma_start(
                            out=u_sb[:, 4 + 2048 * k:4 + 2048 * (k + 1)],
                            in_=T.l2_uzout[4096 * rr + 512 * r + 128 * ct:
                                           4096 * rr + 512 * r + 128 * (ct + 1), :]
                        ).then_inc(h, 16)
                        g.dma_start(
                            out=z_sb[:, 2048 * k:2048 * (k + 1)],
                            in_=T.l2_uzout[4096 * rr + 2048 + 512 * r + 128 * ct:
                                           4096 * rr + 2048 + 512 * r + 128 * (ct + 1), :]
                        ).then_inc(h, 16)
            P.items["gpsimd"].append(uz_load)
            P.counts[SEM_ALIAS.get("d_u", "d_u")] = P.cnt("d_u") + 64
            P.wait("vector", "d_l2p")
            P.wait("vector", "d_u")
            P.wait("vector", "v_ya", P.cnt("v_ya"))
            P.drain("vector")
            P.op("vector", lambda v, ct=ct: v.tensor_scalar(
                accs[:, 0:4096], u_sb[:, 1:4097], T.w_all[:, 16 + 4 * ct:17 + 4 * ct],
                None, OP.mult))
            P.drain("vector")
            for jj in range(1, 4):
                P.op("vector", lambda v, jj=jj, ct=ct: v.scalar_tensor_tensor(
                    accs[:, 0:4096], u_sb[:, 1 + jj:4097 + jj],
                    T.w_all[:, 16 + 4 * ct + jj:17 + 4 * ct + jj],
                    accs[:, 0:4096], OP.mult, OP.add))
                P.drain("vector")
            P.op("vector", lambda v, ct=ct: v.scalar_tensor_tensor(
                mk_ap(ut_sb, 0, [[16384, 128], [512, 8], [1, 512]]),
                mk_ap(accs, 0, [[16384, 128], [512, 8], [1, 512]]),
                T.w_all[:, 32 + ct:33 + ct], sign_b, OP.add, OP.mult), inc="v_cv")
            P.drain("vector")
            P.wait("vector", "s_ae")
            for s_ in range(16):
                a_bc = T.stats[:, 16 * ct + s_:16 * ct + s_ + 1].broadcast_to([128, 4096])
                if th == 0:
                    P.op("vector", lambda v, a_bc=a_bc: v.tensor_tensor_scan(
                        accs[:, 0:4096], a_bc, ut_sb[:, 0:4096], 0.0,
                        OP.mult, OP.add))
                else:
                    P.op("vector", lambda v, a_bc=a_bc, s_=s_: v.tensor_tensor_scan(
                        accs[:, 0:4096], a_bc, ut_sb[:, 0:4096], carry[:, s_:s_ + 1],
                        OP.mult, OP.add))
                P.drain("vector")
                P.op("vector", lambda v, s_=s_: v.tensor_copy(
                    carry[:, s_:s_ + 1], accs[:, 4095:4096]))
                P.drain("vector")
                if s_ == 0:
                    P.op("vector", lambda v: v.tensor_copy(yac[:, 0:4096], accs[:, 0:4096]))
                else:
                    P.op("vector", lambda v: v.tensor_tensor(
                        yac[:, 0:4096], yac[:, 0:4096], accs[:, 0:4096], OP.add))
                P.drain("vector")
            P.op("vector", lambda v: v.tensor_copy(T.small[:, 11:12], T.small[:, 11:12])
                 if False else v.tensor_copy(T.small[:, 11:12], carry[:, 15:16]),
                 inc="v_ya")
            for cc5 in range(8):
                zc = z_sb[:, 512 * cc5:512 * (cc5 + 1)]
                P.wait("scalar", "v_ya", P.cnt("v_ya"))
                P.wait("scalar", "v_ys", P.cnt("v_ys"))
                P.wait("scalar", "d_rs0", P.cnt("d_rs0"))
                P.wait("scalar", "d_rs1", P.cnt("d_rs1"))
                P.op("scalar", lambda sc, zc=zc: sc.activation(
                    T.scratch[:, 0:512], zc, AF.Silu), inc="s_sz")
                P.wait("vector", "s_sz")
                P.drain("vector")
                P.op("vector", lambda v: v.tensor_tensor(
                    T.scratch[:, 512:1024], T.scratch[:, 0:512], T.sign_sb[:], OP.mult))
                P.drain("vector")
                P.op("vector", lambda v, cc5=cc5, ct=ct: v.scalar_tensor_tensor(
                    T.scratch[:, 0:512], ut_sb[:, 512 * cc5:512 * (cc5 + 1)],
                    T.w_all[:, 36 + ct:37 + ct], yac[:, 512 * cc5:512 * (cc5 + 1)],
                    OP.mult, OP.add))
                P.drain("vector")
                P.op("vector", lambda v, zc=zc: v.tensor_tensor(
                    zc, T.scratch[:, 0:512], T.scratch[:, 512:1024], OP.mult),
                    inc="v_ys")
                P.drain("vector")
            for tt in range(32):
                P.wait("tensor", "v_ys", P.cnt("v_ys"))
                P.wait("tensor", "d_wo")
                for nc2 in range(2):
                    bank = T.ps[(tt * 2 + nc2) % 2]
                    P.wait("tensor", "v_woe", P.cnt("v_woe") - 1)
                    P.op("tensor", lambda t, b=bank, tt=tt, n=nc2, ct=ct: t.matmul(
                        b[:, 0:512],
                        z_sb[:, 128 * tt:128 * (tt + 1)],
                        T.ar2[:, 1024 * ct + 512 * n:1024 * ct + 512 * (n + 1)],
                        start=True, stop=True), inc="t_wo")
                    sl = (tt * 2 + nc2) % 2
                    stg = T.scratch[:, 512 * sl:512 * (sl + 1)]
                    P.wait("vector", "t_wo")
                    P.wait("vector", *stage_free[sl])
                    P.op("vector", lambda v, b=bank, stg=stg: v.tensor_copy(stg, b[:, 0:512]),
                         inc="v_woe")
                    P.wait("gpsimd", "v_woe")
                    tG = th * 4096 + 128 * tt
                    nd = P.dma("gpsimd", T.l2_rsin[tG:tG + 128, 512 * nc2:512 * (nc2 + 1)],
                               stg, f"d_rs{sl}",
                               **({} if ct == 0 else {"accum_op": OP.add}))
                    stage_free[sl] = (f"d_rs{sl}", nd)


def emit_l2_rs(P, T):
    P.wait("gpsimd", "d_rs0")
    P.wait("gpsimd", "d_rs1")
    if SKIP_CC:
        P.op("gpsimd", lambda g: g.nop(), inc="cc")
    else:
        P.op("gpsimd", lambda g: g.collective_compute(
            "ReduceScatter", OP.add, replica_groups=GROUPS,
            ins=[T.l2_rsin.opt()], outs=[T.l2_rsout.opt()]), inc="cc")
    P.wait("sync", "cc")
    for j in range(NTT):
        P.wait("sync", "v_fin", P.cnt("v_fin"))
        P.dma("sync", T.scratch[:], T.l2_rsout[128 * j:128 * (j + 1), :], "d_s")
        P.wait("vector", "d_s")
        P.wait("vector", "s_go", P.cnt("s_go"))
        P.drain("vector")
        P.op("vector", lambda v, j=j: v.tensor_tensor(
            T.scratch[:], T.scratch[:], T.ar1b[:, 1024 * j:1024 * (j + 1)], OP.mult))
        P.drain("vector")
        P.op("vector", lambda v, j=j: v.tensor_tensor(
            T.x_res[:, 1024 * j:1024 * (j + 1)], T.x_res[:, 1024 * j:1024 * (j + 1)],
            T.scratch[:], OP.add), inc="v_fin")
        P.drain("vector")



def alloc_l3(nc, T):
    for nm, shp in [("l3_wdq", [D, 512]), ("l3_wuq", [512, D]), ("l3_wqr", [512, D]),
                    ("l3_wkr", [D, D]), ("l3_wdkv", [D, 512]), ("l3_wuk", [512, D]),
                    ("l3_wuv", [512, D]), ("l3_wo", [D, D]), ("l3_gw", [D, 8]),
                    ("l3_ew", [8, D, D]), ("l3_eb", [8, D]), ("l3_cos", [TLOC, D]),
                    ("l3_sin", [TLOC, D]), ("l3_bdq", [128, 4]), ("l3_buq", [128, 8]),
                    ("l3_bdkv", [128, 4]), ("l3_buk", [128, 8]),
                    ("l3_bqr", [1, D]), ("l3_bkr", [1, D]), ("l3_buv", [1, D]),
                    ("l3_bo", [1, D]), ("l3_gb", [1, 8])]:
        setattr(T, nm, nc.dram_tensor(nm, shp, F32, kind="ExternalInput").ap())
    T.l3_kin = nc.dram_tensor("l3_kin", [1025, TLOC], BF16).ap()
    T.l3_kout = nc.dram_tensor("l3_kout", [4 * 1025, TLOC], BF16).ap()
    T.l3_vin = nc.dram_tensor("l3_vin", [TLOC, D], BF16).ap()
    T.l3_vout = nc.dram_tensor("l3_vout", [4 * TLOC, D], BF16).ap()


def emit_l3_ct(P, T, wdq_ap, bias_dram, wsem, vsem):
    P.wait("gpsimd", "t_kv", P.cnt("t_kv"))
    P.wait("gpsimd", "t_kr", P.cnt("t_kr"))
    P.wait("gpsimd", "t_up", P.cnt("t_up"))
    for c in range(NDC):
        P.dma("gpsimd", T.ar2[:, 12288 + 512 * c:12288 + 512 * (c + 1)],
              wdq_ap[128 * c:128 * (c + 1), :], wsem)
    P.dma("sync", T.bkp[:, 0:4], bias_dram, "d_bp3")
    P.wait("tensor", wsem)
    P.wait("tensor", "v_xn")
    for s_ in ("v_ckv", "v_ctq", "v_kc", "v_qc", "v_kv", "v_rpe"):
        P.wait("tensor", s_, P.cnt(s_))
    grp = 0
    for ck in range(4):
        for tg in range(4):
            bank = T.ps[grp % 2]
            P.wait("tensor", vsem, P.cnt(vsem) - 1)
            for c in range(NDC):
                P.op("tensor", lambda t, b=bank, c=c, ck=ck, tg=tg: t.matmul(
                    b[:, 0:512],
                    T.ar2[:, 12288 + 512 * c + 128 * ck:12288 + 512 * c + 128 * (ck + 1)],
                    T.xn_fm[:, TLOC * c + 512 * tg:TLOC * c + 512 * (tg + 1)],
                    start=(c == 0), stop=(c == NDC - 1)),
                    inc="t_ct" if c == NDC - 1 else None)
            P.wait("vector", "t_ct")
            P.wait("vector", "d_bp3")
            P.op("vector", lambda v, b=bank, ck=ck, tg=tg: v.tensor_scalar(
                T.ar2[:, 2048 * ck + 512 * tg:2048 * ck + 512 * (tg + 1)],
                b[:, 0:512], T.bkp[:, ck:ck + 1], None, OP.add), inc=vsem)
            grp += 1


def emit_l3_up_pair(P, T, wu_ap, bup_dram, dst_base, vsem):
    P.wait("gpsimd", "t_ct", P.cnt("t_ct"))
    for c in range(4):
        P.dma("gpsimd", T.ar2[:, 12288 + 1024 * c:12288 + 1024 * (c + 1)],
              wu_ap[128 * c:128 * (c + 1), :], "d_wu3")
    P.dma("sync", T.bqp[:], bup_dram, "d_bp3b")
    P.wait("tensor", "d_wu3")
    for s_ in ("v_ckv", "v_ctq", "v_kc", "v_qc", "v_kv", "v_rpe"):
        P.wait("tensor", s_, P.cnt(s_))
    grp = 0
    for g in range(8):
        for tg in range(4):
            bank = T.ps[grp % 2]
            P.wait("tensor", vsem, P.cnt(vsem) - 1)
            for c in range(4):
                P.op("tensor", lambda t, b=bank, c=c, g=g, tg=tg: t.matmul(
                    b[:, 0:512],
                    T.ar2[:, 12288 + 1024 * c + 128 * g:12288 + 1024 * c + 128 * (g + 1)],
                    T.ar2[:, 2048 * c + 512 * tg:2048 * c + 512 * (tg + 1)],
                    start=(c == 0), stop=(c == 3)),
                    inc="t_up" if c == 3 else None)
            P.wait("vector", "t_up")
            P.wait("vector", "d_bp3b")
            P.op("vector", lambda v, b=bank, g=g, tg=tg, dst_base=dst_base: v.tensor_scalar(
                T.ar1b[:, dst_base + 2048 * g + 512 * tg:dst_base + 2048 * g + 512 * (tg + 1)],
                b[:, 0:512], T.bqp[:, g:g + 1], None, OP.add), inc=vsem)
            grp += 1


def emit_l3_rope_add(P, T, wr_ap, br_ap, rhs_is_ct, dst_base):
    ropef = T.ar2[:, 8192:12288].bitcast(F32)
    nk = 8 if not rhs_is_ct else 4
    for nc2 in range(2):
        P.wait("gpsimd", "t_kr", P.cnt("t_kr"))
        P.wait("gpsimd", "t_up", P.cnt("t_up"))
        P.wait("gpsimd", "t_ct", P.cnt("t_ct"))
        for c in range(nk):
            P.dma("gpsimd", T.ar2[:, 12288 + 512 * c:12288 + 512 * (c + 1)],
                  wr_ap[128 * c:128 * (c + 1), 512 * nc2:512 * (nc2 + 1)], "d_wr3")
        P.dma("gpsimd", T.nstage[0:1, 0:1024], br_ap, "d_wr3")
        P.wait("tensor", "d_wr3")
        for j in range(NTT):
            P.wait("gpsimd", "v_rp", P.cnt("v_rp"))
            P.dma("gpsimd", ropef[:, 512 * nc2:512 * (nc2 + 1)],
                  T.l3_cos[128 * j:128 * (j + 1), 512 * nc2:512 * (nc2 + 1)], "d_rt")
            P.dma("gpsimd", ropef[:, 1024 + 512 * nc2:1024 + 512 * (nc2 + 1)],
                  T.l3_sin[128 * j:128 * (j + 1), 512 * nc2:512 * (nc2 + 1)], "d_rt")
            bank = T.ps[2 + j % 2]
            P.wait("tensor", "v_rp", P.cnt("v_rp") - 1)
            for c in range(nk):
                rhs_src = (T.xn_fm[:, TLOC * c + 128 * j:TLOC * c + 128 * (j + 1)]
                           if not rhs_is_ct else
                           T.ar2[:, 2048 * c + 128 * j:2048 * c + 128 * (j + 1)])
                P.op("tensor", lambda t, b=bank, s=rhs_src, c=c: t.matmul(
                    b[:, 0:512], s, T.ar2[:, 12288 + 512 * c:12288 + 512 * (c + 1)],
                    start=(c == 0), stop=False), inc=None)
            P.op("tensor", lambda t, b=bank, n=nc2: t.matmul(
                b[:, 0:512], T.ones1[:], T.nstage[0:1, 512 * n:512 * (n + 1)],
                start=False, stop=True), inc="t_kr")
            c2 = ropef[:, 512 * nc2:512 * (nc2 + 1)]
            s2 = ropef[:, 1024 + 512 * nc2:1024 + 512 * (nc2 + 1)]
            P.wait("vector", "t_kr")
            P.wait("vector", "d_rt")
            P.drain("vector")
            P.op("vector", lambda v, b=bank, c2=c2: v.tensor_tensor(
                T.scratch[:, 0:512], b[:, 0:512], c2, OP.mult))
            P.drain("vector")
            P.op("vector", lambda v, b=bank: v.tensor_copy(
                mk_ap(T.scratch[:], 512, [[1024, 128], [2, 256]]),
                mk_ap(b[:, 0:1], 1, [[512, 128], [2, 256]])))
            P.op("vector", lambda v, b=bank: v.tensor_copy(
                mk_ap(T.scratch[:], 513, [[1024, 128], [2, 256]]),
                mk_ap(b[:, 0:1], 0, [[512, 128], [2, 256]])))
            P.drain("vector")
            P.op("vector", lambda v, s2=s2: v.tensor_tensor(
                T.scratch[:, 512:1024], T.scratch[:, 512:1024], s2, OP.mult))
            P.drain("vector")
            P.op("vector", lambda v: v.tensor_tensor(
                T.kvstage[:, 0:512], T.scratch[:, 0:512], T.scratch[:, 512:1024], OP.add),
                inc="v_rp")
            P.wait("tensor", "v_rp")
            P.wait("tensor", "v_rpe", P.cnt("v_rpe"))
            for c4 in range(4):
                P.op("tensor", lambda t, c4=c4: t.transpose(
                    T.pst[:, 128 * c4:128 * (c4 + 1)],
                    T.kvstage[:, 128 * c4:128 * (c4 + 1)], T.ident_b[:]),
                    inc="t_rpt" if c4 == 3 else None)
            P.wait("vector", "t_rpt")
            for c4 in range(4):
                gp = nc2 * 4 + c4
                dst = T.ar1b[:, dst_base + 2048 * gp + 128 * j:
                             dst_base + 2048 * gp + 128 * (j + 1)]
                P.op("vector", lambda v, dst=dst, c4=c4: v.tensor_tensor(
                    dst, dst, T.pst[:, 128 * c4:128 * (c4 + 1)], OP.add),
                    inc="v_rpe" if c4 == 3 else None)
            P.wait("tensor", "v_rpe")


def emit_l3_kv(P, T):
    emit_l3_ct(P, T, T.l3_wdkv, T.l3_bdkv, "d_wd3", "v_ckv")
    emit_l3_up_pair(P, T, T.l3_wuk, T.l3_buk, 16384, "v_kc")
    emit_l3_rope_add(P, T, T.l3_wkr, T.l3_bkr, False, 16384)
    slot_free = [("v_setup", 0), ("v_setup", 0)]
    first_nrm = True
    sgrp = 0
    P.wait("vector", "v_rpe", P.cnt("v_rpe"))
    for g in range(8):
        for tg in range(4):
            src = T.ar1b[:, 16384 + 2048 * g + 512 * tg:16384 + 2048 * g + 512 * (tg + 1)]
            slot = sgrp % 2
            P.wait("scalar", "v_rpe", P.cnt("v_rpe"))
            P.wait("scalar", "t_nrm", P.cnt("t_nrm"))
            P.op("scalar", lambda sc, src=src: sc.activation(
                T.kvstage[:, 1024:1536], src, AF.Square), inc="s_kv")
            P.wait("tensor", "s_kv")
            P.wait("tensor", "v_nrm", P.cnt("v_nrm"))
            P.op("tensor", lambda t: t.matmul(
                T.psm[0:1, 0:512], T.ones128[:], T.kvstage[:, 1024:1536],
                start=True, stop=True), inc="t_nrm")
            P.wait("vector", "t_nrm")
            P.op("vector", lambda v: v.reduce_max(T.small[0:1, 3:4], T.psm[0:1, 0:512],
                                                  axis=AX))
            P.drain("vector")
            if first_nrm:
                P.op("vector", lambda v: v.tensor_copy(T.small[0:1, 4:5], T.small[0:1, 3:4]),
                     inc="v_nrm")
                first_nrm = False
            else:
                P.op("vector", lambda v: v.tensor_tensor(
                    T.small[0:1, 4:5], T.small[0:1, 4:5], T.small[0:1, 3:4], OP.max),
                     inc="v_nrm")
            P.drain("vector")
            P.wait("sync", *slot_free[slot])
            P.wait("sync", "v_rpe")
            nd = P.dma("sync", T.l3_kin[128 * g:128 * (g + 1), 512 * tg:512 * (tg + 1)],
                       src, f"d_kst{slot}")
            slot_free[slot] = (f"d_kst{slot}", nd)
            sgrp += 1
    P.wait("vector", "v_nrm")
    P.op("vector", lambda v: v.tensor_copy(T.nstage[0:1, 0:1], T.small[0:1, 4:5]),
         inc="v_kmax")
    P.wait("sync", "v_kmax")
    P.dma("sync", mk_ap(T.l3_kin, 1024 * TLOC, [[1, 1], [1, 1]]), T.nstage[0:1, 0:1],
          "d_nst")
    P.wait("gpsimd", "t_up", P.cnt("t_up"))
    P.wait("gpsimd", "t_kr", P.cnt("t_kr"))
    for c in range(4):
        P.dma("gpsimd", T.ar2[:, 12288 + 1024 * c:12288 + 1024 * (c + 1)],
              T.l3_wuv[128 * c:128 * (c + 1), :], "d_wv3")
    P.wait("gpsimd", "d_nst", P.cnt("d_nst"))
    P.dma("gpsimd", T.nstage[0:1, 0:1024], T.l3_buv, "d_wv3")
    P.wait("tensor", "d_wv3")
    for j in range(NTT):
        for nc2 in range(2):
            bank = T.ps[(j * 2 + nc2) % 2]
            P.wait("tensor", "v_kv", P.cnt("v_kv") - 1)
            for c in range(4):
                P.op("tensor", lambda t, b=bank, c=c, j=j, n=nc2: t.matmul(
                    b[:, 0:512],
                    T.ar2[:, 2048 * c + 128 * j:2048 * c + 128 * (j + 1)],
                    T.ar2[:, 12288 + 1024 * c + 512 * n:12288 + 1024 * c + 512 * (n + 1)],
                    start=(c == 0), stop=False), inc=None)
            P.op("tensor", lambda t, b=bank, n=nc2: t.matmul(
                b[:, 0:512], T.ones1[:], T.nstage[0:1, 512 * n:512 * (n + 1)],
                start=False, stop=True), inc="t_kv")
            slot = (j * 2 + nc2) % 2
            stg = T.kvstage[:, 512 * slot:512 * (slot + 1)]
            P.wait("vector", "t_kv")
            P.wait("vector", *slot_free[slot])
            P.op("vector", lambda v, b=bank, stg=stg: v.tensor_copy(stg, b[:, 0:512]),
                 inc="v_kv")
            P.wait("sync", "v_kv")
            nd = P.dma("sync", T.l3_vin[128 * j:128 * (j + 1), 512 * nc2:512 * (nc2 + 1)],
                       stg, f"d_kst{slot}")
            slot_free[slot] = (f"d_kst{slot}", nd)
    P.wait("gpsimd", "d_kst0")
    P.wait("gpsimd", "d_kst1")
    P.wait("gpsimd", "d_nst")
    if SKIP_CC:
        P.op("gpsimd", lambda g: g.nop(), inc="cc")
    else:
        P.op("gpsimd", lambda g: g.collective_compute(
            "AllGather", OP.bypass, replica_groups=GROUPS,
            ins=[T.l3_kin.opt()], outs=[T.l3_kout.opt()]), inc="cc")
    if SKIP_CC:
        P.op("gpsimd", lambda g: g.nop(), inc="cc")
    else:
        P.op("gpsimd", lambda g: g.collective_compute(
            "AllGather", OP.bypass, replica_groups=GROUPS,
            ins=[T.l3_vin.opt()], outs=[T.l3_vout.opt()]), inc="cc")


def emit_l3_ffn(P, T):
    for c in range(NDC):
        P.dma("sync", T.gw_sb[:, 8 * c:8 * (c + 1)], T.l3_gw[128 * c:128 * (c + 1), :],
              "d_gw3")
    P.dma("sync", T.stats[0:1, 56:64], T.l3_gb, "d_gw3")
    P.wait("tensor", "d_gw3")
    P.wait("tensor", "v_xn")
    P.wait("vector", "d_x")
    for j in range(NTT):
        m = T.stats[:, j:j + 1]
        rs = T.stats[:, 16 + j:16 + j + 1]
        P.wait("vector", "s_ln", P.cnt("s_ln"))
        P.wait("vector", "t_gxp", P.cnt("t_gxp"))
        P.drain("vector")
        P.op("vector", lambda v, j=j, m=m, rs=rs: v.tensor_scalar(
            T.scratch[:], T.x_res[:, 1024 * j:1024 * (j + 1)], m, rs,
            OP.subtract, OP.mult), inc="v_gxn")
        P.wait("tensor", "v_gxn")
        P.wait("tensor", "v_kv", P.cnt("v_kv"))
        P.wait("tensor", "v_ckv", P.cnt("v_ckv"))
        for c in range(NDC):
            P.op("tensor", lambda t, c=c: t.transpose(
                T.ps[c // 4][:, 128 * (c % 4):128 * (c % 4) + 128],
                T.scratch[:, 128 * c:128 * (c + 1)], T.ident_f[:]),
                inc="t_gxp" if c == NDC - 1 else None)
        P.wait("vector", "t_gxp")
        for c in range(NDC):
            P.op("vector", lambda v, c=c: v.tensor_copy(
                T.atmp[:, 128 * c:128 * (c + 1)],
                T.ps[c // 4][:, 128 * (c % 4):128 * (c % 4) + 128]),
                inc="v_gxt" if c == NDC - 1 else None)
        P.wait("tensor", "v_gate", P.cnt("v_gate"))
        P.wait("tensor", "v_gxt")
        for c in range(NDC):
            P.op("tensor", lambda t, c=c, j=j: t.matmul(
                T.psm[:, 0:8],
                T.atmp[:, 128 * c:128 * (c + 1)],
                T.gw_sb[:, 8 * c:8 * (c + 1)],
                start=(c == 0), stop=False), inc=None)
        P.op("tensor", lambda t: t.matmul(
            T.psm[:, 0:8], T.ones1f[:], T.stats[0:1, 56:64],
            start=False, stop=True), inc="t_gate")
        w = T.w_all[:, 14 * j:14 * j + 8]
        P.wait("vector", "t_gate")
        P.drain("vector")
        P.op("vector", lambda v: v.reduce_max(T.small[:, 0:1], T.psm[:, 0:8], axis=AX))
        P.drain("vector")
        P.op("vector", lambda v, w=w: v.tensor_scalar(
            w, T.psm[:, 0:8], T.small[:, 0:1], None, OP.is_ge), inc="v_gate")
        P.drain("vector")
    for j in range(NTT):
        P.wait("tensor", "v_gate", P.cnt("v_gate"))
        P.wait("tensor", "v_b2t", P.cnt("v_b2t"))
        P.wait("tensor", "v_b2p", P.cnt("v_b2p"))
        P.op("tensor", lambda t, j=j: t.transpose(
            T.psm[0:8, 0:128], T.w_all[:, 14 * j:14 * j + 8], T.ident_f[:]), inc="t_b2t")
        P.wait("vector", "t_b2t")
        P.op("vector", lambda v: v.tensor_copy(T.wT16[0:8, 0:128], T.psm[0:8, 0:128]),
             inc="v_b2t")
        for nc2 in range(2):
            P.wait("gpsimd", "t_b2p", P.cnt("t_b2p"))
            P.dma("gpsimd", T.b14[0:8, 0:512], T.l3_eb[0:8, 512 * nc2:512 * (nc2 + 1)],
                  "d_b2")
            P.wait("tensor", "v_b2t")
            P.wait("tensor", "d_b2")
            P.wait("tensor", "v_b2p", P.cnt("v_b2p"))
            P.op("tensor", lambda t: t.matmul(
                T.psm[:, 0:512], T.wT16[0:8, 0:128], T.b14[0:8, 0:512],
                start=True, stop=True), inc="t_b2p")
            P.wait("vector", "t_b2p")
            P.op("vector", lambda v, j=j, n=nc2: v.tensor_copy(
                T.ar1[:, 1024 * j + 512 * n:1024 * j + 512 * (n + 1)], T.psm[:, 0:512]),
                 inc="v_b2p")
    eslot_free = [("v_setup", 0), ("v_setup", 0)]
    for e in range(8):
        es_ = e % 2
        P.wait("gpsimd", *eslot_free[es_])
        P.wait("gpsimd", "t_rpt", P.cnt("t_rpt"))
        for c in range(NDC):
            P.dma("gpsimd", T.ar2[:, 8192 * es_ + 1024 * c:8192 * es_ + 1024 * (c + 1)],
                  mk_ap(T.l3_ew, e * D * D + 128 * c * D, [[1024, 128], [1, 1024]]),
                  f"d_we{es_}")
        P.wait("tensor", f"d_we{es_}")
        last = None
        for tt in range(16):
            for nc2 in range(2):
                yb = T.ps[2 + (tt * 2 + nc2) % 4]
                P.wait("tensor", "v_cmb", P.cnt("v_cmb") - 3)
                for c in range(NDC):
                    P.op("tensor", lambda t, yb=yb, c=c, tt=tt, n=nc2, es_=es_: t.matmul(
                        yb[:, 0:512],
                        T.xn_fm[:, TLOC * c + 128 * tt:TLOC * c + 128 * (tt + 1)],
                        T.ar2[:, 8192 * es_ + 1024 * c + 512 * n:
                              8192 * es_ + 1024 * c + 512 * (n + 1)],
                        start=(c == 0), stop=(c == NDC - 1)),
                        inc="t_y" if c == NDC - 1 else None)
                dst = T.ar1[:, 1024 * tt + 512 * nc2:1024 * tt + 512 * (nc2 + 1)]
                P.wait("vector", "t_y")
                P.wait("vector", "v_b2p", P.cnt("v_b2p"))
                P.op("vector", lambda v, yb=yb, dst=dst, tt=tt, e=e: v.scalar_tensor_tensor(
                    dst, yb[:, 0:512], T.w_all[:, 14 * tt + e:14 * tt + e + 1], dst,
                    OP.mult, OP.add), inc="v_cmb")
                last = P.cnt("t_y")
        eslot_free[es_] = ("t_y", last)
    for j in range(NTT):
        P.wait("scalar", "v_cmb", P.cnt("v_cmb"))
        P.drain("scalar")
        P.op("scalar", lambda sc, j=j: sc.activation(
            T.ar1b[:, 1024 * j:1024 * (j + 1)], T.ar1[:, 1024 * j:1024 * (j + 1)],
            AF.Silu), inc="s_g")


def emit_l3_q(P, T):
    P.wait("gpsimd", "t_y", P.cnt("t_y"))
    P.wait("gpsimd", "s_g", P.cnt("s_g"))
    emit_l3_ct(P, T, T.l3_wdq, T.l3_bdq, "d_wq3", "v_ctq")
    emit_l3_up_pair(P, T, T.l3_wuq, T.l3_buq, 16384, "v_qc")
    emit_l3_rope_add(P, T, T.l3_wqr, T.l3_bqr, True, 16384)
    SCALE = float(1.0 / np.sqrt(1040.0))
    first_qn = True
    P.wait("scalar", "v_rpe", P.cnt("v_rpe"))
    for g in range(8):
        for tg in range(4):
            src = T.ar1b[:, 16384 + 2048 * g + 512 * tg:16384 + 2048 * g + 512 * (tg + 1)]
            P.wait("scalar", "t_qn", P.cnt("t_qn"))
            P.op("scalar", lambda sc, src=src: sc.activation(
                T.kvstage[:, 1024:1536], src, AF.Square), inc="s_qn")
            P.wait("tensor", "s_qn")
            P.wait("tensor", "v_qn", P.cnt("v_qn"))
            P.op("tensor", lambda t: t.matmul(
                T.psm[0:1, 0:512], T.ones128[:], T.kvstage[:, 1024:1536],
                start=True, stop=True), inc="t_qn")
            P.wait("vector", "t_qn")
            P.op("vector", lambda v: v.reduce_max(T.small[0:1, 5:6], T.psm[0:1, 0:512],
                                                  axis=AX))
            P.drain("vector")
            if first_qn:
                P.op("vector", lambda v: v.tensor_copy(T.small[0:1, 6:7], T.small[0:1, 5:6]),
                     inc="v_qn")
                first_qn = False
            else:
                P.op("vector", lambda v: v.tensor_tensor(
                    T.small[0:1, 6:7], T.small[0:1, 6:7], T.small[0:1, 5:6], OP.max),
                     inc="v_qn")
            P.drain("vector")
    P.wait("gpsimd", "cc")
    P.wait("gpsimd", "v_gate", P.cnt("v_gate"))
    for r in range(4):
        P.dma("gpsimd", T.w_all[0:1, 218 + r:219 + r],
              mk_ap(T.l3_kout, (1025 * r + 1024) * TLOC, [[1, 1], [1, 1]]), "d_km")
    P.wait("vector", "d_km")
    P.drain("vector")
    P.op("vector", lambda v: v.reduce_max(T.small[0:1, 7:8], T.w_all[0:1, 218:222],
                                          axis=AX))
    P.drain("vector")
    P.wait("vector", "v_qn")
    P.op("vector", lambda v: v.tensor_tensor(
        T.small[0:1, 8:9], T.small[0:1, 6:7], T.small[0:1, 7:8], OP.mult), inc="v_bnd")
    P.wait("scalar", "v_bnd")
    P.op("scalar", lambda sc: sc.activation(
        T.small[0:1, 9:10], T.small[0:1, 8:9], AF.Sqrt, scale=SCALE * SCALE * 1.05),
        inc="s_bnd")
    P.wait("vector", "s_bnd")
    P.op("vector", lambda v: v.tensor_scalar(
        T.small[0:1, 10:11], T.small[0:1, 9:10], -1.0, -1.0, OP.mult, OP.add), inc="v_bnd")
    P.wait("tensor", "v_bnd")
    P.wait("tensor", "v_qn")
    P.op("tensor", lambda t: t.matmul(
        T.psm[:, 0:1], T.ones1f[:], T.small[0:1, 10:11], start=True, stop=True),
        inc="t_bnd")
    P.wait("vector", "t_bnd")
    P.op("vector", lambda v: v.tensor_copy(T.boundt, T.psm[:, 0:1]), inc="v_bnd")


def emit_l3_attn(P, T):
    SCALE = float(1.0 / np.sqrt(1040.0))
    P.wait("vector", "t_kr", P.cnt("t_kr"))
    P.wait("vector", "t_y", P.cnt("t_y"))
    P.op("vector", lambda v: v.memset(
        mk_ap(T.v65, 64, [[T.xn_fm.shape[1], 128], [130, 16], [65, 2]]), 1.0), inc="v_v65i")
    P.wait("gpsimd", "t_qn", P.cnt("t_qn"))
    P.wait("sync", "t_qn", P.cnt("t_qn"))
    for hp in range(8):
        P.wait("sync", "t_sc", P.cnt("t_sc"))
        for r in range(4):
            P.dma("sync", T.kbuf[:, 2048 * r:2048 * (r + 1)],
                  T.l3_kout[1025 * r + 128 * hp:1025 * r + 128 * (hp + 1), :], "d_ka")
        vblk_copy = {}
        for qg in range(4):
            for b in range(8):
                P.wait("gpsimd", "v_v65", P.cnt("v_v65"))
                P.dma("gpsimd", T.vplain[:],
                      mk_ap(T.l3_vout, (2048 * (b // 2) + 1024 * (b % 2)) * D + 128 * hp,
                            [[D, 128], [128 * D, 8], [1, 128]]), "d_va")
                P.wait("vector", "d_va")
                P.wait("vector", "t_av", P.cnt("t_av") - 16)
                P.op("vector", lambda v, b=b: v.tensor_copy(
                    mk_ap(T.v65, 1040 * (b % 2),
                          [[T.xn_fm.shape[1], 128], [130, 8], [65, 2], [1, 64]]),
                    mk_ap(T.vplain, 0, [[T.xn_fm.shape[1], 128], [128, 8], [64, 2], [1, 64]])),
                    inc="v_v65")
                vblk_copy[b] = P.cnt("v_v65")
                for kc in range(8 * b, 8 * b + 8):
                    pb0 = T.ps[kc % 2]
                    pb1 = T.ps[2 + kc % 2]
                    P.wait("tensor", "d_ka")
                    P.wait("tensor", "s_ex", P.cnt("s_ex") - 2)
                    for h, pb in ((0, pb0), (1, pb1)):
                        P.op("tensor", lambda t, pb=pb, h=h, kc=kc, hp=hp, qg=qg: t.matmul(
                            pb[:, 0:512],
                            T.kbuf[64 * h:64 * h + 64, 128 * kc:128 * (kc + 1)],
                            T.ar1b[64 * h:64 * h + 64,
                                   16384 + 2048 * hp + 512 * qg:
                                   16384 + 2048 * hp + 512 * (qg + 1)],
                            start=True, stop=True),
                            inc="t_sc" if h == 1 else None)
                    P.wait("tensor", "s_ex", P.cnt("s_ex") + 2)
                    P.wait("tensor", "v_v65i")
                    P.wait("tensor", "v_v65", vblk_copy[b])
                    if kc == 0:
                        P.wait("tensor", "v_av", P.cnt("v_av"))
                    for h in (0, 1):
                        P.op("tensor", lambda t, h=h, kc=kc, b=b: t.matmul(
                            T.ps[4 + h][0:65, 0:512],
                            T.v65[:, 1040 * (b % 2) + 130 * (kc % 8) + 65 * h:
                                  1040 * (b % 2) + 130 * (kc % 8) + 65 * h + 65],
                            T.probs[:, 512 * (2 * h + kc % 2):512 * (2 * h + kc % 2 + 1)],
                            start=(kc == 0), stop=(kc == 63)),
                            inc="t_av")
                    P.wait("scalar", "t_sc")
                    P.wait("scalar", "v_bnd")
                    for h, pb in ((0, pb0), (1, pb1)):
                        P.wait("scalar", "t_av", P.cnt("t_av") - 5 + h)
                        P.op("scalar", lambda sc, pb=pb, h=h, kc=kc: sc.activation(
                            T.probs[:, 512 * (2 * h + kc % 2):512 * (2 * h + kc % 2 + 1)],
                            pb[:, 0:512], AF.Exp, bias=T.boundt, scale=SCALE),
                            inc="s_ex")
            for h in (0, 1):
                pav = T.ps[4 + h]
                dst = T.ar1b[64 * h:64 * (h + 1),
                             16384 + 2048 * hp + 512 * qg:16384 + 2048 * hp + 512 * (qg + 1)]
                P.wait("vector", "t_av")
                P.drain("vector")
                P.op("vector", lambda v, pav=pav: v.reciprocal(
                    T.scratch[0:1, 0:512], pav[64:65, 0:512]), inc="v_rc")
                P.wait("tensor", "v_rc")
                P.wait("tensor", "v_av", P.cnt("v_av"))
                P.op("tensor", lambda t: t.matmul(
                    T.psm[:, 0:512], T.ones1f[:], T.scratch[0:1, 0:512],
                    start=True, stop=True), inc="t_rcb")
                P.wait("vector", "t_rcb")
                P.op("vector", lambda v, pav=pav, dst=dst: v.tensor_copy(
                    dst, pav[0:64, 0:512]))
                P.drain("vector")
                P.op("vector", lambda v, dst=dst: v.tensor_tensor(
                    dst, dst, T.psm[0:64, 0:512], OP.mult), inc="v_av")
                P.drain("vector")


def emit_l3_wo(P, T):
    P.wait("gpsimd", "t_av", P.cnt("t_av"))
    for c in range(NDC):
        P.dma("gpsimd", T.ar2[:, 1024 * c:1024 * (c + 1)],
              T.l3_wo[128 * c:128 * (c + 1), :], "d_wo3")
    P.dma("gpsimd", T.nstage[0:1, 0:1024], T.l3_bo, "d_wo3")
    P.wait("tensor", "d_wo3")
    P.wait("tensor", "v_av", P.cnt("v_av"))
    for tt in range(16):
        for nc2 in range(2):
            bank = T.ps[(tt * 2 + nc2) % 2]
            P.wait("tensor", "v_fin", P.cnt("v_fin") - 1)
            for c in range(NDC):
                P.op("tensor", lambda t, b=bank, c=c, tt=tt, n=nc2: t.matmul(
                    b[:, 0:512],
                    T.ar1b[:, 16384 + 2048 * c + 128 * tt:16384 + 2048 * c + 128 * (tt + 1)],
                    T.ar2[:, 1024 * c + 512 * n:1024 * c + 512 * (n + 1)],
                    start=(c == 0), stop=False), inc=None)
            P.op("tensor", lambda t, b=bank, n=nc2: t.matmul(
                b[:, 0:512], T.ones1[:], T.nstage[0:1, 512 * n:512 * (n + 1)],
                start=False, stop=True), inc="t_wo3")
            P.wait("vector", "t_wo3")
            P.wait("vector", "s_g", P.cnt("s_g"))
            P.drain("vector")
            P.op("vector", lambda v, b=bank, tt=tt, n=nc2: v.tensor_tensor(
                T.scratch[:, 0:512], b[:, 0:512],
                T.ar1b[:, 1024 * tt + 512 * n:1024 * tt + 512 * (n + 1)], OP.mult))
            P.drain("vector")
            P.op("vector", lambda v, tt=tt, n=nc2: v.tensor_tensor(
                T.x_res[:, 1024 * tt + 512 * n:1024 * tt + 512 * (n + 1)],
                T.x_res[:, 1024 * tt + 512 * n:1024 * tt + 512 * (n + 1)],
                T.scratch[:, 0:512], OP.add), inc="v_fin")
            P.drain("vector")


# version 05 (final)
# speedup vs baseline: 1.2217x; 1.2116x over previous
import numpy as np
from contextlib import ExitStack

import concourse.bass as bass
import concourse.mybir as mybir
from concourse.alu_op_type import AluOpType
from concourse.bass_utils import run_bass_kernel_spmd

F32 = mybir.dt.float32
BF16 = mybir.dt.bfloat16
I32 = mybir.dt.int32
AF = mybir.ActivationFunctionType
OP = AluOpType
AX = mybir.AxisListType.X

NCORES = 8
GROUPS = [[0, 1, 2, 3], [4, 5, 6, 7]]
B, S, D = 2, 8192, 1024
TLOC = S // 4
NTT = TLOC // 128
NDC = D // 128
EPS = 1e-5

DEBUG = []
SKIP_CC = False


SEM_ALIAS = {
    "t_xz": "t_mix", "t_gu": "t_n1", "t_do": "t_n1b", "t_wo": "t_n2",
    "v_xz": "v_n1e", "v_h": "v_x2", "v_ya": "v_n2e", "v_ys": "v_f1",
    "v_up": "v_gxn", "v_cv": "v_gxt", "v_woe": "v_rxt",
    "s_gu": "s_f1", "s_go": "s_x2", "s_ae": "s_f1", "s_sz": "s_x2",
    "t_ct": "t_mix", "t_up": "t_n1", "t_kr": "t_n1b", "t_rpt": "t_n2",
    "t_rcb": "t_rxt", "t_wo3": "t_rxt",
    "v_ckv": "v_n1e", "v_ctq": "v_x2", "v_kc": "v_n2e", "v_qc": "v_f1",
    "v_rp": "v_gxn", "v_rpe": "v_gxt", "v_rc": "v_f1",
    "t_dn": "t_gate", "t_dnt": "t_b2t", "v_dn": "v_b2t", "v_nm2": "v_b2p",
    "d_gw3": "d_gw", "d_l2p": "d_gw",
    "d_bp3": "d_bp", "d_bp3b": "d_bq",
    "d_s": "d_b1",
    "d_wkv1": "d_wkv", "d_wd3": "d_wkv",
    "d_wq1": "d_wq", "d_wu3": "d_wq",
    "d_wr3": "d_mixw",
    "d_wv3": "d_xz0", "d_wo3": "d_xz1",
    "d_wq3": "d_wg0", "d_rt": "d_wg1",
}




class Prog:
    ENGINES = ("tensor", "vector", "scalar", "gpsimd", "sync")

    def __init__(self, nc, es):
        self.nc = nc
        self.es = es
        self.items = {e: [] for e in self.ENGINES}
        self.sems = {}
        self.counts = {}
        self.watermark = {}

    def sem(self, name):
        name = SEM_ALIAS.get(name, name)
        if name not in self.sems:
            self.sems[name] = self.es.enter_context(self.nc.semaphore(name))
            self.counts[name] = 0
        return self.sems[name]

    def op(self, eng, fn, inc=None, inc_n=1):
        total = None
        if inc is not None:
            inc = SEM_ALIAS.get(inc, inc)
            self.sem(inc)
            self.counts[inc] += inc_n
            total = self.counts[inc]
            h = self.sems[inc]

            def run(e, fn=fn, h=h, n=inc_n):
                fn(e).then_inc(h, n)
        else:
            def run(e, fn=fn):
                fn(e)
        self.items[eng].append(run)
        return total

    def wait(self, eng, sem_name, value=None):
        sem_name = SEM_ALIAS.get(sem_name, sem_name)
        v = self.counts.get(sem_name, 0) if value is None else value
        if v <= 0:
            return
        key = (eng, sem_name)
        if self.watermark.get(key, 0) >= v:
            return
        self.watermark[key] = v
        h = self.sem(sem_name)
        self.items[eng].append(lambda e, h=h, v=v: e.wait_ge(h, v))

    def cnt(self, name):
        return self.counts.get(SEM_ALIAS.get(name, name), 0)

    def drain(self, eng):
        self.items[eng].append(lambda e: e.drain())

    def dma(self, queue, out, in_, sem_name, **kw):
        return self.op(queue, lambda e: e.dma_start(out=out, in_=in_, **kw),
                       inc=sem_name, inc_n=16)

    def build(self):
        with self.nc.Block() as block:
            for eng in self.ENGINES:
                items = self.items[eng]
                if not items:
                    continue

                def make(items):
                    def body(e):
                        for it in items:
                            it(e)
                    return body
                getattr(block, eng)(make(items))


def mk_ap(base, extra_off, dims):
    return bass.AP(tensor=base.tensor, offset=base.offset + extra_off, ap=dims)



def _pack_cols(v, p=128):
    v = np.ascontiguousarray(v, dtype=np.float32)
    return np.ascontiguousarray(v.reshape(-1, p).T)


def _sign_tile():
    s = np.ones((128, 512), np.float32)
    s[:, 1::2] = -1.0
    return s


def _rope_tables(t0, n, dmodel=1024):
    base = 10000.0
    half = dmodel // 2
    pos = np.arange(t0, t0 + n, dtype=np.float64)[:, None]
    ids = np.arange(half, dtype=np.float32)
    theta = ((np.float32(base) ** (-2.0 * ids)).astype(np.float32) / np.float32(half))
    r = (pos * theta[None, :].astype(np.float64)).astype(np.float32)
    cos, sin = np.cos(r), np.sin(r)
    c2 = np.empty((n, dmodel), np.float32)
    s2 = np.empty((n, dmodel), np.float32)
    c2[:, 0::2] = cos
    c2[:, 1::2] = cos
    s2[:, 0::2] = -sin
    s2[:, 1::2] = sin
    return c2, s2


def prep_core_inputs(core, inputs, layers=(0, 1, 2, 3)):
    b = core // 4
    r = core % 4
    t0 = r * TLOC
    g = lambda k: np.asarray(inputs[k], dtype=np.float32)
    d = {}
    d["x_in"] = np.ascontiguousarray(g("L0_attn_x")[b, t0:t0 + TLOC, :])
    d["c_ident"] = np.eye(128, dtype=np.float32)
    d["c_sign"] = _sign_tile()
    if 0 in layers:
        d["l0_wq"] = np.ascontiguousarray(g("L0_attn_W_Q").transpose(1, 0, 2).reshape(D, D))
        d["l0_wk"] = np.ascontiguousarray(g("L0_attn_W_K").transpose(1, 0, 2).reshape(D, D))
        d["l0_wv"] = np.ascontiguousarray(g("L0_attn_W_V").transpose(1, 0, 2).reshape(D, D))
        d["l0_gw"] = np.ascontiguousarray(g("L0_ffn_gate_w").T)
        d["l0_w1"] = np.ascontiguousarray(g("L0_ffn_w1"))
        d["l0_b1"] = np.ascontiguousarray(
            g("L0_ffn_b1").reshape(14, 16, 128).transpose(2, 0, 1).reshape(128, 224))
        d["l0_w2"] = np.ascontiguousarray(g("L0_ffn_w2"))
        d["l0_b2"] = np.ascontiguousarray(g("L0_ffn_b2"))
    if 1 in layers:
        d["l1_wqT"] = np.ascontiguousarray(g("L1_attn_wq").T)
        d["l1_wkT"] = np.ascontiguousarray(g("L1_attn_wk").T)
        d["l1_wvT"] = np.ascontiguousarray(g("L1_attn_wv").T)
        d["l1_bq"] = _pack_cols(g("L1_attn_bq"))
        d["l1_bk"] = _pack_cols(g("L1_attn_bk"))
        d["l1_bv"] = np.ascontiguousarray(g("L1_attn_bv").reshape(1, D))
        d["l1_g1"] = _pack_cols(g("L1_ffn_g1"))
        d["l1_be1"] = _pack_cols(g("L1_ffn_be1"))
        d["l1_g2"] = _pack_cols(g("L1_ffn_g2"))
        d["l1_be2"] = _pack_cols(g("L1_ffn_be2"))
        d["l1_mixw"] = np.ascontiguousarray(g("L1_ffn_mix_w"))
        d["l1_mixb"] = np.ascontiguousarray(g("L1_ffn_mix_b").reshape(1, D))
        d["l1_rw"] = np.ascontiguousarray(g("L1_ffn_router_w"))
        d["l1_w1"] = np.ascontiguousarray(g("L1_ffn_w1"))
        d["l1_b1"] = np.ascontiguousarray(
            g("L1_ffn_b1").reshape(8, 32, 128).transpose(2, 0, 1).reshape(128, 256))
        d["l1_w2"] = np.ascontiguousarray(g("L1_ffn_w2"))
        d["l1_b2"] = np.ascontiguousarray(g("L1_ffn_b2"))
    if 2 in layers:
        chs = slice(512 * r, 512 * (r + 1))
        d["l2_win"] = np.ascontiguousarray(g("L2_attn_W_in"))
        cw = g("L2_attn_conv_w")[:, 0, :]
        d["l2_cw"] = np.ascontiguousarray(
            cw[chs].reshape(4, 128, 4).transpose(1, 0, 2).reshape(128, 16))
        d["l2_cb"] = _pack_cols(g("L2_attn_conv_b")[chs])
        al = g("L2_attn_A_log")[:, chs]
        d["l2_alog"] = np.ascontiguousarray(
            al.T.reshape(4, 128, 16).transpose(1, 0, 2).reshape(128, 64))
        d["l2_dp"] = _pack_cols(g("L2_attn_D_param")[chs])
        d["l2_wout"] = np.ascontiguousarray(g("L2_attn_W_out")[chs])
        d["l2_wg"] = np.ascontiguousarray(np.asarray(inputs["L2_ffn_w_gate"]).T.astype(np.int32))
        d["l2_wu"] = np.ascontiguousarray(np.asarray(inputs["L2_ffn_w_up"]).T.astype(np.int32))
        d["l2_wd"] = np.ascontiguousarray(np.asarray(inputs["L2_ffn_w_down"]).T).astype(np.int32)
    if 3 in layers:
        d["l3_wdq"] = np.ascontiguousarray(g("L3_attn_Wd_q_w"))
        d["l3_bdq"] = _pack_cols(g("L3_attn_Wd_q_b"))
        d["l3_wuq"] = np.ascontiguousarray(g("L3_attn_Wu_q_w"))
        d["l3_buq"] = _pack_cols(g("L3_attn_Wu_q_b"))
        d["l3_wqr"] = np.ascontiguousarray(g("L3_attn_Wq_r_w"))
        d["l3_bqr"] = np.ascontiguousarray(g("L3_attn_Wq_r_b").reshape(1, D))
        d["l3_wkr"] = np.ascontiguousarray(g("L3_attn_Wk_r_w"))
        d["l3_bkr"] = np.ascontiguousarray(g("L3_attn_Wk_r_b").reshape(1, D))
        d["l3_wdkv"] = np.ascontiguousarray(g("L3_attn_Wd_kv_w"))
        d["l3_bdkv"] = _pack_cols(g("L3_attn_Wd_kv_b"))
        d["l3_wuk"] = np.ascontiguousarray(g("L3_attn_Wu_k_w"))
        d["l3_buk"] = _pack_cols(g("L3_attn_Wu_k_b"))
        d["l3_wuv"] = np.ascontiguousarray(g("L3_attn_Wu_v_w"))
        d["l3_buv"] = np.ascontiguousarray(g("L3_attn_Wu_v_b").reshape(1, D))
        d["l3_wo"] = np.ascontiguousarray(g("L3_attn_Wo_w"))
        d["l3_bo"] = np.ascontiguousarray(g("L3_attn_Wo_b").reshape(1, D))
        d["l3_gw"] = np.ascontiguousarray(g("L3_ffn_gate_w"))
        d["l3_gb"] = np.ascontiguousarray(g("L3_ffn_gate_b").reshape(1, 8))
        d["l3_ew"] = np.ascontiguousarray(g("L3_ffn_expert_w"))
        d["l3_eb"] = np.ascontiguousarray(g("L3_ffn_expert_b"))
        c2, s2 = _rope_tables(t0, TLOC)
        d["l3_cos"] = c2
        d["l3_sin"] = s2
    return d




class Tens:
    pass


def alloc_common(nc, es, layers):
    T = Tens()
    T.x_in = nc.dram_tensor("x_in", [TLOC, D], F32, kind="ExternalInput").ap()
    T.x_out = nc.dram_tensor("x_out", [TLOC, D], F32, kind="ExternalOutput").ap()
    T.c_ident = nc.dram_tensor("c_ident", [128, 128], F32, kind="ExternalInput").ap()
    T.c_sign = nc.dram_tensor("c_sign", [128, 512], F32, kind="ExternalInput").ap()

    sb = lambda nm, shp, dt: es.enter_context(nc.sbuf_tensor(nm, shp, dt))
    T.x_res = sb("x_res", [128, NTT * D], F32)
    T.xn_fm = sb("xn_fm", [128, NDC * TLOC], BF16)
    T.ar1 = sb("ar1", [128, NTT * D], F32)
    T.ar1b = T.ar1[:].bitcast(BF16)
    T.ar2 = sb("ar2", [128, 16384], BF16)
    T.scratch = sb("scratch", [128, 1024], F32)
    T.kvstage = sb("kvstage", [128, 1536], BF16)
    T.b1_sb = sb("b1_sb", [128, 256], F32)
    T.b14 = sb("b14", [14, 512], BF16)
    T.ident_b = sb("ident_b", [128, 128], BF16)
    T.ident_f = sb("ident_f", [128, 128], F32)
    T.sign_sb = sb("sign_sb", [128, 512], BF16) if 2 in layers else None
    T.stats = sb("stats", [128, 64], F32)
    T.small = sb("small", [128, 16], F32)
    T.w_all = sb("w_all", [128, 224], F32)
    T.wT16 = sb("wT16", [16, 128], BF16)
    T.ones1 = sb("ones1", [1, 128], BF16)
    T.ones1f = sb("ones1f", [1, 128], F32)
    T.ones128 = sb("ones128", [128, 1], BF16)
    T.gw_sb = sb("gw_sb", [128, 112], F32)

    T.ps = [es.enter_context(nc.psum_tensor(f"ps{i}", [128, 512], F32)) for i in range(6)]
    T.pst = es.enter_context(nc.psum_tensor("pst", [128, 1024], BF16))
    T.psm = es.enter_context(nc.psum_tensor("psm", [128, 512], F32))

    T.g_of = lambda jt, c0, n: T.ar1b[:, 1024 * jt + c0: 1024 * jt + c0 + n]
    T.kbuf = T.xn_fm[:, 0:8192]
    T.v65 = T.xn_fm[:, 8192:10272]
    T.vplain = T.xn_fm[:, 10272:11296]
    T.probs = T.xn_fm[:, 11296:13344]
    T.qw = T.ar1b[:, 16384:24576]
    T.tmp64 = T.ar1[:, 12288:12352]
    T.avr = T.ar1[:, 12352:12864]
    T.atmp = T.ar1[:, 12864:13888]
    T.nstage = sb("nstage", [1, 1024], BF16)
    T.boundt = T.small[:, 15:16]
    T.epst = T.small[:, 14:15]
    T.bkp = sb("bkp", [128, 8], F32)
    T.bqp = sb("bqp", [128, 8], F32)
    T.g1p = sb("g1p", [128, 8], F32)
    T.be1p = sb("be1p", [128, 8], F32)
    T.g2p = sb("g2p", [128, 8], F32)
    T.be2p = sb("be2p", [128, 8], F32)
    return T


def alloc_l0(nc, T):
    T.l0_wq = nc.dram_tensor("l0_wq", [D, D], F32, kind="ExternalInput").ap()
    T.l0_wk = nc.dram_tensor("l0_wk", [D, D], F32, kind="ExternalInput").ap()
    T.l0_wv = nc.dram_tensor("l0_wv", [D, D], F32, kind="ExternalInput").ap()
    T.l0_gw = nc.dram_tensor("l0_gw", [D, 14], F32, kind="ExternalInput").ap()
    T.l0_w1 = nc.dram_tensor("l0_w1", [14, D, 2048], F32, kind="ExternalInput").ap()
    T.l0_b1 = nc.dram_tensor("l0_b1", [128, 224], F32, kind="ExternalInput").ap()
    T.l0_w2 = nc.dram_tensor("l0_w2", [14, 2048, D], F32, kind="ExternalInput").ap()
    T.l0_b2 = nc.dram_tensor("l0_b2", [14, D], F32, kind="ExternalInput").ap()
    T.l0_kin = nc.dram_tensor("l0_kin", [1025, TLOC], BF16).ap()
    T.l0_kout = nc.dram_tensor("l0_kout", [4 * 1025, TLOC], BF16).ap()
    T.l0_vin = nc.dram_tensor("l0_vin", [TLOC, D], BF16).ap()
    T.l0_vout = nc.dram_tensor("l0_vout", [4 * TLOC, D], BF16).ap()



def emit_setup(P, T):
    for j in range(NTT):
        P.dma("sync", T.x_res[:, D * j:D * (j + 1)], T.x_in[128 * j:128 * (j + 1), :], "d_x")
    P.dma("sync", T.ident_f[:], T.c_ident, "d_x")
    P.dma("gpsimd", T.ident_b[:], T.c_ident, "d_g")
    if T.sign_sb is not None:
        P.dma("gpsimd", T.sign_sb[:], T.c_sign, "d_g")
    P.wait("vector", "d_x")
    P.wait("vector", "d_g")
    P.op("vector", lambda v: v.memset(T.ones1[:], 1.0))
    P.op("vector", lambda v: v.memset(T.ones1f[:], 1.0))
    P.op("vector", lambda v: v.memset(T.ones128[:], 1.0))
    P.op("vector", lambda v: v.memset(T.epst, EPS))
    P.op("vector", lambda v: v.memset(T.avr[:], 0.0), inc="v_setup")


def emit_ln(P, T, tag=""):
    P.wait("vector", "d_x")
    P.wait("scalar", "d_x")
    P.wait("tensor", "v_setup")
    P.wait("tensor", "d_g")
    for j in range(NTT):
        xt = T.x_res[:, D * j:D * (j + 1)]
        m = T.stats[:, j:j + 1]
        rs = T.stats[:, 16 + j:16 + j + 1]
        sums = T.small[:, 0:1]
        ssq = T.small[:, 1:2]
        P.op("vector", lambda v, xt=xt, s=sums: v.reduce_sum(s, xt, axis=AX))
        P.drain("vector")
        P.op("vector", lambda v, m=m, s=sums: v.tensor_scalar(m, s, 1.0 / D, None, OP.mult))
        P.drain("vector")
        P.op("vector", lambda v, xt=xt, m=m: v.tensor_scalar(
            T.scratch[:], xt, m, None, OP.subtract), inc="v_ln")
        P.wait("scalar", "v_ln")
        P.op("scalar", lambda sc, ssq=ssq: sc.activation(
            T.ar1[:, 0:1024], T.scratch[:], AF.Square, accum_out=ssq))
        P.drain("scalar")
        P.op("scalar", lambda sc, ssq=ssq: sc.activation(
            T.small[:, 2:3], ssq, AF.Ln, bias=T.epst, scale=1.0 / D))
        P.drain("scalar")
        P.op("scalar", lambda sc, rs=rs: sc.activation(
            rs, T.small[:, 2:3], AF.Exp, scale=-0.5), inc="s_ln")
        P.wait("vector", "s_ln")
        P.op("vector", lambda v, rs=rs: v.tensor_scalar(
            T.kvstage[:, 0:1024], T.scratch[:], rs, None, OP.mult), inc="v_ln")
        P.wait("tensor", "v_ln")
        for c in range(NDC):
            P.op("tensor", lambda t, c=c: t.transpose(
                T.pst[:, 128 * c:128 * (c + 1)],
                T.kvstage[:, 128 * c:128 * (c + 1)], T.ident_b[:]),
                inc="t_ln" if c == NDC - 1 else None)
        P.wait("vector", "t_ln")
        for c in range(NDC):
            P.op("vector", lambda v, c=c, j=j: v.tensor_copy(
                T.xn_fm[:, TLOC * c + 128 * j:TLOC * c + 128 * (j + 1)],
                T.pst[:, 128 * c:128 * (c + 1)]),
                inc="v_xn" if c == NDC - 1 else None)
        P.wait("tensor", "v_xn")


def load_w_chunks(P, T, dst, src, n_k, cols, sem):
    for c in range(n_k):
        P.dma("gpsimd", dst[:, cols * c:cols * (c + 1)], src[128 * c:128 * (c + 1), :], sem)


def emit_l0_kv(P, T):
    load_w_chunks(P, T, T.qw, T.l0_wk, 8, 1024, "d_wkv")
    P.wait("tensor", "d_wkv")
    P.wait("tensor", "v_xn")
    slot_free = [("v_setup", 0), ("v_setup", 0)]
    first_nrm = True
    grp = 0
    for g in range(8):
        for tg in range(4):
            bank = T.ps[grp % 2]
            if grp >= 2:
                P.wait("tensor", "v_kv", grp - 1)
            for c in range(NDC):
                P.op("tensor", lambda t, b=bank, c=c, g=g, tg=tg: t.matmul(
                    b[:, 0:512],
                    T.qw[:, 1024 * c + 128 * g:1024 * c + 128 * g + 128],
                    T.xn_fm[:, TLOC * c + 512 * tg:TLOC * c + 512 * (tg + 1)],
                    start=(c == 0), stop=(c == NDC - 1)),
                    inc="t_kv" if c == NDC - 1 else None)
            slot = grp % 2
            stg = T.kvstage[:, 512 * slot:512 * (slot + 1)]
            P.wait("vector", "t_kv")
            P.wait("vector", *slot_free[slot])
            P.op("vector", lambda v, b=bank, stg=stg: v.tensor_copy(stg, b[:, 0:512]),
                 inc="v_kv")
            P.wait("scalar", "v_kv")
            P.wait("scalar", "t_nrm", P.cnt("t_nrm"))
            P.op("scalar", lambda sc, stg=stg: sc.activation(
                T.kvstage[:, 1024:1536], stg, AF.Square), inc="s_kv")
            P.wait("tensor", "s_kv")
            P.wait("tensor", "v_nrm", P.cnt("v_nrm"))
            P.op("tensor", lambda t: t.matmul(
                T.psm[0:1, 0:512], T.ones128[:], T.kvstage[:, 1024:1536],
                start=True, stop=True), inc="t_nrm")
            P.wait("vector", "t_nrm")
            P.op("vector", lambda v: v.reduce_max(T.small[0:1, 3:4], T.psm[0:1, 0:512], axis=AX))
            P.drain("vector")
            if first_nrm:
                P.op("vector", lambda v: v.tensor_copy(T.small[0:1, 4:5], T.small[0:1, 3:4]),
                     inc="v_nrm")
                first_nrm = False
            else:
                P.op("vector", lambda v: v.tensor_tensor(
                    T.small[0:1, 4:5], T.small[0:1, 4:5], T.small[0:1, 3:4], OP.max),
                     inc="v_nrm")
            P.drain("vector")
            P.wait("sync", "v_kv")
            n = P.dma("sync", T.l0_kin[128 * g:128 * (g + 1), 512 * tg:512 * (tg + 1)], stg,
                      f"d_kst{slot}")
            slot_free[slot] = (f"d_kst{slot}", n)
            grp += 1
    P.wait("vector", "v_nrm")
    P.op("vector", lambda v: v.tensor_copy(T.nstage[0:1, 0:1], T.small[0:1, 4:5]), inc="v_kmax")
    P.wait("sync", "v_kmax")
    P.dma("sync", mk_ap(T.l0_kin, 1024 * TLOC, [[1, 1], [1, 1]]), T.nstage[0:1, 0:1], "d_nst")
    P.wait("gpsimd", "t_kv", P.cnt("t_kv"))
    load_w_chunks(P, T, T.qw, T.l0_wv, 8, 1024, "d_wkv")
    P.wait("tensor", "d_wkv")
    for j in range(NTT):
        for nc2 in range(2):
            bank = T.ps[grp % 2]
            P.wait("tensor", "v_kv", grp - 1)
            for c in range(NDC):
                P.op("tensor", lambda t, b=bank, c=c, j=j, n=nc2: t.matmul(
                    b[:, 0:512],
                    T.xn_fm[:, TLOC * c + 128 * j:TLOC * c + 128 * (j + 1)],
                    T.qw[:, 1024 * c + 512 * n:1024 * c + 512 * (n + 1)],
                    start=(c == 0), stop=(c == NDC - 1)),
                    inc="t_kv" if c == NDC - 1 else None)
            slot = grp % 2
            stg = T.kvstage[:, 512 * slot:512 * (slot + 1)]
            P.wait("vector", "t_kv")
            P.wait("vector", *slot_free[slot])
            P.op("vector", lambda v, b=bank, stg=stg: v.tensor_copy(stg, b[:, 0:512]),
                 inc="v_kv")
            P.wait("sync", "v_kv")
            n = P.dma("sync", T.l0_vin[128 * j:128 * (j + 1), 512 * nc2:512 * (nc2 + 1)], stg,
                      f"d_kst{slot}")
            slot_free[slot] = (f"d_kst{slot}", n)
            grp += 1
    P.wait("gpsimd", "d_kst0")
    P.wait("gpsimd", "d_kst1")
    P.wait("gpsimd", "d_nst")
    if SKIP_CC:
        P.op("gpsimd", lambda g: g.nop(), inc="cc")
    else:
        P.op("gpsimd", lambda g: g.collective_compute(
            "AllGather", OP.bypass, replica_groups=GROUPS,
            ins=[T.l0_kin.opt()], outs=[T.l0_kout.opt()]), inc="cc")
    if SKIP_CC:
        P.op("gpsimd", lambda g: g.nop(), inc="cc")
    else:
        P.op("gpsimd", lambda g: g.collective_compute(
            "AllGather", OP.bypass, replica_groups=GROUPS,
            ins=[T.l0_vin.opt()], outs=[T.l0_vout.opt()]), inc="cc")


def emit_l0_gate(P, T):
    for c in range(NDC):
        P.dma("sync", T.gw_sb[:, 14 * c:14 * (c + 1)], T.l0_gw[128 * c:128 * (c + 1), :],
              "d_gw")
    P.wait("tensor", "d_gw")
    P.wait("tensor", "v_xn")
    P.wait("tensor", "v_nrm")
    P.wait("vector", "d_x")
    for j in range(NTT):
        m = T.stats[:, j:j + 1]
        rs = T.stats[:, 16 + j:16 + j + 1]
        P.wait("vector", "s_ln", P.cnt("s_ln"))
        P.wait("vector", "t_gxp", P.cnt("t_gxp"))
        P.drain("vector")
        P.op("vector", lambda v, j=j, m=m, rs=rs: v.tensor_scalar(
            T.scratch[:], T.x_res[:, 1024 * j:1024 * (j + 1)], m, rs,
            OP.subtract, OP.mult), inc="v_gxn")
        P.wait("tensor", "v_gxn")
        P.wait("tensor", "v_kv", P.cnt("v_kv"))
        for c in range(NDC):
            P.op("tensor", lambda t, c=c: t.transpose(
                T.ps[c // 4][:, 128 * (c % 4):128 * (c % 4) + 128],
                T.scratch[:, 128 * c:128 * (c + 1)], T.ident_f[:]),
                inc="t_gxp" if c == NDC - 1 else None)
        P.wait("vector", "t_gxp")
        for c in range(NDC):
            P.op("vector", lambda v, c=c: v.tensor_copy(
                T.atmp[:, 128 * c:128 * (c + 1)],
                T.ps[c // 4][:, 128 * (c % 4):128 * (c % 4) + 128]),
                inc="v_gxt" if c == NDC - 1 else None)
        P.wait("tensor", "s_gate", P.cnt("s_gate"))
        P.wait("tensor", "v_gxt")
        for c in range(NDC):
            P.op("tensor", lambda t, c=c, j=j: t.matmul(
                T.psm[:, 0:14],
                T.atmp[:, 128 * c:128 * (c + 1)],
                T.gw_sb[:, 14 * c:14 * (c + 1)],
                start=(c == 0), stop=(c == NDC - 1)),
                inc="t_gate" if c == NDC - 1 else None)
        w = T.w_all[:, 14 * j:14 * (j + 1)]
        e_t = T.scratch[:, 0:14]
        P.wait("vector", "t_gate")
        P.op("vector", lambda v: v.reduce_max(T.small[:, 0:1], T.psm[:, 0:14], axis=AX))
        P.drain("vector")
        P.op("vector", lambda v: v.tensor_scalar(
            T.small[:, 1:2], T.small[:, 0:1], -1.0, None, OP.mult), inc="v_gate")
        P.wait("scalar", "v_gate")
        P.op("scalar", lambda sc, e_t=e_t: sc.activation(
            e_t, T.psm[:, 0:14], AF.Exp, bias=T.small[:, 1:2]), inc="s_gate")
        P.wait("vector", "s_gate")
        P.op("vector", lambda v, e_t=e_t: v.tensor_scalar(
            T.scratch[:, 32:46], e_t, 1.0 - 1e-6, None, OP.is_ge))
        P.drain("vector")
        P.op("vector", lambda v, e_t=e_t: v.scalar_tensor_tensor(
            T.scratch[:, 64:78], T.scratch[:, 32:46], -2.0, e_t, OP.mult, OP.add))
        P.drain("vector")
        P.op("vector", lambda v: v.reduce_max(T.small[:, 2:3], T.scratch[:, 64:78], axis=AX))
        P.drain("vector")
        P.op("vector", lambda v, e_t=e_t: v.tensor_scalar(
            T.scratch[:, 96:110], e_t, T.small[:, 2:3], None, OP.is_ge))
        P.op("vector", lambda v: v.tensor_scalar(
            T.small[:, 3:4], T.small[:, 2:3], 1.0, None, OP.add))
        P.drain("vector")
        P.op("vector", lambda v: v.reciprocal(T.small[:, 4:5], T.small[:, 3:4]))
        P.drain("vector")
        P.op("vector", lambda v, e_t=e_t: v.tensor_tensor(
            T.scratch[:, 128:142], e_t, T.scratch[:, 96:110], OP.mult))
        P.drain("vector")
        P.op("vector", lambda v, w=w: v.tensor_scalar(
            w, T.scratch[:, 128:142], T.small[:, 4:5], None, OP.mult), inc="v_gate")


def emit_l0_ffn(P, T):
    P.dma("sync", T.b1_sb[:, 0:224], T.l0_b1, "d_b1")
    P.wait("vector", "v_gate")
    for j in range(NTT):
        P.wait("tensor", "v_gate", P.cnt("v_gate"))
        P.wait("tensor", "v_b2t", P.cnt("v_b2t"))
        P.wait("tensor", "v_b2p", P.cnt("v_b2p"))
        P.wait("tensor", "s_gate")
        P.op("tensor", lambda t, j=j: t.transpose(
            T.psm[0:14, 0:128], T.w_all[:, 14 * j:14 * (j + 1)], T.ident_f[:]), inc="t_b2t")
        P.wait("vector", "t_b2t")
        P.op("vector", lambda v: v.tensor_copy(T.wT16[0:14, 0:128], T.psm[0:14, 0:128]),
             inc="v_b2t")
        for nc2 in range(2):
            P.wait("gpsimd", "t_b2p", P.cnt("t_b2p"))
            P.dma("gpsimd", T.b14[0:14, 0:512], T.l0_b2[0:14, 512 * nc2:512 * (nc2 + 1)],
                  "d_b2")
            P.wait("tensor", "v_b2t")
            P.wait("tensor", "d_b2")
            P.wait("tensor", "v_b2p", P.cnt("v_b2p"))
            P.op("tensor", lambda t: t.matmul(
                T.psm[:, 0:512], T.wT16[0:14, 0:128], T.b14[0:14, 0:512],
                start=True, stop=True), inc="t_b2p")
            P.wait("vector", "t_b2p")
            P.op("vector", lambda v, j=j, n=nc2: v.tensor_copy(
                T.ar1[:, 1024 * j + 512 * n:1024 * j + 512 * (n + 1)], T.psm[:, 0:512]),
                 inc="v_b2p")
    P.wait("tensor", "v_kv")
    P.wait("scalar", "d_kst0")
    P.wait("scalar", "d_kst1")
    P.wait("scalar", "d_nst")
    qslot_free = [("v_setup", 0), ("v_setup", 0)]
    hs_free = [("v_setup", 0), ("v_setup", 0)]
    nq_total = 0
    for e in range(14):
        for q in range(4):
            qs = nq_total % 2
            base = 8192 * qs
            P.wait("gpsimd", *qslot_free[qs])
            for c in range(NDC):
                P.dma("gpsimd",
                      mk_ap(T.ar2[:, 0:1], base + 128 * c,
                            [[16384, 128], [1024, 4], [1, 128]]),
                      mk_ap(T.l0_w1, e * D * 2048 + 2048 * 128 * c + 512 * q,
                            [[2048, 128], [128, 4], [1, 128]]), f"d_w1{qs}")
            P.dma("gpsimd",
                  mk_ap(T.ar2[:, 0:1], base + 4096, [[16384, 128], [1024, 4], [1, 1024]]),
                  mk_ap(T.l0_w2, e * 2048 * D + 512 * q * D,
                        [[1024, 128], [128 * 1024, 4], [1, 1024]]), f"d_w1{qs}")
            P.wait("tensor", f"d_w1{qs}")
            last_t_y = None
            for tg in range(8):
                for f in range(4):
                    ft = 4 * q + f
                    hb = T.ps[f % 2]
                    P.wait("tensor", "s_h", P.cnt("s_h") - 1)
                    for c in range(NDC):
                        P.op("tensor", lambda t, hb=hb, c=c, tg=tg, base=base, f=f: t.matmul(
                            hb[:, 0:256],
                            T.ar2[:, base + 1024 * f + 128 * c:base + 1024 * f + 128 * (c + 1)],
                            T.xn_fm[:, TLOC * c + 256 * tg:TLOC * c + 256 * (tg + 1)],
                            start=(c == 0), stop=(c == NDC - 1)),
                            inc="t_h" if c == NDC - 1 else None)
                    hslot = f % 2
                    hs = T.kvstage[:, 512 * hslot:512 * hslot + 256]
                    P.wait("scalar", "t_h")
                    P.wait("scalar", "d_b1")
                    P.wait("scalar", *hs_free[hslot])
                    P.op("scalar", lambda sc, hb=hb, hs=hs, e=e, ft=ft: sc.activation(
                        hs, hb[:, 0:256], AF.Gelu,
                        bias=T.b1_sb[:, 16 * e + ft:16 * e + ft + 1]), inc="s_h")
                    P.wait("tensor", "s_h")
                    for tt in range(2):
                        for nc2 in range(2):
                            yb = T.ps[2 + tt * 2 + nc2]
                            n = P.op("tensor", lambda t, yb=yb, hs=hs, tt=tt, n=nc2,
                                     base=base, f=f: t.matmul(
                                yb[:, 0:512],
                                hs[:, 128 * tt:128 * (tt + 1)],
                                T.ar2[:, base + 4096 + 1024 * f + 512 * n:
                                      base + 4096 + 1024 * f + 512 * (n + 1)],
                                start=(f == 0), stop=(f == 3)),
                                inc="t_y" if (tt, nc2) == (1, 1) else None)
                    hs_free[hslot] = ("t_y", P.cnt("t_y"))
                    last_t_y = P.cnt("t_y")
                P.wait("vector", "t_y")
                P.wait("vector", "v_b2p", P.cnt("v_b2p"))
                for tt in range(2):
                    jt = tg * 2 + tt
                    for nc2 in range(2):
                        yb = T.ps[2 + tt * 2 + nc2]
                        dst = T.ar1[:, 1024 * jt + 512 * nc2:1024 * jt + 512 * (nc2 + 1)]
                        P.op("vector", lambda v, yb=yb, dst=dst, jt=jt, e=e:
                             v.scalar_tensor_tensor(
                                 dst, yb[:, 0:512],
                                 T.w_all[:, 14 * jt + e:14 * jt + e + 1], dst,
                                 OP.mult, OP.add),
                             inc="v_cmb" if (tt, nc2) == (1, 1) else None)
                P.wait("tensor", "v_cmb")
            qslot_free[qs] = ("t_y", last_t_y)
            nq_total += 1
    for j in range(NTT):
        m = T.stats[:, j:j + 1]
        rs = T.stats[:, 16 + j:16 + j + 1]
        yj = T.ar1[:, 1024 * j:1024 * (j + 1)]
        P.wait("vector", "v_cmb", P.cnt("v_cmb"))
        P.wait("vector", "s_g", P.cnt("s_g"))
        P.op("vector", lambda v, j=j, m=m, rs=rs: v.tensor_scalar(
            T.scratch[:], T.x_res[:, 1024 * j:1024 * (j + 1)], m, rs, OP.subtract, OP.mult))
        P.drain("vector")
        P.op("vector", lambda v, yj=yj: v.tensor_tensor(yj, T.scratch[:], yj, OP.add))
        P.drain("vector")
        P.op("vector", lambda v, yj=yj: v.tensor_scalar(
            T.scratch[:], yj, 100.0, -100.0, OP.min, OP.max), inc="v_g")
        P.drain("vector")
        P.wait("scalar", "v_g")
        P.op("scalar", lambda sc, j=j: sc.activation(
            T.ar1b[:, 1024 * j:1024 * (j + 1)], T.scratch[:], AF.Silu), inc="s_g")


def emit_l0_attn(P, T):
    SCALE = 0.125
    P.wait("gpsimd", "s_g")
    load_w_chunks(P, T, T.qw, T.l0_wq, 8, 1024, "d_wq")
    P.wait("tensor", "d_wq")
    P.wait("tensor", "s_h")
    P.wait("tensor", "v_b2p")
    first_qn = True
    for hp in range(8):
        for tg in range(4):
            bank = T.ps[(hp * 4 + tg) % 2]
            if hp * 4 + tg >= 2:
                P.wait("tensor", "v_qe", hp * 4 + tg - 1)
            for c in range(NDC):
                P.op("tensor", lambda t, b=bank, c=c, hp=hp, tg=tg: t.matmul(
                    b[:, 0:512],
                    T.qw[:, 1024 * c + 128 * hp:1024 * c + 128 * hp + 128],
                    T.xn_fm[:, TLOC * c + 512 * tg:TLOC * c + 512 * (tg + 1)],
                    start=(c == 0), stop=(c == NDC - 1)),
                    inc="t_q" if c == NDC - 1 else None)
            dst = T.ar2[:, 2048 * hp + 512 * tg:2048 * hp + 512 * (tg + 1)]
            P.wait("vector", "t_q")
            P.op("vector", lambda v, b=bank, dst=dst: v.tensor_copy(dst, b[:, 0:512]),
                 inc="v_qe")
            P.wait("scalar", "v_qe")
            P.wait("scalar", "t_qn", P.cnt("t_qn"))
            P.op("scalar", lambda sc, dst=dst: sc.activation(
                T.kvstage[:, 1024:1536], dst, AF.Square), inc="s_qn")
            P.wait("tensor", "s_qn")
            P.wait("tensor", "v_qn", P.cnt("v_qn"))
            P.op("tensor", lambda t: t.matmul(
                T.psm[0:1, 0:512], T.ones128[:], T.kvstage[:, 1024:1536],
                start=True, stop=True), inc="t_qn")
            P.wait("vector", "t_qn")
            P.op("vector", lambda v: v.reduce_max(T.small[0:1, 5:6], T.psm[0:1, 0:512], axis=AX))
            P.drain("vector")
            if first_qn:
                P.op("vector", lambda v: v.tensor_copy(T.small[0:1, 6:7], T.small[0:1, 5:6]),
                     inc="v_qn")
                first_qn = False
            else:
                P.op("vector", lambda v: v.tensor_tensor(
                    T.small[0:1, 6:7], T.small[0:1, 6:7], T.small[0:1, 5:6], OP.max),
                     inc="v_qn")
            P.drain("vector")
    P.wait("gpsimd", "cc")
    for r in range(4):
        P.dma("gpsimd", T.atmp[0:1, r:r + 1],
              mk_ap(T.l0_kout, (1025 * r + 1024) * TLOC, [[1, 1], [1, 1]]), "d_km")
    P.wait("vector", "d_km")
    P.op("vector", lambda v: v.reduce_max(T.small[0:1, 7:8], T.atmp[0:1, 0:4], axis=AX))
    P.drain("vector")
    P.wait("vector", "v_qn")
    P.op("vector", lambda v: v.tensor_tensor(
        T.small[0:1, 8:9], T.small[0:1, 6:7], T.small[0:1, 7:8], OP.mult), inc="v_bnd")
    P.wait("scalar", "v_bnd")
    P.op("scalar", lambda sc: sc.activation(
        T.small[0:1, 9:10], T.small[0:1, 8:9], AF.Sqrt, scale=SCALE * SCALE * 1.05),
        inc="s_bnd")
    P.wait("vector", "s_bnd")
    P.op("vector", lambda v: v.tensor_scalar(
        T.small[0:1, 10:11], T.small[0:1, 9:10], -1.0, -1.0, OP.mult, OP.add), inc="v_bnd")
    P.wait("tensor", "v_bnd")
    P.wait("tensor", "v_qn")
    P.op("tensor", lambda t: t.matmul(
        T.psm[:, 0:1], T.ones1f[:], T.small[0:1, 10:11], start=True, stop=True), inc="t_bnd")
    P.wait("vector", "t_bnd")
    P.op("vector", lambda v: v.tensor_copy(T.boundt, T.psm[:, 0:1]), inc="v_bnd")
    P.wait("vector", "t_q")
    P.op("vector", lambda v: v.memset(
        mk_ap(T.v65, 64, [[T.xn_fm.shape[1], 128], [130, 16], [65, 2]]), 1.0), inc="v_v65i")
    P.wait("gpsimd", "t_q")

    def ksync(hp):
        P.wait("sync", "t_q")
        if hp > 0:
            P.wait("sync", "t_sc", P.cnt("t_sc"))

    def q_ap(hp, qg, h):
        return T.ar2[64 * h:64 * h + 64, 2048 * hp + 512 * qg:2048 * hp + 512 * (qg + 1)]

    def finish(hp, qg):
        for h in (0, 1):
                hh = 2 * hp + h
                P.wait("vector", "t_av")
                P.op("vector", lambda v, h=h: v.tensor_copy(
                    T.avr[0:65, 0:512], T.ps[4 + h][0:65, 0:512]), inc="v_av")
                P.wait("tensor", "v_av")
                P.wait("tensor", "v_nm", P.cnt("v_nm"))
                for c in range(4):
                    P.op("tensor", lambda t, c=c: t.transpose(
                        T.psm[:, 128 * c:128 * (c + 1)],
                        T.avr[:, 128 * c:128 * (c + 1)], T.ident_f[:]),
                        inc="t_nm" if c == 3 else None)
                P.wait("vector", "t_nm")
                for c in range(4):
                    jt = qg * 4 + c
                    P.op("vector", lambda v, c=c: v.reciprocal(
                        T.small[:, 11:12], T.psm[:, 128 * c + 64:128 * c + 65]))
                    P.drain("vector")
                    P.op("vector", lambda v, c=c, jt=jt, hh=hh: v.scalar_tensor_tensor(
                        T.tmp64[:], T.psm[:, 128 * c:128 * c + 64], T.small[:, 11:12],
                        T.ar1b[:, 1024 * jt + 64 * hh:1024 * jt + 64 * (hh + 1)],
                        OP.mult, OP.mult))
                    P.drain("vector")
                    P.op("vector", lambda v, jt=jt, hh=hh: v.tensor_tensor(
                        T.x_res[:, 1024 * jt + 64 * hh:1024 * jt + 64 * (hh + 1)],
                        T.x_res[:, 1024 * jt + 64 * hh:1024 * jt + 64 * (hh + 1)],
                        T.tmp64[:], OP.add),
                        inc="v_nm" if c == 3 else None)
                    P.drain("vector")

    emit_flash(P, T, T.l0_kout, T.l0_vout, q_ap, SCALE, ksync, finish)


def emit_flash(P, T, kout, vout, q_ap, scale, ksync, finish):
    av_val = {}
    sblk = 0

    def emit_sc(hp, qg, kc):
        P.wait("tensor", "d_ka")
        P.wait("tensor", "s_ex", P.cnt("s_ex") - 2)
        for h, pb in ((0, T.ps[kc % 2]), (1, T.ps[2 + kc % 2])):
            P.op("tensor", lambda t, pb=pb, h=h, kc=kc, hp=hp, qg=qg: t.matmul(
                pb[:, 0:512],
                T.kbuf[64 * h:64 * h + 64, 128 * kc:128 * (kc + 1)],
                q_ap(hp, qg, h),
                start=True, stop=True),
                inc="t_sc" if h == 1 else None)

    def emit_av(gk, kc, vblk_copy):
        b = kc // 8
        P.wait("tensor", "s_ex", P.cnt("s_ex"))
        P.wait("tensor", "v_v65i")
        P.wait("tensor", "v_v65", vblk_copy[b])
        if kc == 0:
            P.wait("tensor", "v_av", P.cnt("v_av"))
        for h in (0, 1):
            P.op("tensor", lambda t, h=h, kc=kc, b=b: t.matmul(
                T.ps[4 + h][0:65, 0:512],
                T.v65[:, 1040 * (b % 2) + 130 * (kc % 8) + 65 * h:
                      1040 * (b % 2) + 130 * (kc % 8) + 65 * h + 65],
                T.probs[:, 512 * (2 * h + kc % 2):512 * (2 * h + kc % 2 + 1)],
                start=(kc == 0), stop=(kc == 63)),
                inc="t_av")
        av_val[gk] = P.cnt("t_av")

    def emit_exp(kc):
        P.wait("scalar", "t_sc")
        P.wait("scalar", "v_bnd")
        P.wait("scalar", "t_av", P.cnt("t_av") - 2)
        for h, pb in ((0, T.ps[kc % 2]), (1, T.ps[2 + kc % 2])):
            P.op("scalar", lambda sc, pb=pb, h=h, kc=kc: sc.activation(
                T.probs[:, 512 * (2 * h + kc % 2):512 * (2 * h + kc % 2 + 1)],
                pb[:, 0:512], AF.Exp, bias=T.boundt, scale=scale),
                inc="s_ex")

    for hp in range(8):
        ksync(hp)
        for r in range(4):
            P.dma("sync", T.kbuf[:, 2048 * r:2048 * (r + 1)],
                  kout[1025 * r + 128 * hp:1025 * r + 128 * (hp + 1), :], "d_ka")
        for qg in range(4):
            gk0 = (hp * 4 + qg) * 64
            vblk_copy = {}
            for b in range(8):
                P.wait("gpsimd", "v_v65", P.cnt("v_v65"))
                P.dma("gpsimd", T.vplain[:],
                      mk_ap(vout, (2048 * (b // 2) + 1024 * (b % 2)) * D + 128 * hp,
                            [[D, 128], [128 * D, 8], [1, 128]]), "d_va")
                P.wait("vector", "d_va")
                need = av_val.get(8 * sblk - 9)
                if need:
                    P.wait("vector", "t_av", need)
                P.op("vector", lambda v, b=b: v.tensor_copy(
                    mk_ap(T.v65, 1040 * (b % 2),
                          [[T.xn_fm.shape[1], 128], [130, 8], [65, 2], [1, 64]]),
                    mk_ap(T.vplain, 0, [[T.xn_fm.shape[1], 128], [128, 8], [64, 2], [1, 64]])),
                    inc="v_v65")
                vblk_copy[b] = P.cnt("v_v65")
                for kc in range(8 * b, 8 * b + 8):
                    emit_sc(hp, qg, kc)
                    if kc > 0:
                        emit_av(gk0 + kc - 1, kc - 1, vblk_copy)
                    emit_exp(kc)
                sblk += 1
            emit_av(gk0 + 63, 63, vblk_copy)
            finish(hp, qg)


def emit_store_x(P, T):
    P.wait("sync", "v_nm", P.cnt("v_nm"))
    P.wait("sync", "v_nm2", P.cnt("v_nm2"))
    P.wait("sync", "v_fin", P.cnt("v_fin"))
    P.wait("sync", "v_xn", P.cnt("v_xn"))
    for j in range(NTT):
        P.dma("sync", T.x_out[128 * j:128 * (j + 1), :], T.x_res[:, D * j:D * (j + 1)], "d_out")
    P.wait("sync", "d_out")



PHASES = ("ln", "kv", "gate", "ffn", "q", "attn", "cmb")

BOUNDARY_SEMS = ("v_xn", "v_ln", "s_ln", "t_ln", "v_kv", "t_kv", "s_kv", "t_nrm", "v_nrm",
                 "d_kst0", "d_kst1", "d_nst", "cc", "v_gate", "s_gate", "t_gate", "v_gxt",
                 "t_gxp", "v_b2t", "t_b2t", "v_b2p", "t_b2p", "d_b2", "t_h", "s_h", "t_y",
                 "v_cmb", "s_g", "v_g", "t_q", "v_qe", "s_qn", "t_qn", "v_qn", "v_bnd",
                 "s_bnd", "t_bnd", "d_ka", "d_va", "v_v65", "v_v65i", "t_sc", "s_ex",
                 "t_av", "v_av", "t_nm", "v_nm", "v_nm2", "v_rc", "t_rcb", "d_km",
                 "t_mix", "t_n1", "v_n1e", "t_n1b", "v_x2", "s_x2", "t_n2", "v_n2e",
                 "t_rxt", "v_rxt", "v_f1", "s_f1", "d_w10", "d_w11", "t_dnt", "t_dn",
                 "v_dn", "v_ye", "t_xz", "v_xz", "t_gu", "s_gu", "v_h", "t_do", "s_go",
                 "d_u", "v_up", "v_cv", "s_ae", "v_ya", "s_sz", "v_ys", "t_wo", "v_woe",
                 "d_rs0", "d_rs1", "d_s", "v_fin", "t_ct", "v_ckv", "v_ctq", "t_up",
                 "v_kc", "v_qc", "t_kr", "v_rp", "v_rpe", "t_rpt", "s_qn", "d_rt",
                 "t_wo3", "d_wo", "d_l2p", "d_wkv", "d_wq", "d_gw", "d_b1", "d_wkv1",
                 "d_wq1", "d_bq", "d_bp", "d_mixw", "d_xz0", "d_xz1", "d_wg0", "d_wg1",
                 "d_wd0", "d_wd1", "d_wd3", "d_wu3", "d_wr3", "d_wv3", "d_we0", "d_we1",
                 "d_gw3", "d_wq3", "d_bp3", "d_bp3b", "d_wo3", "d_w3", "d_w3k", "d_km")


def emit_boundary(P, T):
    for eng in Prog.ENGINES:
        for s_, v in sorted(P.counts.items()):
            if v > 0:
                P.wait(eng, s_, v)


def build_program(layers=(0,), phases=PHASES):
    nc = bass.Bass("TRN2", target_bir_lowering=False)
    es = ExitStack()
    with es:
        T = alloc_common(nc, es, layers)
        if 0 in layers:
            alloc_l0(nc, T)
        if 1 in layers:
            alloc_l1(nc, T)
        if 2 in layers:
            alloc_l2(nc, T)
        if 3 in layers:
            alloc_l3(nc, T)
        P = Prog(nc, es)
        emit_setup(P, T)
        first = True
        if 0 in layers:
            first = False
            if "ln" in phases:
                emit_ln(P, T)
            if "kv" in phases:
                emit_l0_kv(P, T)
            if "gate" in phases:
                emit_l0_gate(P, T)
            if "ffn" in phases:
                emit_l0_ffn(P, T)
            if "attn" in phases:
                emit_l0_attn(P, T)
        if 1 in layers:
            if not first:
                emit_boundary(P, T)
            first = False
            if "ln" in phases:
                emit_ln(P, T)
            if "kv" in phases:
                emit_l1_kv(P, T)
            if "ffn" in phases:
                emit_l1_ffn(P, T)
            if "q" in phases:
                emit_l1_reln_q(P, T)
            if "attn" in phases:
                emit_l1_attn(P, T)
        if 2 in layers:
            if not first:
                emit_boundary(P, T)
            first = False
            if "ln" in phases:
                emit_ln(P, T)
            if "kv" in phases:
                emit_l2_xz(P, T)
            if "ffn" in phases:
                emit_l2_ffn(P, T)
            if "attn" in phases:
                emit_l2_scan(P, T)
            if "cmb" in phases:
                emit_l2_rs(P, T)
        if 3 in layers:
            if not first:
                emit_boundary(P, T)
            first = False
            if "ln" in phases:
                emit_ln(P, T)
            if "kv" in phases:
                emit_l3_kv(P, T)
            if "ffn" in phases:
                emit_l3_ffn(P, T)
            if "q" in phases:
                emit_l3_q(P, T)
            if "attn" in phases:
                emit_l3_attn(P, T)
            if "cmb" in phases:
                emit_l3_wo(P, T)
        emit_store_x(P, T)
        for fn in DEBUG:
            fn(P, T)
        P.build()
    global LAST_PROG
    LAST_PROG = P
    return nc


FUSED = True


def kernel(**inputs):
    if FUSED:
        nc = build_program((0, 1, 2, 3))
        in_maps = [prep_core_inputs(c, inputs, (0, 1, 2, 3)) for c in range(NCORES)]
        res = run_bass_kernel_spmd(nc, in_maps, core_ids=list(range(NCORES)))
        x_cur = [res.results[c]["x_out"] for c in range(NCORES)]
    else:
        x_cur = None
        for li in (0, 1, 2, 3):
            nc = build_program((li,))
            in_maps = [prep_core_inputs(c, inputs, (li,)) for c in range(NCORES)]
            if x_cur is not None:
                for c in range(NCORES):
                    in_maps[c]["x_in"] = x_cur[c]
            res = run_bass_kernel_spmd(nc, in_maps, core_ids=list(range(NCORES)))
            x_cur = [np.ascontiguousarray(res.results[c]["x_out"]) for c in range(NCORES)]
    out = np.empty((B, S, D), np.float32)
    for c in range(NCORES):
        b, r = c // 4, c % 4
        out[b, r * TLOC:(r + 1) * TLOC, :] = x_cur[c]
    return out



def alloc_l1(nc, T):
    T.l1_wqT = nc.dram_tensor("l1_wqT", [D, D], F32, kind="ExternalInput").ap()
    T.l1_wkT = nc.dram_tensor("l1_wkT", [D, D], F32, kind="ExternalInput").ap()
    T.l1_wvT = nc.dram_tensor("l1_wvT", [D, D], F32, kind="ExternalInput").ap()
    T.l1_bq = nc.dram_tensor("l1_bq", [128, 8], F32, kind="ExternalInput").ap()
    T.l1_bk = nc.dram_tensor("l1_bk", [128, 8], F32, kind="ExternalInput").ap()
    T.l1_bv = nc.dram_tensor("l1_bv", [1, D], F32, kind="ExternalInput").ap()
    T.l1_g1 = nc.dram_tensor("l1_g1", [128, 8], F32, kind="ExternalInput").ap()
    T.l1_be1 = nc.dram_tensor("l1_be1", [128, 8], F32, kind="ExternalInput").ap()
    T.l1_g2 = nc.dram_tensor("l1_g2", [128, 8], F32, kind="ExternalInput").ap()
    T.l1_be2 = nc.dram_tensor("l1_be2", [128, 8], F32, kind="ExternalInput").ap()
    T.l1_mixw = nc.dram_tensor("l1_mixw", [D, D], F32, kind="ExternalInput").ap()
    T.l1_mixb = nc.dram_tensor("l1_mixb", [1, D], F32, kind="ExternalInput").ap()
    T.l1_rw = nc.dram_tensor("l1_rw", [D, 8], F32, kind="ExternalInput").ap()
    T.l1_w1 = nc.dram_tensor("l1_w1", [8, D, 4096], F32, kind="ExternalInput").ap()
    T.l1_b1 = nc.dram_tensor("l1_b1", [128, 256], F32, kind="ExternalInput").ap()
    T.l1_w2 = nc.dram_tensor("l1_w2", [8, 4096, D], F32, kind="ExternalInput").ap()
    T.l1_b2 = nc.dram_tensor("l1_b2", [8, D], F32, kind="ExternalInput").ap()
    T.l1_kin = nc.dram_tensor("l1_kin", [1025, TLOC], BF16).ap()
    T.l1_kout = nc.dram_tensor("l1_kout", [4 * 1025, TLOC], BF16).ap()
    T.l1_vin = nc.dram_tensor("l1_vin", [TLOC, D], BF16).ap()
    T.l1_vout = nc.dram_tensor("l1_vout", [4 * TLOC, D], BF16).ap()


def emit_row_to_packed(P, T, src_row_ap, dst_packed, tag):
    P.dma("sync", dst_packed[:], mk_ap(src_row_ap, 0, [[1, 128], [128, NDC]]), tag)


def emit_l1_kv(P, T):
    for c in range(NDC):
        P.dma("gpsimd", T.ar2[:, 1024 * c:1024 * (c + 1)],
              T.l1_wkT[128 * c:128 * (c + 1), :], "d_wkv1")
    P.dma("sync", T.bkp[:], T.l1_bk, "d_bp")
    P.wait("tensor", "d_wkv1")
    P.wait("tensor", "v_xn")
    P.wait("vector", "d_bp")
    grp = P.cnt("t_kv")
    slot_free = [("v_setup", 0), ("v_setup", 0)]
    first_nrm = True
    for tg in range(4):
        for g in range(8):
            bank = T.ps[grp % 2]
            P.wait("tensor", "v_kv", P.cnt("v_kv") - 1)
            for c in range(NDC):
                P.op("tensor", lambda t, b=bank, c=c, g=g, tg=tg: t.matmul(
                    b[:, 0:512],
                    T.ar2[:, 1024 * c + 128 * g:1024 * c + 128 * g + 128],
                    T.xn_fm[:, TLOC * c + 512 * tg:TLOC * c + 512 * (tg + 1)],
                    start=(c == 0), stop=(c == NDC - 1)),
                    inc="t_kv" if c == NDC - 1 else None)
            slot = grp % 2
            stg = T.kvstage[:, 512 * slot:512 * (slot + 1)]
            P.wait("vector", "t_kv")
            P.wait("vector", *slot_free[slot])
            P.op("vector", lambda v, b=bank, stg=stg, g=g: v.tensor_scalar(
                stg, b[:, 0:512], T.bkp[:, g:g + 1], None, OP.add), inc="v_kv")
            P.wait("scalar", "v_kv")
            P.wait("scalar", "t_nrm", P.cnt("t_nrm"))
            P.op("scalar", lambda sc, stg=stg: sc.activation(
                T.kvstage[:, 1024:1536], stg, AF.Square), inc="s_kv")
            P.wait("tensor", "s_kv")
            if g == 0:
                P.wait("tensor", "v_nrm", P.cnt("v_nrm"))
            P.op("tensor", lambda t, g=g: t.matmul(
                T.psm[0:1, 0:512], T.ones128[:], T.kvstage[:, 1024:1536],
                start=(g == 0), stop=(g == 7)), inc="t_nrm")
            P.wait("sync", "v_kv")
            n = P.dma("sync", T.l1_kin[128 * g:128 * (g + 1), 512 * tg:512 * (tg + 1)], stg,
                      f"d_kst{slot}")
            slot_free[slot] = (f"d_kst{slot}", n)
            grp += 1
        P.wait("vector", "t_nrm")
        P.op("vector", lambda v: v.reduce_max(T.small[0:1, 3:4], T.psm[0:1, 0:512], axis=AX))
        P.drain("vector")
        if first_nrm:
            P.op("vector", lambda v: v.tensor_copy(T.small[0:1, 4:5], T.small[0:1, 3:4]),
                 inc="v_nrm")
            first_nrm = False
        else:
            P.op("vector", lambda v: v.tensor_tensor(
                T.small[0:1, 4:5], T.small[0:1, 4:5], T.small[0:1, 3:4], OP.max),
                 inc="v_nrm")
        P.drain("vector")
    P.wait("vector", "v_nrm")
    P.op("vector", lambda v: v.tensor_copy(T.nstage[0:1, 0:1], T.small[0:1, 4:5]),
         inc="v_kmax")
    P.wait("sync", "v_kmax")
    P.dma("sync", mk_ap(T.l1_kin, 1024 * TLOC, [[1, 1], [1, 1]]), T.nstage[0:1, 0:1], "d_nst")
    P.wait("gpsimd", "t_kv", P.cnt("t_kv"))
    for c in range(NDC):
        P.dma("gpsimd", T.ar2[:, 1024 * c:1024 * (c + 1)],
              T.l1_wvT[128 * c:128 * (c + 1), :], "d_wkv1")
    P.wait("gpsimd", "d_nst", P.cnt("d_nst"))
    P.dma("gpsimd", T.nstage[0:1, 0:1024], T.l1_bv, "d_wkv1")
    P.wait("tensor", "d_wkv1")
    for j in range(NTT):
        for nc2 in range(2):
            bank = T.ps[grp % 2]
            P.wait("tensor", "v_kv", P.cnt("v_kv") - 1)
            for c in range(NDC):
                P.op("tensor", lambda t, b=bank, c=c, j=j, n=nc2: t.matmul(
                    b[:, 0:512],
                    T.xn_fm[:, TLOC * c + 128 * j:TLOC * c + 128 * (j + 1)],
                    T.ar2[:, 1024 * c + 512 * n:1024 * c + 512 * (n + 1)],
                    start=(c == 0), stop=False),
                    inc=None)
            P.op("tensor", lambda t, b=bank, n=nc2: t.matmul(
                b[:, 0:512], T.ones1[:], T.nstage[0:1, 512 * n:512 * (n + 1)],
                start=False, stop=True), inc="t_kv")
            slot = grp % 2
            stg = T.kvstage[:, 512 * slot:512 * (slot + 1)]
            P.wait("vector", "t_kv")
            P.wait("vector", *slot_free[slot])
            P.op("vector", lambda v, b=bank, stg=stg: v.tensor_copy(stg, b[:, 0:512]),
                 inc="v_kv")
            P.wait("sync", "v_kv")
            n = P.dma("sync", T.l1_vin[128 * j:128 * (j + 1), 512 * nc2:512 * (nc2 + 1)], stg,
                      f"d_kst{slot}")
            slot_free[slot] = (f"d_kst{slot}", n)
            grp += 1
    P.wait("gpsimd", "d_kst0")
    P.wait("gpsimd", "d_kst1")
    P.wait("gpsimd", "d_nst")
    if SKIP_CC:
        P.op("gpsimd", lambda g: g.nop(), inc="cc")
    else:
        P.op("gpsimd", lambda g: g.collective_compute(
            "AllGather", OP.bypass, replica_groups=GROUPS,
            ins=[T.l1_kin.opt()], outs=[T.l1_kout.opt()]), inc="cc")
    if SKIP_CC:
        P.op("gpsimd", lambda g: g.nop(), inc="cc")
    else:
        P.op("gpsimd", lambda g: g.collective_compute(
            "AllGather", OP.bypass, replica_groups=GROUPS,
            ins=[T.l1_vin.opt()], outs=[T.l1_vout.opt()]), inc="cc")


def emit_l1_ffn(P, T):
    atmp2 = T.ar2[:, 8192:10240].bitcast(F32)
    """x2 = xn + n1 + n1@mixw + mixb; n2 = LN(x2)*g2+be2; top-2 router (fp32);
    y_acc(ar1) = x2 + B2tot + sum_e w_e*(silu(n2@w1+b1)@w2); g = silu(y_acc)."""
    P.dma("sync", T.g1p[:], T.l1_g1, "d_bp")
    P.dma("sync", T.be1p[:], T.l1_be1, "d_bp")
    P.dma("sync", T.g2p[:], T.l1_g2, "d_bp")
    P.dma("sync", T.be2p[:], T.l1_be2, "d_bp")
    P.dma("sync", T.b1_sb[:, 0:256], T.l1_b1, "d_b1")
    for c in range(NDC):
        P.dma("sync", T.gw_sb[:, 8 * c:8 * (c + 1)], T.l1_rw[128 * c:128 * (c + 1), :],
              "d_gw")
    P.wait("gpsimd", "t_kv", P.cnt("t_kv"))
    for c in range(NDC):
        P.dma("gpsimd", T.ar2[:, 1024 * c:1024 * (c + 1)],
              T.l1_mixw[128 * c:128 * (c + 1), :], "d_mixw")
    P.dma("gpsimd", T.nstage[0:1, 0:1024], T.l1_mixb, "d_mixw")
    P.wait("vector", "d_bp")
    P.wait("vector", "d_x")
    P.wait("vector", "d_kst0")
    P.wait("vector", "d_kst1")
    P.wait("vector", "d_nst")
    P.wait("vector", "t_kv", P.cnt("t_kv"))
    for j in range(NTT):
        m = T.stats[:, j:j + 1]
        rs = T.stats[:, 16 + j:16 + j + 1]
        m2 = T.stats[:, 32 + j:32 + j + 1]
        rs2 = T.stats[:, 48 + j:48 + j + 1]
        P.wait("vector", "s_ln", P.cnt("s_ln"))
        P.drain("vector")
        P.op("vector", lambda v, j=j, m=m, rs=rs: v.tensor_scalar(
            T.scratch[:], T.x_res[:, 1024 * j:1024 * (j + 1)], m, rs,
            OP.subtract, OP.mult), inc="v_f1")
        P.drain("vector")
        P.op("vector", lambda v: v.reduce_sum(T.small[:, 0:1], T.scratch[:], axis=AX))
        P.drain("vector")
        P.op("vector", lambda v, m2=m2: v.tensor_scalar(
            T.small[:, 1:2], T.small[:, 0:1], 1.0 / D, None, OP.mult))
        P.drain("vector")
        P.op("vector", lambda v: v.tensor_scalar(
            atmp2, T.scratch[:], T.small[:, 1:2], None, OP.subtract), inc="v_f1")
        P.wait("scalar", "v_f1")
        P.op("scalar", lambda sc: sc.activation(
            atmp2 if False else T.ar1[:, 0:1024], atmp2, AF.Square,
            accum_out=T.small[:, 2:3]))
        P.drain("scalar")
        P.op("scalar", lambda sc: sc.activation(
            T.small[:, 3:4], T.small[:, 2:3], AF.Ln, bias=T.epst, scale=1.0 / D))
        P.drain("scalar")
        P.op("scalar", lambda sc, rs2=rs2: sc.activation(
            rs2, T.small[:, 3:4], AF.Exp, scale=-0.5), inc="s_f1")
        P.wait("vector", "s_f1")
        P.op("vector", lambda v, m2=m2: v.tensor_copy(m2, T.small[:, 1:2]))
        P.drain("vector")
        P.op("vector", lambda v, rs2=rs2: v.tensor_scalar(
            T.kvstage[:, 0:1024], atmp2, rs2, None, OP.mult), inc="v_f1")
        P.wait("tensor", "v_f1")
        P.wait("tensor", "v_n1e", P.cnt("v_n1e"))
        for c in range(NDC):
            P.op("tensor", lambda t, c=c: t.transpose(
                T.pst[:, 128 * c:128 * (c + 1)], T.kvstage[:, 128 * c:128 * (c + 1)],
                T.ident_b[:]), inc="t_n1" if c == NDC - 1 else None)
        P.wait("vector", "t_n1")
        for c in range(NDC):
            P.op("vector", lambda v, c=c, j=j: v.tensor_scalar(
                T.xn_fm[:, TLOC * c + 128 * j:TLOC * c + 128 * (j + 1)],
                T.pst[:, 128 * c:128 * (c + 1)],
                T.g1p[:, c:c + 1], T.be1p[:, c:c + 1], OP.mult, OP.add),
                inc="v_n1e" if c == NDC - 1 else None)
    P.wait("tensor", "d_mixw")
    P.wait("tensor", "d_gw")
    for j in range(NTT):
        m = T.stats[:, j:j + 1]
        rs = T.stats[:, 16 + j:16 + j + 1]
        P.wait("tensor", "v_n1e", P.cnt("v_n1e"))
        P.wait("tensor", "v_x2", P.cnt("v_x2"))
        for nc2 in range(2):
            bank = T.ps[2 + nc2]
            for c in range(NDC):
                P.op("tensor", lambda t, b=bank, c=c, j=j, n=nc2: t.matmul(
                    b[:, 0:512],
                    T.xn_fm[:, TLOC * c + 128 * j:TLOC * c + 128 * (j + 1)],
                    T.ar2[:, 1024 * c + 512 * n:1024 * c + 512 * (n + 1)],
                    start=(c == 0), stop=False), inc=None)
            P.op("tensor", lambda t, b=bank, n=nc2: t.matmul(
                b[:, 0:512], T.ones1[:], T.nstage[0:1, 512 * n:512 * (n + 1)],
                start=False, stop=True), inc="t_mix")
        for c in range(NDC):
            P.op("tensor", lambda t, c=c, j=j: t.transpose(
                T.pst[:, 128 * c:128 * (c + 1)],
                T.xn_fm[:, TLOC * c + 128 * j:TLOC * c + 128 * (j + 1)],
                T.ident_b[:]), inc="t_n1b" if c == NDC - 1 else None)
        P.wait("vector", "t_mix")
        P.wait("vector", "t_n1b")
        P.drain("vector")
        P.op("vector", lambda v, j=j, m=m, rs=rs: v.tensor_scalar(
            T.scratch[:], T.x_res[:, 1024 * j:1024 * (j + 1)], m, rs,
            OP.subtract, OP.mult))
        P.drain("vector")
        for nc2 in range(2):
            P.op("vector", lambda v, j=j, n=nc2: v.tensor_tensor(
                T.ar1[:, 1024 * j + 512 * n:1024 * j + 512 * (n + 1)],
                T.scratch[:, 512 * n:512 * (n + 1)], T.ps[2 + n][:, 0:512], OP.add))
        P.drain("vector")
        for c in range(NDC):
            P.op("vector", lambda v, c=c, j=j: v.tensor_tensor(
                T.ar1[:, 1024 * j + 128 * c:1024 * j + 128 * (c + 1)],
                T.ar1[:, 1024 * j + 128 * c:1024 * j + 128 * (c + 1)],
                T.pst[:, 128 * c:128 * (c + 1)], OP.add),
                inc="v_x2" if c == NDC - 1 else None)
        P.drain("vector")
        P.op("vector", lambda v, j=j: v.reduce_sum(
            T.small[:, 0:1], T.ar1[:, 1024 * j:1024 * (j + 1)], axis=AX))
        P.drain("vector")
        P.op("vector", lambda v: v.tensor_scalar(
            T.small[:, 1:2], T.small[:, 0:1], 1.0 / D, None, OP.mult))
        P.drain("vector")
        P.op("vector", lambda v, j=j: v.tensor_scalar(
            T.scratch[:], T.ar1[:, 1024 * j:1024 * (j + 1)], T.small[:, 1:2], None,
            OP.subtract), inc="v_x2")
        P.wait("scalar", "v_x2")
        P.op("scalar", lambda sc: sc.activation(
            T.ar1[:, 0:1024] if False else atmp2, T.scratch[:], AF.Square,
            accum_out=T.small[:, 2:3]))
        P.drain("scalar")
        P.op("scalar", lambda sc: sc.activation(
            T.small[:, 3:4], T.small[:, 2:3], AF.Ln, bias=T.epst, scale=1.0 / D))
        P.drain("scalar")
        P.op("scalar", lambda sc: sc.activation(
            T.small[:, 4:5], T.small[:, 3:4], AF.Exp, scale=-0.5), inc="s_x2")
        P.wait("vector", "s_x2")
        P.op("vector", lambda v: v.tensor_scalar(
            T.kvstage[:, 0:1024], T.scratch[:], T.small[:, 4:5], None, OP.mult),
            inc="v_x2")
        P.op("vector", lambda v: v.tensor_scalar(
            atmp2, T.scratch[:], T.small[:, 4:5], None, OP.mult), inc="v_x2")
        P.wait("tensor", "v_x2")
        P.wait("tensor", "v_n2e", P.cnt("v_n2e"))
        for c in range(NDC):
            P.op("tensor", lambda t, c=c: t.transpose(
                T.pst[:, 128 * c:128 * (c + 1)], T.kvstage[:, 128 * c:128 * (c + 1)],
                T.ident_b[:]), inc="t_n2" if c == NDC - 1 else None)
        P.wait("vector", "t_n2")
        for c in range(NDC):
            P.op("vector", lambda v, c=c, j=j: v.tensor_scalar(
                T.xn_fm[:, TLOC * c + 128 * j:TLOC * c + 128 * (j + 1)],
                T.pst[:, 128 * c:128 * (c + 1)],
                T.g2p[:, c:c + 1], T.be2p[:, c:c + 1], OP.mult, OP.add),
                inc="v_n2e" if c == NDC - 1 else None)
        P.wait("tensor", "v_x2")
        P.wait("tensor", "v_rxt", P.cnt("v_rxt"))
        P.wait("tensor", "v_kv", P.cnt("v_kv"))
        for c in range(NDC):
            P.op("tensor", lambda t, c=c: t.transpose(
                T.ps[c // 4][:, 128 * (c % 4):128 * (c % 4) + 128],
                atmp2[:, 128 * c:128 * (c + 1)], T.ident_f[:]),
                inc="t_rxt" if c == NDC - 1 else None)
        P.wait("vector", "t_rxt")
        for c in range(NDC):
            P.op("vector", lambda v, c=c: v.tensor_scalar(
                T.scratch[:, 128 * c:128 * (c + 1)],
                T.ps[c // 4][:, 128 * (c % 4):128 * (c % 4) + 128],
                T.g2p[:, c:c + 1], T.be2p[:, c:c + 1], OP.mult, OP.add),
                inc="v_rxt" if c == NDC - 1 else None)
        P.wait("tensor", "v_rxt")
        P.wait("tensor", "s_gate", P.cnt("s_gate"))
        for c in range(NDC):
            P.op("tensor", lambda t, c=c, j=j: t.matmul(
                T.psm[:, 0:8],
                T.scratch[:, 128 * c:128 * (c + 1)],
                T.gw_sb[:, 8 * c:8 * (c + 1)],
                start=(c == 0), stop=(c == NDC - 1)),
                inc="t_gate" if c == NDC - 1 else None)
        w = T.w_all[:, 14 * j:14 * j + 8]
        e_t = T.scratch[:, 0:8]
        P.wait("vector", "t_gate")
        P.drain("vector")
        P.op("vector", lambda v: v.reduce_max(T.small[:, 0:1], T.psm[:, 0:8], axis=AX))
        P.drain("vector")
        P.op("vector", lambda v: v.tensor_scalar(
            T.small[:, 1:2], T.small[:, 0:1], -1.0, None, OP.mult), inc="v_gate")
        P.wait("scalar", "v_gate")
        P.op("scalar", lambda sc, e_t=e_t: sc.activation(
            e_t, T.psm[:, 0:8], AF.Exp, bias=T.small[:, 1:2],
            accum_out=T.small[:, 5:6]), inc="s_gate")
        P.wait("vector", "s_gate")
        P.op("vector", lambda v: v.reciprocal(T.small[:, 6:7], T.small[:, 5:6]))
        P.drain("vector")
        P.op("vector", lambda v, e_t=e_t: v.tensor_scalar(
            T.scratch[:, 32:40], e_t, 1.0 - 1e-6, None, OP.is_ge))
        P.drain("vector")
        P.op("vector", lambda v, e_t=e_t: v.scalar_tensor_tensor(
            T.scratch[:, 64:72], T.scratch[:, 32:40], -2.0, e_t, OP.mult, OP.add))
        P.drain("vector")
        P.op("vector", lambda v: v.reduce_max(T.small[:, 2:3], T.scratch[:, 64:72], axis=AX))
        P.drain("vector")
        P.op("vector", lambda v, e_t=e_t: v.tensor_scalar(
            T.scratch[:, 96:104], e_t, T.small[:, 2:3], None, OP.is_ge))
        P.drain("vector")
        P.op("vector", lambda v, e_t=e_t: v.tensor_tensor(
            T.scratch[:, 128:136], e_t, T.scratch[:, 96:104], OP.mult))
        P.drain("vector")
        P.op("vector", lambda v, w=w: v.tensor_scalar(
            w, T.scratch[:, 128:136], T.small[:, 6:7], None, OP.mult), inc="v_gate")
    for j in range(NTT):
        P.wait("tensor", "v_gate", P.cnt("v_gate"))
        P.wait("tensor", "v_b2t", P.cnt("v_b2t"))
        P.wait("tensor", "v_b2p", P.cnt("v_b2p"))
        P.wait("tensor", "s_gate")
        P.op("tensor", lambda t, j=j: t.transpose(
            T.psm[0:8, 0:128], T.w_all[:, 14 * j:14 * j + 8].bitcast(F32) if False else
            T.w_all[:, 14 * j:14 * j + 8], T.ident_f[:]), inc="t_b2t")
        P.wait("vector", "t_b2t")
        P.op("vector", lambda v: v.tensor_copy(T.wT16[0:8, 0:128], T.psm[0:8, 0:128]),
             inc="v_b2t")
        for nc2 in range(2):
            P.wait("gpsimd", "t_b2p", P.cnt("t_b2p"))
            P.dma("gpsimd", T.b14[0:8, 0:512], T.l1_b2[0:8, 512 * nc2:512 * (nc2 + 1)],
                  "d_b2")
            P.wait("tensor", "v_b2t")
            P.wait("tensor", "d_b2")
            P.wait("tensor", "v_b2p", P.cnt("v_b2p"))
            P.op("tensor", lambda t: t.matmul(
                T.psm[:, 0:512], T.wT16[0:8, 0:128], T.b14[0:8, 0:512],
                start=True, stop=True), inc="t_b2p")
            P.wait("vector", "t_b2p")
            P.op("vector", lambda v, j=j, n=nc2: v.tensor_tensor(
                T.ar1[:, 1024 * j + 512 * n:1024 * j + 512 * (n + 1)],
                T.ar1[:, 1024 * j + 512 * n:1024 * j + 512 * (n + 1)],
                T.psm[:, 0:512], OP.add), inc="v_b2p")
            P.drain("vector")
    P.wait("scalar", "d_b1")
    P.wait("gpsimd", "t_rxt", P.cnt("t_rxt"))
    P.wait("gpsimd", "v_gate", P.cnt("v_gate"))
    P.wait("gpsimd", "t_gate", P.cnt("t_gate"))
    qslot_free = [("v_setup", 0), ("v_setup", 0)]
    nq_total = 0
    for e in range(8):
        for q in range(8):
            qs = nq_total % 2
            base = 8192 * qs
            P.wait("gpsimd", *qslot_free[qs])
            P.wait("gpsimd", "t_mix", P.cnt("t_mix"))
            for c in range(NDC):
                P.dma("gpsimd",
                      mk_ap(T.ar2[:, 0:1], base + 128 * c,
                            [[16384, 128], [1024, 4], [1, 128]]),
                      mk_ap(T.l1_w1, e * D * 4096 + 4096 * 128 * c + 512 * q,
                            [[4096, 128], [128, 4], [1, 128]]), f"d_w1{qs}")
            P.dma("gpsimd",
                  mk_ap(T.ar2[:, 0:1], base + 4096, [[16384, 128], [1024, 4], [1, 1024]]),
                  mk_ap(T.l1_w2, e * 4096 * D + 512 * q * D,
                        [[1024, 128], [128 * 1024, 4], [1, 1024]]), f"d_w1{qs}")
            P.wait("tensor", f"d_w1{qs}")
            last_t_y = None
            for tg in range(8):
                for f in range(4):
                    ft = 4 * q + f
                    hb = T.ps[f % 2]
                    P.wait("tensor", "s_h", P.cnt("s_h") - 1)
                    for c in range(NDC):
                        P.op("tensor", lambda t, hb=hb, c=c, tg=tg, base=base, f=f: t.matmul(
                            hb[:, 0:256],
                            T.ar2[:, base + 1024 * f + 128 * c:base + 1024 * f + 128 * (c + 1)],
                            T.xn_fm[:, TLOC * c + 256 * tg:TLOC * c + 256 * (tg + 1)],
                            start=(c == 0), stop=(c == NDC - 1)),
                            inc="t_h" if c == NDC - 1 else None)
                    hslot = f % 2
                    hs = T.kvstage[:, 512 * hslot:512 * hslot + 256]
                    P.wait("scalar", "t_h")
                    P.op("scalar", lambda sc, hb=hb, hs=hs, e=e, ft=ft: sc.activation(
                        hs, hb[:, 0:256], AF.Silu,
                        bias=T.b1_sb[:, 32 * e + ft:32 * e + ft + 1]), inc="s_h")
                    P.wait("tensor", "s_h")
                    for tt in range(2):
                        for nc2 in range(2):
                            yb = T.ps[2 + tt * 2 + nc2]
                            P.op("tensor", lambda t, yb=yb, hs=hs, tt=tt, n=nc2,
                                 base=base, f=f: t.matmul(
                                yb[:, 0:512],
                                hs[:, 128 * tt:128 * (tt + 1)],
                                T.ar2[:, base + 4096 + 1024 * f + 512 * n:
                                      base + 4096 + 1024 * f + 512 * (n + 1)],
                                start=(f == 0), stop=(f == 3)),
                                inc="t_y" if (tt, nc2) == (1, 1) else None)
                P.wait("vector", "t_y")
                P.wait("vector", "v_b2p", P.cnt("v_b2p"))
                for tt in range(2):
                    jt = tg * 2 + tt
                    for nc2 in range(2):
                        yb = T.ps[2 + tt * 2 + nc2]
                        dst = T.ar1[:, 1024 * jt + 512 * nc2:1024 * jt + 512 * (nc2 + 1)]
                        P.op("vector", lambda v, yb=yb, dst=dst, jt=jt, e=e:
                             v.scalar_tensor_tensor(
                                 dst, yb[:, 0:512],
                                 T.w_all[:, 14 * jt + e:14 * jt + e + 1], dst,
                                 OP.mult, OP.add),
                             inc="v_cmb" if (tt, nc2) == (1, 1) else None)
                P.wait("tensor", "v_cmb")
                last_t_y = P.cnt("t_y")
            qslot_free[qs] = ("t_y", last_t_y)
            nq_total += 1
    for j in range(NTT):
        P.wait("scalar", "v_cmb", P.cnt("v_cmb"))
        P.drain("scalar")
        P.op("scalar", lambda sc, j=j: sc.activation(
            T.ar1b[:, 1024 * j:1024 * (j + 1)], T.ar1[:, 1024 * j:1024 * (j + 1)],
            AF.Silu), inc="s_g")


def emit_l1_reln_q(P, T):
    for j in range(NTT):
        m = T.stats[:, j:j + 1]
        rs = T.stats[:, 16 + j:16 + j + 1]
        P.wait("vector", "t_y", P.cnt("t_y"))
        P.wait("vector", "s_g", P.cnt("s_g"))
        P.drain("vector")
        P.op("vector", lambda v, j=j, m=m, rs=rs: v.tensor_scalar(
            T.kvstage[:, 0:1024], T.x_res[:, 1024 * j:1024 * (j + 1)], m, rs,
            OP.subtract, OP.mult), inc="v_ln")
        P.wait("tensor", "v_ln")
        P.wait("tensor", "v_xn", P.cnt("v_xn"))
        for c in range(NDC):
            P.op("tensor", lambda t, c=c: t.transpose(
                T.pst[:, 128 * c:128 * (c + 1)],
                T.kvstage[:, 128 * c:128 * (c + 1)], T.ident_b[:]),
                inc="t_ln" if c == NDC - 1 else None)
        P.wait("vector", "t_ln")
        for c in range(NDC):
            P.op("vector", lambda v, c=c, j=j: v.tensor_copy(
                T.xn_fm[:, TLOC * c + 128 * j:TLOC * c + 128 * (j + 1)],
                T.pst[:, 128 * c:128 * (c + 1)]),
                inc="v_xn" if c == NDC - 1 else None)
        P.wait("tensor", "v_xn")
    P.wait("gpsimd", "s_g")
    P.wait("gpsimd", "t_y", P.cnt("t_y"))
    for c in range(NDC):
        P.dma("gpsimd", T.ar2[:, 1024 * c:1024 * (c + 1)],
              T.l1_wqT[128 * c:128 * (c + 1), :], "d_wq1")
    P.dma("sync", T.bqp[:], T.l1_bq, "d_bq")
    P.wait("vector", "d_bq")
    P.wait("tensor", "d_wq1")
    P.wait("tensor", "v_xn")
    first_qn = True
    grp = 0
    for tg in range(4):
        for g in range(8):
            bank = T.ps[grp % 2]
            P.wait("tensor", "v_qe", P.cnt("v_qe") - 1)
            for c in range(NDC):
                P.op("tensor", lambda t, b=bank, c=c, g=g, tg=tg: t.matmul(
                    b[:, 0:512],
                    T.ar2[:, 1024 * c + 128 * g:1024 * c + 128 * g + 128],
                    T.xn_fm[:, TLOC * c + 512 * tg:TLOC * c + 512 * (tg + 1)],
                    start=(c == 0), stop=(c == NDC - 1)),
                    inc="t_q" if c == NDC - 1 else None)
            dst = T.ar1b[:, 16384 + 2048 * g + 512 * tg:16384 + 2048 * g + 512 * (tg + 1)]
            P.wait("vector", "t_q")
            P.op("vector", lambda v, b=bank, dst=dst, g=g: v.tensor_scalar(
                dst, b[:, 0:512], T.bqp[:, g:g + 1], None, OP.add), inc="v_qe")
            P.wait("scalar", "v_qe")
            P.wait("scalar", "t_qn", P.cnt("t_qn"))
            P.op("scalar", lambda sc, dst=dst: sc.activation(
                T.kvstage[:, 1024:1536], dst, AF.Square), inc="s_qn")
            P.wait("tensor", "s_qn")
            if g == 0:
                P.wait("tensor", "v_qn", P.cnt("v_qn"))
            P.op("tensor", lambda t, g=g: t.matmul(
                T.psm[0:1, 0:512], T.ones128[:], T.kvstage[:, 1024:1536],
                start=(g == 0), stop=(g == 7)), inc="t_qn")
            grp += 1
        P.wait("vector", "t_qn")
        P.op("vector", lambda v: v.reduce_max(T.small[0:1, 5:6], T.psm[0:1, 0:512], axis=AX))
        P.drain("vector")
        if first_qn:
            P.op("vector", lambda v: v.tensor_copy(T.small[0:1, 6:7], T.small[0:1, 5:6]),
                 inc="v_qn")
            first_qn = False
        else:
            P.op("vector", lambda v: v.tensor_tensor(
                T.small[0:1, 6:7], T.small[0:1, 6:7], T.small[0:1, 5:6], OP.max),
                 inc="v_qn")
        P.drain("vector")
    SCALE = 0.03125
    P.wait("gpsimd", "cc")
    for r in range(4):
        P.dma("gpsimd", T.scratch[0:1, 896 + r:897 + r],
              mk_ap(T.l1_kout, (1025 * r + 1024) * TLOC, [[1, 1], [1, 1]]), "d_km")
    P.wait("vector", "d_km")
    P.drain("vector")
    P.op("vector", lambda v: v.reduce_max(T.small[0:1, 7:8], T.scratch[0:1, 896:900], axis=AX))
    P.drain("vector")
    P.wait("vector", "v_qn")
    P.op("vector", lambda v: v.tensor_tensor(
        T.small[0:1, 8:9], T.small[0:1, 6:7], T.small[0:1, 7:8], OP.mult), inc="v_bnd")
    P.wait("scalar", "v_bnd")
    P.op("scalar", lambda sc: sc.activation(
        T.small[0:1, 9:10], T.small[0:1, 8:9], AF.Sqrt, scale=SCALE * SCALE * 1.05),
        inc="s_bnd")
    P.wait("vector", "s_bnd")
    P.op("vector", lambda v: v.tensor_scalar(
        T.small[0:1, 10:11], T.small[0:1, 9:10], -1.0, -1.0, OP.mult, OP.add), inc="v_bnd")
    P.wait("tensor", "v_bnd")
    P.wait("tensor", "v_qn")
    P.op("tensor", lambda t: t.matmul(
        T.psm[:, 0:1], T.ones1f[:], T.small[0:1, 10:11], start=True, stop=True), inc="t_bnd")
    P.wait("vector", "t_bnd")
    P.op("vector", lambda v: v.tensor_copy(T.boundt, T.psm[:, 0:1]), inc="v_bnd")


def emit_l1_attn(P, T):
    SCALE = 0.03125
    kb = [T.xn_fm[:, 0:1024], T.xn_fm[:, 1024:2048]]
    vb = [T.xn_fm[:, 2048:3072], T.xn_fm[:, 3072:4096]]
    prb = [T.xn_fm[:, 4224:4480], T.xn_fm[:, 4480:4736]]
    P.wait("gpsimd", "t_q")
    dka = [0, 0]
    dva = [0, 0]
    for qg in range(8):
        for kc in range(64):
            r = kc // 16
            sl = kc % 2
            P.wait("gpsimd", "t_sc", P.cnt("t_sc") - 1)
            dka[sl] = P.dma("gpsimd", kb[sl],
                            mk_ap(T.l1_kout, 1025 * r * TLOC + 128 * (kc % 16),
                                  [[TLOC, 128], [128 * TLOC, 8], [1, 128]]), f"d_ka{sl}")
            P.wait("gpsimd", "t_av", P.cnt("t_av") - 4)
            dva[sl] = P.dma("gpsimd", vb[sl],
                            T.l1_vout[2048 * r + 128 * (kc % 16):
                                      2048 * r + 128 * (kc % 16) + 128, :], f"d_va{sl}")
            P.wait("tensor", f"d_ka{sl}", dka[sl])
            P.wait("tensor", "s_ex", P.cnt("s_ex") - 1)
            pb = T.ps[kc % 2]
            for c in range(NDC):
                P.op("tensor", lambda t, pb=pb, c=c, kc=kc, qg=qg: t.matmul(
                    pb[:, 0:256],
                    kb[kc % 2][:, 128 * c:128 * (c + 1)],
                    T.ar1b[:, 16384 + 2048 * c + 256 * qg:16384 + 2048 * c + 256 * (qg + 1)],
                    start=(c == 0), stop=(c == NDC - 1)),
                    inc="t_sc" if c == NDC - 1 else None)
            P.wait("tensor", "s_ex", P.cnt("s_ex") + 1)
            P.wait("tensor", f"d_va{sl}", dva[sl])
            if kc == 0:
                P.wait("tensor", "v_nm", P.cnt("v_nm"))
                P.wait("tensor", "v_dn", P.cnt("v_dn"))
            for tt in range(2):
                for nc2 in range(2):
                    pav = T.ps[2 + 2 * tt + nc2]
                    P.op("tensor", lambda t, pav=pav, tt=tt, n=nc2, sl=sl, kc=kc: t.matmul(
                        pav[:, 0:512],
                        prb[kc % 2][:, 128 * tt:128 * (tt + 1)],
                        vb[sl][:, 512 * n:512 * (n + 1)],
                        start=(kc == 0), stop=(kc == 63)),
                        inc="t_av")
            P.op("tensor", lambda t, kc=kc: t.matmul(
                T.psm[0:1, 0:256], T.ones128[:], prb[kc % 2][:],
                start=(kc == 0), stop=(kc == 63)), inc="t_dn")
            P.wait("scalar", "t_sc")
            P.wait("scalar", "v_bnd")
            P.wait("scalar", "t_av", P.cnt("t_av") - 4)
            P.wait("scalar", "t_dn", P.cnt("t_dn") - 1)
            P.op("scalar", lambda sc, pb=pb, kc=kc: sc.activation(
                prb[kc % 2][:], pb[:, 0:256], AF.Exp, bias=T.boundt, scale=SCALE),
                inc="s_ex")
        P.wait("vector", "t_dn")
        P.op("vector", lambda v: v.tensor_copy(T.scratch[0:1, 768:1024], T.psm[0:1, 0:256]),
             inc="v_dn")
        P.wait("tensor", "v_dn")
        for tt in range(2):
            P.op("tensor", lambda t, tt=tt: t.transpose(
                T.psm[:, 256 + 128 * tt:256 + 128 * tt + 1],
                T.scratch[0:1, 768 + 128 * tt:768 + 128 * (tt + 1)], T.ident_f[0:1, 0:1]),
                inc="t_dnt" if tt == 1 else None)
        P.wait("vector", "t_dnt")
        for tt in range(2):
            P.op("vector", lambda v, tt=tt: v.reciprocal(
                T.small[:, 8 + tt:9 + tt], T.psm[:, 256 + 128 * tt:256 + 128 * tt + 1]))
        P.drain("vector")
        P.wait("vector", "t_av")
        for tt in range(2):
            jt = 2 * qg + tt
            for nc2 in range(2):
                pav = T.ps[2 + 2 * tt + nc2]
                P.op("vector", lambda v, pav=pav, tt=tt, jt=jt, n=nc2: v.scalar_tensor_tensor(
                    T.scratch[:, 0:512], pav[:, 0:512], T.small[:, 8 + tt:9 + tt],
                    T.ar1b[:, 1024 * jt + 512 * n:1024 * jt + 512 * (n + 1)],
                    OP.mult, OP.mult))
                P.drain("vector")
                P.op("vector", lambda v, jt=jt, n=nc2: v.tensor_tensor(
                    T.x_res[:, 1024 * jt + 512 * n:1024 * jt + 512 * (n + 1)],
                    T.x_res[:, 1024 * jt + 512 * n:1024 * jt + 512 * (n + 1)],
                    T.scratch[:, 0:512], OP.add),
                    inc="v_nm" if (tt, nc2) == (1, 1) else None)
                P.drain("vector")



def alloc_l2(nc, T):
    T.l2_win = nc.dram_tensor("l2_win", [D, 4096], F32, kind="ExternalInput").ap()
    T.l2_cw = nc.dram_tensor("l2_cw", [128, 16], F32, kind="ExternalInput").ap()
    T.l2_cb = nc.dram_tensor("l2_cb", [128, 4], F32, kind="ExternalInput").ap()
    T.l2_alog = nc.dram_tensor("l2_alog", [128, 64], F32, kind="ExternalInput").ap()
    T.l2_dp = nc.dram_tensor("l2_dp", [128, 4], F32, kind="ExternalInput").ap()
    T.l2_wout = nc.dram_tensor("l2_wout", [512, D], F32, kind="ExternalInput").ap()
    T.l2_wg = nc.dram_tensor("l2_wg", [D, 3072], I32, kind="ExternalInput").ap()
    T.l2_wu = nc.dram_tensor("l2_wu", [D, 3072], I32, kind="ExternalInput").ap()
    T.l2_wd = nc.dram_tensor("l2_wd", [3072, D], I32, kind="ExternalInput").ap()
    T.l2_uzin = nc.dram_tensor("l2_uzin", [4096, TLOC], BF16).ap()
    T.l2_uzout = nc.dram_tensor("l2_uzout", [4 * 4096, TLOC], BF16).ap()
    T.l2_rsin = nc.dram_tensor("l2_rsin", [S, D], F32).ap()
    T.l2_rsout = nc.dram_tensor("l2_rsout", [TLOC, D], F32).ap()


def emit_l2_xz(P, T):
    slot_free = [("v_setup", 0), ("v_setup", 0)]
    wslot_free = [("v_setup", 0), ("v_setup", 0)]
    grp = 0
    P.wait("tensor", "v_xn")
    for chcol in range(32):
        ws = chcol % 2
        P.wait("gpsimd", *wslot_free[ws])
        P.dma("gpsimd", T.ar2[:, 2048 * ws:2048 * ws + 1024],
              mk_ap(T.l2_win, 128 * chcol,
                    [[4096, 128], [128 * 4096, 8], [1, 128]]), f"d_xz{ws}")
        P.wait("tensor", f"d_xz{ws}")
        for tg in range(4):
            bank = T.ps[grp % 2]
            P.wait("tensor", "v_xz", P.cnt("v_xz") - 1)
            for c in range(NDC):
                P.op("tensor", lambda t, b=bank, c=c, ws=ws, tg=tg: t.matmul(
                    b[:, 0:512],
                    T.ar2[:, 2048 * ws + 128 * c:2048 * ws + 128 * (c + 1)],
                    T.xn_fm[:, TLOC * c + 512 * tg:TLOC * c + 512 * (tg + 1)],
                    start=(c == 0), stop=(c == NDC - 1)),
                    inc="t_xz" if c == NDC - 1 else None)
            slot = grp % 2
            stg = T.kvstage[:, 512 * slot:512 * (slot + 1)]
            P.wait("vector", "t_xz")
            P.wait("vector", *slot_free[slot])
            P.op("vector", lambda v, b=bank, stg=stg: v.tensor_copy(stg, b[:, 0:512]),
                 inc="v_xz")
            P.wait("sync", "v_xz")
            nd = P.dma("sync", T.l2_uzin[128 * chcol:128 * (chcol + 1),
                                         512 * tg:512 * (tg + 1)], stg, f"d_kst{slot}")
            slot_free[slot] = (f"d_kst{slot}", nd)
            grp += 1
        wslot_free[ws] = ("t_xz", P.cnt("t_xz"))
    P.wait("gpsimd", "d_kst0")
    P.wait("gpsimd", "d_kst1")
    if SKIP_CC:
        P.op("gpsimd", lambda g: g.nop(), inc="cc")
    else:
        P.op("gpsimd", lambda g: g.collective_compute(
            "AllGather", OP.bypass, replica_groups=GROUPS,
            ins=[T.l2_uzin.opt()], outs=[T.l2_uzout.opt()]), inc="cc")


def emit_l2_ffn(P, T):
    wbase = 12288
    wslot_free = [("v_setup", 0), ("v_setup", 0)]
    dslot_free = [("v_setup", 0), ("v_setup", 0)]
    hs_free = [("v_setup", 0), ("v_setup", 0)]
    P.wait("gpsimd", "t_xz", P.cnt("t_xz"))
    P.wait("scalar", "d_kst0")
    P.wait("scalar", "d_kst1")
    for tg in range(8):
        P.wait("gpsimd", "t_do", P.cnt("t_do"))
        for hk in range(24):
            ws = hk % 2
            P.wait("gpsimd", *wslot_free[ws])
            P.dma("gpsimd", mk_ap(T.ar2[:, 0:1], wbase + 2048 * ws,
                                  [[16384, 128], [128, 8], [1, 128]]),
                  mk_ap(T.l2_wg, 128 * hk, [[3072, 128], [128 * 3072, 8], [1, 128]]),
                  f"d_wg{ws}")
            P.dma("gpsimd", mk_ap(T.ar2[:, 0:1], wbase + 2048 * ws + 1024,
                                  [[16384, 128], [128, 8], [1, 128]]),
                  mk_ap(T.l2_wu, 128 * hk, [[3072, 128], [128 * 3072, 8], [1, 128]]),
                  f"d_wg{ws}")
            gb = T.ps[hk % 2]
            ub = T.ps[2 + hk % 2]
            P.wait("tensor", f"d_wg{ws}")
            P.wait("tensor", "v_h", P.cnt("v_h") - 1)
            for c in range(NDC):
                P.op("tensor", lambda t, gb=gb, c=c, ws=ws, tg=tg: t.matmul(
                    gb[:, 0:256],
                    T.ar2[:, wbase + 2048 * ws + 128 * c:wbase + 2048 * ws + 128 * (c + 1)],
                    T.xn_fm[:, TLOC * c + 256 * tg:TLOC * c + 256 * (tg + 1)],
                    start=(c == 0), stop=(c == NDC - 1)), inc=None)
            for c in range(NDC):
                P.op("tensor", lambda t, ub=ub, c=c, ws=ws, tg=tg: t.matmul(
                    ub[:, 0:256],
                    T.ar2[:, wbase + 2048 * ws + 1024 + 128 * c:
                          wbase + 2048 * ws + 1024 + 128 * (c + 1)],
                    T.xn_fm[:, TLOC * c + 256 * tg:TLOC * c + 256 * (tg + 1)],
                    start=(c == 0), stop=(c == NDC - 1)),
                    inc="t_gu" if c == NDC - 1 else None)
            hsl = hk % 2
            gsb = T.kvstage[:, 256 * hsl:256 * (hsl + 1)]
            P.wait("scalar", "t_gu")
            P.wait("scalar", *hs_free[hsl])
            P.op("scalar", lambda sc, gb=gb, gsb=gsb: sc.activation(
                gsb, gb[:, 0:256], AF.Silu, scale=1.0 / 127.0), inc="s_gu")
            P.wait("vector", "s_gu")
            nh = P.op("vector", lambda v, ub=ub, gsb=gsb, hk=hk: v.scalar_tensor_tensor(
                T.ar2[:, 256 * hk:256 * (hk + 1)], ub[:, 0:256], 1.0 / 127.0, gsb,
                OP.mult, OP.mult), inc="v_h")
            hs_free[hsl] = ("v_h", nh)
            wslot_free[ws] = ("t_gu", P.cnt("t_gu"))
        for nc2 in range(2):
            P.wait("gpsimd", "t_gu", P.cnt("t_gu"))
            for hk in range(24):
                ds = hk % 2
                P.wait("gpsimd", *dslot_free[ds])
                P.dma("gpsimd", T.ar2[:, wbase + 2048 * ds:wbase + 2048 * ds + 512],
                      T.l2_wd[128 * hk:128 * (hk + 1), 512 * nc2:512 * (nc2 + 1)],
                      f"d_wd{ds}")
                P.wait("tensor", f"d_wd{ds}")
                P.wait("tensor", "v_h", P.cnt("v_h"))
                if hk == 0:
                    P.wait("tensor", "s_go", P.cnt("s_go"))
                for tt in range(2):
                    P.op("tensor", lambda t, tt=tt, hk=hk, ds=ds: t.matmul(
                        T.ps[4 + tt][:, 0:512],
                        T.ar2[:, 256 * hk + 128 * tt:256 * hk + 128 * (tt + 1)],
                        T.ar2[:, wbase + 2048 * ds:wbase + 2048 * ds + 512],
                        start=(hk == 0), stop=(hk == 23)),
                        inc="t_do" if tt == 1 else None)
                dslot_free[ds] = ("t_do", P.cnt("t_do"))
            P.wait("scalar", "t_do")
            for tt in range(2):
                jt = 2 * tg + tt
                P.op("scalar", lambda sc, tt=tt, jt=jt, n=nc2: sc.activation(
                    T.ar1b[:, 1024 * jt + 512 * n:1024 * jt + 512 * (n + 1)],
                    T.ps[4 + tt][:, 0:512], AF.Silu, scale=1.0 / 127.0), inc="s_go")


def emit_l2_scan(P, T):
    P.wait("sync", "v_xn", P.cnt("v_xn"))
    P.wait("sync", "s_ln", P.cnt("s_ln"))
    P.wait("sync", "v_gate", P.cnt("v_gate"))
    P.wait("sync", "v_cmb", P.cnt("v_cmb"))
    P.wait("sync", "t_b2t", P.cnt("t_b2t"))
    P.dma("sync", T.w_all[:, 16:32], T.l2_cw, "d_l2p")
    P.dma("sync", T.w_all[:, 32:36], T.l2_cb, "d_l2p")
    P.dma("sync", T.w_all[:, 36:40], T.l2_dp, "d_l2p")
    P.dma("sync", T.stats[:, 0:64], T.l2_alog, "d_l2p")
    P.wait("scalar", "d_l2p")
    P.wait("scalar", "s_go", P.cnt("s_go"))
    P.drain("scalar")
    P.op("scalar", lambda sc: sc.activation(T.stats[:, 0:64], T.stats[:, 0:64], AF.Exp),
         inc="s_ae")
    P.wait("gpsimd", "t_do", P.cnt("t_do"))
    P.wait("gpsimd", "v_h", P.cnt("v_h"))
    for ct in range(4):
        P.dma("gpsimd", T.ar2[:, 1024 * ct:1024 * (ct + 1)],
              T.l2_wout[128 * ct:128 * (ct + 1), :], "d_wo")
    u_sb = T.xn_fm[:, 0:4104]
    ut_sb = T.xn_fm[:, 4104:8200]
    z_sb = T.xn_fm[:, 8200:12296]
    accs = T.ar1[:, 8192:12288]
    yac = T.ar1[:, 12288:16384]
    carry = T.w_all[:, 64:80]
    sign_b = mk_ap(T.sign_sb[:], 0, [[512, 128], [0, 8], [1, 512]])
    P.wait("gpsimd", "cc")
    P.wait("gpsimd", "t_gu", P.cnt("t_gu"))
    P.wait("vector", "s_go", P.cnt("s_go"))
    stage_free = [("v_setup", 0), ("v_setup", 0)]
    for ct in range(4):
        for th in range(2):
            P.wait("vector", "d_u", P.cnt("d_u"))
            P.drain("vector")
            if th == 0:
                P.op("vector", lambda v: v.memset(u_sb[:, 0:4], 0.0), inc="v_up")
            else:
                P.op("vector", lambda v: v.tensor_copy(u_sb[:, 0:4], u_sb[:, 4096:4100]),
                     inc="v_up")
            P.drain("vector")
            P.wait("gpsimd", "v_up")
            P.wait("gpsimd", "v_cv", P.cnt("v_cv"))
            P.wait("gpsimd", "t_wo", P.cnt("t_wo"))
            h = P.sem("d_u")

            def uz_load(g, ct=ct, th=th, h=h, u_sb=u_sb, z_sb=z_sb, T=T):
                for arm in g.switch_core_id(n=128):
                    r = (arm.logical if hasattr(arm, "logical") else int(arm)) % 4
                    for k, rr in enumerate((2 * th, 2 * th + 1)):
                        g.dma_start(
                            out=u_sb[:, 4 + 2048 * k:4 + 2048 * (k + 1)],
                            in_=T.l2_uzout[4096 * rr + 512 * r + 128 * ct:
                                           4096 * rr + 512 * r + 128 * (ct + 1), :]
                        ).then_inc(h, 16)
                        g.dma_start(
                            out=z_sb[:, 2048 * k:2048 * (k + 1)],
                            in_=T.l2_uzout[4096 * rr + 2048 + 512 * r + 128 * ct:
                                           4096 * rr + 2048 + 512 * r + 128 * (ct + 1), :]
                        ).then_inc(h, 16)
            P.items["gpsimd"].append(uz_load)
            P.counts[SEM_ALIAS.get("d_u", "d_u")] = P.cnt("d_u") + 64
            P.wait("vector", "d_l2p")
            P.wait("vector", "d_u")
            P.wait("vector", "v_ya", P.cnt("v_ya"))
            P.drain("vector")
            P.op("vector", lambda v, ct=ct: v.tensor_scalar(
                accs[:, 0:4096], u_sb[:, 1:4097], T.w_all[:, 16 + 4 * ct:17 + 4 * ct],
                None, OP.mult))
            P.drain("vector")
            for jj in range(1, 4):
                P.op("vector", lambda v, jj=jj, ct=ct: v.scalar_tensor_tensor(
                    accs[:, 0:4096], u_sb[:, 1 + jj:4097 + jj],
                    T.w_all[:, 16 + 4 * ct + jj:17 + 4 * ct + jj],
                    accs[:, 0:4096], OP.mult, OP.add))
                P.drain("vector")
            P.op("vector", lambda v, ct=ct: v.scalar_tensor_tensor(
                mk_ap(ut_sb, 0, [[16384, 128], [512, 8], [1, 512]]),
                mk_ap(accs, 0, [[16384, 128], [512, 8], [1, 512]]),
                T.w_all[:, 32 + ct:33 + ct], sign_b, OP.add, OP.mult), inc="v_cv")
            P.drain("vector")
            P.wait("vector", "s_ae")
            for s_ in range(16):
                a_bc = T.stats[:, 16 * ct + s_:16 * ct + s_ + 1].broadcast_to([128, 4096])
                if th == 0:
                    P.op("vector", lambda v, a_bc=a_bc: v.tensor_tensor_scan(
                        accs[:, 0:4096], a_bc, ut_sb[:, 0:4096], 0.0,
                        OP.mult, OP.add))
                else:
                    P.op("vector", lambda v, a_bc=a_bc, s_=s_: v.tensor_tensor_scan(
                        accs[:, 0:4096], a_bc, ut_sb[:, 0:4096], carry[:, s_:s_ + 1],
                        OP.mult, OP.add))
                P.drain("vector")
                P.op("vector", lambda v, s_=s_: v.tensor_copy(
                    carry[:, s_:s_ + 1], accs[:, 4095:4096]))
                P.drain("vector")
                if s_ == 0:
                    P.op("vector", lambda v: v.tensor_copy(yac[:, 0:4096], accs[:, 0:4096]))
                else:
                    P.op("vector", lambda v: v.tensor_tensor(
                        yac[:, 0:4096], yac[:, 0:4096], accs[:, 0:4096], OP.add))
                P.drain("vector")
            P.op("vector", lambda v: v.tensor_copy(T.small[:, 11:12], T.small[:, 11:12])
                 if False else v.tensor_copy(T.small[:, 11:12], carry[:, 15:16]),
                 inc="v_ya")
            for cc5 in range(8):
                zc = z_sb[:, 512 * cc5:512 * (cc5 + 1)]
                P.wait("scalar", "v_ya", P.cnt("v_ya"))
                P.wait("scalar", "v_ys", P.cnt("v_ys"))
                P.wait("scalar", "d_rs0", P.cnt("d_rs0"))
                P.wait("scalar", "d_rs1", P.cnt("d_rs1"))
                P.op("scalar", lambda sc, zc=zc: sc.activation(
                    T.scratch[:, 0:512], zc, AF.Silu), inc="s_sz")
                P.wait("vector", "s_sz")
                P.drain("vector")
                P.op("vector", lambda v: v.tensor_tensor(
                    T.scratch[:, 512:1024], T.scratch[:, 0:512], T.sign_sb[:], OP.mult))
                P.drain("vector")
                P.op("vector", lambda v, cc5=cc5, ct=ct: v.scalar_tensor_tensor(
                    T.scratch[:, 0:512], ut_sb[:, 512 * cc5:512 * (cc5 + 1)],
                    T.w_all[:, 36 + ct:37 + ct], yac[:, 512 * cc5:512 * (cc5 + 1)],
                    OP.mult, OP.add))
                P.drain("vector")
                P.op("vector", lambda v, zc=zc: v.tensor_tensor(
                    zc, T.scratch[:, 0:512], T.scratch[:, 512:1024], OP.mult),
                    inc="v_ys")
                P.drain("vector")
            for tt in range(32):
                P.wait("tensor", "v_ys", P.cnt("v_ys"))
                P.wait("tensor", "d_wo")
                for nc2 in range(2):
                    bank = T.ps[(tt * 2 + nc2) % 2]
                    P.wait("tensor", "v_woe", P.cnt("v_woe") - 1)
                    P.op("tensor", lambda t, b=bank, tt=tt, n=nc2, ct=ct: t.matmul(
                        b[:, 0:512],
                        z_sb[:, 128 * tt:128 * (tt + 1)],
                        T.ar2[:, 1024 * ct + 512 * n:1024 * ct + 512 * (n + 1)],
                        start=True, stop=True), inc="t_wo")
                    sl = (tt * 2 + nc2) % 2
                    stg = T.scratch[:, 512 * sl:512 * (sl + 1)]
                    P.wait("vector", "t_wo")
                    P.wait("vector", *stage_free[sl])
                    P.op("vector", lambda v, b=bank, stg=stg: v.tensor_copy(stg, b[:, 0:512]),
                         inc="v_woe")
                    P.wait("gpsimd", "v_woe")
                    tG = th * 4096 + 128 * tt
                    nd = P.dma("gpsimd", T.l2_rsin[tG:tG + 128, 512 * nc2:512 * (nc2 + 1)],
                               stg, f"d_rs{sl}",
                               **({} if ct == 0 else {"accum_op": OP.add}))
                    stage_free[sl] = (f"d_rs{sl}", nd)


def emit_l2_rs(P, T):
    P.wait("gpsimd", "d_rs0")
    P.wait("gpsimd", "d_rs1")
    if SKIP_CC:
        P.op("gpsimd", lambda g: g.nop(), inc="cc")
    else:
        P.op("gpsimd", lambda g: g.collective_compute(
            "ReduceScatter", OP.add, replica_groups=GROUPS,
            ins=[T.l2_rsin.opt()], outs=[T.l2_rsout.opt()]), inc="cc")
    P.wait("sync", "cc")
    for j in range(NTT):
        P.wait("sync", "v_fin", P.cnt("v_fin"))
        P.dma("sync", T.scratch[:], T.l2_rsout[128 * j:128 * (j + 1), :], "d_s")
        P.wait("vector", "d_s")
        P.wait("vector", "s_go", P.cnt("s_go"))
        P.drain("vector")
        P.op("vector", lambda v, j=j: v.tensor_tensor(
            T.scratch[:], T.scratch[:], T.ar1b[:, 1024 * j:1024 * (j + 1)], OP.mult))
        P.drain("vector")
        P.op("vector", lambda v, j=j: v.tensor_tensor(
            T.x_res[:, 1024 * j:1024 * (j + 1)], T.x_res[:, 1024 * j:1024 * (j + 1)],
            T.scratch[:], OP.add), inc="v_fin")
        P.drain("vector")



def alloc_l3(nc, T):
    for nm, shp in [("l3_wdq", [D, 512]), ("l3_wuq", [512, D]), ("l3_wqr", [512, D]),
                    ("l3_wkr", [D, D]), ("l3_wdkv", [D, 512]), ("l3_wuk", [512, D]),
                    ("l3_wuv", [512, D]), ("l3_wo", [D, D]), ("l3_gw", [D, 8]),
                    ("l3_ew", [8, D, D]), ("l3_eb", [8, D]), ("l3_cos", [TLOC, D]),
                    ("l3_sin", [TLOC, D]), ("l3_bdq", [128, 4]), ("l3_buq", [128, 8]),
                    ("l3_bdkv", [128, 4]), ("l3_buk", [128, 8]),
                    ("l3_bqr", [1, D]), ("l3_bkr", [1, D]), ("l3_buv", [1, D]),
                    ("l3_bo", [1, D]), ("l3_gb", [1, 8])]:
        setattr(T, nm, nc.dram_tensor(nm, shp, F32, kind="ExternalInput").ap())
    T.l3_kin = nc.dram_tensor("l3_kin", [1025, TLOC], BF16).ap()
    T.l3_kout = nc.dram_tensor("l3_kout", [4 * 1025, TLOC], BF16).ap()
    T.l3_vin = nc.dram_tensor("l3_vin", [TLOC, D], BF16).ap()
    T.l3_vout = nc.dram_tensor("l3_vout", [4 * TLOC, D], BF16).ap()


def emit_l3_ct(P, T, wdq_ap, bias_dram, wsem, vsem):
    P.wait("gpsimd", "t_kv", P.cnt("t_kv"))
    P.wait("gpsimd", "t_kr", P.cnt("t_kr"))
    P.wait("gpsimd", "t_up", P.cnt("t_up"))
    for c in range(NDC):
        P.dma("gpsimd", T.ar2[:, 12288 + 512 * c:12288 + 512 * (c + 1)],
              wdq_ap[128 * c:128 * (c + 1), :], wsem)
    P.dma("sync", T.bkp[:, 0:4], bias_dram, "d_bp3")
    P.wait("tensor", wsem)
    P.wait("tensor", "v_xn")
    for s_ in ("v_ckv", "v_ctq", "v_kc", "v_qc", "v_kv", "v_rpe"):
        P.wait("tensor", s_, P.cnt(s_))
    grp = 0
    for ck in range(4):
        for tg in range(4):
            bank = T.ps[grp % 2]
            P.wait("tensor", vsem, P.cnt(vsem) - 1)
            for c in range(NDC):
                P.op("tensor", lambda t, b=bank, c=c, ck=ck, tg=tg: t.matmul(
                    b[:, 0:512],
                    T.ar2[:, 12288 + 512 * c + 128 * ck:12288 + 512 * c + 128 * (ck + 1)],
                    T.xn_fm[:, TLOC * c + 512 * tg:TLOC * c + 512 * (tg + 1)],
                    start=(c == 0), stop=(c == NDC - 1)),
                    inc="t_ct" if c == NDC - 1 else None)
            P.wait("vector", "t_ct")
            P.wait("vector", "d_bp3")
            P.op("vector", lambda v, b=bank, ck=ck, tg=tg: v.tensor_scalar(
                T.ar2[:, 2048 * ck + 512 * tg:2048 * ck + 512 * (tg + 1)],
                b[:, 0:512], T.bkp[:, ck:ck + 1], None, OP.add), inc=vsem)
            grp += 1


def emit_l3_up_pair(P, T, wu_ap, bup_dram, dst_base, vsem):
    P.wait("gpsimd", "t_ct", P.cnt("t_ct"))
    for c in range(4):
        P.dma("gpsimd", T.ar2[:, 12288 + 1024 * c:12288 + 1024 * (c + 1)],
              wu_ap[128 * c:128 * (c + 1), :], "d_wu3")
    P.dma("sync", T.bqp[:], bup_dram, "d_bp3b")
    P.wait("tensor", "d_wu3")
    for s_ in ("v_ckv", "v_ctq", "v_kc", "v_qc", "v_kv", "v_rpe"):
        P.wait("tensor", s_, P.cnt(s_))
    grp = 0
    for g in range(8):
        for tg in range(4):
            bank = T.ps[grp % 2]
            P.wait("tensor", vsem, P.cnt(vsem) - 1)
            for c in range(4):
                P.op("tensor", lambda t, b=bank, c=c, g=g, tg=tg: t.matmul(
                    b[:, 0:512],
                    T.ar2[:, 12288 + 1024 * c + 128 * g:12288 + 1024 * c + 128 * (g + 1)],
                    T.ar2[:, 2048 * c + 512 * tg:2048 * c + 512 * (tg + 1)],
                    start=(c == 0), stop=(c == 3)),
                    inc="t_up" if c == 3 else None)
            P.wait("vector", "t_up")
            P.wait("vector", "d_bp3b")
            P.op("vector", lambda v, b=bank, g=g, tg=tg, dst_base=dst_base: v.tensor_scalar(
                T.ar1b[:, dst_base + 2048 * g + 512 * tg:dst_base + 2048 * g + 512 * (tg + 1)],
                b[:, 0:512], T.bqp[:, g:g + 1], None, OP.add), inc=vsem)
            grp += 1


def emit_l3_rope_add(P, T, wr_ap, br_ap, rhs_is_ct, dst_base):
    ropef = T.ar2[:, 8192:12288].bitcast(F32)
    nk = 8 if not rhs_is_ct else 4
    for nc2 in range(2):
        P.wait("gpsimd", "t_kr", P.cnt("t_kr"))
        P.wait("gpsimd", "t_up", P.cnt("t_up"))
        P.wait("gpsimd", "t_ct", P.cnt("t_ct"))
        for c in range(nk):
            P.dma("gpsimd", T.ar2[:, 12288 + 512 * c:12288 + 512 * (c + 1)],
                  wr_ap[128 * c:128 * (c + 1), 512 * nc2:512 * (nc2 + 1)], "d_wr3")
        P.dma("gpsimd", T.nstage[0:1, 0:1024], br_ap, "d_wr3")
        P.wait("tensor", "d_wr3")
        for j in range(NTT):
            P.wait("gpsimd", "v_rp", P.cnt("v_rp"))
            P.dma("gpsimd", ropef[:, 512 * nc2:512 * (nc2 + 1)],
                  T.l3_cos[128 * j:128 * (j + 1), 512 * nc2:512 * (nc2 + 1)], "d_rt")
            P.dma("gpsimd", ropef[:, 1024 + 512 * nc2:1024 + 512 * (nc2 + 1)],
                  T.l3_sin[128 * j:128 * (j + 1), 512 * nc2:512 * (nc2 + 1)], "d_rt")
            bank = T.ps[2 + j % 2]
            P.wait("tensor", "v_rp", P.cnt("v_rp") - 1)
            for c in range(nk):
                rhs_src = (T.xn_fm[:, TLOC * c + 128 * j:TLOC * c + 128 * (j + 1)]
                           if not rhs_is_ct else
                           T.ar2[:, 2048 * c + 128 * j:2048 * c + 128 * (j + 1)])
                P.op("tensor", lambda t, b=bank, s=rhs_src, c=c: t.matmul(
                    b[:, 0:512], s, T.ar2[:, 12288 + 512 * c:12288 + 512 * (c + 1)],
                    start=(c == 0), stop=False), inc=None)
            P.op("tensor", lambda t, b=bank, n=nc2: t.matmul(
                b[:, 0:512], T.ones1[:], T.nstage[0:1, 512 * n:512 * (n + 1)],
                start=False, stop=True), inc="t_kr")
            c2 = ropef[:, 512 * nc2:512 * (nc2 + 1)]
            s2 = ropef[:, 1024 + 512 * nc2:1024 + 512 * (nc2 + 1)]
            P.wait("vector", "t_kr")
            P.wait("vector", "d_rt")
            P.drain("vector")
            P.op("vector", lambda v, b=bank, c2=c2: v.tensor_tensor(
                T.scratch[:, 0:512], b[:, 0:512], c2, OP.mult))
            P.drain("vector")
            P.op("vector", lambda v, b=bank: v.tensor_copy(
                mk_ap(T.scratch[:], 512, [[1024, 128], [2, 256]]),
                mk_ap(b[:, 0:1], 1, [[512, 128], [2, 256]])))
            P.op("vector", lambda v, b=bank: v.tensor_copy(
                mk_ap(T.scratch[:], 513, [[1024, 128], [2, 256]]),
                mk_ap(b[:, 0:1], 0, [[512, 128], [2, 256]])))
            P.drain("vector")
            P.op("vector", lambda v, s2=s2: v.tensor_tensor(
                T.scratch[:, 512:1024], T.scratch[:, 512:1024], s2, OP.mult))
            P.drain("vector")
            P.op("vector", lambda v: v.tensor_tensor(
                T.kvstage[:, 0:512], T.scratch[:, 0:512], T.scratch[:, 512:1024], OP.add),
                inc="v_rp")
            P.wait("tensor", "v_rp")
            P.wait("tensor", "v_rpe", P.cnt("v_rpe"))
            for c4 in range(4):
                P.op("tensor", lambda t, c4=c4: t.transpose(
                    T.pst[:, 128 * c4:128 * (c4 + 1)],
                    T.kvstage[:, 128 * c4:128 * (c4 + 1)], T.ident_b[:]),
                    inc="t_rpt" if c4 == 3 else None)
            P.wait("vector", "t_rpt")
            for c4 in range(4):
                gp = nc2 * 4 + c4
                dst = T.ar1b[:, dst_base + 2048 * gp + 128 * j:
                             dst_base + 2048 * gp + 128 * (j + 1)]
                P.op("vector", lambda v, dst=dst, c4=c4: v.tensor_tensor(
                    dst, dst, T.pst[:, 128 * c4:128 * (c4 + 1)], OP.add),
                    inc="v_rpe" if c4 == 3 else None)
            P.wait("tensor", "v_rpe")


def emit_l3_kv(P, T):
    emit_l3_ct(P, T, T.l3_wdkv, T.l3_bdkv, "d_wd3", "v_ckv")
    emit_l3_up_pair(P, T, T.l3_wuk, T.l3_buk, 16384, "v_kc")
    emit_l3_rope_add(P, T, T.l3_wkr, T.l3_bkr, False, 16384)
    slot_free = [("v_setup", 0), ("v_setup", 0)]
    first_nrm = True
    sgrp = 0
    P.wait("vector", "v_rpe", P.cnt("v_rpe"))
    for g in range(8):
        for tg in range(4):
            src = T.ar1b[:, 16384 + 2048 * g + 512 * tg:16384 + 2048 * g + 512 * (tg + 1)]
            slot = sgrp % 2
            P.wait("scalar", "v_rpe", P.cnt("v_rpe"))
            P.wait("scalar", "t_nrm", P.cnt("t_nrm"))
            P.op("scalar", lambda sc, src=src: sc.activation(
                T.kvstage[:, 1024:1536], src, AF.Square), inc="s_kv")
            P.wait("tensor", "s_kv")
            P.wait("tensor", "v_nrm", P.cnt("v_nrm"))
            P.op("tensor", lambda t: t.matmul(
                T.psm[0:1, 0:512], T.ones128[:], T.kvstage[:, 1024:1536],
                start=True, stop=True), inc="t_nrm")
            P.wait("vector", "t_nrm")
            P.op("vector", lambda v: v.reduce_max(T.small[0:1, 3:4], T.psm[0:1, 0:512],
                                                  axis=AX))
            P.drain("vector")
            if first_nrm:
                P.op("vector", lambda v: v.tensor_copy(T.small[0:1, 4:5], T.small[0:1, 3:4]),
                     inc="v_nrm")
                first_nrm = False
            else:
                P.op("vector", lambda v: v.tensor_tensor(
                    T.small[0:1, 4:5], T.small[0:1, 4:5], T.small[0:1, 3:4], OP.max),
                     inc="v_nrm")
            P.drain("vector")
            P.wait("sync", *slot_free[slot])
            P.wait("sync", "v_rpe")
            nd = P.dma("sync", T.l3_kin[128 * g:128 * (g + 1), 512 * tg:512 * (tg + 1)],
                       src, f"d_kst{slot}")
            slot_free[slot] = (f"d_kst{slot}", nd)
            sgrp += 1
    P.wait("vector", "v_nrm")
    P.op("vector", lambda v: v.tensor_copy(T.nstage[0:1, 0:1], T.small[0:1, 4:5]),
         inc="v_kmax")
    P.wait("sync", "v_kmax")
    P.dma("sync", mk_ap(T.l3_kin, 1024 * TLOC, [[1, 1], [1, 1]]), T.nstage[0:1, 0:1],
          "d_nst")
    P.wait("gpsimd", "t_up", P.cnt("t_up"))
    P.wait("gpsimd", "t_kr", P.cnt("t_kr"))
    for c in range(4):
        P.dma("gpsimd", T.ar2[:, 12288 + 1024 * c:12288 + 1024 * (c + 1)],
              T.l3_wuv[128 * c:128 * (c + 1), :], "d_wv3")
    P.wait("gpsimd", "d_nst", P.cnt("d_nst"))
    P.dma("gpsimd", T.nstage[0:1, 0:1024], T.l3_buv, "d_wv3")
    P.wait("tensor", "d_wv3")
    for j in range(NTT):
        for nc2 in range(2):
            bank = T.ps[(j * 2 + nc2) % 2]
            P.wait("tensor", "v_kv", P.cnt("v_kv") - 1)
            for c in range(4):
                P.op("tensor", lambda t, b=bank, c=c, j=j, n=nc2: t.matmul(
                    b[:, 0:512],
                    T.ar2[:, 2048 * c + 128 * j:2048 * c + 128 * (j + 1)],
                    T.ar2[:, 12288 + 1024 * c + 512 * n:12288 + 1024 * c + 512 * (n + 1)],
                    start=(c == 0), stop=False), inc=None)
            P.op("tensor", lambda t, b=bank, n=nc2: t.matmul(
                b[:, 0:512], T.ones1[:], T.nstage[0:1, 512 * n:512 * (n + 1)],
                start=False, stop=True), inc="t_kv")
            slot = (j * 2 + nc2) % 2
            stg = T.kvstage[:, 512 * slot:512 * (slot + 1)]
            P.wait("vector", "t_kv")
            P.wait("vector", *slot_free[slot])
            P.op("vector", lambda v, b=bank, stg=stg: v.tensor_copy(stg, b[:, 0:512]),
                 inc="v_kv")
            P.wait("sync", "v_kv")
            nd = P.dma("sync", T.l3_vin[128 * j:128 * (j + 1), 512 * nc2:512 * (nc2 + 1)],
                       stg, f"d_kst{slot}")
            slot_free[slot] = (f"d_kst{slot}", nd)
    P.wait("gpsimd", "d_kst0")
    P.wait("gpsimd", "d_kst1")
    P.wait("gpsimd", "d_nst")
    if SKIP_CC:
        P.op("gpsimd", lambda g: g.nop(), inc="cc")
    else:
        P.op("gpsimd", lambda g: g.collective_compute(
            "AllGather", OP.bypass, replica_groups=GROUPS,
            ins=[T.l3_kin.opt()], outs=[T.l3_kout.opt()]), inc="cc")
    if SKIP_CC:
        P.op("gpsimd", lambda g: g.nop(), inc="cc")
    else:
        P.op("gpsimd", lambda g: g.collective_compute(
            "AllGather", OP.bypass, replica_groups=GROUPS,
            ins=[T.l3_vin.opt()], outs=[T.l3_vout.opt()]), inc="cc")


def emit_l3_ffn(P, T):
    for c in range(NDC):
        P.dma("sync", T.gw_sb[:, 8 * c:8 * (c + 1)], T.l3_gw[128 * c:128 * (c + 1), :],
              "d_gw3")
    P.dma("sync", T.stats[0:1, 56:64], T.l3_gb, "d_gw3")
    P.wait("tensor", "d_gw3")
    P.wait("tensor", "v_xn")
    P.wait("vector", "d_x")
    for j in range(NTT):
        m = T.stats[:, j:j + 1]
        rs = T.stats[:, 16 + j:16 + j + 1]
        P.wait("vector", "s_ln", P.cnt("s_ln"))
        P.wait("vector", "t_gxp", P.cnt("t_gxp"))
        P.drain("vector")
        P.op("vector", lambda v, j=j, m=m, rs=rs: v.tensor_scalar(
            T.scratch[:], T.x_res[:, 1024 * j:1024 * (j + 1)], m, rs,
            OP.subtract, OP.mult), inc="v_gxn")
        P.wait("tensor", "v_gxn")
        P.wait("tensor", "v_kv", P.cnt("v_kv"))
        P.wait("tensor", "v_ckv", P.cnt("v_ckv"))
        for c in range(NDC):
            P.op("tensor", lambda t, c=c: t.transpose(
                T.ps[c // 4][:, 128 * (c % 4):128 * (c % 4) + 128],
                T.scratch[:, 128 * c:128 * (c + 1)], T.ident_f[:]),
                inc="t_gxp" if c == NDC - 1 else None)
        P.wait("vector", "t_gxp")
        for c in range(NDC):
            P.op("vector", lambda v, c=c: v.tensor_copy(
                T.atmp[:, 128 * c:128 * (c + 1)],
                T.ps[c // 4][:, 128 * (c % 4):128 * (c % 4) + 128]),
                inc="v_gxt" if c == NDC - 1 else None)
        P.wait("tensor", "v_gate", P.cnt("v_gate"))
        P.wait("tensor", "v_gxt")
        for c in range(NDC):
            P.op("tensor", lambda t, c=c, j=j: t.matmul(
                T.psm[:, 0:8],
                T.atmp[:, 128 * c:128 * (c + 1)],
                T.gw_sb[:, 8 * c:8 * (c + 1)],
                start=(c == 0), stop=False), inc=None)
        P.op("tensor", lambda t: t.matmul(
            T.psm[:, 0:8], T.ones1f[:], T.stats[0:1, 56:64],
            start=False, stop=True), inc="t_gate")
        w = T.w_all[:, 14 * j:14 * j + 8]
        P.wait("vector", "t_gate")
        P.drain("vector")
        P.op("vector", lambda v: v.reduce_max(T.small[:, 0:1], T.psm[:, 0:8], axis=AX))
        P.drain("vector")
        P.op("vector", lambda v, w=w: v.tensor_scalar(
            w, T.psm[:, 0:8], T.small[:, 0:1], None, OP.is_ge), inc="v_gate")
        P.drain("vector")
    for j in range(NTT):
        P.wait("tensor", "v_gate", P.cnt("v_gate"))
        P.wait("tensor", "v_b2t", P.cnt("v_b2t"))
        P.wait("tensor", "v_b2p", P.cnt("v_b2p"))
        P.op("tensor", lambda t, j=j: t.transpose(
            T.psm[0:8, 0:128], T.w_all[:, 14 * j:14 * j + 8], T.ident_f[:]), inc="t_b2t")
        P.wait("vector", "t_b2t")
        P.op("vector", lambda v: v.tensor_copy(T.wT16[0:8, 0:128], T.psm[0:8, 0:128]),
             inc="v_b2t")
        for nc2 in range(2):
            P.wait("gpsimd", "t_b2p", P.cnt("t_b2p"))
            P.dma("gpsimd", T.b14[0:8, 0:512], T.l3_eb[0:8, 512 * nc2:512 * (nc2 + 1)],
                  "d_b2")
            P.wait("tensor", "v_b2t")
            P.wait("tensor", "d_b2")
            P.wait("tensor", "v_b2p", P.cnt("v_b2p"))
            P.op("tensor", lambda t: t.matmul(
                T.psm[:, 0:512], T.wT16[0:8, 0:128], T.b14[0:8, 0:512],
                start=True, stop=True), inc="t_b2p")
            P.wait("vector", "t_b2p")
            P.op("vector", lambda v, j=j, n=nc2: v.tensor_copy(
                T.ar1[:, 1024 * j + 512 * n:1024 * j + 512 * (n + 1)], T.psm[:, 0:512]),
                 inc="v_b2p")
    eslot_free = [("v_setup", 0), ("v_setup", 0)]
    for e in range(8):
        es_ = e % 2
        P.wait("gpsimd", *eslot_free[es_])
        P.wait("gpsimd", "t_rpt", P.cnt("t_rpt"))
        for c in range(NDC):
            P.dma("gpsimd", T.ar2[:, 8192 * es_ + 1024 * c:8192 * es_ + 1024 * (c + 1)],
                  mk_ap(T.l3_ew, e * D * D + 128 * c * D, [[1024, 128], [1, 1024]]),
                  f"d_we{es_}")
        P.wait("tensor", f"d_we{es_}")
        last = None
        for tt in range(16):
            for nc2 in range(2):
                yb = T.ps[2 + (tt * 2 + nc2) % 4]
                P.wait("tensor", "v_cmb", P.cnt("v_cmb") - 3)
                for c in range(NDC):
                    P.op("tensor", lambda t, yb=yb, c=c, tt=tt, n=nc2, es_=es_: t.matmul(
                        yb[:, 0:512],
                        T.xn_fm[:, TLOC * c + 128 * tt:TLOC * c + 128 * (tt + 1)],
                        T.ar2[:, 8192 * es_ + 1024 * c + 512 * n:
                              8192 * es_ + 1024 * c + 512 * (n + 1)],
                        start=(c == 0), stop=(c == NDC - 1)),
                        inc="t_y" if c == NDC - 1 else None)
                dst = T.ar1[:, 1024 * tt + 512 * nc2:1024 * tt + 512 * (nc2 + 1)]
                P.wait("vector", "t_y")
                P.wait("vector", "v_b2p", P.cnt("v_b2p"))
                P.op("vector", lambda v, yb=yb, dst=dst, tt=tt, e=e: v.scalar_tensor_tensor(
                    dst, yb[:, 0:512], T.w_all[:, 14 * tt + e:14 * tt + e + 1], dst,
                    OP.mult, OP.add), inc="v_cmb")
                last = P.cnt("t_y")
        eslot_free[es_] = ("t_y", last)
    for j in range(NTT):
        P.wait("scalar", "v_cmb", P.cnt("v_cmb"))
        P.drain("scalar")
        P.op("scalar", lambda sc, j=j: sc.activation(
            T.ar1b[:, 1024 * j:1024 * (j + 1)], T.ar1[:, 1024 * j:1024 * (j + 1)],
            AF.Silu), inc="s_g")


def emit_l3_q(P, T):
    P.wait("gpsimd", "t_y", P.cnt("t_y"))
    P.wait("gpsimd", "s_g", P.cnt("s_g"))
    emit_l3_ct(P, T, T.l3_wdq, T.l3_bdq, "d_wq3", "v_ctq")
    emit_l3_up_pair(P, T, T.l3_wuq, T.l3_buq, 16384, "v_qc")
    emit_l3_rope_add(P, T, T.l3_wqr, T.l3_bqr, True, 16384)
    SCALE = float(1.0 / np.sqrt(1040.0))
    first_qn = True
    P.wait("scalar", "v_rpe", P.cnt("v_rpe"))
    for g in range(8):
        for tg in range(4):
            src = T.ar1b[:, 16384 + 2048 * g + 512 * tg:16384 + 2048 * g + 512 * (tg + 1)]
            P.wait("scalar", "t_qn", P.cnt("t_qn"))
            P.op("scalar", lambda sc, src=src: sc.activation(
                T.kvstage[:, 1024:1536], src, AF.Square), inc="s_qn")
            P.wait("tensor", "s_qn")
            P.wait("tensor", "v_qn", P.cnt("v_qn"))
            P.op("tensor", lambda t: t.matmul(
                T.psm[0:1, 0:512], T.ones128[:], T.kvstage[:, 1024:1536],
                start=True, stop=True), inc="t_qn")
            P.wait("vector", "t_qn")
            P.op("vector", lambda v: v.reduce_max(T.small[0:1, 5:6], T.psm[0:1, 0:512],
                                                  axis=AX))
            P.drain("vector")
            if first_qn:
                P.op("vector", lambda v: v.tensor_copy(T.small[0:1, 6:7], T.small[0:1, 5:6]),
                     inc="v_qn")
                first_qn = False
            else:
                P.op("vector", lambda v: v.tensor_tensor(
                    T.small[0:1, 6:7], T.small[0:1, 6:7], T.small[0:1, 5:6], OP.max),
                     inc="v_qn")
            P.drain("vector")
    P.wait("gpsimd", "cc")
    P.wait("gpsimd", "v_gate", P.cnt("v_gate"))
    for r in range(4):
        P.dma("gpsimd", T.w_all[0:1, 218 + r:219 + r],
              mk_ap(T.l3_kout, (1025 * r + 1024) * TLOC, [[1, 1], [1, 1]]), "d_km")
    P.wait("vector", "d_km")
    P.drain("vector")
    P.op("vector", lambda v: v.reduce_max(T.small[0:1, 7:8], T.w_all[0:1, 218:222],
                                          axis=AX))
    P.drain("vector")
    P.wait("vector", "v_qn")
    P.op("vector", lambda v: v.tensor_tensor(
        T.small[0:1, 8:9], T.small[0:1, 6:7], T.small[0:1, 7:8], OP.mult), inc="v_bnd")
    P.wait("scalar", "v_bnd")
    P.op("scalar", lambda sc: sc.activation(
        T.small[0:1, 9:10], T.small[0:1, 8:9], AF.Sqrt, scale=SCALE * SCALE * 1.05),
        inc="s_bnd")
    P.wait("vector", "s_bnd")
    P.op("vector", lambda v: v.tensor_scalar(
        T.small[0:1, 10:11], T.small[0:1, 9:10], -1.0, -1.0, OP.mult, OP.add), inc="v_bnd")
    P.wait("tensor", "v_bnd")
    P.wait("tensor", "v_qn")
    P.op("tensor", lambda t: t.matmul(
        T.psm[:, 0:1], T.ones1f[:], T.small[0:1, 10:11], start=True, stop=True),
        inc="t_bnd")
    P.wait("vector", "t_bnd")
    P.op("vector", lambda v: v.tensor_copy(T.boundt, T.psm[:, 0:1]), inc="v_bnd")


def emit_l3_attn(P, T):
    SCALE = float(1.0 / np.sqrt(1040.0))
    P.wait("vector", "t_kr", P.cnt("t_kr"))
    P.wait("vector", "t_y", P.cnt("t_y"))
    P.op("vector", lambda v: v.memset(
        mk_ap(T.v65, 64, [[T.xn_fm.shape[1], 128], [130, 16], [65, 2]]), 1.0), inc="v_v65i")
    P.wait("gpsimd", "t_qn", P.cnt("t_qn"))
    P.wait("sync", "t_qn", P.cnt("t_qn"))

    def ksync(hp):
        P.wait("sync", "t_sc", P.cnt("t_sc"))

    def q_ap(hp, qg, h):
        return T.ar1b[64 * h:64 * h + 64,
                      16384 + 2048 * hp + 512 * qg:16384 + 2048 * hp + 512 * (qg + 1)]

    def finish(hp, qg):
        for h in (0, 1):
                pav = T.ps[4 + h]
                dst = T.ar1b[64 * h:64 * (h + 1),
                             16384 + 2048 * hp + 512 * qg:16384 + 2048 * hp + 512 * (qg + 1)]
                P.wait("vector", "t_av")
                P.drain("vector")
                P.op("vector", lambda v, pav=pav: v.reciprocal(
                    T.scratch[0:1, 0:512], pav[64:65, 0:512]), inc="v_rc")
                P.wait("tensor", "v_rc")
                P.wait("tensor", "v_av", P.cnt("v_av"))
                P.op("tensor", lambda t: t.matmul(
                    T.psm[:, 0:512], T.ones1f[:], T.scratch[0:1, 0:512],
                    start=True, stop=True), inc="t_rcb")
                P.wait("vector", "t_rcb")
                P.op("vector", lambda v, pav=pav, dst=dst: v.tensor_copy(
                    dst, pav[0:64, 0:512]))
                P.drain("vector")
                P.op("vector", lambda v, dst=dst: v.tensor_tensor(
                    dst, dst, T.psm[0:64, 0:512], OP.mult), inc="v_av")
                P.drain("vector")

    emit_flash(P, T, T.l3_kout, T.l3_vout, q_ap, SCALE, ksync, finish)


def emit_l3_wo(P, T):
    P.wait("gpsimd", "t_av", P.cnt("t_av"))
    for c in range(NDC):
        P.dma("gpsimd", T.ar2[:, 1024 * c:1024 * (c + 1)],
              T.l3_wo[128 * c:128 * (c + 1), :], "d_wo3")
    P.dma("gpsimd", T.nstage[0:1, 0:1024], T.l3_bo, "d_wo3")
    P.wait("tensor", "d_wo3")
    P.wait("tensor", "v_av", P.cnt("v_av"))
    for tt in range(16):
        for nc2 in range(2):
            bank = T.ps[(tt * 2 + nc2) % 2]
            P.wait("tensor", "v_fin", P.cnt("v_fin") - 1)
            for c in range(NDC):
                P.op("tensor", lambda t, b=bank, c=c, tt=tt, n=nc2: t.matmul(
                    b[:, 0:512],
                    T.ar1b[:, 16384 + 2048 * c + 128 * tt:16384 + 2048 * c + 128 * (tt + 1)],
                    T.ar2[:, 1024 * c + 512 * n:1024 * c + 512 * (n + 1)],
                    start=(c == 0), stop=False), inc=None)
            P.op("tensor", lambda t, b=bank, n=nc2: t.matmul(
                b[:, 0:512], T.ones1[:], T.nstage[0:1, 512 * n:512 * (n + 1)],
                start=False, stop=True), inc="t_wo3")
            P.wait("vector", "t_wo3")
            P.wait("vector", "s_g", P.cnt("s_g"))
            P.drain("vector")
            P.op("vector", lambda v, b=bank, tt=tt, n=nc2: v.tensor_tensor(
                T.scratch[:, 0:512], b[:, 0:512],
                T.ar1b[:, 1024 * tt + 512 * n:1024 * tt + 512 * (n + 1)], OP.mult))
            P.drain("vector")
            P.op("vector", lambda v, tt=tt, n=nc2: v.tensor_tensor(
                T.x_res[:, 1024 * tt + 512 * n:1024 * tt + 512 * (n + 1)],
                T.x_res[:, 1024 * tt + 512 * n:1024 * tt + 512 * (n + 1)],
                T.scratch[:, 0:512], OP.add), inc="v_fin")
            P.drain("vector")
